# version 03
# baseline (speedup 1.0000x reference)
import numpy as np
import ml_dtypes
from contextlib import ExitStack
import concourse.bass as bass
import concourse.mybir as mybir
from concourse.bass_utils import run_bass_kernel_spmd

F32 = mybir.dt.float32
BF16 = mybir.dt.bfloat16
I32 = mybir.dt.int32
AF = mybir.ActivationFunctionType
ALU = mybir.AluOpType
AX = mybir.AxisListType
BF16NP = ml_dtypes.bfloat16

ENGINES = ("sync", "gpsimd", "tensor", "vector", "scalar")

B, T, D = 4, 8192, 1024
TL = 4096
HALO = 2048
NCOL = HALO + TL
TC = 512
NKT = D // 128



class Sem:
    __slots__ = ("h", "n", "name", "base", "home")

    def __init__(self, h, name):
        self.h, self.n, self.name = h, 0, name
        self.base = 0
        self.home = None


SELF_CHAIN = ("vector", "scalar", "gpsimd")


class Prog:

    ARENA_BYTES = 200 * 1024
    PSUM_BANKS = 8

    def __init__(self, nc, st):
        self.nc = nc
        self.st = st
        self.ops = {e: [] for e in ENGINES}
        self._semi = 0
        self._sbi = 0
        self._sem_pool = []
        self._sem_ptr = 0
        self._swdge_ok = set()
        self._dma_active = {}
        self.chain = {}
        for e in SELF_CHAIN:
            self.chain[e] = self.sem(f"ch_{e}")
        self._sem_pool = []
        self._sem_ptr = 0
        self.arena = st.enter_context(
            nc.sbuf_tensor("arena", [128, self.ARENA_BYTES], mybir.dt.uint8)
        )
        self.arena_off = 0
        self.psum_arena = st.enter_context(nc.psum_tensor("psarena", [128, 4096], F32))
        self.psum_off = 0

    def mark(self):
        return (self.arena_off, self.psum_off, self._sem_ptr)

    def reset(self, m):
        self.arena_off, self.psum_off = m[0], m[1]
        if len(m) > 2:
            self._sem_ptr = m[2]

    def sem(self, name=None):
        if self._sem_ptr < len(self._sem_pool):
            sm = self._sem_pool[self._sem_ptr]
        else:
            self._semi += 1
            nm = f"s{self._semi}"
            h = self.st.enter_context(self.nc.semaphore(nm))
            sm = Sem(h, nm)
            self._sem_pool.append(sm)
        self._sem_ptr += 1
        if id(sm) in self._swdge_ok:
            sm.h, sm.n = sm.home
            sm.home = None
            self._swdge_ok.discard(id(sm))
        sm.base = sm.n
        return sm

    def sems(self, name, k):
        return [self.sem(f"{name}{i}") for i in range(k)]

    def sb(self, shape, dtype, name=None):
        nbytes = int(shape[1]) * {F32: 4, BF16: 2, I32: 4}[dtype]
        nbytes = (nbytes + 31) & ~31
        assert self.arena_off + nbytes <= self.ARENA_BYTES, (
            "SBUF arena overflow", name, self.arena_off, nbytes)
        ap = self.arena[0 : shape[0], self.arena_off : self.arena_off + nbytes].bitcast(dtype)
        self.arena_off += nbytes
        return ap[0 : shape[0], 0 : int(shape[1])]

    def ps(self, shape, dtype=F32, name=None):
        eb = {F32: 4, BF16: 2}[dtype]
        nbanks = (int(shape[1]) * eb + 2047) // 2048
        assert self.psum_off + nbanks <= self.PSUM_BANKS, ("PSUM overflow", name)
        b0 = self.psum_off
        self.psum_off += nbanks
        ap = self.psum_arena[:, b0 * 512 : (b0 + nbanks) * 512]
        if dtype != F32:
            ap = ap.bitcast(dtype)
        return ap[0 : shape[0], 0 : int(shape[1])]

    def raw(self, eng, fn):
        self.ops[eng].append(fn)

    def cop(self, eng, fn):
        ch = self.chain[eng]
        self.ops[eng].append(lambda engine, h=ch.h, v=ch.n: engine.wait_ge(h, v))
        ch.n += 1
        self.ops[eng].append(lambda engine, fn=fn, h=ch.h: fn(engine).then_inc(h, 1))
        return ch.n

    def top(self, fn, sem=None, amt=1):
        if sem is None:
            self.ops["tensor"].append(fn)
            return None
        sem.n += amt
        self.ops["tensor"].append(
            lambda engine, fn=fn, h=sem.h, a=amt: fn(engine).then_inc(h, a)
        )
        return sem.n

    def dma(self, q, out, in_, sem, **kw):
        if q == "gpsimd" and id(sem) not in self._swdge_ok:
            self._semi += 1
            nm = f"sw{self._semi}"
            sem.home = (sem.h, sem.n)
            sem.h = self.st.enter_context(self.nc.semaphore(nm))
            sem.n = 0
            sem.base = 0
            self._swdge_ok.add(id(sem))
        sem.n += 16
        h = sem.h
        self._dma_active[id(sem)] = sem

        def run(engine, out=out, in_=in_, h=h, kw=kw):
            engine.dma_start(out=out, in_=in_, **kw).then_inc(h, 16)

        self.ops[q].append(run)
        return sem.n

    def wait(self, eng, sem, val=None):
        v = sem.n if val is None else val
        self.ops[eng].append(lambda engine, h=sem.h, v=v: engine.wait_ge(h, v))
        return v

    def vchain(self, eng):
        return self.chain[eng].n

    def barrier(self, waits=()):
        bs = self.sem()
        for e in ENGINES:
            for sm, v in waits:
                self.wait(e, sm, v)
        for e in SELF_CHAIN:
            self.wait(e, self.chain[e])
        for sm in self._dma_active.values():
            self.wait("sync", sm)
        self._dma_active = {}
        for e in ENGINES:
            bs.n += 1
            self.ops[e].append(lambda eng, h=bs.h: eng.nop().then_inc(h, 1))
        for e in ENGINES:
            self.wait(e, bs, bs.n)
        return bs

    def build(self):
        block = self.st.enter_context(self.nc.Block())
        for e in ENGINES:
            fns = self.ops[e]
            if not fns:
                continue

            def body(engine, fns=fns):
                for f in fns:
                    f(engine)

            getattr(block, e)(body)




def pack_w(w):
    d_in, d_out = w.shape
    nkt = d_in // 128
    return np.ascontiguousarray(
        w.reshape(nkt, 128, d_out).transpose(1, 0, 2).reshape(128, nkt * d_out)
    ).astype(np.float32)


def pack_wT(w):
    return pack_w(np.ascontiguousarray(w.T))


def pack_vec(v, pad_to=None):
    d = v.shape[0]
    if pad_to is not None and d < pad_to:
        v = np.concatenate([v, np.zeros(pad_to - d, v.dtype)])
        d = pad_to
    return np.ascontiguousarray(v.reshape(d // 128, 128).T).astype(np.float32)


def pad_cols(w, to):
    d_in, d_out = w.shape
    if d_out == to:
        return w
    return np.concatenate([w, np.zeros((d_in, to - d_out), w.dtype)], axis=1)


def pad_rows(w, to):
    d_in, d_out = w.shape
    if d_in == to:
        return w
    return np.concatenate([w, np.zeros((to - d_in, d_out), w.dtype)], axis=0)




def emit_gemm(
    p,
    *,
    name,
    src,
    src_c0,
    wsrc,
    d_in,
    d_out,
    cols,
    epi,
    pre=None,
    src_ready=None,
    tc=TC,
    n_ps=4,
    w_bf16=False,
    shift_half=False,
):
    nkt = d_in // 128
    not_ = (d_out + 127) // 128
    nch = cols // tc
    assert cols % tc == 0, (name, cols, tc)

    W = p.sb([128, nkt * d_out], BF16, f"{name}_W")
    X = [p.sb([128, nkt * tc], BF16, f"{name}_X{i}") for i in range(2)]
    PS = [p.ps([128, tc], F32, f"{name}_PS{i}") for i in range(n_ps)]
    s_wp = p.sem(f"{name}_wp")
    s_xp = p.sems(f"{name}_xp", 2)
    s_mm = p.sem(f"{name}_mm")

    p.dma("sync" if w_bf16 else "gpsimd", W[:, :], wsrc, s_wp)
    p.wait("tensor", s_wp)

    psum_freed = []
    for c in range(nch):
        tc_here = tc
        xs = c % 2
        if c >= 2:
            p.wait("sync", s_mm, s_mm.base + (c - 1) * not_)
        if src_ready is not None:
            sm, v = src_ready(c)
            p.wait("sync", sm, v)
        if not shift_half:
            p.dma(
                "sync",
                X[xs][:, :].rearrange("q (kt t) -> q kt t", t=tc),
                src.rearrange("(kt q) t -> q kt t", q=128)[
                    :, :, src_c0 + c * tc : src_c0 + (c + 1) * tc
                ],
                s_xp[xs],
            )
        else:
            hk = nkt // 2
            p.dma(
                "sync",
                X[xs][:, : hk * tc].rearrange("q (kt t) -> q kt t", t=tc),
                src.rearrange("(kt q) t -> q kt t", q=128)[
                    :, :hk, src_c0 + c * tc : src_c0 + (c + 1) * tc
                ],
                s_xp[xs],
            )
            sc0 = max(src_c0 + c * tc - 1, 0)
            p.dma(
                "sync",
                X[xs][:, hk * tc :].rearrange("q (kt t) -> q kt t", t=tc),
                src.rearrange("(kt q) t -> q kt t", q=128)[:, hk:, sc0 : sc0 + tc],
                s_xp[xs],
            )
        p.wait("tensor", s_xp[xs])
        if pre is not None:
            pre(p, c, tc_here)
        for o in range(not_):
            grp = c * not_ + o
            b = grp % n_ps
            ow = min(128, d_out - o * 128)
            if grp >= n_ps:
                sm, v = psum_freed[grp - n_ps]
                p.wait("tensor", sm, v)

            def mms(t, xs=xs, o=o, b=b, ow=ow, tc_here=tc_here):
                for k in range(nkt):
                    ins = t.matmul(
                        PS[b][:ow, :tc_here],
                        W[:, k * d_out + o * 128 : k * d_out + o * 128 + ow],
                        X[xs][:, k * tc : k * tc + tc_here],
                        start=(k == 0),
                        stop=(k == nkt - 1),
                    )
                return ins

            p.top(mms, sem=s_mm)
            psum_freed.append(epi(p, c, o, PS[b][:ow, :tc_here], tc_here, s_mm))
    return s_mm


def epi_store(
    p, *, name, dst, dst_c0, bias_sb=None, act=None, scale=None, tc=TC,
    eng="vector", nslots=4, extra=None,
):
    OB = [p.sb([128, tc], dst.dtype, f"{name}_OB{i}") for i in range(nslots)]
    s_oc = p.sems(f"{name}_oc", nslots)
    ctr = [0]
    engine_for_act = "scalar"

    def epi(p, c, o, psum, tc_here, s_mm):
        i = ctr[0]
        ctr[0] += 1
        sl = i % nslots
        e = engine_for_act if act is not None else eng
        p.wait(e, s_mm)
        if i >= nslots:
            p.wait(e, s_oc[sl])

        def ev(v, psum=psum, sl=sl, tc_here=tc_here, o=o):
            ob = OB[sl][: psum.shape[0], :tc_here]
            if act is not None:
                return v.activation(
                    ob, psum, act,
                    bias=bias_sb[:, o : o + 1] if bias_sb is not None else 0.0,
                    scale=1.0 if scale is None else float(scale),
                )
            if bias_sb is not None:
                return v.tensor_scalar(ob, psum, bias_sb[:, o : o + 1], None, ALU.add)
            if scale is not None:
                return v.tensor_scalar(ob, psum, float(scale), None, ALU.mult)
            return v.tensor_copy(ob, psum)

        val = p.cop(e, ev)
        if extra is not None:
            extra(p, e, OB[sl], c, o, tc_here)
            val = p.vchain(e)
        p.wait("scalar", p.chain[e], val)
        p.dma(
            "scalar",
            dst[o * 128 : o * 128 + psum.shape[0], dst_c0 + c * tc : dst_c0 + c * tc + tc_here],
            OB[sl][: psum.shape[0], :tc_here],
            s_oc[sl],
        )
        return (p.chain[e], val)

    return epi, s_oc




def emit_lngb(p, *, name, xsrc, xdst, c0, cols, gamma_sb=None, beta_sb=None,
              src_f32=True, eps=1e-5, tc=TC, dst_c0=None, shift_half=False):
    if dst_c0 is None:
        dst_c0 = c0

    def _load(p, buf, c, sem):
        if not shift_half:
            p.dma(
                "sync",
                buf[:, :].rearrange("q (kt t) -> q kt t", t=tc),
                xsrc.rearrange("(kt q) t -> q kt t", q=128)[
                    :, :, c0 + c * tc : c0 + (c + 1) * tc
                ],
                sem,
            )
        else:
            hk = NKT // 2
            p.dma(
                "sync",
                buf[:, : hk * tc].rearrange("q (kt t) -> q kt t", t=tc),
                xsrc.rearrange("(kt q) t -> q kt t", q=128)[
                    :, :hk, c0 + c * tc : c0 + (c + 1) * tc
                ],
                sem,
            )
            p.dma(
                "sync",
                buf[:, hk * tc :].rearrange("q (kt t) -> q kt t", t=tc),
                xsrc.rearrange("(kt q) t -> q kt t", q=128)[
                    :, hk:, c0 + c * tc - 1 : c0 + (c + 1) * tc - 1
                ],
                sem,
            )
    """LN over partitions with optional per-partition affine (gamma/beta columns
    (128, NKT)). xsrc dtype fp32 or bf16; xdst bf16."""
    nch = cols // tc
    sdt = F32 if src_f32 else BF16
    X = [p.sb([128, NKT * tc], sdt, f"{name}_X{i}") for i in range(2)]
    XB = [p.sb([128, NKT * tc], BF16, f"{name}_XB{i}") for i in range(2)] if src_f32 else X
    SQ = [p.sb([128, tc], BF16, f"{name}_SQ{i}") for i in range(2)]
    ONES = p.sb([128, 1], BF16, f"{name}_ones")
    ONES1 = p.sb([1, 128], F32, f"{name}_ones1")
    STATS = p.sb([1, 2 * nch * tc], F32, f"{name}_stats")
    AB = p.sb([1, 2 * nch * tc], F32, f"{name}_ab")
    VAR = p.sb([1, nch * tc], F32, f"{name}_var")
    EPST = p.sb([1, 1], F32, f"{name}_eps")
    PSSTa = p.ps([1, tc], F32, f"{name}_pssta")
    PSSTb = p.ps([1, tc], F32, f"{name}_psstb")
    PSBC = [p.ps([128, tc], F32, f"{name}_psbc{i}") for i in range(2)]
    s_xp = p.sems(f"{name}_xp", 2)
    s_x2p = p.sems(f"{name}_x2p", 2)
    s_mmk = p.sem(f"{name}_mmk")
    s_bc = p.sem(f"{name}_bc")
    s_oc = p.sems(f"{name}_oc", 2)
    chv = p.chain["vector"]
    chs = p.chain["scalar"]

    p.cop("vector", lambda v: v.memset(ONES[:, :], 1.0))
    p.cop("vector", lambda v: v.memset(ONES1[:, :], 1.0))
    init_done = p.cop("vector", lambda v: v.memset(EPST[:, :], float(eps)))
    p.wait("tensor", chv, init_done)

    cast_vals = {}
    for c in range(nch):
        xs = c % 2
        if c >= 2:
            if src_f32:
                p.wait("sync", chv, cast_vals[c - 2])
            else:
                p.wait("sync", s_mmk, s_mmk.base + (c - 1) * NKT)
        _load(p, X[xs], c, s_xp[xs])
        if src_f32:
            p.wait("vector", s_xp[xs])
            if c >= 2:
                p.wait("vector", s_mmk, s_mmk.base + (c - 1) * NKT)
            cast_vals[c] = p.cop(
                "vector", lambda v, xs=xs: v.tensor_copy(XB[xs][:, :], X[xs][:, :])
            )
            p.wait("scalar", chv, cast_vals[c])
            p.wait("tensor", chv, cast_vals[c])
        else:
            p.wait("scalar", s_xp[xs])
            p.wait("tensor", s_xp[xs])
        mmv = {}
        for k in range(NKT):
            sqs = k % 2
            if k >= 2:
                p.wait("scalar", s_mmk, mmv[k - 2])
            elif c > 0:
                p.wait("scalar", s_mmk, s_mmk.base + c * NKT - 2 + k + 1)

            def sqf(sc, xs=xs, k=k, sqs=sqs):
                return sc.activation(SQ[sqs][:, :], XB[xs][:, k * tc : (k + 1) * tc], AF.Square)

            sv = p.cop("scalar", sqf)
            p.wait("tensor", chs, sv)

            def mm_sq(t, xs=xs, k=k, sqs=sqs):
                t.matmul(PSSTa[0:1, 0:tc], ONES[:, :], XB[xs][:, k * tc : (k + 1) * tc],
                         start=(k == 0), stop=(k == NKT - 1))
                return t.matmul(PSSTb[0:1, 0:tc], ONES[:, :], SQ[sqs][:, :],
                                start=(k == 0), stop=(k == NKT - 1))

            mmv[k] = p.top(mm_sq, sem=s_mmk)
        p.wait("vector", s_mmk, s_mmk.base + (c + 1) * NKT)
        p.cop(
            "vector",
            lambda v, c=c: v.tensor_copy(STATS[0:1, c * 2 * tc : c * 2 * tc + tc], PSSTa[0:1, :]),
        )
        ev = p.cop(
            "vector",
            lambda v, c=c: v.tensor_copy(
                STATS[0:1, c * 2 * tc + tc : (c + 1) * 2 * tc], PSSTb[0:1, :]
            ),
        )
        p.wait("tensor", chv, ev)

    stats3 = STATS[0:1, :].rearrange("o (c two t) -> o c two t", two=2, t=tc)
    sums = stats3[:, :, 0, :]
    sqs_ = stats3[:, :, 1, :]
    ab3 = AB[0:1, :].rearrange("o (c two t) -> o c two t", two=2, t=tc)
    a_v = ab3[:, :, 0, :]
    b_v = ab3[:, :, 1, :]
    var2 = VAR[0:1, :].rearrange("o (c t) -> o c t", t=tc)
    p.cop("vector", lambda v: v.tensor_scalar(a_v, sums, 1.0 / D, None, ALU.mult))
    p.cop("vector", lambda v: v.tensor_scalar(var2, sqs_, 1.0 / D, None, ALU.mult))
    p.cop("vector", lambda v: v.tensor_mul(b_v, a_v, a_v))
    vv = p.cop("vector", lambda v: v.tensor_sub(var2, var2, b_v))
    p.wait("scalar", chv, vv)
    p.cop("scalar", lambda s: s.activation(VAR[0:1, :], VAR[0:1, :], AF.Ln, bias=EPST[0:1, 0:1]))
    sv = p.cop("scalar", lambda s: s.activation(VAR[0:1, :], VAR[0:1, :], AF.Exp, scale=-0.5))
    p.wait("vector", chs, sv)
    p.cop("vector", lambda v: v.tensor_mul(b_v, a_v, var2))
    p.cop("vector", lambda v: v.tensor_scalar(b_v, b_v, -1.0, None, ALU.mult))
    ab_ready = p.cop("vector", lambda v: v.tensor_copy(a_v, var2))
    p.wait("tensor", chv, ab_ready)

    X2 = X
    XO = XB
    p2_vals = {}
    done = []
    for c in range(nch):
        xs = c % 2
        if c >= 2:
            p.wait("sync", chv, p2_vals[c - 2])
            if not src_f32:
                p.wait("sync", s_oc[xs])
        else:
            p.wait("sync", chv, ab_ready)
        _load(p, X2[xs], c, s_x2p[xs])
        if c >= 1:
            p.wait("tensor", chv, p2_vals[c - 1])

        def bc(t, c=c):
            t.matmul(PSBC[0][:, :], ONES1[:, :], a_v[:, c, :], start=True, stop=True)
            return t.matmul(PSBC[1][:, :], ONES1[:, :], b_v[:, c, :], start=True, stop=True)

        p.top(bc, sem=s_bc)
        p.wait("vector", s_bc)
        p.wait("vector", s_x2p[xs])
        if c >= 2:
            p.wait("vector", s_oc[xs])
        TMP = X2
        for k in range(NKT):
            p.cop(
                "vector",
                lambda v, xs=xs, k=k: v.tensor_tensor(
                    X2[xs][:, k * tc : (k + 1) * tc],
                    X2[xs][:, k * tc : (k + 1) * tc],
                    PSBC[0][:, :],
                    ALU.mult,
                ),
            )
            if gamma_sb is None:
                v2 = p.cop(
                    "vector",
                    lambda v, xs=xs, k=k: v.tensor_tensor(
                        XO[xs][:, k * tc : (k + 1) * tc],
                        X2[xs][:, k * tc : (k + 1) * tc],
                        PSBC[1][:, :],
                        ALU.add,
                    ),
                )
            else:
                p.cop(
                    "vector",
                    lambda v, xs=xs, k=k: v.tensor_tensor(
                        X2[xs][:, k * tc : (k + 1) * tc],
                        X2[xs][:, k * tc : (k + 1) * tc],
                        PSBC[1][:, :],
                        ALU.add,
                    ),
                )
                v2 = p.cop(
                    "vector",
                    lambda v, xs=xs, k=k: v.tensor_scalar(
                        XO[xs][:, k * tc : (k + 1) * tc],
                        X2[xs][:, k * tc : (k + 1) * tc],
                        gamma_sb[:, k : k + 1],
                        beta_sb[:, k : k + 1],
                        ALU.mult,
                        ALU.add,
                    ),
                )
        p2_vals[c] = v2
        p.wait("scalar", chv, v2)
        p.dma(
            "scalar",
            xdst.rearrange("(kt q) t -> q kt t", q=128)[
                :, :, dst_c0 + c * tc : dst_c0 + (c + 1) * tc
            ],
            XO[xs][:, :].rearrange("q (kt t) -> q kt t", t=tc),
            s_oc[xs],
        )
        done.append((s_oc[xs], s_oc[xs].n))
    return done






def emit_toeplitz_build(p, *, name, z_t, z_off, t_t, t_off, nch=1024, K=128):
    s_bt = p.sem(f"{name}_bt")
    for q in range(128):
        p.dma(
            "scalar",
            bass.AP(t_t, t_off + q * nch * K, [[K, nch], [1, K]]),
            bass.AP(z_t, z_off + K - q, [[2 * K, nch], [1, K]]),
            s_bt,
        )
    return s_bt


def emit_l0attn(p, d, *, name):
    C0P = HALO - 512
    PCOLS = 4608
    NB = 36
    TSZ = 128 * 1024 * 128

    m0 = p.mark()

    s_zc = p.sem(f"{name}_zc")
    zb = d["zb"]
    for i, znm in enumerate(("zlo0", "zhi0", "zlo1", "zhi1")):
        p.dma("gpsimd", zb[i], d[znm], s_zc)
    p.wait("scalar", s_zc)
    s_bt = [
        emit_toeplitz_build(
            p, name=f"{name}_tb{i}", z_t=zb.tensor, z_off=i * 1024 * 256,
            t_t=d["tmats"].tensor, t_off=i * TSZ,
        )
        for i in range(4)
    ]

    GB = p.sb([128, 16], F32, f"{name}_gb")
    BIN = p.sb([128, 24], F32, f"{name}_bin")
    BOUT = p.sb([128, 8], F32, f"{name}_bout")
    IDENT = p.sb([128, 128], BF16, f"{name}_id")
    HMC = p.sb([128, 1], F32, f"{name}_hmc")
    s_vec = p.sem(f"{name}_vec")
    p.dma("sync", HMC[:, :], d["hm_col"], s_vec)
    p.dma("sync", GB[:, :], d["gb_p"], s_vec)
    p.dma("sync", BIN[:, :], d["b_in_p"], s_vec)
    p.dma("sync", BOUT[:, :], d["b_out_p"], s_vec)
    s_idl = p.sem(f"{name}_idl")
    p.dma("gpsimd", IDENT[:, :], d["ident_p"], s_idl)
    p.wait("vector", s_vec)
    p.wait("scalar", s_vec)
    p.wait("tensor", s_vec)
    p.wait("tensor", s_idl)

    mln = p.mark()
    ln_done = emit_lngb(
        p, name=f"{name}_ln", xsrc=d["xn"], xdst=d["xln"], c0=C0P, cols=PCOLS,
        gamma_sb=GB[:, 0:8], beta_sb=GB[:, 8:16], src_f32=False, dst_c0=0,
    )
    p.barrier(waits=ln_done)
    p.reset(mln)

    epi, s_oc_pj = epi_store(p, name=f"{name}_pj", dst=d["proj"], dst_c0=0, bias_sb=BIN)
    emit_gemm(
        p, name=f"{name}_pg", src=d["xln"], src_c0=0, wsrc=d["w_in_p"],
        d_in=D, d_out=3072, cols=PCOLS, epi=epi,
    )
    p.barrier(waits=[(sm, sm.n) for sm in s_oc_pj])
    p.reset(mln)

    TB = [p.sb([128, 128 * 128], BF16, f"{name}_TB{i}") for i in range(2)]
    VT = [p.sb([128, NB * 128], BF16, f"{name}_VT{i}") for i in range(3)]
    Y0G = p.sb([128, NB * 128], BF16, f"{name}_y0g")
    Y1G = p.sb([128, 32 * 128], BF16, f"{name}_y1g")
    YO = p.sb([128, 32 * 128], BF16, f"{name}_yo")
    PSC = [p.ps([128, 35], F32) for i in range(4)]
    PSTa = p.ps([128, 128], BF16)
    PSTb = p.ps([128, 128], BF16)
    s_tb = p.sems(f"{name}_stb", 2)
    s_vt = p.sems(f"{name}_svt", 3)
    s_cm = p.sem(f"{name}_cm")
    s_tr = p.sem(f"{name}_tr")
    s_yo = p.sem(f"{name}_yo")
    chv = p.chain["vector"]

    for i in range(4):
        p.wait("sync", s_bt[i])
        p.wait("scalar", s_bt[i])

    gate2_last = 0
    conv_state = {}
    for dt_i in range(8):
        ch0 = dt_i * 128
        if dt_i > 0:
            p.wait("sync", s_cm, conv_state["cm_after_conv1"])
        p.dma("sync", TB[0][:, :],
              bass.AP(d["tmats"].tensor, 0 * TSZ + ch0 * 128, [[1024 * 128, 128], [1, 128 * 128]]),
              s_tb[0])
        p.dma("sync", TB[1][:, :],
              bass.AP(d["tmats"].tensor, 1 * TSZ + ch0 * 128, [[1024 * 128, 128], [1, 128 * 128]]),
              s_tb[1])
        tb_ready = (s_tb[0].n, s_tb[1].n)
        if dt_i > 0:
            p.wait("sync", s_cm, conv_state["cm_after_conv0"])
            p.wait("sync", chv, conv_state["gate1_last"])
            p.wait("sync", chv, conv_state["gate2_last"])
        for vi in range(3):
            for tb_j in range(NB):
                p.dma(
                    "sync",
                    VT[vi][:, tb_j * 128 : (tb_j + 1) * 128],
                    d["proj"][vi * 1024 + ch0 : vi * 1024 + ch0 + 128,
                              tb_j * 128 : (tb_j + 1) * 128],
                    s_vt[vi],
                    transpose=True,
                )
        vt_ready = [s_vt[vi].n for vi in range(3)]
        for vi in range(3):
            p.wait("vector", s_vt[vi], vt_ready[vi])
        for vi in range(3):
            p.cop("vector", lambda v, vi=vi: v.tensor_scalar(
                VT[vi][:, 0:512], VT[vi][:, 0:512], HMC[:, 0:1], None, ALU.mult))
        zdone = p.vchain("vector")
        p.wait("tensor", chv, zdone)

        p.wait("tensor", s_tb[0], tb_ready[0])
        p.wait("tensor", s_tb[1], tb_ready[1])
        p.wait("tensor", s_vt[0], vt_ready[0])
        p.wait("vector", s_vt[1], vt_ready[1])

        if dt_i > 0:
            p.wait("tensor", chv, conv_state["gate2_last"])
        gate1_vals = {}
        for ch in range(128):
            bnk = ch % 4
            if ch >= 4:
                p.wait("tensor", chv, gate1_vals[ch - 4])

            def c0mm(t, ch=ch, bnk=bnk):
                t.matmul(PSC[bnk][:, :], TB[0][:, ch * 128 : (ch + 1) * 128],
                         VT[0][:, 128 + ch : 128 + ch + 34 * 128 + 1 : 128],
                         start=True, stop=False)
                return t.matmul(PSC[bnk][:, :], TB[1][:, ch * 128 : (ch + 1) * 128],
                                VT[0][:, ch : ch + 34 * 128 + 1 : 128],
                                start=False, stop=True)

            mmv = p.top(c0mm, sem=s_cm)
            p.wait("vector", s_cm, mmv)
            gate1_vals[ch] = p.cop(
                "vector",
                lambda v, ch=ch, bnk=bnk: v.tensor_tensor(
                    Y0G[:, 128 + ch : 128 + ch + 34 * 128 + 1 : 128],
                    PSC[bnk][:, :],
                    VT[1][:, 128 + ch : 128 + ch + 34 * 128 + 1 : 128],
                    ALU.mult,
                ),
            )
        conv_state["cm_after_conv0"] = s_cm.n
        conv_state["gate1_last"] = gate1_vals[127]

        p.wait("sync", s_cm, conv_state["cm_after_conv0"])
        p.dma("sync", TB[0][:, :],
              bass.AP(d["tmats"].tensor, 2 * TSZ + ch0 * 128, [[1024 * 128, 128], [1, 128 * 128]]),
              s_tb[0])
        p.dma("sync", TB[1][:, :],
              bass.AP(d["tmats"].tensor, 3 * TSZ + ch0 * 128, [[1024 * 128, 128], [1, 128 * 128]]),
              s_tb[1])
        p.wait("tensor", s_tb[0])
        p.wait("tensor", s_tb[1])
        p.wait("vector", s_vt[2], vt_ready[2])

        gate2_vals = {}
        p.wait("tensor", chv, conv_state["gate1_last"])
        for ch in range(128):
            bnk = ch % 4
            if ch >= 4:
                p.wait("tensor", chv, gate2_vals[ch - 4])

            def c1mm(t, ch=ch, bnk=bnk):
                t.matmul(PSC[bnk][:, :32], TB[0][:, ch * 128 : (ch + 1) * 128],
                         Y0G[:, 4 * 128 + ch : 4 * 128 + ch + 31 * 128 + 1 : 128],
                         start=True, stop=False)
                return t.matmul(PSC[bnk][:, :32], TB[1][:, ch * 128 : (ch + 1) * 128],
                                Y0G[:, 3 * 128 + ch : 3 * 128 + ch + 31 * 128 + 1 : 128],
                                start=False, stop=True)

            mmv = p.top(c1mm, sem=s_cm)
            p.wait("vector", s_cm, mmv)
            gate2_vals[ch] = p.cop(
                "vector",
                lambda v, ch=ch, bnk=bnk: v.tensor_tensor(
                    Y1G[:, ch : ch + 31 * 128 + 1 : 128],
                    PSC[bnk][:, :32],
                    VT[2][:, 4 * 128 + ch : 4 * 128 + ch + 31 * 128 + 1 : 128],
                    ALU.mult,
                ),
            )
        conv_state["cm_after_conv1"] = s_cm.n
        conv_state["gate2_last"] = gate2_vals[127]

        p.wait("tensor", chv, gate2_vals[127])
        cp_vals = {}
        for tb_j in range(32):
            pst = PSTa if tb_j % 2 == 0 else PSTb
            if tb_j >= 2:
                p.wait("tensor", chv, cp_vals[tb_j - 2])
            if dt_i > 0 and tb_j < 2:
                p.wait("tensor", chv, conv_state.get("cp_last", 0))
            trv = p.top(
                lambda t, tb_j=tb_j, pst=pst: t.transpose(
                    pst[:, :], Y1G[:, tb_j * 128 : (tb_j + 1) * 128], IDENT[:, :]
                ),
                sem=s_tr,
            )
            p.wait("vector", s_tr, trv)
            if dt_i > 0 and tb_j == 0:
                p.wait("vector", s_yo, s_yo.base + (dt_i) * 16)
            cp_vals[tb_j] = p.cop(
                "vector",
                lambda v, tb_j=tb_j, pst=pst: v.tensor_copy(
                    YO[:, tb_j * 128 : (tb_j + 1) * 128], pst[:, :]
                ),
            )
        conv_state["cp_last"] = cp_vals[31]
        p.wait("scalar", chv, cp_vals[31])
        p.dma("scalar", d["ycv"][ch0 : ch0 + 128, :], YO[:, :], s_yo)

    p.barrier(waits=[(s_yo, s_yo.n)])
    p.reset(mln)

    RES = [p.sb([128, TC], BF16, f"{name}_res{i}") for i in range(4)]
    OBF = [p.sb([128, TC], F32, f"{name}_obf{i}") for i in range(4)]
    s_res = p.sems(f"{name}_res", 4)
    s_so = p.sems(f"{name}_so", 4)
    ctr = [0]

    res_consumed = {}

    def epi_wout(p, c, o, psum, tc_here, s_mm):
        i = ctr[0]
        ctr[0] += 1
        sl = i % 4
        if i >= 4:
            p.wait("sync", chv, res_consumed[i - 4])
        p.dma(
            "sync", RES[sl][:, :tc_here],
            d["xn"][o * 128 : (o + 1) * 128, HALO + c * TC : HALO + c * TC + tc_here],
            s_res[sl],
        )
        p.wait("vector", s_mm)
        p.wait("vector", s_res[sl])
        if i >= 4:
            p.wait("vector", s_so[sl])
        val = p.cop(
            "vector",
            lambda v, sl=sl, psum=psum, o=o, tc_here=tc_here: v.scalar_tensor_tensor(
                OBF[sl][:, :tc_here], psum, BOUT[:, o : o + 1], RES[sl][:, :tc_here],
                ALU.add, ALU.add,
            ),
        )
        res_consumed[i] = val
        p.wait("scalar", chv, val)
        p.dma(
            "scalar", d["s0"][o * 128 : (o + 1) * 128, c * TC : c * TC + tc_here],
            OBF[sl][:, :tc_here], s_so[sl],
        )
        return (chv, val)

    emit_gemm(
        p, name=f"{name}_wo", src=d["ycv"], src_c0=0, wsrc=d["w_out_p"],
        d_in=D, d_out=D, cols=TL, epi=epi_wout,
    )
    p.barrier(waits=[(sm, sm.n) for sm in s_so])
    p.reset(m0)


def l0attn_host(inp, core_args):
    K0 = np.asarray(inp["L0_attn_k0"])[0]
    K1 = np.asarray(inp["L0_attn_k1"])[0]
    Z = np.zeros((1024, 128), np.float32)
    out = {
        "zlo0": np.concatenate([Z, K0], 1),
        "zhi0": np.concatenate([K0, Z], 1),
        "zlo1": np.concatenate([Z, K1], 1),
        "zhi1": np.concatenate([K1, Z], 1),
        "w_in_p": pack_w(np.asarray(inp["L0_attn_W_in"])),
        "b_in_p": pack_vec(np.asarray(inp["L0_attn_b_in"])),
        "gb_p": np.concatenate(
            [pack_vec(np.asarray(inp["L0_attn_gamma"])), pack_vec(np.asarray(inp["L0_attn_beta"]))], 1
        ),
        "w_out_p": pack_w(np.asarray(inp["L0_attn_W_out"])),
        "b_out_p": pack_vec(np.asarray(inp["L0_attn_b_out"])),
        "ident_p": np.eye(128, dtype=np.float32),
    }
    return out



FFN_HID = 2816
FFN_W1O = 2 * FFN_HID


def epi_res_store(p, *, name, dst, dst_c0, res, res_c0, bias_sb=None, dtype=BF16,
                  post=None):
    RES = [p.sb([128, TC], BF16, f"{name}_res{i}") for i in range(4)]
    OBF = [p.sb([128, TC], dtype, f"{name}_obf{i}") for i in range(4)]
    s_res = p.sems(f"{name}_resp", 4)
    s_so = p.sems(f"{name}_so", 4)
    chv = p.chain["vector"]
    ctr = [0]
    consumed = {}

    def epi(p, c, o, psum, tc_here, s_mm):
        i = ctr[0]
        ctr[0] += 1
        sl = i % 4
        if i >= 4:
            p.wait("sync", chv, consumed[i - 4])
        p.dma(
            "sync", RES[sl][:, :tc_here],
            res[o * 128 : o * 128 + psum.shape[0], res_c0 + c * TC : res_c0 + c * TC + tc_here],
            s_res[sl],
        )
        p.wait("vector", s_mm)
        p.wait("vector", s_res[sl])
        if i >= 4:
            p.wait("vector", s_so[sl])
        if bias_sb is not None:
            p.cop(
                "vector",
                lambda v, sl=sl, psum=psum, o=o, tc_here=tc_here: v.scalar_tensor_tensor(
                    OBF[sl][: psum.shape[0], :tc_here], psum, bias_sb[:, o : o + 1],
                    RES[sl][:, :tc_here], ALU.add, ALU.add,
                ),
            )
        else:
            p.cop(
                "vector",
                lambda v, sl=sl, psum=psum, tc_here=tc_here: v.tensor_tensor(
                    OBF[sl][: psum.shape[0], :tc_here], psum, RES[sl][:, :tc_here], ALU.add
                ),
            )
        if post is not None:
            p.cop("vector", lambda v, sl=sl, c=c, o=o, tc_here=tc_here: post(
                v, OBF[sl][:, :tc_here], c, o, tc_here))
        val = p.vchain("vector")
        consumed[i] = val
        p.wait("scalar", chv, val)
        p.dma(
            "scalar", dst[o * 128 : o * 128 + psum.shape[0], dst_c0 + c * TC : dst_c0 + c * TC + tc_here],
            OBF[sl][: psum.shape[0], :tc_here], s_so[sl],
        )
        return (chv, val)

    return epi, s_so


def emit_ffn_inner(p, d, *, name, src, dst, q0, wk):
    kv0 = q0 - 512
    CK = HALO + kv0
    ncols_kv = TL - kv0
    nch_kv = ncols_kv // TC
    nkb = ncols_kv // 128
    nw = (TL - q0) // 512
    NH = 8
    m0 = p.mark()
    chv = p.chain["vector"]
    chs = p.chain["scalar"]
    chg = p.chain["gpsimd"]

    QKC = p.sb([128, 2], F32, f"{name}_qkc")
    QKR = p.sb([1, 128], F32, f"{name}_qkr")
    ROTQ = p.sb([128, 128], BF16, f"{name}_rot")
    IDT = p.sb([128, 128], BF16, f"{name}_idt")
    MSK = p.sb([128, 256], F32, f"{name}_msk")
    MSKH = p.sb([128, 128], F32, f"{name}_mskh")
    IND2 = p.sb([2, 128], BF16, f"{name}_ind2")
    IND2T = p.sb([128, 2], BF16, f"{name}_ind2t")
    ONESR = p.sb([1, 128], F32, f"{name}_onesr")
    NEGC = p.sb([128, 1], F32, f"{name}_negc")
    TMP1 = p.sb([1, 128], F32, f"{name}_tmp1")
    MX = p.sb([1, 4], F32, f"{name}_mx")
    s_k0 = p.sem(f"{name}_k0")
    s_k1 = p.sem(f"{name}_k1")
    p.dma("sync", QKC[:, :], wk["qk_col"], s_k0)
    p.dma("sync", QKR[:, :], wk["qk_row"], s_k0)
    p.dma("sync", MSK[:, :], wk["masks"], s_k0)
    p.dma("sync", MSKH[:, :], wk["maskh"], s_k0)
    p.dma("sync", ONESR[:, :], wk["ones_p"], s_k0)
    p.dma("gpsimd", ROTQ[:, :], wk["rotq_lhsT"], s_k1)
    p.dma("gpsimd", IDT[:, :], wk["ident_p"], s_k1)
    p.dma("gpsimd", IND2[:, :], wk["ind2"], s_k1)
    p.dma("gpsimd", IND2T[:, :], wk["ind2t"], s_k1)
    p.wait("vector", s_k0)
    p.wait("tensor", s_k0)
    p.wait("tensor", s_k1)
    p.cop("vector", lambda v: v.tensor_scalar(TMP1[0:1, :], QKR[0:1, :], -1.0, None, ALU.mult))
    p.cop("vector", lambda v: v.tensor_tensor(TMP1[0:1, :], TMP1[0:1, :], QKR[0:1, :], ALU.max))
    p.cop("vector", lambda v: v.reduce_max(MX[0:1, 0:1], TMP1[0:1, 0:64], axis=AX.X))
    p.cop("vector", lambda v: v.reduce_max(MX[0:1, 1:2], TMP1[0:1, 64:128], axis=AX.X))
    p.cop("vector", lambda v: v.tensor_mul(MX[0:1, 2:3], MX[0:1, 0:1], MX[0:1, 1:2]))
    cv = p.cop("vector", lambda v: v.tensor_scalar(MX[0:1, 2:3], MX[0:1, 2:3], -8.0, None, ALU.mult))
    p.wait("tensor", chv, cv)
    PSB = p.ps([128, 512], F32)
    s_bc0 = p.sem(f"{name}_bc0")
    p.top(lambda t: t.matmul(PSB[:, 0:1], ONESR[0:1, :], MX[0:1, 2:3], start=True, stop=True),
          sem=s_bc0)
    p.wait("vector", s_bc0)
    ncv = p.cop("vector", lambda v: v.tensor_copy(NEGC[:, :], PSB[:, 0:1]))
    p.wait("scalar", chv, ncv)

    ZC = p.sb([128, 8], BF16, f"{name}_zc")
    s_zc0 = p.sem(f"{name}_zc0")
    zcv = p.cop("vector", lambda v: v.memset(ZC[:, :], 0.0))
    p.wait("sync", chv, zcv)
    if CK >= 1:
        p.dma("sync",
              src.rearrange("(kt q) t -> q kt t", q=128)[:, :, CK - 1 : CK],
              ZC[:, :].rearrange("q (kt t) -> q kt t", t=1), s_zc0,
              allow_slow_non_contiguous=True)
    p.dma("sync",
          d["xmid"].rearrange("(kt q) t -> q kt t", q=128)[:, :, HALO + q0 - 1 : HALO + q0],
          ZC[:, :].rearrange("q (kt t) -> q kt t", t=1), s_zc0,
          allow_slow_non_contiguous=True)
    p.wait("sync", s_zc0)

    mqkv = p.mark()
    epi_q, s_oc_q = epi_store(p, name=f"{name}_qe", dst=d["qkv"], dst_c0=CK)
    emit_gemm(
        p, name=f"{name}_qg", src=src, src_c0=CK, wsrc=wk["wqkv_p"],
        d_in=D, d_out=1536, cols=ncols_kv, epi=epi_q, shift_half=True,
    )
    p.barrier(waits=[(sm, sm.n) for sm in s_oc_q])
    p.reset(mqkv)

    QR = p.sb([128, 4 * ncols_kv], BF16, f"{name}_QR")
    KR = p.sb([128, 4 * ncols_kv], BF16, f"{name}_KR")
    VTT = p.sb([128, 65 * NH * nkb], BF16, f"{name}_VTT")
    QKB = [p.sb([128, 4 * TC], BF16, f"{name}_qkb{i}") for i in range(2)]
    VB = [p.sb([128, 4 * TC], BF16, f"{name}_vb{i}") for i in range(2)]
    TAB = [p.sb([128, 2 * TC], BF16, f"{name}_tab{i}") for i in range(2)]
    SQB = p.sb([128, TC], BF16, f"{name}_sqb")
    SSF = p.sb([2, TC], F32, f"{name}_ssf")
    SSR = p.sb([2, TC], BF16, f"{name}_ssr")
    T1 = p.sb([128, TC], F32, f"{name}_t1")
    T2 = p.sb([128, TC], F32, f"{name}_t2")
    PS_SUM = p.ps([2, 512], F32)
    PS_BC = p.ps([128, 512], F32)
    PS_ROT = p.ps([128, 512], F32)
    PS_TR = p.ps([128, 128], BF16)
    PS_TR2 = p.ps([128, 128], BF16)
    s_qkl = p.sems(f"{name}_qkl", 2)
    s_vl = p.sems(f"{name}_vl", 2)
    s_tl = p.sems(f"{name}_tl", 2)
    s_sum = p.sem(f"{name}_sum")
    s_bcq = p.sem(f"{name}_bcq")
    s_rot = p.sem(f"{name}_rot")
    s_vtr = p.sem(f"{name}_vtr")

    ov = p.cop("vector", lambda v: v.memset(
        VTT[:, 64 : 64 + (NH * nkb - 1) * 65 + 1 : 65], 1.0))
    p.wait("tensor", chv, ov)

    prev = dict(sq=None, sm=None, sq2=None, bc=None, qn=None, rot=None, t2=None,
                g1=None, o2=None)
    prep_done = {}
    for qk in range(2):
        DSTB = QR if qk == 0 else KR
        for c in range(nch_kv):
            ci = qk * nch_kv + c
            sl = ci % 2
            if ci >= 2:
                p.wait("sync", chv, prep_done[ci - 2])
                p.wait("gpsimd", chv, prep_done[ci - 2])
            p.dma(
                "sync",
                QKB[sl][:, :].rearrange("q (kt t) -> q kt t", t=TC),
                d["qkv"].rearrange("(kt q) t -> q kt t", q=128)[
                    :, 4 * qk : 4 * qk + 4, CK + c * TC : CK + (c + 1) * TC
                ],
                s_qkl[sl],
            )
            p.dma(
                "gpsimd",
                TAB[sl][:, :].rearrange("q (a t) -> q a t", t=TC),
                wk["tab"].rearrange("(a q) t -> q a t", q=128)[
                    :, 2 * qk : 2 * qk + 2, c * TC : (c + 1) * TC
                ],
                s_tl[sl],
            )
            qkl_v = s_qkl[sl].n
            tl_v = s_tl[sl].n
            p.wait("vector", s_tl[sl], tl_v)
            p.wait("vector", s_qkl[sl], qkl_v)
            for kt in range(4):
                p.wait("scalar", s_qkl[sl], qkl_v)
                if prev["sm"] is not None:
                    p.wait("scalar", s_sum, prev["sm"])
                prev["sq"] = p.cop("scalar", lambda sc, sl=sl, kt=kt: sc.activation(
                    SQB[:, :], QKB[sl][:, kt * TC : (kt + 1) * TC], AF.Square))
                p.wait("tensor", chs, prev["sq"])
                prev["sm"] = p.top(lambda t: t.matmul(
                    PS_SUM[:, :], IND2T[:, :], SQB[:, :], start=True, stop=True), sem=s_sum)
                p.wait("scalar", s_sum, prev["sm"])
                prev["sq2"] = p.cop("scalar", lambda sc: sc.activation(SSF[:, :], PS_SUM[:, :], AF.Sqrt))
                p.wait("vector", chs, prev["sq2"])
                p.cop("vector", lambda v: v.tensor_scalar(SSF[:, :], SSF[:, :], 1e-12, None, ALU.max))
                rv = p.cop("vector", lambda v: v.reciprocal(SSF[:, :], SSF[:, :]))
                cvv = p.cop("vector", lambda v: v.tensor_copy(SSR[:, :], SSF[:, :]))
                p.wait("tensor", chv, cvv)
                if prev["qn"] is not None:
                    p.wait("tensor", chv, prev["qn"])
                prev["bc"] = p.top(lambda t: t.matmul(
                    PS_BC[:, :], IND2[:, :], SSR[:, :], start=True, stop=True), sem=s_bcq)
                p.wait("vector", s_bcq, prev["bc"])
                p.cop("vector", lambda v, sl=sl, kt=kt: v.tensor_tensor(
                    T2[:, :], QKB[sl][:, kt * TC : (kt + 1) * TC], PS_BC[:, :], ALU.mult))
                prev["qn"] = p.cop("vector", lambda v, sl=sl, kt=kt, qk=qk: v.tensor_scalar(
                    QKB[sl][:, kt * TC : (kt + 1) * TC], T2[:, :], QKC[:, qk : qk + 1],
                    None, ALU.mult))
                p.wait("tensor", chv, prev["qn"])
                if prev["t2"] is not None:
                    p.wait("tensor", chv, prev["t2"])
                prev["rot"] = p.top(lambda t, sl=sl, kt=kt: t.matmul(
                    PS_ROT[:, :], ROTQ[:, :], QKB[sl][:, kt * TC : (kt + 1) * TC],
                    start=True, stop=True), sem=s_rot)
                p.wait("gpsimd", s_tl[sl], tl_v)
                p.wait("gpsimd", chv, prev["qn"])
                if prev["o2"] is not None:
                    p.wait("gpsimd", chv, prev["o2"])
                prev["g1"] = p.cop("gpsimd", lambda g, sl=sl, kt=kt: g.tensor_tensor(
                    T1[:, :], QKB[sl][:, kt * TC : (kt + 1) * TC], TAB[sl][:, 0:TC], ALU.mult))
                p.wait("vector", s_rot, prev["rot"])
                prev["t2"] = p.cop("vector", lambda v, sl=sl: v.tensor_tensor(
                    T2[:, :], PS_ROT[:, :], TAB[sl][:, TC:], ALU.mult))
                p.wait("vector", chg, prev["g1"])
                prev["o2"] = p.cop("vector", lambda v, c=c, kt=kt, DSTB=DSTB: v.tensor_tensor(
                    DSTB[:, kt * ncols_kv + c * TC : kt * ncols_kv + (c + 1) * TC],
                    T1[:, :], T2[:, :], ALU.add))
            prep_done[ci] = prev["o2"]

    if "dbg" in d:
        s_dbg = p.sem(f"{name}_dbg")
        p.wait("scalar", chv)
        p.dma("scalar", d["dbg"][0:128, 0 : 4 * ncols_kv], QR[:, :], s_dbg)
        p.dma("scalar", d["dbg"][128:256, 0 : 4 * ncols_kv], KR[:, :], s_dbg)
        p.wait("scalar", s_dbg)

    cp_hist = {}
    for c in range(nch_kv):
        sl = c % 2
        if c >= 2:
            p.wait("sync", chv, cp_hist[("blkdone", c - 2)])
        p.dma(
            "sync",
            VB[sl][:, :].rearrange("q (kt t) -> q kt t", t=TC),
            d["qkv"].rearrange("(kt q) t -> q kt t", q=128)[
                :, 8:12, CK + c * TC : CK + (c + 1) * TC
            ],
            s_vl[sl],
        )
        p.wait("tensor", s_vl[sl])
        for kb in range(4):
            blk = c * 4 + kb
            for h in range(NH):
                gi = blk * NH + h
                tile, base = h // 2, (h % 2) * 64
                pst = PS_TR if gi % 2 == 0 else PS_TR2
                pv = cp_hist.get(gi - 2)
                if pv is not None:
                    p.wait("tensor", chv, pv)
                tv = p.top(lambda t, sl=sl, kb=kb, tile=tile, base=base, pst=pst: t.transpose(
                    pst[0:128, 0:64],
                    QKB_V_SLICE(VB[sl], tile, base, kb),
                    IDT[base : base + 64, base : base + 64],
                ), sem=s_vtr)
                p.wait("vector", s_vtr, tv)
                slot = h * nkb + blk
                cp_hist[gi] = p.cop("vector", lambda v, slot=slot, pst=pst: v.tensor_copy(
                    VTT[:, slot * 65 : slot * 65 + 64], pst[0:128, 0:64]))
        cp_hist[("blkdone", c)] = p.vchain("vector")
    p.wait("tensor", chv, cp_hist[("blkdone", nch_kv - 1)])
    p.wait("tensor", chv, cp_hist[("blkdone", nch_kv - 2)])
    p.barrier()
    p.psum_off = mqkv[1]

    OS = p.sb([128, 512], BF16, f"{name}_os")
    SROW = p.sb([1, 1024], F32, f"{name}_srow")
    SRECIP = p.sb([1, 512], F32, f"{name}_srec")
    ONRMW = [p.sb([128, 4 * 512], BF16, f"{name}_onrmw{i}") for i in range(2)]
    EXPT = [p.sb([128, 640], BF16, f"{name}_expt{i}") for i in range(2)]
    PS_SIM = [p.ps([128, 640], F32) for i in range(2)]
    PS_AV = [p.ps([128, 512], F32) for i in range(2)]
    PS_NBC = p.ps([128, 512], F32)
    s_sim = p.sem(f"{name}_sim")
    s_av = p.sem(f"{name}_av")
    s_nbc = p.sem(f"{name}_nbc")
    s_ow = p.sems(f"{name}_ow", 2)
    ahist = {}
    gctr = [0]
    for w in range(nw):
        qa0 = q0 + w * 512
        wsl = w % 2
        if w >= 2:
            p.wait("vector", s_ow[wsl])
        for tile in range(4):
            for half in range(2):
                h = 2 * tile + half
                base = half * 64
                for a in range(4):
                    gi = gctr[0]
                    gctr[0] += 1
                    par = gi % 2
                    qt0 = qa0 + a * 128
                    kt0 = qt0 - 512
                    qc = qt0 - kv0
                    kc = kt0 - kv0
                    w0m = (0 - q0) // 512
                    pv = ahist.get(("exp", gi - 2))
                    if pv is not None:
                        p.wait("tensor", chs, pv)

                    def simf(t, tile=tile, base=base, qc=qc, kc=kc, par=par):
                        ins = None
                        for j in range(5):
                            ins = t.matmul(
                                PS_SIM[par][0:128, j * 128 : (j + 1) * 128],
                                KR[base : base + 64,
                                   tile * ncols_kv + kc + j * 128 :
                                   tile * ncols_kv + kc + (j + 1) * 128],
                                QR[base : base + 64,
                                   tile * ncols_kv + qc : tile * ncols_kv + qc + 128],
                                start=True, stop=True,
                            )
                        return ins

                    simv = p.top(simf, sem=s_sim)
                    p.wait("vector", s_sim, simv)
                    p.cop("vector", lambda v, par=par: v.tensor_tensor(
                        PS_SIM[par][0:128, 0:128], PS_SIM[par][0:128, 0:128],
                        MSK[:, 0:128], ALU.add))
                    mv = p.cop("vector", lambda v, par=par: v.tensor_tensor(
                        PS_SIM[par][0:128, 512:640], PS_SIM[par][0:128, 512:640],
                        MSK[:, 128:256], ALU.add))
                    if w == w0m and a < 4:
                        for j in range(4 - a):
                            mv = p.cop("vector", lambda v, par=par, j=j: v.tensor_tensor(
                                PS_SIM[par][0:128, j * 128 : (j + 1) * 128],
                                PS_SIM[par][0:128, j * 128 : (j + 1) * 128],
                                MSKH[:, :], ALU.add))
                    p.wait("scalar", chv, mv)
                    pv = ahist.get(("avv", gi - 2))
                    if pv is not None:
                        p.wait("scalar", s_av, pv)
                    ev = p.cop("scalar", lambda sc, par=par: sc.activation(
                        EXPT[par][:, :], PS_SIM[par][0:128, :],
                        AF.Exp, bias=NEGC[:, 0:1], scale=8.0))
                    ahist[("exp", gi)] = ev
                    pv = ahist.get(("evac", gi - 2))
                    if pv is not None:
                        p.wait("tensor", chv, pv)
                    p.wait("tensor", chs, ev)

                    def avf(t, h=h, kc=kc, par=par):
                        ins = None
                        kb0 = kc // 128
                        for j in range(5):
                            slot = h * nkb + kb0 + j
                            ins = t.matmul(
                                PS_AV[par][0:65, 0:128],
                                VTT[:, slot * 65 : (slot + 1) * 65],
                                EXPT[par][:, j * 128 : (j + 1) * 128],
                                start=(j == 0), stop=(j == 4),
                            )
                        return ins

                    avv = p.top(avf, sem=s_av)
                    ahist[("avv", gi)] = avv
                    p.wait("vector", s_av, avv)
                    qcol = qt0 - qa0
                    p.cop("vector", lambda v, base=base, qcol=qcol, par=par: v.tensor_copy(
                        OS[base : base + 64, qcol : qcol + 128], PS_AV[par][0:64, 0:128]))
                    ahist[("evac", gi)] = p.cop(
                        "vector", lambda v, half=half, qcol=qcol, par=par: v.tensor_copy(
                            SROW[0:1, half * 512 + qcol : half * 512 + qcol + 128],
                            PS_AV[par][64:65, 0:128]))
                p.cop("vector", lambda v, half=half: v.tensor_scalar(
                    SROW[0:1, half * 512 : (half + 1) * 512],
                    SROW[0:1, half * 512 : (half + 1) * 512], 1e-30, None, ALU.max))
                rv = p.cop("vector", lambda v, half=half: v.reciprocal(
                    SRECIP[0:1, :], SROW[0:1, half * 512 : (half + 1) * 512]))
                p.wait("tensor", chv, rv)
                pv = ahist.get(("nm", h - 1, w), ahist.get(("nm", 7, w - 1)))
                if pv is not None:
                    p.wait("tensor", chv, pv)
                bv = p.top(lambda t, base=base: t.matmul(
                    PS_NBC[base : base + 64, 0:512], ONESR[0:1, 0:64], SRECIP[0:1, :],
                    start=True, stop=True), sem=s_nbc)
                p.wait("vector", s_nbc, bv)
                ahist[("nm", h, w)] = p.cop(
                    "vector", lambda v, wsl=wsl, tile=tile, base=base: v.tensor_tensor(
                        ONRMW[wsl][base : base + 64, tile * 512 : (tile + 1) * 512],
                        OS[base : base + 64, :],
                        PS_NBC[base : base + 64, 0:512], ALU.mult))
        dn = p.vchain("vector")
        p.wait("scalar", chv, dn)
        p.dma("scalar",
              d["onrm"].rearrange("(kt q) t -> q kt t", q=128)[
                  :, :, HALO + qa0 : HALO + qa0 + 512],
              ONRMW[wsl][:, :].rearrange("q (kt t) -> q kt t", t=512), s_ow[wsl])
    p.barrier(waits=[(s_ow[0], s_ow[0].n), (s_ow[1], s_ow[1].n)])
    p.reset(mqkv)

    epi_wo, s_so_wo = epi_res_store(
        p, name=f"{name}_woe", dst=d["xmid"], dst_c0=HALO + q0, res=src,
        res_c0=HALO + q0)
    emit_gemm(
        p, name=f"{name}_wog", src=d["onrm"], src_c0=HALO + q0, wsrc=wk["wo_p"],
        d_in=512, d_out=D, cols=TL - q0, epi=epi_wo,
    )
    p.barrier(waits=[(sm, sm.n) for sm in s_so_wo])
    p.reset(mqkv)

    fq0 = q0
    fcols = TL - fq0
    fch = fcols // TC
    LNG = p.sb([128, 16], F32, f"{name}_lng")
    s_lng = p.sem(f"{name}_lng")
    p.dma("sync", LNG[:, :], wk["lng_col"], s_lng)
    p.wait("vector", s_lng)
    mln2 = p.mark()
    ln2_done = emit_lngb(
        p, name=f"{name}_ln2", xsrc=d["xmid"], xdst=d["xf"], c0=HALO + fq0,
        cols=fcols, gamma_sb=LNG[:, 0:8], beta_sb=LNG[:, 8:16], src_f32=False,
        shift_half=True,
    )
    p.barrier(waits=ln2_done)
    p.reset(mln2)

    NO1 = FFN_W1O // 128
    W1 = p.sb([128, 8 * FFN_W1O], BF16, f"{name}_w1")
    W2 = p.sb([128, 22 * 1024], BF16, f"{name}_w2")
    B1 = p.sb([128, 44], F32, f"{name}_b1")
    B2 = p.sb([128, 8], F32, f"{name}_b2")
    XF = [p.sb([128, 8 * TC], BF16, f"{name}_xf{i}") for i in range(2)]
    HB = p.sb([128, 22 * TC], BF16, f"{name}_hb")
    ASB = p.sb([128, TC], F32, f"{name}_asb")
    GSB = p.sb([128, TC], F32, f"{name}_gsb")
    PS_W1 = [p.ps([128, 512], F32) for i in range(2)]
    PS_W2 = [p.ps([128, 512], F32) for i in range(2)]
    s_w1l = p.sem(f"{name}_w1l")
    s_bl = p.sem(f"{name}_bl")
    s_xfl = p.sems(f"{name}_xfl", 2)
    s_m1 = p.sem(f"{name}_m1")
    s_m2 = p.sem(f"{name}_m2")
    p.dma("gpsimd", W1[:, :], wk["w1_p"], s_w1l)
    p.dma("gpsimd", W2[:, :], wk["w2_p"], s_w1l)
    p.dma("sync", B1[:, :], wk["b1_col"], s_bl)
    p.dma("sync", B2[:, :], wk["b2_col"], s_bl)
    p.wait("tensor", s_w1l)
    p.wait("vector", s_bl)
    p.wait("scalar", s_bl)
    epi_f, s_so_f = epi_res_store(
        p, name=f"{name}_fe", dst=dst, dst_c0=HALO + fq0, res=d["xmid"],
        res_c0=HALO + fq0, bias_sb=B2)
    fh = {}
    for c in range(fch):
        sl = c % 2
        if c >= 2:
            p.wait("sync", s_m1, fh[("xf_cons", c - 2)])
        p.dma(
            "sync",
            XF[sl][:, :].rearrange("q (kt t) -> q kt t", t=TC),
            d["xf"].rearrange("(kt q) t -> q kt t", q=128)[
                :, :, HALO + fq0 + c * TC : HALO + fq0 + (c + 1) * TC],
            s_xfl[sl],
        )
        p.wait("tensor", s_xfl[sl])
        if c >= 1:
            p.wait("vector", s_m2, s_m2.base + c * 8)
        for o2 in range(22):
            for ag in range(2):
                o = o2 * 2 + ag
                par = o % 2
                pv = fh.get(("h", c, o2 - 1))
                if ag == 0 and pv is not None:
                    p.wait("tensor", chv, pv)
                if ag == 1 and fh.get(("g", c, o2 - 1)) is not None:
                    p.wait("tensor", chs, fh[("g", c, o2 - 1)])

                def w1f(t, sl=sl, o=o, par=par):
                    ins = None
                    for k in range(8):
                        ins = t.matmul(
                            PS_W1[par][:, :],
                            W1[:, k * FFN_W1O + o * 128 : k * FFN_W1O + (o + 1) * 128],
                            XF[sl][:, k * TC : (k + 1) * TC],
                            start=(k == 0), stop=(k == 7))
                    return ins

                mv = p.top(w1f, sem=s_m1)
                if ag == 0:
                    p.wait("vector", s_m1, mv)
                    fh[("a", c, o2)] = p.cop("vector", lambda v, o2=o2, o=o: v.tensor_scalar(
                        ASB[:, :], PS_W1[0][:, :], B1[:, o : o + 1], None, ALU.add))
                else:
                    p.wait("scalar", s_m1, mv)
                    if fh.get(("hh", c, o2 - 1)) is not None:
                        p.wait("scalar", chv, fh[("hh", c, o2 - 1)])
                    fh[("g", c, o2)] = p.cop("scalar", lambda sc, o=o: sc.activation(
                        GSB[:, :], PS_W1[1][:, :], AF.Gelu, bias=B1[:, o : o + 1]))
                    p.wait("vector", chs, fh[("g", c, o2)])
                    p.wait("vector", chv, fh[("a", c, o2)])
                    fh[("hh", c, o2)] = p.cop("vector", lambda v, o2=o2: v.tensor_mul(
                        HB[:, o2 * TC : (o2 + 1) * TC], ASB[:, :], GSB[:, :]))
                    fh[("h", c, o2)] = fh[("hh", c, o2)]
        fh[("xf_cons", c)] = s_m1.n
        hready = fh[("hh", c, 21)]
        p.wait("tensor", chv, hready)
        for o in range(8):
            par = o % 2
            pv = fh.get(("w2e", c, o - 2))
            if pv is None and c > 0:
                pv = fh.get(("w2e", c - 1, o + 6))
            if pv is not None:
                p.wait("tensor", pv[0], pv[1])

            def w2f(t, o=o, par=par):
                ins = None
                for k in range(22):
                    ins = t.matmul(
                        PS_W2[par][:, :],
                        W2[:, k * 1024 + o * 128 : k * 1024 + (o + 1) * 128],
                        HB[:, k * TC : (k + 1) * TC],
                        start=(k == 0), stop=(k == 21))
                return ins

            mv = p.top(w2f, sem=s_m2)
            fh[("w2e", c, o)] = epi_f(p, c, o, PS_W2[par][:, :], TC, s_m2)
    p.barrier(waits=[(sm, sm.n) for sm in s_so_f])
    p.reset(m0)


def QKB_V_SLICE(vb, tile, base, kb):
    return vb.rearrange("q (kt t) -> q kt t", t=TC)[base : base + 64, tile, kb * 128 : (kb + 1) * 128]


def ffn_host_layer(wqkv, qs, ks, wo, lg, lb, w1, b1, w2, b2):
    HID = 2730
    w1a, w1g = w1[:, :HID], w1[:, HID:]
    w1a = pad_cols(w1a, FFN_HID)
    w1g = pad_cols(w1g, FFN_HID)
    w1i = np.zeros((1024, FFN_W1O), np.float32)
    b1i = np.zeros(FFN_W1O, np.float32)
    b1a = np.concatenate([b1[:HID], np.zeros(FFN_HID - HID, np.float32)])
    b1g = np.concatenate([b1[HID:], np.zeros(FFN_HID - HID, np.float32)])
    for j in range(22):
        w1i[:, (2 * j) * 128 : (2 * j + 1) * 128] = w1a[:, j * 128 : (j + 1) * 128]
        w1i[:, (2 * j + 1) * 128 : (2 * j + 2) * 128] = w1g[:, j * 128 : (j + 1) * 128]
        b1i[(2 * j) * 128 : (2 * j + 1) * 128] = b1a[j * 128 : (j + 1) * 128]
        b1i[(2 * j + 1) * 128 : (2 * j + 2) * 128] = b1g[j * 128 : (j + 1) * 128]
    return dict(
        wqkv_p=pack_w(wqkv),
        wo_p=pack_w(wo),
        qk_col=np.stack([np.tile(qs, 2), np.tile(ks, 2)], 1).astype(np.float32),
        qk_row=np.concatenate([qs, ks]).astype(np.float32).reshape(1, 128),
        lng_col=np.concatenate([pack_vec(lg), pack_vec(lb)], 1),
        w1_p=pack_w(w1i),
        b1_col=pack_vec(b1i),
        w2_p=pack_w(pad_rows(w2, FFN_HID)),
        b2_col=pack_vec(b2),
    )


def ffn_host_consts():
    TW = T + 1024
    tg = np.arange(-512, TW - 512, dtype=np.float64)
    half = np.arange(0, 64, 2, dtype=np.float64)
    inv_freq = 1.0 / 10000.0 ** (half / 64.0)
    freqs32 = tg[None, :] * inv_freq[:, None]
    freqs = np.concatenate([freqs32, freqs32], 0)
    base = (half + 0.4 * 64.0) / (1.4 * 64.0)
    power = (tg - 4096.0) / 256.0
    scale32 = base[:, None] ** power[None, :]
    scale = np.concatenate([scale32, scale32], 0)
    cs_q = (np.cos(freqs) * scale).astype(np.float32)
    ss_q = (np.sin(freqs) * scale).astype(np.float32)
    cs_k = (np.cos(freqs) / scale).astype(np.float32)
    ss_k = (np.sin(freqs) / scale).astype(np.float32)
    tab = np.concatenate(
        [np.tile(x, (2, 1)) for x in (cs_q, ss_q, cs_k, ss_k)], 0
    )
    R = np.zeros((64, 64), np.float32)
    for i in range(32):
        R[i, i + 32] = -1.0
        R[i + 32, i] = 1.0
    R128 = np.zeros((128, 128), np.float32)
    R128[:64, :64] = R
    R128[64:, 64:] = R
    rotq_lhsT = np.ascontiguousarray(R128.T)
    j_ = np.arange(128)[:, None]
    i_ = np.arange(128)[None, :]
    M0 = np.where(j_ >= i_, 0.0, -1e30).astype(np.float32)
    M4 = np.where(j_ <= i_, 0.0, -1e30).astype(np.float32)
    masks = np.concatenate([M0, M4], 1)
    ind2 = np.zeros((2, 128), np.float32)
    ind2[0, :64] = 1.0
    ind2[1, 64:] = 1.0
    ind2t = np.ascontiguousarray(ind2.T)
    ones_p = np.ones((1, 128), np.float32)
    ident_p = np.eye(128, dtype=np.float32)
    return dict(tab=tab, rotq_lhsT=rotq_lhsT, masks=masks, ind2=ind2,
                ind2t=ind2t, ones_p=ones_p, ident_p=ident_p)




def emit_l3ffn(p, d, *, name, wk):
    m0 = p.mark()
    chv = p.chain["vector"]
    MU = p.sb([128, 32], F32, f"{name}_mu")
    s_mu = p.sem(f"{name}_mu")
    p.dma("sync", MU[:, 0:8], wk["mur_col"], s_mu)
    p.dma("sync", MU[:, 16:24], wk["muk_col"], s_mu)
    p.wait("vector", s_mu)
    p.cop("vector", lambda v: v.tensor_scalar(MU[:, 8:16], MU[:, 0:8], -1.0, 1.0,
                                              ALU.mult, ALU.add))
    p.cop("vector", lambda v: v.tensor_scalar(MU[:, 24:32], MU[:, 16:24], -1.0, 1.0,
                                              ALU.mult, ALU.add))

    XL = [p.sb([128, 8 * TC], BF16, f"{name}_xl{i}") for i in range(2)]
    XP = [p.sb([128, 8 * TC], BF16, f"{name}_xp{i}") for i in range(2)]
    TT = p.sb([128, TC], F32, f"{name}_tt")
    XRO = p.sb([128, 8 * TC], BF16, f"{name}_xro")
    XKO = p.sb([128, 8 * TC], BF16, f"{name}_xko")
    s_xl = p.sems(f"{name}_xl", 2)
    s_xpl = p.sems(f"{name}_xpl", 2)
    s_ro = p.sems(f"{name}_ro", 2)
    lerp_done = []
    nch = TL // TC
    for c in range(nch):
        sl = c % 2
        if c >= 2:
            p.wait("sync", chv, lerp_done[c - 2][2])
        p.dma("sync", XL[sl][:, :].rearrange("q (kt t) -> q kt t", t=TC),
              d["xn"].rearrange("(kt q) t -> q kt t", q=128)[
                  :, :, HALO + c * TC : HALO + (c + 1) * TC], s_xl[sl])
        p.dma("sync", XP[sl][:, :].rearrange("q (kt t) -> q kt t", t=TC),
              d["xn"].rearrange("(kt q) t -> q kt t", q=128)[
                  :, :, HALO + c * TC - 1 : HALO + (c + 1) * TC - 1], s_xpl[sl])
        p.wait("vector", s_xl[sl])
        p.wait("vector", s_xpl[sl])
        if c >= 1:
            p.wait("vector", s_ro[1 - sl])
        if c >= 2:
            p.wait("vector", s_ro[sl])
        for kt in range(NKT):
            for rk in range(2):
                p.cop("vector", lambda v, sl=sl, kt=kt, rk=rk: v.tensor_scalar(
                    TT[:, :], XL[sl][:, kt * TC : (kt + 1) * TC],
                    MU[:, 16 * rk + kt : 16 * rk + kt + 1], None, ALU.mult))
                p.cop("vector", lambda v, sl=sl, kt=kt, rk=rk: v.scalar_tensor_tensor(
                    (XRO if rk == 0 else XKO)[:, kt * TC : (kt + 1) * TC],
                    XP[sl][:, kt * TC : (kt + 1) * TC],
                    MU[:, 16 * rk + 8 + kt : 16 * rk + 8 + kt + 1],
                    TT[:, :], ALU.mult, ALU.add))
        dn = p.vchain("vector")
        p.wait("scalar", chv, dn)
        p.dma("scalar", d["xr"].rearrange("(kt q) t -> q kt t", q=128)[
            :, :, c * TC : (c + 1) * TC],
            XRO[:, :].rearrange("q (kt t) -> q kt t", t=TC), s_ro[sl])
        p.dma("scalar", d["xk"].rearrange("(kt q) t -> q kt t", q=128)[
            :, :, c * TC : (c + 1) * TC],
            XKO[:, :].rearrange("q (kt t) -> q kt t", t=TC), s_ro[sl])
        lerp_done.append((s_ro[sl], s_ro[sl].n, p.vchain("vector")))
    p.barrier(waits=[(a, b) for a, b, _ in lerp_done])
    p.reset(m0)

    epi_r, s_so_r = epi_store(p, name=f"{name}_re", dst=d["rform"], dst_c0=0,
                              act=AF.Sigmoid)
    emit_gemm(p, name=f"{name}_rg", src=d["xr"], src_c0=0, wsrc=wk["wr_p"],
              d_in=D, d_out=D, cols=TL, epi=epi_r)
    p.barrier(waits=[(sm, sm.n) for sm in s_so_r])
    p.reset(m0)

    OBK = [p.sb([128, TC], BF16, f"{name}_obk{i}") for i in range(4)]
    TK = p.sb([128, TC], F32, f"{name}_tk")
    s_sok = p.sems(f"{name}_sok", 4)
    kctr = [0]

    def epi_k(p, c, o, psum, tc_here, s_mm):
        i = kctr[0]
        kctr[0] += 1
        sl = i % 4
        p.wait("vector", s_mm)
        if i >= 4:
            p.wait("vector", s_sok[sl])
        p.cop("vector", lambda v, psum=psum: v.tensor_scalar(
            TK[:, :], psum, 0.0, None, ALU.max))
        val = p.cop("vector", lambda v, sl=sl: v.tensor_mul(OBK[sl][:, :], TK[:, :], TK[:, :]))
        p.wait("scalar", chv, val)
        p.dma("scalar", d["k2"][o * 128 : (o + 1) * 128, c * TC : (c + 1) * TC],
              OBK[sl][:, :], s_sok[sl])
        return (chv, val)

    emit_gemm(p, name=f"{name}_kg", src=d["xk"], src_c0=0, wsrc=wk["wk_p"],
              d_in=D, d_out=D, cols=TL, epi=epi_k)
    p.barrier(waits=[(sm, sm.n) for sm in s_sok])
    p.reset(m0)

    OBV = [p.sb([128, TC], BF16, f"{name}_obv{i}") for i in range(4)]
    RV = [p.sb([128, TC], BF16, f"{name}_rv{i}") for i in range(4)]
    s_sov = p.sems(f"{name}_sov", 4)
    s_rl = p.sems(f"{name}_rl", 4)
    vctr = [0]
    vcons = {}

    def epi_v(p, c, o, psum, tc_here, s_mm):
        i = vctr[0]
        vctr[0] += 1
        sl = i % 4
        if i >= 4:
            p.wait("sync", chv, vcons[i - 4])
        p.dma("sync", RV[sl][:, :], d["rform"][o * 128 : (o + 1) * 128,
              c * TC : (c + 1) * TC], s_rl[sl])
        p.wait("vector", s_mm)
        p.wait("vector", s_rl[sl])
        if i >= 4:
            p.wait("vector", s_sov[sl])
        val = p.cop("vector", lambda v, sl=sl, psum=psum: v.tensor_tensor(
            OBV[sl][:, :], psum, RV[sl][:, :], ALU.mult))
        vcons[i] = val
        p.wait("scalar", chv, val)
        p.dma("scalar", d["g3"][o * 128 : (o + 1) * 128, c * TC : (c + 1) * TC],
              OBV[sl][:, :], s_sov[sl])
        return (chv, val)

    emit_gemm(p, name=f"{name}_vg", src=d["k2"], src_c0=0, wsrc=wk["wv_p"],
              d_in=D, d_out=D, cols=TL, epi=epi_v)
    p.barrier(waits=[(sm, sm.n) for sm in s_sov])
    p.reset(m0)


def l3ffn_host(inp):
    mur = np.asarray(inp["L3_ffn_mu_r"], np.float32)
    muk = np.asarray(inp["L3_ffn_mu_k"], np.float32)
    return dict(
        wr_p=pack_wT(np.asarray(inp["L3_ffn_W_r"])),
        wk_p=pack_wT(np.asarray(inp["L3_ffn_W_k"])),
        wv_p=pack_wT(np.asarray(inp["L3_ffn_W_v"])),
        mur_col=pack_vec(mur),
        muk_col=pack_vec(muk),
    )




def emit_combine(p, d, *, name, s_key, g_key, xin_key, xout_key, s_c0=0, g_c0=0,
                 s_dt=BF16, g_dt=BF16, out_is_flat=False):
    m0 = p.mark()
    chv = p.chain["vector"]
    chs = p.chain["scalar"]
    SB_ = [p.sb([128, 8 * TC], s_dt, f"{name}_s{i}") for i in range(2)]
    GB_ = [p.sb([128, 8 * TC], g_dt, f"{name}_g{i}") for i in range(2)]
    XB_ = [p.sb([128, 8 * TC], F32, f"{name}_x{i}") for i in range(2)]
    SGT = p.sb([128, TC], F32, f"{name}_sg")
    XO_ = [p.sb([128, 8 * TC], F32, f"{name}_xo{i}") for i in range(2)]
    s_sl = p.sems(f"{name}_sl", 2)
    s_gl = p.sems(f"{name}_gl", 2)
    s_xl = p.sems(f"{name}_xl", 2)
    s_oc = p.sems(f"{name}_oc", 2)
    done = []
    nch = TL // TC
    out_c0 = 0 if out_is_flat else HALO
    for c in range(nch):
        sl = c % 2
        if c >= 2:
            p.wait("sync", chv, done[c - 2][2])
        p.dma("sync", SB_[sl][:, :].rearrange("q (kt t) -> q kt t", t=TC),
              d[s_key].rearrange("(kt q) t -> q kt t", q=128)[
                  :, :, s_c0 + c * TC : s_c0 + (c + 1) * TC], s_sl[sl])
        p.dma("sync", GB_[sl][:, :].rearrange("q (kt t) -> q kt t", t=TC),
              d[g_key].rearrange("(kt q) t -> q kt t", q=128)[
                  :, :, g_c0 + c * TC : g_c0 + (c + 1) * TC], s_gl[sl])
        p.dma("sync", XB_[sl][:, :].rearrange("q (kt t) -> q kt t", t=TC),
              d[xin_key].rearrange("(kt q) t -> q kt t", q=128)[
                  :, :, HALO + c * TC : HALO + (c + 1) * TC], s_xl[sl])
        p.wait("scalar", s_gl[sl])
        p.wait("vector", s_sl[sl])
        p.wait("vector", s_xl[sl])
        if c >= 2:
            p.wait("vector", s_oc[sl])
            p.wait("scalar", s_oc[sl])
        for kt in range(NKT):
            gv = p.cop("scalar", lambda sc, sl=sl, kt=kt: sc.activation(
                GB_[sl][:, kt * TC : (kt + 1) * TC], GB_[sl][:, kt * TC : (kt + 1) * TC],
                AF.Silu))
            p.wait("vector", chs, gv)
            p.cop("vector", lambda v, sl=sl, kt=kt: v.tensor_tensor(
                SGT[:, :], SB_[sl][:, kt * TC : (kt + 1) * TC],
                GB_[sl][:, kt * TC : (kt + 1) * TC], ALU.mult))
            p.cop("vector", lambda v, sl=sl, kt=kt: v.tensor_tensor(
                XO_[sl][:, kt * TC : (kt + 1) * TC], XB_[sl][:, kt * TC : (kt + 1) * TC],
                SGT[:, :], ALU.add))
        dn = p.vchain("vector")
        p.wait("scalar", chv, dn)
        p.dma("scalar", d[xout_key].rearrange("(kt q) t -> q kt t", q=128)[
            :, :, out_c0 + c * TC : out_c0 + (c + 1) * TC],
            XO_[sl][:, :].rearrange("q (kt t) -> q kt t", t=TC), s_oc[sl])
        done.append((s_oc[sl], s_oc[sl].n, p.vchain("vector")))
    p.barrier(waits=[(a_, b_) for a_, b_, _ in done])
    p.reset(m0)
    return done




def emit_l2attn(p, d, *, name, wk):
    m0 = p.mark()
    chv = p.chain["vector"]
    chs = p.chain["scalar"]
    for nm, wkey, dst, act in (("r", "wr_p", "r2", AF.Sigmoid), ("k", "wk_p", "k2b", None),
                               ("v", "wv_p", "v2", None)):
        mm = p.mark()
        epi, s_so = epi_store(p, name=f"{name}_{nm}e", dst=d[dst], dst_c0=0, act=act)
        emit_gemm(p, name=f"{name}_{nm}g", src=d["xn"], src_c0=HALO, wsrc=wk[wkey],
                  d_in=D, d_out=D, cols=TL, epi=epi)
        p.barrier(waits=[(sm, sm.n) for sm in s_so])
        p.reset(mm)

    DC = p.sb([128, 8], F32, f"{name}_dc")
    LNW = p.sb([128, 8], F32, f"{name}_lnw")
    s_dc = p.sem(f"{name}_dc")
    p.dma("sync", DC[:, :], wk["decay_col"], s_dc)
    p.dma("sync", LNW[:, :], wk["lnw_col"], s_dc)
    p.wait("scalar", s_dc)
    p.wait("vector", s_dc)
    sv = p.cop("scalar", lambda sc: sc.activation(DC[:, :], DC[:, :], AF.Sigmoid))
    p.wait("vector", chs, sv)
    cv = p.cop("vector", lambda v: v.tensor_scalar(DC[:, :], DC[:, :], 1e-7, None, ALU.max))
    p.wait("scalar", chv, cv)
    lv = p.cop("scalar", lambda sc: sc.activation(DC[:, :], DC[:, :], AF.Ln))
    p.wait("vector", chs, lv)
    p.wait("tensor", chs, lv)

    KT_ = p.sb([128, TL], BF16, f"{name}_kt")
    VT_ = p.sb([128, TL], BF16, f"{name}_vt")
    RT_ = p.sb([128, TL], BF16, f"{name}_rt")
    IOT = p.sb([128, TL], F32, f"{name}_iot")
    SCL = p.sb([128, TL], F32, f"{name}_scl")
    KV = p.sb([128, TL], F32, f"{name}_kv")
    CUM = p.sb([128, TL], F32, f"{name}_cum")
    YT2 = p.sb([128, TL], F32, f"{name}_yt2")
    SQT = p.sb([128, TC], BF16, f"{name}_sqt")
    ZR = p.sb([128, 8], F32, f"{name}_zr")
    STL = p.sb([128, 16], F32, f"{name}_stl")
    ONESB = p.sb([128, 1], BF16, f"{name}_onesb")
    ONES1 = p.sb([1, 128], F32, f"{name}_ones1f")
    RSD = p.sb([1, 8 * TC], F32, f"{name}_rsd")
    YN = [p.sb([128, 8 * TC], BF16, f"{name}_yn{i}") for i in range(2)]
    PS_SS = p.ps([1, 512], F32)
    PS_RB = p.ps([128, 512], F32)
    s_ld = p.sems(f"{name}_ld", 4)
    s_io = p.sems(f"{name}_io", 2)
    s_ag = p.sem(f"{name}_ag")
    s_st = p.sem(f"{name}_st")
    s_ssum = p.sem(f"{name}_ssum")
    s_rb = p.sem(f"{name}_rb")
    s_yn = p.sems(f"{name}_yn", 2)
    iv = p.cop("vector", lambda v: v.memset(ONESB[:, :], 1.0))
    iv2 = p.cop("vector", lambda v: v.memset(ONES1[:, :], 1.0))
    zv = p.cop("vector", lambda v: v.memset(ZR[:, :], 0.0))
    p.wait("tensor", chv, zv)

    tile_done = {}
    prev_ev = None
    for dt_i in range(8):
        if dt_i >= 1:
            p.wait("sync", chv, tile_done.get(dt_i - 1, 0))
            p.wait("vector", s_st)
            p.wait("gpsimd", chs, prev_ev)
        p.dma("sync", KT_[:, :], d["k2b"][dt_i * 128 : (dt_i + 1) * 128, :], s_ld[0])
        p.dma("sync", VT_[:, :], d["v2"][dt_i * 128 : (dt_i + 1) * 128, :], s_ld[1])
        p.dma("gpsimd", IOT[:, :], wk["iota"][:, 0:TL], s_io[0])
        p.wait("vector", s_ld[0])
        p.wait("vector", s_ld[1])
        p.wait("scalar", s_io[0])
        ev = p.cop("scalar", lambda sc, dt_i=dt_i: sc.activation(
            SCL[:, :], IOT[:, :], AF.Exp, scale=DC[:, dt_i : dt_i + 1]))
        prev_ev = ev
        p.wait("vector", chs, ev)
        p.cop("vector", lambda v: v.tensor_mul(KV[:, :], KT_[:, :], VT_[:, :]))
        p.cop("vector", lambda v: v.tensor_scalar(YT2[:, :], SCL[:, :], 1e-10, None, ALU.max))
        p.cop("vector", lambda v: v.reciprocal(SCL[:, :], YT2[:, :]))
        p.cop("vector", lambda v: v.tensor_tensor(KV[:, :], KV[:, :], SCL[:, :], ALU.mult))
        p.cop("vector", lambda v: v.tensor_tensor_scan(
            CUM[:, :], KV[:, :], KV[:, :], 0.0, ALU.add, ALU.bypass))
        sv2 = p.cop("vector", lambda v, dt_i=dt_i: v.tensor_copy(
            STL[:, dt_i : dt_i + 1], CUM[:, TL - 1 : TL]))
        tile_done[dt_i] = sv2
        p.wait("scalar", chv, sv2)
        p.dma("scalar", d["cum2"][dt_i * 128 : (dt_i + 1) * 128, :], CUM[:, :], s_st)
    p.wait("gpsimd", chv, tile_done[7])
    s_cc = p.sem(f"{name}_cc")
    p.dma("gpsimd", d["cc2_in"], STL[:, 0:8], s_cc)
    p.wait("gpsimd", s_cc)
    p.raw("gpsimd", lambda g: g.collective_compute(
        "AllGather", ALU.bypass, replica_groups=[[0, 1], [2, 3], [4, 5], [6, 7]],
        ins=[d["cc2_in"].opt()], outs=[d["cc2_out"].opt()],
    ).then_inc(s_ag.h, 1))
    s_ag.n += 1
    p.wait("gpsimd", s_ag)
    s_si = p.sem(f"{name}_si")
    p.dma("gpsimd", STL[:, 8:16], d["cc2_out"][0:128, :], s_si)
    HM2 = p.sb([128, 1], F32, f"{name}_hm2")
    s_hm2 = p.sem(f"{name}_hm2")
    p.dma("sync", HM2[:, :], d["hm_col"], s_hm2)
    p.wait("vector", s_si)
    p.wait("vector", s_hm2)
    p.cop("vector", lambda v: v.tensor_scalar(STL[:, 8:16], STL[:, 8:16], HM2[:, 0:1],
                                              None, ALU.mult))
    p.wait("vector", s_st)
    p.wait("sync", s_st)

    prev_ev2 = None
    prev_yv = None
    for dt_i in range(8):
        if dt_i >= 1:
            p.wait("vector", s_st)
            p.wait("gpsimd", chs, prev_ev2)
            p.wait("sync", chv, prev_yv)
            p.wait("sync", s_st)
        p.dma("sync", CUM[:, :], d["cum2"][dt_i * 128 : (dt_i + 1) * 128, :], s_ld[2])
        p.dma("sync", RT_[:, :], d["r2"][dt_i * 128 : (dt_i + 1) * 128, :], s_ld[3])
        p.dma("gpsimd", IOT[:, :], wk["iota"][:, 0:TL], s_io[1])
        p.wait("vector", s_ld[2])
        p.wait("vector", s_ld[3])
        p.wait("scalar", s_io[1])
        ev = p.cop("scalar", lambda sc, dt_i=dt_i: sc.activation(
            SCL[:, :], IOT[:, :], AF.Exp, scale=DC[:, dt_i : dt_i + 1]))
        prev_ev2 = ev
        p.wait("vector", chs, ev)
        stcol = STL[:, 8 + dt_i : 9 + dt_i]
        p.cop("vector", lambda v, stcol=stcol: v.tensor_scalar(
            CUM[:, :], CUM[:, :], stcol, None, ALU.add))
        p.cop("vector", lambda v: v.tensor_mul(CUM[:, :], CUM[:, :], SCL[:, :]))
        yv = p.cop("vector", lambda v: v.tensor_mul(YT2[:, :], CUM[:, :], RT_[:, :]))
        prev_yv = yv
        p.wait("scalar", chv, yv)
        p.dma("scalar", d["y2s"][dt_i * 128 : (dt_i + 1) * 128, :], YT2[:, :], s_st)
        for c in range(TL // TC):
            if dt_i > 0 or c > 0:
                p.wait("scalar", s_ssum)
            sq = p.cop("scalar", lambda sc, c=c: sc.activation(
                SQT[:, :], YT2[:, c * TC : (c + 1) * TC], AF.Square))
            last_sq = sq
            p.wait("tensor", chs, sq)
            p.wait("tensor", chv)
            mv = p.top(lambda t: t.matmul(
                PS_SS[0:1, :], ONESB[:, :], SQT[:, :], start=True, stop=True),
                sem=s_ssum)
            p.wait("vector", s_ssum, mv)
            if dt_i == 0:
                p.cop("vector", lambda v, c=c: v.tensor_copy(
                    RSD[0:1, c * TC : (c + 1) * TC], PS_SS[0:1, :]))
            else:
                p.cop("vector", lambda v, c=c: v.tensor_tensor(
                    RSD[0:1, c * TC : (c + 1) * TC],
                    RSD[0:1, c * TC : (c + 1) * TC], PS_SS[0:1, :], ALU.add))
    EPS2 = p.sb([1, 1], F32, f"{name}_eps2")
    p.cop("vector", lambda v: v.memset(EPS2[:, :], 1e-6))
    rv = p.cop("vector", lambda v: v.tensor_scalar(RSD[0:1, :], RSD[0:1, :], 1.0 / D, None, ALU.mult))
    p.wait("scalar", chv, rv)
    p.cop("scalar", lambda sc: sc.activation(RSD[0:1, :], RSD[0:1, :], AF.Ln, bias=EPS2[0:1, 0:1]))
    rv2 = p.cop("scalar", lambda sc: sc.activation(RSD[0:1, :], RSD[0:1, :], AF.Exp, scale=-0.5))
    p.wait("tensor", chs, rv2)
    p.wait("vector", s_st)
    p.wait("sync", chs, last_sq)
    p.wait("sync", s_st)

    ynctr = [0]
    for dt_i in range(8):
        sl = dt_i % 2
        if dt_i >= 1:
            p.wait("sync", chv)
        p.dma("sync", YT2[:, :], d["y2s"][dt_i * 128 : (dt_i + 1) * 128, :], s_ld[0])
        p.wait("vector", s_ld[0])
        if dt_i >= 2:
            p.wait("vector", s_yn[sl])
        for c in range(TL // TC):
            if ynctr[0] > 0:
                p.wait("tensor", chv, ynctr[1] if False else ynctr[0])
            bv = p.top(lambda t, c=c: t.matmul(
                PS_RB[:, :], ONES1[0:1, :], RSD[0:1, c * TC : (c + 1) * TC],
                start=True, stop=True), sem=s_rb)
            p.wait("vector", s_rb, bv)
            p.cop("vector", lambda v, c=c: v.tensor_tensor(
                YT2[:, c * TC : (c + 1) * TC], YT2[:, c * TC : (c + 1) * TC],
                PS_RB[:, :], ALU.mult))
            vv = p.cop("vector", lambda v, sl=sl, c=c, dt_i=dt_i: v.tensor_scalar(
                YN[sl][:, c * TC : (c + 1) * TC], YT2[:, c * TC : (c + 1) * TC],
                LNW[:, dt_i : dt_i + 1], None, ALU.mult))
            ynctr[0] = vv
        dn = p.vchain("vector")
        p.wait("scalar", chv, dn)
        p.dma("scalar", d["yn2"][dt_i * 128 : (dt_i + 1) * 128, :],
              YN[sl][:, :].rearrange("q (c t) -> q (c t)", t=TC), s_yn[sl])
    p.barrier(waits=[(s_yn[0], s_yn[0].n), (s_yn[1], s_yn[1].n)])
    p.reset(m0)

    epi_o, s_so_o = epi_store(p, name=f"{name}_oe", dst=d["s2"], dst_c0=0)
    emit_gemm(p, name=f"{name}_og", src=d["yn2"], src_c0=0, wsrc=wk["wo_p"],
              d_in=D, d_out=D, cols=TL, epi=epi_o)
    p.barrier(waits=[(sm, sm.n) for sm in s_so_o])
    p.reset(m0)


def l2attn_host(inp):
    return dict(
        wr_p=pack_wT(np.asarray(inp["L2_attn_Wr"])),
        wk_p=pack_wT(np.asarray(inp["L2_attn_Wk"])),
        wv_p=pack_wT(np.asarray(inp["L2_attn_Wv"])),
        wo_p=pack_wT(np.asarray(inp["L2_attn_Wo"])),
        decay_col=pack_vec(np.asarray(inp["L2_attn_decay"])),
        lnw_col=pack_vec(np.asarray(inp["L2_attn_ln_w"])),
        iota=np.tile(np.arange(T, dtype=np.float32), (128, 1)),
    )


def emit_l2ffn(p, d, *, name, wk):
    m0 = p.mark()
    chv = p.chain["vector"]
    CV = p.sb([128, 4096], I32, f"{name}_cv")
    CB = p.sb([128, 4096], BF16, f"{name}_cb")
    s_cl = p.sem(f"{name}_cl")
    s_cs = p.sem(f"{name}_cs")
    for wi, wb, ncols in (("wg_i", "wg_b", 8 * 3072), ("wu_i", "wu_b", 8 * 3072),
                          ("wd_i", "wd_b", 24 * 1024)):
        for c0 in range(0, ncols, 4096):
            cw = min(4096, ncols - c0)
            p.wait("sync", chv)
            p.dma("sync", CV[:, :cw], d[wi][:, c0 : c0 + cw], s_cl)
            p.wait("vector", s_cl)
            if c0 > 0 or wi != "wg_i":
                p.wait("vector", s_cs)
            vv = p.cop("vector", lambda v, cw=cw: v.tensor_copy(CB[:, :cw], CV[:, :cw]))
            p.wait("scalar", chv, vv)
            p.dma("scalar", d[wb][:, c0 : c0 + cw], CB[:, :cw], s_cs)
    p.wait("sync", s_cs)
    p.wait("gpsimd", s_cs)

    epi_g, s_so_g = epi_store(p, name=f"{name}_ge", dst=d["gsig"], dst_c0=0,
                              act=AF.Silu, scale=1.0 / 127.0)
    emit_gemm(p, name=f"{name}_gg", src=d["xn"], src_c0=HALO, wsrc=d["wg_b"],
              d_in=D, d_out=3072, cols=TL, epi=epi_g, w_bf16=True)
    p.barrier(waits=[(sm, sm.n) for sm in s_so_g])
    p.reset(m0)

    OBU = [p.sb([128, TC], BF16, f"{name}_obu{i}") for i in range(4)]
    GSB2 = [p.sb([128, TC], BF16, f"{name}_gsb{i}") for i in range(4)]
    s_gl2 = p.sems(f"{name}_gl2", 4)
    s_sou = p.sems(f"{name}_sou", 4)
    uctr = [0]
    ucons = {}

    def epi_u(p, c, o, psum, tc_here, s_mm):
        i = uctr[0]
        uctr[0] += 1
        sl = i % 4
        if i >= 4:
            p.wait("sync", chv, ucons[i - 4])
        p.dma("sync", GSB2[sl][:, :], d["gsig"][o * 128 : (o + 1) * 128,
              c * TC : (c + 1) * TC], s_gl2[sl])
        p.wait("vector", s_mm)
        p.wait("vector", s_gl2[sl])
        if i >= 4:
            p.wait("vector", s_sou[sl])
        val = p.cop("vector", lambda v, sl=sl, psum=psum: v.scalar_tensor_tensor(
            OBU[sl][:, :], psum, 1.0 / 127.0, GSB2[sl][:, :], ALU.mult, ALU.mult))
        ucons[i] = val
        p.wait("scalar", chv, val)
        p.dma("scalar", d["up_h"][o * 128 : (o + 1) * 128, c * TC : (c + 1) * TC],
              OBU[sl][:, :], s_sou[sl])
        return (chv, val)

    emit_gemm(p, name=f"{name}_ug", src=d["xn"], src_c0=HALO, wsrc=d["wu_b"],
              d_in=D, d_out=3072, cols=TL, epi=epi_u, w_bf16=True)
    p.barrier(waits=[(sm, sm.n) for sm in s_sou])
    p.reset(m0)

    epi_d, s_so_d = epi_store(p, name=f"{name}_de", dst=d["g2"], dst_c0=0,
                              scale=1.0 / 127.0)
    emit_gemm(p, name=f"{name}_dg", src=d["up_h"], src_c0=0, wsrc=d["wd_b"],
              d_in=3072, d_out=D, cols=TL, epi=epi_d, w_bf16=True)
    p.barrier(waits=[(sm, sm.n) for sm in s_so_d])
    p.reset(m0)


def l2ffn_host(inp):
    return dict(
        wg_i=pack_w(np.ascontiguousarray(np.asarray(inp["L2_ffn_w_gate"]).T)).astype(np.int32),
        wu_i=pack_w(np.ascontiguousarray(np.asarray(inp["L2_ffn_w_up"]).T)).astype(np.int32),
        wd_i=pack_w(np.ascontiguousarray(np.asarray(inp["L2_ffn_w_down"]).T)).astype(np.int32),
    )




def emit_l1attn(p, d, *, name, wk):
    m0 = p.mark()
    chv = p.chain["vector"]
    chs = p.chain["scalar"]
    NBLK = TL // 128

    ABC = p.sb([128, 56], F32, f"{name}_abc")
    IDT = p.sb([128, 128], BF16, f"{name}_idt")
    s_k0 = p.sem(f"{name}_k0")
    s_k1 = p.sem(f"{name}_k1")
    p.dma("sync", ABC[:, :], wk["abc_col"], s_k0)
    p.dma("gpsimd", IDT[:, :], wk["ident_p"], s_k1)
    p.wait("vector", s_k0)
    p.wait("tensor", s_k1)

    PR = p.sb([128, 8 * 128], F32, f"{name}_pr")
    PI = p.sb([128, 8 * 128], F32, f"{name}_pi")
    KK = p.sb([128, 8 * 128], BF16, f"{name}_kk")
    WR = p.sb([128, 8 * 128], F32, f"{name}_wr")
    WI = p.sb([128, 8 * 128], F32, f"{name}_wi")
    ER = p.sb([128, 8 * 128], F32, f"{name}_er")
    EI = p.sb([128, 8 * 128], F32, f"{name}_ei")
    A128 = p.sb([128, 8 * 4], F32, f"{name}_a128")
    TC1 = p.sb([128, 128], F32, f"{name}_tc1")
    TC2 = p.sb([128, 128], F32, f"{name}_tc2")
    for dt_i in range(8):
        pr = PR[:, dt_i * 128 : (dt_i + 1) * 128]
        pi = PI[:, dt_i * 128 : (dt_i + 1) * 128]
        ar = ABC[:, dt_i * 7 + 0 : dt_i * 7 + 1]
        ai = ABC[:, dt_i * 7 + 1 : dt_i * 7 + 2]
        br = ABC[:, dt_i * 7 + 2 : dt_i * 7 + 3]
        bi = ABC[:, dt_i * 7 + 3 : dt_i * 7 + 4]
        cr = ABC[:, dt_i * 7 + 4 : dt_i * 7 + 5]
        ci = ABC[:, dt_i * 7 + 5 : dt_i * 7 + 6]
        p.cop("vector", lambda v, pr=pr: v.memset(pr[:, 0:1], 1.0))
        p.cop("vector", lambda v, pi=pi: v.memset(pi[:, 0:1], 0.0))
        p.cop("vector", lambda v, pr=pr, ar=ar: v.tensor_copy(pr[:, 1:2], ar))
        p.cop("vector", lambda v, pi=pi, ai=ai: v.tensor_copy(pi[:, 1:2], ai))
        kk = 1
        while kk < 127:
            cnt = min(kk, 127 - kk)
            p.cop("vector", lambda v, pr=pr, kk=kk, cnt=cnt: v.tensor_scalar(
                TC1[:, :cnt], pr[:, 1 : 1 + cnt], pr[:, kk : kk + 1], None, ALU.mult))
            p.cop("vector", lambda v, pi=pi, kk=kk, cnt=cnt: v.tensor_scalar(
                TC2[:, :cnt], pi[:, 1 : 1 + cnt], pi[:, kk : kk + 1], None, ALU.mult))
            p.cop("vector", lambda v, cnt=cnt: v.tensor_sub(
                TC1[:, :cnt], TC1[:, :cnt], TC2[:, :cnt]))
            p.cop("vector", lambda v, pr=pr, pi=pi, kk=kk, cnt=cnt: v.tensor_scalar(
                TC2[:, :cnt], pr[:, 1 : 1 + cnt], pi[:, kk : kk + 1], None, ALU.mult))
            p.cop("vector", lambda v, pr=pr, pi=pi, kk=kk, cnt=cnt: v.scalar_tensor_tensor(
                pi[:, kk + 1 : kk + 1 + cnt], pi[:, 1 : 1 + cnt], pr[:, kk : kk + 1],
                TC2[:, :cnt], ALU.mult, ALU.add))
            p.cop("vector", lambda v, pr=pr, kk=kk, cnt=cnt: v.tensor_copy(
                pr[:, kk + 1 : kk + 1 + cnt], TC1[:, :cnt]))
            kk *= 2
        p.cop("vector", lambda v, pr=pr: v.tensor_mul(TC1[:, 0:1], pr[:, 64:65], pr[:, 64:65]))
        p.cop("vector", lambda v, pi=pi: v.tensor_mul(TC2[:, 0:1], pi[:, 64:65], pi[:, 64:65]))
        p.cop("vector", lambda v, dt_i=dt_i: v.tensor_sub(
            A128[:, 4 * dt_i : 4 * dt_i + 1], TC1[:, 0:1], TC2[:, 0:1]))
        p.cop("vector", lambda v, pr=pr, pi=pi: v.tensor_mul(TC1[:, 0:1], pr[:, 64:65], pi[:, 64:65]))
        p.cop("vector", lambda v, dt_i=dt_i: v.tensor_scalar(
            A128[:, 4 * dt_i + 1 : 4 * dt_i + 2], TC1[:, 0:1], 2.0, None, ALU.mult))
        p.cop("vector", lambda v, cr=cr, br=br: v.tensor_mul(TC2[:, 4:5], cr, br))
        p.cop("vector", lambda v, ci=ci, bi=bi: v.tensor_mul(TC2[:, 6:7], ci, bi))
        p.cop("vector", lambda v: v.tensor_sub(TC2[:, 4:5], TC2[:, 4:5], TC2[:, 6:7]))
        p.cop("vector", lambda v, cr=cr, bi=bi: v.tensor_mul(TC2[:, 5:6], cr, bi))
        p.cop("vector", lambda v, ci=ci, br=br: v.scalar_tensor_tensor(
            TC2[:, 5:6], br, ci[:, 0:1], TC2[:, 5:6], ALU.mult, ALU.add))
        p.cop("vector", lambda v, pr=pr: v.tensor_scalar(
            TC1[:, 0:128], pr, TC2[:, 4:5], None, ALU.mult))
        p.cop("vector", lambda v, pi=pi, dt_i=dt_i: v.tensor_scalar(
            ER[:, dt_i * 128 : (dt_i + 1) * 128], pi, TC2[:, 5:6], None, ALU.mult))
        p.cop("vector", lambda v, dt_i=dt_i: v.tensor_sub(
            KK[:, dt_i * 128 : (dt_i + 1) * 128], TC1[:, 0:128],
            ER[:, dt_i * 128 : (dt_i + 1) * 128]))
        p.cop("vector", lambda v, pr=pr, br=br, dt_i=dt_i: v.tensor_scalar(
            WR[:, dt_i * 128 : (dt_i + 1) * 128], pr[:, 127::-1], br[:, 0:1], None, ALU.mult))
        p.cop("vector", lambda v, pi=pi, bi=bi: v.tensor_scalar(
            TC1[:, 0:128], pi[:, 127::-1], bi[:, 0:1], None, ALU.mult))
        p.cop("vector", lambda v, dt_i=dt_i: v.tensor_sub(
            WR[:, dt_i * 128 : (dt_i + 1) * 128],
            WR[:, dt_i * 128 : (dt_i + 1) * 128], TC1[:, 0:128]))
        p.cop("vector", lambda v, pi=pi, br=br, dt_i=dt_i: v.tensor_scalar(
            WI[:, dt_i * 128 : (dt_i + 1) * 128], pi[:, 127::-1], br[:, 0:1], None, ALU.mult))
        p.cop("vector", lambda v, pr=pr, bi=bi, dt_i=dt_i: v.scalar_tensor_tensor(
            WI[:, dt_i * 128 : (dt_i + 1) * 128], pr[:, 127::-1], bi[:, 0:1],
            WI[:, dt_i * 128 : (dt_i + 1) * 128], ALU.mult, ALU.add))
        p.cop("vector", lambda v, pr=pr, cr=cr, dt_i=dt_i: v.tensor_scalar(
            ER[:, dt_i * 128 : dt_i * 128 + 127], pr[:, 1:128], cr[:, 0:1], None, ALU.mult))
        p.cop("vector", lambda v, pi=pi, ci=ci: v.tensor_scalar(
            TC1[:, 0:127], pi[:, 1:128], ci[:, 0:1], None, ALU.mult))
        p.cop("vector", lambda v, dt_i=dt_i: v.tensor_sub(
            ER[:, dt_i * 128 : dt_i * 128 + 127],
            ER[:, dt_i * 128 : dt_i * 128 + 127], TC1[:, 0:127]))
        p.cop("vector", lambda v, pi=pi, cr=cr, dt_i=dt_i: v.tensor_scalar(
            EI[:, dt_i * 128 : dt_i * 128 + 127], pi[:, 1:128], cr[:, 0:1], None, ALU.mult))
        p.cop("vector", lambda v, pr=pr, ci=ci, dt_i=dt_i: v.scalar_tensor_tensor(
            EI[:, dt_i * 128 : dt_i * 128 + 127], pr[:, 1:128], ci[:, 0:1],
            EI[:, dt_i * 128 : dt_i * 128 + 127], ALU.mult, ALU.add))
        p.cop("vector", lambda v, cr=cr, dt_i=dt_i: v.tensor_mul(
            TC1[:, 2:3], cr, A128[:, 4 * dt_i : 4 * dt_i + 1]))
        p.cop("vector", lambda v, ci=ci, dt_i=dt_i: v.tensor_mul(
            TC2[:, 2:3], ci, A128[:, 4 * dt_i + 1 : 4 * dt_i + 2]))
        p.cop("vector", lambda v, dt_i=dt_i: v.tensor_sub(
            ER[:, dt_i * 128 + 127 : (dt_i + 1) * 128], TC1[:, 2:3], TC2[:, 2:3]))
        p.cop("vector", lambda v, ci=ci, dt_i=dt_i: v.tensor_mul(
            TC1[:, 2:3], ci, A128[:, 4 * dt_i : 4 * dt_i + 1]))
        p.cop("vector", lambda v, cr=cr, dt_i=dt_i: v.scalar_tensor_tensor(
            EI[:, dt_i * 128 + 127 : (dt_i + 1) * 128],
            A128[:, 4 * dt_i + 1 : 4 * dt_i + 2], cr[:, 0:1], TC1[:, 2:3],
            ALU.mult, ALU.add))
        kprep = p.vchain("vector")
    p.wait("scalar", chv, kprep)

    ZB1 = p.sb([128, 1024], BF16, f"{name}_zb1")
    s_z1 = p.sem(f"{name}_z1")
    zv = p.cop("vector", lambda v: v.memset(ZB1[:, :], 0.0))
    p.wait("scalar", chv, zv)
    p.dma("scalar", d["z1"].rearrange("(dt q) c -> q dt c", q=128)[:, :, 0:128],
          ZB1[:, :].rearrange("q (dt c) -> q dt c", c=128), s_z1)
    p.dma("scalar", d["z1"].rearrange("(dt q) c -> q dt c", q=128)[:, :, 128:256],
          KK[:, :].rearrange("q (dt c) -> q dt c", c=128), s_z1)
    p.wait("scalar", s_z1)
    s_bt = emit_toeplitz_build(p, name=f"{name}_tb", z_t=d["z1"].tensor, z_off=0,
                               t_t=d["t1m"].tensor, t_off=0)

    UT = p.sb([128, NBLK * 128], BF16, f"{name}_ut")
    TB1 = p.sb([128, 128 * 128], BF16, f"{name}_tb1")
    UB = p.sb([128, TL], BF16, f"{name}_ub")
    SJR = p.sb([128, 8 * NBLK], F32, f"{name}_sjr")
    SJI = p.sb([128, 8 * NBLK], F32, f"{name}_sji")
    TMW = p.sb([128, TL], F32, f"{name}_tmw")
    YTM = p.sb([128, NBLK * 128], BF16, f"{name}_ytm")
    YO1 = p.sb([128, NBLK * 128], F32, f"{name}_yo1")
    PS_C1 = [p.ps([128, 64], F32) for i in range(2)]
    PS_T1 = p.ps([128, 128], BF16)
    PS_T1b = p.ps([128, 128], BF16)
    s_utl = p.sem(f"{name}_utl")
    s_ubl = p.sem(f"{name}_ubl")
    s_tbl = p.sem(f"{name}_tbl")
    s_cm1 = p.sem(f"{name}_cm1")
    s_tr1 = p.sem(f"{name}_tr1")
    s_y1 = p.sem(f"{name}_y1")
    p.wait("sync", s_bt)
    p.wait("scalar", s_bt)

    hist = {}
    for dt_i in range(8):
        if dt_i >= 1:
            p.wait("sync", chv)
            p.wait("sync", s_cm1)
        p.dma("sync", UB[:, :], d["xn"][dt_i * 128 : (dt_i + 1) * 128, HALO:], s_ubl)
        for blk in range(NBLK):
            p.dma("sync", UT[:, blk * 128 : (blk + 1) * 128],
                  d["xn"][dt_i * 128 : (dt_i + 1) * 128,
                          HALO + blk * 128 : HALO + (blk + 1) * 128],
                  s_utl, transpose=True)
        p.dma("sync", TB1[:, :],
              bass.AP(d["t1m"].tensor, dt_i * 128 * 128, [[1024 * 128, 128], [1, 128 * 128]]),
              s_tbl)
        p.wait("vector", s_ubl)
        wre = WR[:, dt_i * 128 : (dt_i + 1) * 128]
        wim = WI[:, dt_i * 128 : (dt_i + 1) * 128]
        for ri, wv_ in ((0, wre), (1, wim)):
            p.cop("vector", lambda v, wv_=wv_: v.tensor_tensor(
                TMW[:, :].rearrange("q (b s) -> q b s", s=128), 
                UB[:, :].rearrange("q (b s) -> q b s", s=128),
                wv_[:, None, :].broadcast_to([128, NBLK, 128]), ALU.mult))
            p.cop("vector", lambda v, ri=ri, dt_i=dt_i: v.reduce_sum(
                (SJR if ri == 0 else SJI)[:, dt_i * NBLK : (dt_i + 1) * NBLK].rearrange(
                    "q b -> q b 1" if False else "q b -> q b", ) if False else
                (SJR if ri == 0 else SJI)[:, dt_i * NBLK : (dt_i + 1) * NBLK],
                TMW[:, :].rearrange("q (b s) -> q b s", s=128), axis=AX.X))
        p.wait("tensor", s_tbl)
        p.wait("tensor", s_utl)
        p.wait("vector", s_tr1)
        cg = {}
        for ch in range(128):
            bnk = ch % 2
            if ch >= 2:
                p.wait("tensor", chv, cg[ch - 2])
            elif dt_i > 0:
                p.wait("tensor", chv, hist[("cglast", dt_i - 1)])
            mv = p.top(lambda t, ch=ch, bnk=bnk: t.matmul(
                PS_C1[bnk][:, :NBLK], TB1[:, ch * 128 : (ch + 1) * 128],
                UT[:, ch : ch + (NBLK - 1) * 128 + 1 : 128], start=True, stop=True),
                sem=s_cm1)
            p.wait("vector", s_cm1, mv)
            cg[ch] = p.cop("vector", lambda v, ch=ch, bnk=bnk: v.tensor_copy(
                YTM[:, ch : ch + (NBLK - 1) * 128 + 1 : 128], PS_C1[bnk][:, :NBLK]))
        hist[("cglast", dt_i)] = cg[127]
        p.wait("tensor", chv, cg[127])
        p.wait("vector", s_y1)
        cpv = {}
        for blk in range(NBLK):
            pst = PS_T1 if blk % 2 == 0 else PS_T1b
            if blk >= 2:
                p.wait("tensor", chv, cpv[blk - 2])
            tv = p.top(lambda t, blk=blk, pst=pst: t.transpose(
                pst[:, :], YTM[:, blk * 128 : (blk + 1) * 128], IDT[:, :]), sem=s_tr1)
            p.wait("vector", s_tr1, tv)
            cpv[blk] = p.cop("vector", lambda v, blk=blk, pst=pst: v.tensor_copy(
                YO1[:, blk * 128 : (blk + 1) * 128], pst[:, :]))
        p.wait("scalar", chv, cpv[NBLK - 1])
        p.dma("scalar", d["yraw1"][dt_i * 128 : (dt_i + 1) * 128, :], YO1[:, :], s_y1)
    sj_done = p.vchain("vector")

    HR = p.sb([128, 8 * (NBLK + 1)], F32, f"{name}_hr")
    HI = p.sb([128, 8 * (NBLK + 1)], F32, f"{name}_hi")
    T3 = p.sb([128, 8], F32, f"{name}_t3")
    T4 = p.sb([128, 8], F32, f"{name}_t4")
    STI = p.sb([128, 16], F32, f"{name}_sti")
    s_ag1 = p.sem(f"{name}_ag1")
    s_ci = p.sem(f"{name}_ci")
    hr3 = HR[:, :].rearrange("q (dt j) -> q j dt", dt=8)
    hi3 = HI[:, :].rearrange("q (dt j) -> q j dt", dt=8)
    sjr3 = SJR[:, :].rearrange("q (dt j) -> q j dt", dt=8)
    sji3 = SJI[:, :].rearrange("q (dt j) -> q j dt", dt=8)
    a128r = A128[:, 0::4]
    a128i = A128[:, 1::4]
    def emit_hscan(init_from_sti):
        p.cop("vector", lambda v: v.memset(hr3[:, 0, :], 0.0))
        p.cop("vector", lambda v: v.memset(hi3[:, 0, :], 0.0))
        for J in range(NBLK):
            p.cop("vector", lambda v, J=J: v.tensor_tensor(
                T3[:, :], hr3[:, J, :], a128r.broadcast_to([128, 8]) if False else a128r,
                ALU.mult))
            p.cop("vector", lambda v, J=J: v.tensor_tensor(
                T4[:, :], hi3[:, J, :], a128i, ALU.mult))
            p.cop("vector", lambda v, J=J: v.tensor_sub(T3[:, :], T3[:, :], T4[:, :]))
            p.cop("vector", lambda v, J=J: v.tensor_tensor(
                hr3[:, J + 1, :], T3[:, :], sjr3[:, J, :], ALU.add))
            p.cop("vector", lambda v, J=J: v.tensor_tensor(
                T3[:, :], hr3[:, J, :], a128i, ALU.mult))
            p.cop("vector", lambda v, J=J: v.tensor_tensor(
                T4[:, :], hi3[:, J, :], a128r, ALU.mult))
            p.cop("vector", lambda v, J=J: v.tensor_add(T3[:, :], T3[:, :], T4[:, :]))
            p.cop("vector", lambda v, J=J: v.tensor_tensor(
                hi3[:, J + 1, :], T3[:, :], sji3[:, J, :], ALU.add))

    emit_hscan(False)
    hs_done = p.vchain("vector")
    p.wait("gpsimd", chv, hs_done)
    p.dma("gpsimd", d["cc1_in"][:, 0:8], hr3[:, NBLK, :], s_ci,
          allow_slow_non_contiguous=True)
    p.dma("gpsimd", d["cc1_in"][:, 8:16], hi3[:, NBLK, :], s_ci,
          allow_slow_non_contiguous=True)
    p.wait("gpsimd", s_ci)
    p.raw("gpsimd", lambda g: g.collective_compute(
        "AllGather", ALU.bypass, replica_groups=[[0, 1], [2, 3], [4, 5], [6, 7]],
        ins=[d["cc1_in"].opt()], outs=[d["cc1_out"].opt()]).then_inc(s_ag1.h, 1))
    s_ag1.n += 1
    p.wait("gpsimd", s_ag1)
    s_si1 = p.sem(f"{name}_si1")
    p.dma("gpsimd", STI[:, :], d["cc1_out"][0:128, :], s_si1)
    HM1 = p.sb([128, 1], F32, f"{name}_hm1")
    s_hm1 = p.sem(f"{name}_hm1")
    p.dma("sync", HM1[:, :], d["hm_col"], s_hm1)
    p.wait("vector", s_si1)
    p.wait("vector", s_hm1)
    p.cop("vector", lambda v: v.tensor_scalar(STI[:, :], STI[:, :], HM1[:, 0:1], None, ALU.mult))
    APR = p.sb([128, 8 * (NBLK + 1)], F32, f"{name}_apr")
    API = p.sb([128, 8 * (NBLK + 1)], F32, f"{name}_api")
    apr3 = APR[:, :].rearrange("q (dt j) -> q j dt", dt=8)
    api3 = API[:, :].rearrange("q (dt j) -> q j dt", dt=8)
    p.cop("vector", lambda v: v.memset(apr3[:, 0, :], 1.0))
    p.cop("vector", lambda v: v.memset(api3[:, 0, :], 0.0))
    for J in range(NBLK):
        p.cop("vector", lambda v, J=J: v.tensor_tensor(T3[:, :], apr3[:, J, :], a128r, ALU.mult))
        p.cop("vector", lambda v, J=J: v.tensor_tensor(T4[:, :], api3[:, J, :], a128i, ALU.mult))
        p.cop("vector", lambda v, J=J: v.tensor_sub(apr3[:, J + 1, :], T3[:, :], T4[:, :]))
        p.cop("vector", lambda v, J=J: v.tensor_tensor(T3[:, :], apr3[:, J, :], a128i, ALU.mult))
        p.cop("vector", lambda v, J=J: v.tensor_tensor(T4[:, :], api3[:, J, :], a128r, ALU.mult))
        p.cop("vector", lambda v, J=J: v.tensor_add(api3[:, J + 1, :], T3[:, :], T4[:, :]))
    for J in range(NBLK + 1):
        p.cop("vector", lambda v, J=J: v.tensor_tensor(T3[:, :], apr3[:, J, :], STI[:, 0:8], ALU.mult))
        p.cop("vector", lambda v, J=J: v.tensor_tensor(T4[:, :], api3[:, J, :], STI[:, 8:16], ALU.mult))
        p.cop("vector", lambda v, J=J: v.tensor_sub(T3[:, :], T3[:, :], T4[:, :]))
        p.cop("vector", lambda v, J=J: v.tensor_add(hr3[:, J, :], hr3[:, J, :], T3[:, :]))
        p.cop("vector", lambda v, J=J: v.tensor_tensor(T3[:, :], apr3[:, J, :], STI[:, 8:16], ALU.mult))
        p.cop("vector", lambda v, J=J: v.tensor_tensor(T4[:, :], api3[:, J, :], STI[:, 0:8], ALU.mult))
        p.cop("vector", lambda v, J=J: v.tensor_add(T3[:, :], T3[:, :], T4[:, :]))
        p.cop("vector", lambda v, J=J: v.tensor_add(hi3[:, J, :], hi3[:, J, :], T3[:, :]))

    hcdone = p.vchain("vector")
    p.wait("sync", chv, hcdone)
    YF = [p.sb([128, TL], F32, f"{name}_yf{i}") for i in range(1)]
    UB2 = p.sb([128, TL], BF16, f"{name}_ub2")
    SO1 = p.sb([128, TL], BF16, f"{name}_so1")
    s_yl = p.sem(f"{name}_yl")
    s_ul = p.sem(f"{name}_ul")
    s_os = p.sem(f"{name}_os")
    p.wait("vector", s_y1)
    for dt_i in range(8):
        if dt_i >= 1:
            p.wait("sync", chv)
            p.wait("sync", s_os)
        p.dma("sync", YF[0][:, :], d["yraw1"][dt_i * 128 : (dt_i + 1) * 128, :], s_yl)
        p.dma("sync", UB2[:, :], d["xn"][dt_i * 128 : (dt_i + 1) * 128, HALO:], s_ul)
        p.wait("vector", s_yl)
        p.wait("vector", s_ul)
        er = ER[:, dt_i * 128 : (dt_i + 1) * 128]
        ei = EI[:, dt_i * 128 : (dt_i + 1) * 128]
        for J in range(NBLK):
            p.cop("vector", lambda v, J=J, dt_i=dt_i, er=er: v.tensor_scalar(
                TMW[:, J * 128 : (J + 1) * 128], er,
                hr3[:, J, dt_i : dt_i + 1], None, ALU.mult))
            p.cop("vector", lambda v, J=J, dt_i=dt_i, ei=ei: v.tensor_scalar(
                TC1[:, 0:128], ei, hi3[:, J, dt_i : dt_i + 1], None, ALU.mult))
            p.cop("vector", lambda v, J=J: v.tensor_sub(
                TMW[:, J * 128 : (J + 1) * 128], TMW[:, J * 128 : (J + 1) * 128],
                TC1[:, 0:128]))
        p.cop("vector", lambda v: v.tensor_add(YF[0][:, :], YF[0][:, :], TMW[:, :]))
        vv = p.cop("vector", lambda v, dt_i=dt_i: v.scalar_tensor_tensor(
            SO1[:, :], UB2[:, :], ABC[:, dt_i * 7 + 6 : dt_i * 7 + 7], YF[0][:, :],
            ALU.mult, ALU.add))
        p.wait("scalar", chv, vv)
        p.dma("scalar", d["s1"][dt_i * 128 : (dt_i + 1) * 128, :], SO1[:, :], s_os)
    p.barrier(waits=[(s_os, s_os.n)])
    p.reset(m0)


def l1attn_host(inp):
    cols = []
    for nm in ("A_real", "A_imag", "B_real", "B_imag", "C_real", "C_imag", "D"):
        cols.append(pack_vec(np.asarray(inp[f"L1_attn_{nm}"])))
    out = np.zeros((128, 56), np.float32)
    for i in range(7):
        for dt in range(8):
            out[:, dt * 7 + i] = cols[i][:, dt]
    return dict(abc_col=out, ident_p=np.eye(128, dtype=np.float32))



LN1E8 = -18.420680743952367
E20 = 485165195.4097903


def emit_l3attn(p, d, *, name, wk):
    m0 = p.mark()
    chv = p.chain["vector"]
    chs = p.chain["scalar"]
    TLh = TL + 128

    CVW = p.sb([128, 64], F32, f"{name}_cvw")
    CVB = p.sb([128, 16], F32, f"{name}_cvb")
    DTW = p.sb([128, 16], F32, f"{name}_dtw")
    DTB = p.sb([128, 16], F32, f"{name}_dtb")
    ALOG = p.sb([128, 256], F32, f"{name}_alog")
    DPAR = p.sb([128, 16], F32, f"{name}_dpar")
    ONESR3 = p.sb([1, 128], F32, f"{name}_ones")
    HM3 = p.sb([128, 1], F32, f"{name}_hm")
    s_c0 = p.sem(f"{name}_c0")
    for nmv, buf in (("convw_col", CVW), ("convb_col", CVB), ("dtw_col", DTW),
                     ("dtb_col", DTB), ("alog_col", ALOG), ("dpar_col", DPAR),
                     ("ones_p", ONESR3)):
        p.dma("sync", buf[:, :], wk[nmv], s_c0)
    p.dma("sync", HM3[:, :], d["hm_col"], s_c0)
    p.wait("vector", s_c0)
    p.wait("scalar", s_c0)
    p.wait("tensor", s_c0)
    av = p.cop("scalar", lambda sc: sc.activation(ALOG[:, :], ALOG[:, :], AF.Exp))
    p.wait("vector", chs, av)
    av2 = p.cop("vector", lambda v: v.tensor_scalar(ALOG[:, :], ALOG[:, :], -1.0, None, ALU.mult))

    mseg = p.mark()

    def epi_split(p, c, o, psum, tc_here, s_mm):
        pass

    epi_xz, s_so_xz = epi_store(p, name=f"{name}_xze", dst=d["xz"], dst_c0=0)
    emit_gemm(p, name=f"{name}_xzg", src=d["xn"], src_c0=HALO - 512,
              wsrc=wk["inproj_p"], d_in=D, d_out=4096, cols=TL + 512, epi=epi_xz)
    p.barrier(waits=[(sm, sm.n) for sm in s_so_xz])
    p.reset(mseg)

    XI = p.sb([128, TLh], BF16, f"{name}_xi")
    XC = p.sb([128, TL], BF16, f"{name}_xcb")
    TFD = [p.sb([128, TL], F32, f"{name}_tf{i}") for i in range(2)]
    TF2 = p.sb([128, TL], F32, f"{name}_tf2")
    s_xil = p.sem(f"{name}_xil")
    s_xcs = p.sem(f"{name}_xcs")
    prev_silu = [None]
    for dt16 in range(16):
        if dt16 >= 1:
            p.wait("sync", chv)
            p.wait("sync", s_xcs)
        p.dma("sync", XI[:, :], d["xz"][dt16 * 128 : (dt16 + 1) * 128, 384 : 384 + TLh],
              s_xil)
        p.wait("vector", s_xil)
        p.wait("vector", chs)
        TF = TFD[dt16 % 2]
        p.cop("vector", lambda v, dt16=dt16, TF=TF: v.tensor_scalar(
            TF[:, :], XI[:, 125 : 125 + TL], CVW[:, dt16 * 4 : dt16 * 4 + 1], None, ALU.mult))
        for s_ in (1, 2, 3):
            p.cop("vector", lambda v, dt16=dt16, s_=s_, TF=TF: v.scalar_tensor_tensor(
                TF[:, :], XI[:, 125 + s_ : 125 + s_ + TL],
                CVW[:, dt16 * 4 + s_ : dt16 * 4 + s_ + 1],
                TF[:, :], ALU.mult, ALU.add))
        p.wait("scalar", s_xcs)
        p.wait("scalar", chv)
        vv = p.cop("scalar", lambda sc, dt16=dt16, TF=TF: sc.activation(
            XC[:, :], TF[:, :], AF.Silu, bias=CVB[:, dt16 : dt16 + 1]))
        prev_silu[0] = vv
        p.wait("vector", chs, vv)
        p.wait("scalar", chs, vv)
        p.dma("scalar", d["xc"][dt16 * 128 : (dt16 + 1) * 128, :], XC[:, :], s_xcs)
    p.wait("sync", s_xcs)
    p.wait("gpsimd", s_xcs)

    PS33 = p.ps([33, 512], F32)
    BC3S = p.sb([33, TL], F32, f"{name}_bc3s")
    s_mm3 = p.sem(f"{name}_mm3")
    W33 = p.sb([128, 16 * 33], BF16, f"{name}_w33")
    X33 = [p.sb([128, 16 * TC], BF16, f"{name}_x33{i}") for i in range(2)]
    s_w33 = p.sem(f"{name}_w33")
    s_x33 = p.sems(f"{name}_x33", 2)
    p.dma("gpsimd", W33[:, :], wk["xproj_p"], s_w33)
    p.wait("tensor", s_w33)
    ev33 = {}
    for c in range(TL // TC):
        sl = c % 2
        if c >= 2:
            p.wait("sync", s_mm3, s_mm3.base + c - 1)
        p.dma("sync", X33[sl][:, :].rearrange("q (kt t) -> q kt t", t=TC),
              d["xc"].rearrange("(kt q) t -> q kt t", q=128)[:, :, c * TC : (c + 1) * TC],
              s_x33[sl])
        p.wait("tensor", s_x33[sl])
        if c >= 1:
            p.wait("tensor", chv, ev33[c - 1])

        def mm33(t, sl=sl):
            ins = None
            for k in range(16):
                ins = t.matmul(PS33[:, :], W33[:, k * 33 : (k + 1) * 33],
                               X33[sl][:, k * TC : (k + 1) * TC],
                               start=(k == 0), stop=(k == 15))
            return ins

        p.top(mm33, sem=s_mm3)
        p.wait("vector", s_mm3)
        ev33[c] = p.cop("vector", lambda v, c=c: v.tensor_copy(
            BC3S[:, c * TC : (c + 1) * TC], PS33[:, :]))
    s_bc3 = p.sem(f"{name}_bc3")
    p.wait("scalar", chv, ev33[TL // TC - 1])
    p.dma("scalar", d["bc3"], BC3S[0:32, :], s_bc3)

    PS_OU = p.ps([128, 512], F32)
    DTT = p.sb([128, TL], F32, f"{name}_dtt")
    DTRAW = p.sb([1, TL], F32, f"{name}_dtraw")
    ONE1C = p.sb([128, 1], F32, f"{name}_one1c")
    s_ou = p.sem(f"{name}_ou")
    s_dts = p.sem(f"{name}_dts")
    p.cop("vector", lambda v: v.memset(ONE1C[:, :], 1.0))
    dtr = p.cop("vector", lambda v: v.tensor_copy(DTRAW[0:1, :], BC3S[32:33, :]))
    p.wait("tensor", chv, dtr)
    TF = TFD[0]
    p.wait("vector", chs)
    for dt16 in range(16):
        if dt16 >= 1:
            p.wait("sync", s_dts)
            p.wait("vector", s_dts)
        for c in range(TL // TC):
            if dt16 > 0 or c > 0:
                p.wait("tensor", chv)
            mv = p.top(lambda t, c=c: t.matmul(
                PS_OU[:, :], ONESR3[0:1, :], DTRAW[0:1, c * TC : (c + 1) * TC],
                start=True, stop=True), sem=s_ou)
            p.wait("vector", s_ou, mv)
            p.cop("vector", lambda v, dt16=dt16, c=c: v.tensor_scalar(
                TF[:, c * TC : (c + 1) * TC], PS_OU[:, :],
                DTW[:, dt16 : dt16 + 1], DTB[:, dt16 : dt16 + 1], ALU.mult, ALU.add))
        p.cop("vector", lambda v: v.tensor_scalar(TF2[:, :], TF[:, :], -1.0, None, ALU.mult))
        p.cop("vector", lambda v: v.tensor_tensor(TF2[:, :], TF[:, :], TF2[:, :], ALU.min))
        p.wait("scalar", chv, p.vchain("vector"))
        sv = p.cop("scalar", lambda sc: sc.activation(TF2[:, :], TF2[:, :], AF.Exp))
        sv2 = p.cop("scalar", lambda sc: sc.activation(TF2[:, :], TF2[:, :], AF.Ln,
                                                       bias=ONE1C[:, 0:1]))
        p.wait("vector", chs, sv2)
        p.cop("vector", lambda v: v.tensor_scalar(DTT[:, :], TF[:, :], 0.0, None, ALU.max))
        dv = p.cop("vector", lambda v: v.tensor_add(DTT[:, :], DTT[:, :], TF2[:, :]))
        p.wait("scalar", chv, dv)
        p.dma("scalar", d["dt3"][dt16 * 128 : (dt16 + 1) * 128, :], DTT[:, :], s_dts)
    p.wait("sync", s_dts)
    p.barrier(waits=[(s_bc3, s_bc3.n), (s_dts, s_dts.n), (s_xcs, s_xcs.n)])
    p.reset(mseg)

    B2 = p.sb([128, TL], F32, f"{name}_b2")
    B4 = p.sb([128, TL], F32, f"{name}_b4")
    B1S = p.sb([128, TL], BF16, f"{name}_b1s")
    B3S = p.sb([128, TL], BF16, f"{name}_b3s")
    YA = p.sb([128, TL], F32, f"{name}_ya")
    DTT2 = p.sb([128, TL], F32, f"{name}_dtt2")
    DUB = p.sb([128, TL], BF16, f"{name}_dub")
    XCB2 = p.sb([128, TL], BF16, f"{name}_xcb2")
    ZB3 = p.sb([128, TL], BF16, f"{name}_zb3")
    CMSK = p.sb([128, TL], BF16, f"{name}_cmsk")
    BCROW = p.sb([1, TL], F32, f"{name}_bcrow")
    H0S = p.sb([128, 128], F32, f"{name}_h0s")
    CPS = p.sb([128, 128], F32, f"{name}_cps")
    HPS = p.sb([128, 128], F32, f"{name}_hps")
    T5 = p.sb([128, 128], F32, f"{name}_t5")
    STO = p.sb([128, 16], F32, f"{name}_sto")
    STI3 = p.sb([128, 16], F32, f"{name}_sti3")
    YMB = p.sb([128, TL], BF16, f"{name}_ymb")
    PS_B = [p.ps([128, 512], F32) for i in range(2)]
    s_ld3 = p.sems(f"{name}_ld3", 4)
    s_bcr = p.sem(f"{name}_bcr")
    s_b13 = p.sem(f"{name}_b13")
    s_b13r = p.sems(f"{name}_b13r", 2)
    s_cmk = p.sem(f"{name}_cmk")
    s_bb = p.sem(f"{name}_bb")
    s_ag3 = p.sem(f"{name}_ag3")
    s_si3 = p.sem(f"{name}_si3")
    s_ci3 = p.sem(f"{name}_ci3")
    s_ym = p.sem(f"{name}_ym")
    p.dma("gpsimd", CMSK[:, :], wk["cmask"], s_cmk)
    p.wait("vector", s_cmk)

    NCH8 = TL // TC
    pb_hist = []
    for dt16 in range(16):
        if dt16 >= 1:
            p.wait("sync", chv)
            p.wait("sync", chs)
            p.wait("sync", s_ym)
        p.dma("sync", DTT2[:, :], d["dt3"][dt16 * 128 : (dt16 + 1) * 128, :], s_ld3[0])
        p.dma("sync", XCB2[:, :], d["xc"][dt16 * 128 : (dt16 + 1) * 128, :], s_ld3[1])
        p.wait("vector", s_ld3[0])
        p.wait("vector", s_ld3[1])
        p.cop("vector", lambda v: v.tensor_tensor(DUB[:, :], DTT2[:, :], XCB2[:, :], ALU.mult))
        prev_e1 = None
        for n in range(16):
            acol = ALOG[:, dt16 * 16 + n : dt16 * 16 + n + 1]
            p.wait("sync", s_bb)
            p.dma("sync", BCROW[0:1, :], d["bc3"][n : n + 1, :], s_bcr)
            p.cop("vector", lambda v, acol=acol: v.tensor_scalar(
                B2[:, :], DTT2[:, :], acol, LN1E8, ALU.mult, ALU.max))
            if prev_e1 is not None:
                p.wait("vector", chs, prev_e1)
            p.cop("vector", lambda v: v.tensor_tensor_scan(
                B4[:, :], CMSK[:, :], B2[:, :], 0.0, ALU.mult, ALU.add))
            p.wait("scalar", s_b13)
            p.wait("scalar", chv)
            e1 = p.cop("scalar", lambda sc: sc.activation(B1S[:, :], B4[:, :], AF.Exp))
            prev_e1 = e1
            p.wait("vector", chs, e1)
            p.wait("vector", s_b13)
            p.cop("vector", lambda v: v.reciprocal(B2[:, :], B1S[:, :]))
            p.cop("vector", lambda v: v.tensor_scalar(B2[:, :], B2[:, :], E20, None, ALU.min))
            p.wait("tensor", s_bcr)
            for c in range(NCH8):
                pb = PS_B[c % 2]
                if len(pb_hist) >= 2:
                    p.wait("tensor", chv, pb_hist[-2])
                bv = p.top(lambda t, c=c, pb=pb: t.matmul(
                    pb[:, :], ONESR3[0:1, :], BCROW[0:1, c * TC : (c + 1) * TC],
                    start=True, stop=True), sem=s_bb)
                p.wait("vector", s_bb, bv)
                pb_hist.append(p.cop("vector", lambda v, c=c, pb=pb: v.tensor_tensor(
                    B4[:, c * TC : (c + 1) * TC], DUB[:, c * TC : (c + 1) * TC],
                    pb[:, :], ALU.mult)))
            p.cop("vector", lambda v: v.tensor_tensor(B3S[:, :], B4[:, :], B2[:, :], ALU.mult))
            p.cop("vector", lambda v: v.tensor_tensor_scan(
                B4[:, :], CMSK[:, :], B3S[:, :], 0.0, ALU.mult, ALU.add))
            p.cop("vector", lambda v: v.tensor_tensor(
                T5[:, :], B1S[:, 31::32], B4[:, 31::32], ALU.mult))
            p.cop("vector", lambda v: v.tensor_tensor_scan(
                H0S[:, :], B1S[:, 31::32], T5[:, :], 0.0, ALU.mult, ALU.add))
            sv3 = p.cop("vector", lambda v, n=n: v.tensor_copy(
                STO[:, n : n + 1], H0S[:, 127:128]))
            p.wait("scalar", chv, sv3)
            p.dma("scalar", d["b1d"][dt16 * 16 + n], B1S[:, :], s_b13)
            p.dma("scalar", d["b3d"][dt16 * 16 + n], B3S[:, :], s_b13)
            p.dma("scalar", d["h0d"][dt16 * 16 + n], H0S[:, :], s_b13)
        p.wait("gpsimd", chv)
        p.dma("gpsimd", d["cc3_in"], STO[:, :], s_ci3)
        p.wait("gpsimd", s_ci3)
        p.raw("gpsimd", lambda g: g.collective_compute(
            "AllGather", ALU.bypass, replica_groups=[[0, 1], [2, 3], [4, 5], [6, 7]],
            ins=[d["cc3_in"].opt()], outs=[d["cc3_out"].opt()]).then_inc(s_ag3.h, 1))
        s_ag3.n += 1
        p.wait("gpsimd", s_ag3)
        p.dma("gpsimd", STI3[:, :], d["cc3_out"][0:128, :], s_si3)
        p.wait("vector", s_si3)
        hm_v = p.cop("vector", lambda v: v.tensor_scalar(
            STI3[:, :], STI3[:, :], HM3[:, 0:1], None, ALU.mult))
        p.wait("vector", s_b13)
        yz = p.cop("vector", lambda v: v.memset(YA[:, :], 0.0))
        for n in range(16):
            sl = n % 2
            p.wait("sync", chv)
            p.wait("sync", s_bb)
            p.dma("sync", B1S[:, :], d["b1d"][dt16 * 16 + n], s_b13r[0])
            p.dma("sync", B3S[:, :], d["b3d"][dt16 * 16 + n], s_b13r[0])
            p.dma("sync", H0S[:, :], d["h0d"][dt16 * 16 + n], s_b13r[1])
            p.dma("sync", BCROW[0:1, :], d["bc3"][16 + n : 17 + n, :], s_bcr)
            p.wait("vector", s_b13r[0])
            p.wait("vector", s_b13r[1])
            p.cop("vector", lambda v: v.tensor_tensor_scan(
                CPS[:, :], B1S[:, 31::32], B1S[:, 31::32], 1.0, ALU.mult, ALU.bypass))
            hcol = STI3[:, n : n + 1]
            p.cop("vector", lambda v, hcol=hcol: v.tensor_scalar(
                T5[:, :], CPS[:, :], hcol, None, ALU.mult))
            p.cop("vector", lambda v: v.tensor_add(T5[:, :], T5[:, :], H0S[:, :]))
            p.cop("vector", lambda v: v.tensor_copy(HPS[:, 1:128], T5[:, 0:127]))
            p.cop("vector", lambda v, hcol=hcol: v.tensor_scalar(
                HPS[:, 0:1], hcol, 1.0, None, ALU.mult))
            p.cop("vector", lambda v: v.tensor_tensor(
                B3S[:, 0::32], B3S[:, 0::32], HPS[:, :], ALU.add))
            p.cop("vector", lambda v: v.tensor_tensor_scan(
                B4[:, :], CMSK[:, :], B3S[:, :], 0.0, ALU.mult, ALU.add))
            p.cop("vector", lambda v: v.tensor_tensor(B4[:, :], B4[:, :], B1S[:, :], ALU.mult))
            p.wait("tensor", s_bcr)
            for c in range(NCH8):
                pb = PS_B[c % 2]
                if len(pb_hist) >= 2:
                    p.wait("tensor", chv, pb_hist[-2])
                bv = p.top(lambda t, c=c, pb=pb: t.matmul(
                    pb[:, :], ONESR3[0:1, :], BCROW[0:1, c * TC : (c + 1) * TC],
                    start=True, stop=True), sem=s_bb)
                p.wait("vector", s_bb, bv)
                pb_hist.append(p.cop("vector", lambda v, c=c, pb=pb: v.tensor_tensor(
                    B2[:, c * TC : (c + 1) * TC], B4[:, c * TC : (c + 1) * TC],
                    pb[:, :], ALU.mult)))
            p.cop("vector", lambda v: v.tensor_add(YA[:, :], YA[:, :], B2[:, :]))
        p.dma("sync", ZB3[:, :], d["xz"][2048 + dt16 * 128 : 2048 + (dt16 + 1) * 128,
                                         512 : 512 + TL], s_ld3[2])
        p.wait("vector", s_ld3[2])
        p.wait("scalar", s_ld3[2])
        p.cop("vector", lambda v, dt16=dt16: v.scalar_tensor_tensor(
            YA[:, :], XCB2[:, :], DPAR[:, dt16 : dt16 + 1], YA[:, :], ALU.mult, ALU.add))
        zs = p.cop("scalar", lambda sc: sc.activation(ZB3[:, :], ZB3[:, :], AF.Silu))
        p.wait("vector", chs, zs)
        ymv = p.cop("vector", lambda v: v.tensor_tensor(YMB[:, :], YA[:, :], ZB3[:, :], ALU.mult))
        p.wait("scalar", chv, ymv)
        p.dma("scalar", d["ym"][dt16 * 128 : (dt16 + 1) * 128, :], YMB[:, :], s_ym)
    p.barrier(waits=[(s_ym, s_ym.n)])
    p.reset(mseg)

    epi_o3, s_so_o3 = epi_store(p, name=f"{name}_oe", dst=d["s3"], dst_c0=0)
    emit_gemm(p, name=f"{name}_og", src=d["ym"], src_c0=0, wsrc=wk["outproj_p"],
              d_in=2048, d_out=D, cols=TL, epi=epi_o3)
    p.barrier(waits=[(sm, sm.n) for sm in s_so_o3])
    p.reset(m0)


def l3attn_host(inp):
    cw = np.asarray(inp["L3_attn_conv_w"])[:, 0, :]
    cwp = np.zeros((128, 64), np.float32)
    for dt in range(16):
        for s_ in range(4):
            cwp[:, dt * 4 + s_] = cw[dt * 128 : (dt + 1) * 128, s_]
    al = np.asarray(inp["L3_attn_A_log"])
    alp = np.zeros((128, 256), np.float32)
    for dt in range(16):
        for n in range(16):
            alp[:, dt * 16 + n] = al[dt * 128 : (dt + 1) * 128, n]
    cm = np.ones(TL, np.float32)
    cm[0::32] = 0.0
    return dict(
        inproj_p=pack_w(np.asarray(inp["L3_attn_in_proj_w"])),
        convw_col=cwp,
        convb_col=pack_vec(np.asarray(inp["L3_attn_conv_b"])),
        xproj_p=pack_w(np.asarray(inp["L3_attn_x_proj_w"])),
        dtw_col=pack_vec(np.asarray(inp["L3_attn_dt_w"])[0]),
        dtb_col=pack_vec(np.asarray(inp["L3_attn_dt_b"])),
        alog_col=alp,
        dpar_col=pack_vec(np.asarray(inp["L3_attn_D_param"])),
        outproj_p=pack_w(np.asarray(inp["L3_attn_out_proj_w"])),
        cmask=np.tile(cm, (128, 1)),
        ones_p=np.ones((1, 128), np.float32),
    )




def emit_halo_exchange(p, d, *, name, xkey):
    m0 = p.mark()
    chv = p.chain["vector"]
    s_hi = p.sems(f"{name}_hi", 2)
    s_ag = p.sem(f"{name}_ag")
    s_ho = p.sems(f"{name}_ho", 2)
    HT = [p.sb([128, 8 * TC], F32, f"{name}_ht{i}") for i in range(2)]
    HMX = p.sb([128, 1], F32, f"{name}_hm")
    s_hm = p.sem(f"{name}_hm")
    p.dma("sync", HMX[:, :], d["hm_col"], s_hm)
    s_hg = p.sem(f"{name}_hg")
    p.dma("gpsimd", d["cch_in"], d[xkey][:, HALO + TL - 2048 : HALO + TL], s_hg)
    p.wait("gpsimd", s_hg)
    p.raw("gpsimd", lambda g: g.collective_compute(
        "AllGather", ALU.bypass, replica_groups=[[0, 1], [2, 3], [4, 5], [6, 7]],
        ins=[d["cch_in"].opt()], outs=[d["cch_out"].opt()]).then_inc(s_ag.h, 1))
    s_ag.n += 1
    p.wait("sync", s_ag)
    p.wait("vector", s_hm)
    done = []
    for c in range(4):
        sl = c % 2
        if c >= 2:
            p.wait("sync", chv, done[c - 2])
            p.wait("sync", s_ho[sl])
        p.dma("sync",
              HT[sl][:, :].rearrange("q (kt t) -> q kt t", t=TC),
              d["cch_out"][0:1024].rearrange("(kt q) t -> q kt t", q=128)[
                  :, :, c * TC : (c + 1) * TC],
              s_hi[sl])
        p.wait("vector", s_hi[sl])
        if c >= 2:
            p.wait("vector", s_ho[sl])
        vv = p.cop("vector", lambda v, sl=sl: v.tensor_scalar(
            HT[sl][:, :], HT[sl][:, :], HMX[:, 0:1], None, ALU.mult))
        done.append(vv)
        p.wait("scalar", chv, vv)
        p.dma("scalar",
              d[xkey].rearrange("(kt q) t -> q kt t", q=128)[:, :, c * TC : (c + 1) * TC],
              HT[sl][:, :].rearrange("q (kt t) -> q kt t", t=TC),
              s_ho[sl])
    p.barrier(waits=[(s_ho[0], s_ho[0].n), (s_ho[1], s_ho[1].n)])
    p.reset(m0)




def build_megakernel():
    nc = bass.Bass("TRN2", target_bir_lowering=False, debug=False)
    d = {}

    def din(name, shape, dtype=F32):
        d[name] = nc.dram_tensor(name, list(shape), dtype, kind="ExternalInput").ap()

    def dint(name, shape, dtype):
        d[name] = nc.dram_tensor(name, list(shape), dtype, kind="Internal").ap()

    din("xin0", [D, NCOL])
    d["y"] = nc.dram_tensor("y", [D, TL], F32, kind="ExternalOutput").ap()
    din("hm_col", [128, 1])
    for i in (1, 2, 3):
        dint(f"xh{i}", [D, NCOL], F32)
    dint("xn", [D, NCOL], BF16)
    dint("sbr", [D, TL], F32)
    dint("sbr16", [D, TL], BF16)
    dint("gbr", [D, TL], BF16)
    dint("cch_in", [1024, 2048], F32)
    dint("cch_out", [2048, 2048], F32)

    for nm in ("zlo0", "zhi0", "zlo1", "zhi1"):
        din(nm, [1024, 256])
    din("w_in_p", [128, 8 * 3072]); din("b_in_p", [128, 24]); din("gb_p", [128, 16])
    din("w_out_p", [128, 8 * 1024]); din("b_out_p", [128, 8]); din("ident_p", [128, 128])
    dint("zb", [4, 1024, 256], BF16)
    dint("tmats", [4 * 128 * 1024 * 128], BF16)
    dint("xln", [D, 4608], BF16)
    dint("proj", [3072, 4608], BF16)
    dint("ycv", [D, TL], BF16)

    for li in range(4):
        for nm in ("wqkv_p", "wo_p", "qk_col", "qk_row", "lng_col", "w1_p", "b1_col",
                   "w2_p", "b2_col"):
            shp = {"wqkv_p": [128, 8 * 1536], "wo_p": [128, 4 * 1024],
                   "qk_col": [128, 2], "qk_row": [1, 128], "lng_col": [128, 16],
                   "w1_p": [128, 8 * FFN_W1O], "b1_col": [128, 44],
                   "w2_p": [128, 22 * 1024], "b2_col": [128, 8]}[nm]
            din(f"f{li}_{nm}", shp)
    din("tab_i1", [512, TL + 2048]); din("tab_i2", [512, TL + 1024])
    din("rotq_lhsT", [128, 128]); din("masks", [128, 256]); din("maskh", [128, 128])
    din("ind2", [2, 128]); din("ind2t", [128, 2]); din("ones_p", [1, 128])
    dint("qkv", [1536, NCOL], BF16)
    dint("onrm", [512, NCOL], BF16)
    dint("xmid", [D, NCOL], BF16)
    dint("xf", [D, NCOL], BF16)
    dint("fmid", [D, NCOL], BF16)
    dint("graw", [D, NCOL], BF16)

    din("abc_col", [128, 56])
    dint("z1", [1024, 256], BF16)
    dint("t1m", [128 * 1024 * 128], BF16)
    dint("yraw1", [D, TL], F32)
    dint("cc1_in", [128, 16], F32)
    dint("cc1_out", [256, 16], F32)

    din("l2_wr_p", [128, 8 * 1024]); din("l2_wk_p", [128, 8 * 1024])
    din("l2_wv_p", [128, 8 * 1024]); din("l2_wo_p", [128, 8 * 1024])
    din("l2_decay_col", [128, 8]); din("l2_lnw_col", [128, 8])
    din("l2_iota", [128, TL])
    din("wg_i", [128, 8 * 3072], I32); din("wu_i", [128, 8 * 3072], I32)
    din("wd_i", [128, 24 * 1024], I32)
    for nm, shp in (("r2", [D, TL]), ("k2b", [D, TL]), ("v2", [D, TL]),
                    ("yn2", [D, TL])):
        dint(nm, shp, BF16)
    dint("cum2", [D, TL], F32)
    dint("y2s", [D, TL], F32)
    dint("cc2_in", [128, 8], F32)
    dint("cc2_out", [256, 8], F32)
    dint("wg_b", [128, 8 * 3072], BF16); dint("wu_b", [128, 8 * 3072], BF16)
    dint("wd_b", [128, 24 * 1024], BF16)
    dint("gsig", [3072, TL], BF16)
    dint("up_h", [3072, TL], BF16)

    din("inproj_p", [128, 8 * 4096]); din("convw_col", [128, 64])
    din("convb_col", [128, 16]); din("xproj_p", [128, 16 * 33])
    din("dtw_col", [128, 16]); din("dtb_col", [128, 16]); din("alog_col", [128, 256])
    din("dpar_col", [128, 16]); din("outproj_p", [128, 16 * 1024])
    din("cmask", [128, TL])
    dint("xz", [4096, TL + 512], BF16)
    dint("xc", [2048, TL], BF16)
    dint("dt3", [2048, TL], F32)
    dint("bc3", [32, TL], F32)
    dint("b1d", [256, 128, TL], BF16)
    dint("b3d", [256, 128, TL], BF16)
    dint("h0d", [256, 128, 128], F32)
    dint("ym", [2048, TL], BF16)
    dint("cc3_in", [128, 16], F32)
    dint("cc3_out", [256, 16], F32)
    din("l3f_wr_p", [128, 8 * 1024]); din("l3f_wk_p", [128, 8 * 1024])
    din("l3f_wv_p", [128, 8 * 1024])
    din("l3f_mur_col", [128, 8]); din("l3f_muk_col", [128, 8])
    dint("xr", [D, TL], BF16); dint("xk", [D, TL], BF16)
    dint("rform", [D, TL], BF16); dint("k2", [D, TL], BF16)

    with ExitStack() as st:
        p = Prog(nc, st)

        def wkv(prefix, names):
            return {nm: d[f"{prefix}{nm}"] for nm in names}

        for li in range(4):
            xsrc = d["xin0"] if li == 0 else d[f"xh{li}"]
            m = p.mark()
            lnd = emit_lngb(p, name=f"ln{li}", xsrc=xsrc, xdst=d["xn"], c0=0,
                            cols=NCOL, src_f32=True)
            p.barrier(waits=lnd)
            p.reset(m)

            if li == 0:
                d["s0"] = d["sbr"]
                emit_l0attn(p, d, name="l0a")
                s_key, s_c0 = "sbr", 0
            elif li == 1:
                d["s1"] = d["sbr16"]
                emit_l1attn(p, d, name="l1a", wk=wkv("", ["abc_col", "ident_p"]))
                s_key, s_c0 = "sbr16", 0
            elif li == 2:
                d["s2"] = d["sbr16"]
                emit_l2attn(p, d, name="l2a", wk={
                    "wr_p": d["l2_wr_p"], "wk_p": d["l2_wk_p"], "wv_p": d["l2_wv_p"],
                    "wo_p": d["l2_wo_p"], "decay_col": d["l2_decay_col"],
                    "lnw_col": d["l2_lnw_col"], "iota": d["l2_iota"]})
                s_key, s_c0 = "sbr16", 0
            else:
                d["s3"] = d["sbr16"]
                emit_l3attn(p, d, name="l3a", wk=wkv("", [
                    "inproj_p", "convw_col", "convb_col", "xproj_p", "dtw_col",
                    "dtb_col", "alog_col", "dpar_col", "outproj_p", "cmask", "ones_p"]))
                s_key, s_c0 = "sbr16", 0

            if li in (0, 1):
                base = 2 * li
                for inner in range(2):
                    wk = {nm: d[f"f{base+inner}_{nm}"] for nm in (
                        "wqkv_p", "wo_p", "qk_col", "qk_row", "lng_col", "w1_p",
                        "b1_col", "w2_p", "b2_col")}
                    wk["tab"] = d["tab_i1"] if inner == 0 else d["tab_i2"]
                    for nm in ("rotq_lhsT", "masks", "maskh", "ind2", "ind2t",
                               "ones_p", "ident_p"):
                        wk[nm] = d[nm]
                    src = d["xn"] if inner == 0 else d["fmid"]
                    dst = d["fmid"] if inner == 0 else d["graw"]
                    q0 = -1536 if inner == 0 else -512
                    emit_ffn_inner(p, d, name=f"ff{li}{inner}", src=src, dst=dst,
                                   q0=q0, wk=wk)
                g_key, g_c0, g_dt = "graw", HALO, BF16
            elif li == 2:
                d["g2"] = d["gbr"]
                emit_l2ffn(p, d, name="l2f", wk={})
                g_key, g_c0, g_dt = "gbr", 0, BF16
            else:
                d["g3"] = d["gbr"]
                emit_l3ffn(p, d, name="l3f", wk={
                    "wr_p": d["l3f_wr_p"], "wk_p": d["l3f_wk_p"],
                    "wv_p": d["l3f_wv_p"], "mur_col": d["l3f_mur_col"],
                    "muk_col": d["l3f_muk_col"]})
                g_key, g_c0, g_dt = "gbr", 0, BF16

            xout_key = f"xh{li+1}" if li < 3 else "y"
            emit_combine(p, d, name=f"cmb{li}", s_key=s_key, s_c0=s_c0,
                         s_dt=F32 if li == 0 else BF16,
                         g_key=g_key, g_c0=g_c0, xin_key=("xin0" if li == 0 else f"xh{li}"),
                         xout_key=xout_key, out_is_flat=(li == 3))
            if li < 3:
                emit_halo_exchange(p, d, name=f"hx{li}", xkey=f"xh{li+1}")
        p.build()
    return nc, d


def host_prep(inputs):
    sh = {}
    sh.update(l0attn_host(inputs, None))
    c = ffn_host_consts()
    sh.update({k: v for k, v in c.items() if k != "tab"})
    tabfull = None
    TW = T + 2048
    tg = np.arange(-2048, T, dtype=np.float64)
    half = np.arange(0, 64, 2, dtype=np.float64)
    inv_freq = 1.0 / 10000.0 ** (half / 64.0)
    freqs = np.concatenate([tg[None, :] * inv_freq[:, None]] * 2, 0)
    base = (half + 25.6) / 89.6
    power = (tg - 4096.0) / 256.0
    scale = np.concatenate([base[:, None] ** power[None, :]] * 2, 0)
    tabfull = np.concatenate(
        [np.tile(x, (2, 1)) for x in (
            (np.cos(freqs) * scale), (np.sin(freqs) * scale),
            (np.cos(freqs) / scale), (np.sin(freqs) / scale))], 0
    ).astype(np.float32)
    for li in range(2):
        for inner in range(2):
            wqkv = np.asarray(inputs[f"L{li}_ffn_wqkv"])[inner]
            pk = ffn_host_layer(
                wqkv,
                np.asarray(inputs[f"L{li}_ffn_q_scale"])[inner],
                np.asarray(inputs[f"L{li}_ffn_k_scale"])[inner],
                np.asarray(inputs[f"L{li}_ffn_wo"])[inner],
                np.asarray(inputs[f"L{li}_ffn_ln_g"])[inner],
                np.asarray(inputs[f"L{li}_ffn_ln_b"])[inner],
                np.asarray(inputs[f"L{li}_ffn_w1"])[inner],
                np.asarray(inputs[f"L{li}_ffn_b1"])[inner],
                np.asarray(inputs[f"L{li}_ffn_w2"])[inner],
                np.asarray(inputs[f"L{li}_ffn_b2"])[inner],
            )
            for k_, v_ in pk.items():
                sh[f"f{2*li+inner}_{k_}"] = v_
    sh["abc_col"] = l1attn_host(inputs)["abc_col"]
    l2 = l2attn_host(inputs)
    sh.update({"l2_wr_p": l2["wr_p"], "l2_wk_p": l2["wk_p"], "l2_wv_p": l2["wv_p"],
               "l2_wo_p": l2["wo_p"], "l2_decay_col": l2["decay_col"],
               "l2_lnw_col": l2["lnw_col"]})
    sh.update(l2ffn_host(inputs))
    sh.update(l3attn_host(inputs))
    l3f = l3ffn_host(inputs)
    sh.update({"l3f_wr_p": l3f["wr_p"], "l3f_wk_p": l3f["wk_p"],
               "l3f_wv_p": l3f["wv_p"], "l3f_mur_col": l3f["mur_col"],
               "l3f_muk_col": l3f["muk_col"]})

    x0 = np.asarray(inputs["L0_attn_x"], np.float32)
    iota_full = np.tile(np.arange(T, dtype=np.float32), (128, 1))
    percore = []
    for core in range(8):
        b, h = core // 2, core % 2
        g_t0 = h * TL
        m = dict(sh)
        xin = np.zeros((D, NCOL), np.float32)
        xin[:, HALO:] = x0[b, g_t0 : g_t0 + TL].T
        if h == 1:
            xin[:, :HALO] = x0[b, g_t0 - HALO : g_t0].T
        m["xin0"] = xin
        m["hm_col"] = np.full((128, 1), float(h), np.float32)
        m["maskh"] = np.full((128, 128), -1e30 if h == 0 else 0.0, np.float32)
        m["tab_i1"] = np.ascontiguousarray(
            tabfull[:, g_t0 + (-2048) + 2048 : g_t0 + TL + 2048])
        m["tab_i2"] = np.ascontiguousarray(
            tabfull[:, g_t0 + (-1024) + 2048 : g_t0 + TL + 2048])
        m["l2_iota"] = np.ascontiguousarray(iota_full[:, g_t0 : g_t0 + TL])
        percore.append(m)
    return percore


def kernel(**inputs):
    nc, _ = build_megakernel()
    percore = host_prep(inputs)
    res = run_bass_kernel_spmd(nc, percore, core_ids=list(range(8)))
    out = np.zeros((B, T, D), np.float32)
    for core in range(8):
        b, h = core // 2, core % 2
        out[b, h * TL : (h + 1) * TL] = res.results[core]["y"].T
    return out


# version 04
# speedup vs baseline: 1.2021x; 1.2021x over previous
import numpy as np
import ml_dtypes
from contextlib import ExitStack
import concourse.bass as bass
import concourse.mybir as mybir
from concourse.bass_utils import run_bass_kernel_spmd

F32 = mybir.dt.float32
BF16 = mybir.dt.bfloat16
I32 = mybir.dt.int32
AF = mybir.ActivationFunctionType
ALU = mybir.AluOpType
AX = mybir.AxisListType
BF16NP = ml_dtypes.bfloat16

ENGINES = ("sync", "gpsimd", "tensor", "vector", "scalar")

B, T, D = 4, 8192, 1024
TL = 4096
HALO = 2048
NCOL = HALO + TL
TC = 512
NKT = D // 128



class Sem:
    __slots__ = ("h", "n", "name", "base", "home")

    def __init__(self, h, name):
        self.h, self.n, self.name = h, 0, name
        self.base = 0
        self.home = None


SELF_CHAIN = ("vector", "scalar", "gpsimd")


class Prog:

    ARENA_BYTES = 200 * 1024
    PSUM_BANKS = 8

    def __init__(self, nc, st):
        self.nc = nc
        self.st = st
        self.ops = {e: [] for e in ENGINES}
        self._semi = 0
        self._sbi = 0
        self._sem_pool = []
        self._sem_ptr = 0
        self._swdge_ok = set()
        self._dma_active = {}
        self.chain = {}
        for e in SELF_CHAIN:
            self.chain[e] = self.sem(f"ch_{e}")
        self._sem_pool = []
        self._sem_ptr = 0
        self.arena = st.enter_context(
            nc.sbuf_tensor("arena", [128, self.ARENA_BYTES], mybir.dt.uint8)
        )
        self.arena_off = 0
        self.psum_arena = st.enter_context(nc.psum_tensor("psarena", [128, 4096], F32))
        self.psum_off = 0

    def mark(self):
        return (self.arena_off, self.psum_off, self._sem_ptr)

    def reset(self, m):
        self.arena_off, self.psum_off = m[0], m[1]
        if len(m) > 2:
            self._sem_ptr = m[2]

    def sem(self, name=None):
        if self._sem_ptr < len(self._sem_pool):
            sm = self._sem_pool[self._sem_ptr]
        else:
            self._semi += 1
            nm = f"s{self._semi}"
            h = self.st.enter_context(self.nc.semaphore(nm))
            sm = Sem(h, nm)
            self._sem_pool.append(sm)
        self._sem_ptr += 1
        if id(sm) in self._swdge_ok:
            sm.h, sm.n = sm.home
            sm.home = None
            self._swdge_ok.discard(id(sm))
        sm.base = sm.n
        return sm

    def sems(self, name, k):
        return [self.sem(f"{name}{i}") for i in range(k)]

    def sb(self, shape, dtype, name=None):
        nbytes = int(shape[1]) * {F32: 4, BF16: 2, I32: 4}[dtype]
        nbytes = (nbytes + 31) & ~31
        assert self.arena_off + nbytes <= self.ARENA_BYTES, (
            "SBUF arena overflow", name, self.arena_off, nbytes)
        ap = self.arena[0 : shape[0], self.arena_off : self.arena_off + nbytes].bitcast(dtype)
        self.arena_off += nbytes
        return ap[0 : shape[0], 0 : int(shape[1])]

    def ps(self, shape, dtype=F32, name=None):
        eb = {F32: 4, BF16: 2}[dtype]
        nbanks = (int(shape[1]) * eb + 2047) // 2048
        assert self.psum_off + nbanks <= self.PSUM_BANKS, ("PSUM overflow", name)
        b0 = self.psum_off
        self.psum_off += nbanks
        ap = self.psum_arena[:, b0 * 512 : (b0 + nbanks) * 512]
        if dtype != F32:
            ap = ap.bitcast(dtype)
        return ap[0 : shape[0], 0 : int(shape[1])]

    def raw(self, eng, fn):
        self.ops[eng].append(fn)

    def cop(self, eng, fn):
        ch = self.chain[eng]
        self.ops[eng].append(lambda engine, h=ch.h, v=ch.n: engine.wait_ge(h, v))
        ch.n += 1
        self.ops[eng].append(lambda engine, fn=fn, h=ch.h: fn(engine).then_inc(h, 1))
        return ch.n

    def top(self, fn, sem=None, amt=1):
        if sem is None:
            self.ops["tensor"].append(fn)
            return None
        sem.n += amt
        self.ops["tensor"].append(
            lambda engine, fn=fn, h=sem.h, a=amt: fn(engine).then_inc(h, a)
        )
        return sem.n

    def dma(self, q, out, in_, sem, **kw):
        if q == "gpsimd" and id(sem) not in self._swdge_ok:
            self._semi += 1
            nm = f"sw{self._semi}"
            sem.home = (sem.h, sem.n)
            sem.h = self.st.enter_context(self.nc.semaphore(nm))
            sem.n = 0
            sem.base = 0
            self._swdge_ok.add(id(sem))
        sem.n += 16
        h = sem.h
        self._dma_active[id(sem)] = sem

        def run(engine, out=out, in_=in_, h=h, kw=kw):
            engine.dma_start(out=out, in_=in_, **kw).then_inc(h, 16)

        self.ops[q].append(run)
        return sem.n

    def wait(self, eng, sem, val=None):
        v = sem.n if val is None else val
        self.ops[eng].append(lambda engine, h=sem.h, v=v: engine.wait_ge(h, v))
        return v

    def vchain(self, eng):
        return self.chain[eng].n

    def barrier(self, waits=()):
        bs = self.sem()
        for e in ENGINES:
            for sm, v in waits:
                self.wait(e, sm, v)
        for e in SELF_CHAIN:
            self.wait(e, self.chain[e])
        for sm in self._dma_active.values():
            self.wait("sync", sm)
        self._dma_active = {}
        for e in ENGINES:
            bs.n += 1
            self.ops[e].append(lambda eng, h=bs.h: eng.nop().then_inc(h, 1))
        for e in ENGINES:
            self.wait(e, bs, bs.n)
        return bs

    def build(self):
        block = self.st.enter_context(self.nc.Block())
        for e in ENGINES:
            fns = self.ops[e]
            if not fns:
                continue

            def body(engine, fns=fns):
                for f in fns:
                    f(engine)

            getattr(block, e)(body)




def pack_w(w):
    d_in, d_out = w.shape
    nkt = d_in // 128
    return np.ascontiguousarray(
        w.reshape(nkt, 128, d_out).transpose(1, 0, 2).reshape(128, nkt * d_out)
    ).astype(np.float32)


def pack_wT(w):
    return pack_w(np.ascontiguousarray(w.T))


def pack_vec(v, pad_to=None):
    d = v.shape[0]
    if pad_to is not None and d < pad_to:
        v = np.concatenate([v, np.zeros(pad_to - d, v.dtype)])
        d = pad_to
    return np.ascontiguousarray(v.reshape(d // 128, 128).T).astype(np.float32)


def pad_cols(w, to):
    d_in, d_out = w.shape
    if d_out == to:
        return w
    return np.concatenate([w, np.zeros((d_in, to - d_out), w.dtype)], axis=1)


def pad_rows(w, to):
    d_in, d_out = w.shape
    if d_in == to:
        return w
    return np.concatenate([w, np.zeros((to - d_in, d_out), w.dtype)], axis=0)




def emit_gemm(
    p,
    *,
    name,
    src,
    src_c0,
    wsrc,
    d_in,
    d_out,
    cols,
    epi,
    pre=None,
    src_ready=None,
    tc=TC,
    n_ps=4,
    w_bf16=False,
    shift_half=False,
):
    nkt = d_in // 128
    not_ = (d_out + 127) // 128
    nch = cols // tc
    assert cols % tc == 0, (name, cols, tc)

    W = p.sb([128, nkt * d_out], BF16, f"{name}_W")
    X = [p.sb([128, nkt * tc], BF16, f"{name}_X{i}") for i in range(2)]
    PS = [p.ps([128, tc], F32, f"{name}_PS{i}") for i in range(n_ps)]
    s_wp = p.sem(f"{name}_wp")
    s_xp = p.sems(f"{name}_xp", 2)
    s_mm = p.sem(f"{name}_mm")

    p.dma("sync" if w_bf16 else "gpsimd", W[:, :], wsrc, s_wp)
    p.wait("tensor", s_wp)

    psum_freed = []
    for c in range(nch):
        tc_here = tc
        xs = c % 2
        if c >= 2:
            p.wait("sync", s_mm, s_mm.base + (c - 1) * not_)
        if src_ready is not None:
            sm, v = src_ready(c)
            p.wait("sync", sm, v)
        if not shift_half:
            p.dma(
                "sync",
                X[xs][:, :].rearrange("q (kt t) -> q kt t", t=tc),
                src.rearrange("(kt q) t -> q kt t", q=128)[
                    :, :, src_c0 + c * tc : src_c0 + (c + 1) * tc
                ],
                s_xp[xs],
            )
        else:
            hk = nkt // 2
            p.dma(
                "sync",
                X[xs][:, : hk * tc].rearrange("q (kt t) -> q kt t", t=tc),
                src.rearrange("(kt q) t -> q kt t", q=128)[
                    :, :hk, src_c0 + c * tc : src_c0 + (c + 1) * tc
                ],
                s_xp[xs],
            )
            sc0 = max(src_c0 + c * tc - 1, 0)
            p.dma(
                "sync",
                X[xs][:, hk * tc :].rearrange("q (kt t) -> q kt t", t=tc),
                src.rearrange("(kt q) t -> q kt t", q=128)[:, hk:, sc0 : sc0 + tc],
                s_xp[xs],
            )
        p.wait("tensor", s_xp[xs])
        if pre is not None:
            pre(p, c, tc_here)
        for o in range(not_):
            grp = c * not_ + o
            b = grp % n_ps
            ow = min(128, d_out - o * 128)
            if grp >= n_ps:
                sm, v = psum_freed[grp - n_ps]
                p.wait("tensor", sm, v)

            def mms(t, xs=xs, o=o, b=b, ow=ow, tc_here=tc_here):
                for k in range(nkt):
                    ins = t.matmul(
                        PS[b][:ow, :tc_here],
                        W[:, k * d_out + o * 128 : k * d_out + o * 128 + ow],
                        X[xs][:, k * tc : k * tc + tc_here],
                        start=(k == 0),
                        stop=(k == nkt - 1),
                    )
                return ins

            p.top(mms, sem=s_mm)
            psum_freed.append(epi(p, c, o, PS[b][:ow, :tc_here], tc_here, s_mm))
    return s_mm


def epi_store(
    p, *, name, dst, dst_c0, bias_sb=None, act=None, scale=None, tc=TC,
    eng="vector", nslots=4, extra=None,
):
    OB = [p.sb([128, tc], dst.dtype, f"{name}_OB{i}") for i in range(nslots)]
    s_oc = p.sems(f"{name}_oc", nslots)
    ctr = [0]
    engine_for_act = "scalar"

    def epi(p, c, o, psum, tc_here, s_mm):
        i = ctr[0]
        ctr[0] += 1
        sl = i % nslots
        e = engine_for_act if act is not None else eng
        p.wait(e, s_mm)
        if i >= nslots:
            p.wait(e, s_oc[sl])

        def ev(v, psum=psum, sl=sl, tc_here=tc_here, o=o):
            ob = OB[sl][: psum.shape[0], :tc_here]
            if act is not None:
                return v.activation(
                    ob, psum, act,
                    bias=bias_sb[:, o : o + 1] if bias_sb is not None else 0.0,
                    scale=1.0 if scale is None else float(scale),
                )
            if bias_sb is not None:
                return v.tensor_scalar(ob, psum, bias_sb[:, o : o + 1], None, ALU.add)
            if scale is not None:
                return v.tensor_scalar(ob, psum, float(scale), None, ALU.mult)
            return v.tensor_copy(ob, psum)

        val = p.cop(e, ev)
        if extra is not None:
            extra(p, e, OB[sl], c, o, tc_here)
            val = p.vchain(e)
        p.wait("scalar", p.chain[e], val)
        p.dma(
            "scalar",
            dst[o * 128 : o * 128 + psum.shape[0], dst_c0 + c * tc : dst_c0 + c * tc + tc_here],
            OB[sl][: psum.shape[0], :tc_here],
            s_oc[sl],
        )
        return (p.chain[e], val)

    return epi, s_oc




def emit_lngb(p, *, name, xsrc, xdst, c0, cols, gamma_sb=None, beta_sb=None,
              src_f32=True, eps=1e-5, tc=TC, dst_c0=None, shift_half=False):
    if dst_c0 is None:
        dst_c0 = c0

    def _load(p, buf, c, sem):
        if not shift_half:
            p.dma(
                "sync",
                buf[:, :].rearrange("q (kt t) -> q kt t", t=tc),
                xsrc.rearrange("(kt q) t -> q kt t", q=128)[
                    :, :, c0 + c * tc : c0 + (c + 1) * tc
                ],
                sem,
            )
        else:
            hk = NKT // 2
            p.dma(
                "sync",
                buf[:, : hk * tc].rearrange("q (kt t) -> q kt t", t=tc),
                xsrc.rearrange("(kt q) t -> q kt t", q=128)[
                    :, :hk, c0 + c * tc : c0 + (c + 1) * tc
                ],
                sem,
            )
            p.dma(
                "sync",
                buf[:, hk * tc :].rearrange("q (kt t) -> q kt t", t=tc),
                xsrc.rearrange("(kt q) t -> q kt t", q=128)[
                    :, hk:, c0 + c * tc - 1 : c0 + (c + 1) * tc - 1
                ],
                sem,
            )
    """LN over partitions with optional per-partition affine (gamma/beta columns
    (128, NKT)). xsrc dtype fp32 or bf16; xdst bf16."""
    nch = cols // tc
    sdt = F32 if src_f32 else BF16
    X = [p.sb([128, NKT * tc], sdt, f"{name}_X{i}") for i in range(2)]
    XB = [p.sb([128, NKT * tc], BF16, f"{name}_XB{i}") for i in range(2)] if src_f32 else X
    SQ = [p.sb([128, tc], BF16, f"{name}_SQ{i}") for i in range(2)]
    ONES = p.sb([128, 1], BF16, f"{name}_ones")
    ONES1 = p.sb([1, 128], F32, f"{name}_ones1")
    STATS = p.sb([1, 2 * nch * tc], F32, f"{name}_stats")
    AB = p.sb([1, 2 * nch * tc], F32, f"{name}_ab")
    VAR = p.sb([1, nch * tc], F32, f"{name}_var")
    EPST = p.sb([1, 1], F32, f"{name}_eps")
    PSSTa = p.ps([1, tc], F32, f"{name}_pssta")
    PSSTb = p.ps([1, tc], F32, f"{name}_psstb")
    PSBC = [p.ps([128, tc], F32, f"{name}_psbc{i}") for i in range(2)]
    s_xp = p.sems(f"{name}_xp", 2)
    s_x2p = p.sems(f"{name}_x2p", 2)
    s_mmk = p.sem(f"{name}_mmk")
    s_bc = p.sem(f"{name}_bc")
    s_oc = p.sems(f"{name}_oc", 2)
    chv = p.chain["vector"]
    chs = p.chain["scalar"]

    p.cop("vector", lambda v: v.memset(ONES[:, :], 1.0))
    p.cop("vector", lambda v: v.memset(ONES1[:, :], 1.0))
    init_done = p.cop("vector", lambda v: v.memset(EPST[:, :], float(eps)))
    p.wait("tensor", chv, init_done)

    cast_vals = {}
    for c in range(nch):
        xs = c % 2
        if c >= 2:
            if src_f32:
                p.wait("sync", chv, cast_vals[c - 2])
            else:
                p.wait("sync", s_mmk, s_mmk.base + (c - 1) * NKT)
        _load(p, X[xs], c, s_xp[xs])
        if src_f32:
            p.wait("vector", s_xp[xs])
            if c >= 2:
                p.wait("vector", s_mmk, s_mmk.base + (c - 1) * NKT)
            cast_vals[c] = p.cop(
                "vector", lambda v, xs=xs: v.tensor_copy(XB[xs][:, :], X[xs][:, :])
            )
            p.wait("scalar", chv, cast_vals[c])
            p.wait("tensor", chv, cast_vals[c])
        else:
            p.wait("scalar", s_xp[xs])
            p.wait("tensor", s_xp[xs])
        mmv = {}
        for k in range(NKT):
            sqs = k % 2
            if k >= 2:
                p.wait("scalar", s_mmk, mmv[k - 2])
            elif c > 0:
                p.wait("scalar", s_mmk, s_mmk.base + c * NKT - 2 + k + 1)

            def sqf(sc, xs=xs, k=k, sqs=sqs):
                return sc.activation(SQ[sqs][:, :], XB[xs][:, k * tc : (k + 1) * tc], AF.Square)

            sv = p.cop("scalar", sqf)
            p.wait("tensor", chs, sv)

            def mm_sq(t, xs=xs, k=k, sqs=sqs):
                t.matmul(PSSTa[0:1, 0:tc], ONES[:, :], XB[xs][:, k * tc : (k + 1) * tc],
                         start=(k == 0), stop=(k == NKT - 1))
                return t.matmul(PSSTb[0:1, 0:tc], ONES[:, :], SQ[sqs][:, :],
                                start=(k == 0), stop=(k == NKT - 1))

            mmv[k] = p.top(mm_sq, sem=s_mmk)
        p.wait("vector", s_mmk, s_mmk.base + (c + 1) * NKT)
        p.cop(
            "vector",
            lambda v, c=c: v.tensor_copy(STATS[0:1, c * 2 * tc : c * 2 * tc + tc], PSSTa[0:1, :]),
        )
        ev = p.cop(
            "vector",
            lambda v, c=c: v.tensor_copy(
                STATS[0:1, c * 2 * tc + tc : (c + 1) * 2 * tc], PSSTb[0:1, :]
            ),
        )
        p.wait("tensor", chv, ev)

    stats3 = STATS[0:1, :].rearrange("o (c two t) -> o c two t", two=2, t=tc)
    sums = stats3[:, :, 0, :]
    sqs_ = stats3[:, :, 1, :]
    ab3 = AB[0:1, :].rearrange("o (c two t) -> o c two t", two=2, t=tc)
    a_v = ab3[:, :, 0, :]
    b_v = ab3[:, :, 1, :]
    var2 = VAR[0:1, :].rearrange("o (c t) -> o c t", t=tc)
    p.cop("vector", lambda v: v.tensor_scalar(a_v, sums, 1.0 / D, None, ALU.mult))
    p.cop("vector", lambda v: v.tensor_scalar(var2, sqs_, 1.0 / D, None, ALU.mult))
    p.cop("vector", lambda v: v.tensor_mul(b_v, a_v, a_v))
    vv = p.cop("vector", lambda v: v.tensor_sub(var2, var2, b_v))
    p.wait("scalar", chv, vv)
    p.cop("scalar", lambda s: s.activation(VAR[0:1, :], VAR[0:1, :], AF.Ln, bias=EPST[0:1, 0:1]))
    sv = p.cop("scalar", lambda s: s.activation(VAR[0:1, :], VAR[0:1, :], AF.Exp, scale=-0.5))
    p.wait("vector", chs, sv)
    p.cop("vector", lambda v: v.tensor_mul(b_v, a_v, var2))
    p.cop("vector", lambda v: v.tensor_scalar(b_v, b_v, -1.0, None, ALU.mult))
    ab_ready = p.cop("vector", lambda v: v.tensor_copy(a_v, var2))
    p.wait("tensor", chv, ab_ready)

    X2 = X
    XO = XB
    p2_vals = {}
    done = []
    for c in range(nch):
        xs = c % 2
        if c >= 2:
            p.wait("sync", chv, p2_vals[c - 2])
            if not src_f32:
                p.wait("sync", s_oc[xs])
        else:
            p.wait("sync", chv, ab_ready)
        _load(p, X2[xs], c, s_x2p[xs])
        if c >= 1:
            p.wait("tensor", chv, p2_vals[c - 1])

        def bc(t, c=c):
            t.matmul(PSBC[0][:, :], ONES1[:, :], a_v[:, c, :], start=True, stop=True)
            return t.matmul(PSBC[1][:, :], ONES1[:, :], b_v[:, c, :], start=True, stop=True)

        p.top(bc, sem=s_bc)
        p.wait("vector", s_bc)
        p.wait("vector", s_x2p[xs])
        if c >= 2:
            p.wait("vector", s_oc[xs])
        TMP = X2
        for k in range(NKT):
            p.cop(
                "vector",
                lambda v, xs=xs, k=k: v.tensor_tensor(
                    X2[xs][:, k * tc : (k + 1) * tc],
                    X2[xs][:, k * tc : (k + 1) * tc],
                    PSBC[0][:, :],
                    ALU.mult,
                ),
            )
            if gamma_sb is None:
                v2 = p.cop(
                    "vector",
                    lambda v, xs=xs, k=k: v.tensor_tensor(
                        XO[xs][:, k * tc : (k + 1) * tc],
                        X2[xs][:, k * tc : (k + 1) * tc],
                        PSBC[1][:, :],
                        ALU.add,
                    ),
                )
            else:
                p.cop(
                    "vector",
                    lambda v, xs=xs, k=k: v.tensor_tensor(
                        X2[xs][:, k * tc : (k + 1) * tc],
                        X2[xs][:, k * tc : (k + 1) * tc],
                        PSBC[1][:, :],
                        ALU.add,
                    ),
                )
                v2 = p.cop(
                    "vector",
                    lambda v, xs=xs, k=k: v.tensor_scalar(
                        XO[xs][:, k * tc : (k + 1) * tc],
                        X2[xs][:, k * tc : (k + 1) * tc],
                        gamma_sb[:, k : k + 1],
                        beta_sb[:, k : k + 1],
                        ALU.mult,
                        ALU.add,
                    ),
                )
        p2_vals[c] = v2
        p.wait("scalar", chv, v2)
        p.dma(
            "scalar",
            xdst.rearrange("(kt q) t -> q kt t", q=128)[
                :, :, dst_c0 + c * tc : dst_c0 + (c + 1) * tc
            ],
            XO[xs][:, :].rearrange("q (kt t) -> q kt t", t=tc),
            s_oc[xs],
        )
        done.append((s_oc[xs], s_oc[xs].n))
    return done






def emit_toeplitz_build(p, *, name, z_t, z_off, t_t, t_off, nch=1024, K=128):
    s_bt = p.sem(f"{name}_bt")
    for q in range(128):
        p.dma(
            "scalar",
            bass.AP(t_t, t_off + q * nch * K, [[K, nch], [1, K]]),
            bass.AP(z_t, z_off + K - q, [[2 * K, nch], [1, K]]),
            s_bt,
        )
    return s_bt


def emit_l0attn(p, d, *, name):
    C0P = HALO - 512
    PCOLS = 4608
    NB = 36
    TSZ = 128 * 1024 * 128

    m0 = p.mark()

    s_zc = p.sem(f"{name}_zc")
    zb = d["zb"]
    for i, znm in enumerate(("zlo0", "zhi0", "zlo1", "zhi1")):
        p.dma("gpsimd", zb[i], d[znm], s_zc)
    p.wait("scalar", s_zc)
    s_bt = [
        emit_toeplitz_build(
            p, name=f"{name}_tb{i}", z_t=zb.tensor, z_off=i * 1024 * 256,
            t_t=d["tmats"].tensor, t_off=i * TSZ,
        )
        for i in range(4)
    ]

    GB = p.sb([128, 16], F32, f"{name}_gb")
    BIN = p.sb([128, 24], F32, f"{name}_bin")
    BOUT = p.sb([128, 8], F32, f"{name}_bout")
    IDENT = p.sb([128, 128], BF16, f"{name}_id")
    HMC = p.sb([128, 1], F32, f"{name}_hmc")
    s_vec = p.sem(f"{name}_vec")
    p.dma("sync", HMC[:, :], d["hm_col"], s_vec)
    p.dma("sync", GB[:, :], d["gb_p"], s_vec)
    p.dma("sync", BIN[:, :], d["b_in_p"], s_vec)
    p.dma("sync", BOUT[:, :], d["b_out_p"], s_vec)
    s_idl = p.sem(f"{name}_idl")
    p.dma("gpsimd", IDENT[:, :], d["ident_p"], s_idl)
    p.wait("vector", s_vec)
    p.wait("scalar", s_vec)
    p.wait("tensor", s_vec)
    p.wait("tensor", s_idl)

    mln = p.mark()
    ln_done = emit_lngb(
        p, name=f"{name}_ln", xsrc=d["xn"], xdst=d["xln"], c0=C0P, cols=PCOLS,
        gamma_sb=GB[:, 0:8], beta_sb=GB[:, 8:16], src_f32=False, dst_c0=0,
    )
    p.barrier(waits=ln_done)
    p.reset(mln)

    epi, s_oc_pj = epi_store(p, name=f"{name}_pj", dst=d["proj"], dst_c0=0, bias_sb=BIN)
    emit_gemm(
        p, name=f"{name}_pg", src=d["xln"], src_c0=0, wsrc=d["w_in_p"],
        d_in=D, d_out=3072, cols=PCOLS, epi=epi,
    )
    p.barrier(waits=[(sm, sm.n) for sm in s_oc_pj])
    p.reset(mln)

    TB = [p.sb([128, 128 * 128], BF16, f"{name}_TB{i}") for i in range(2)]
    VT = [p.sb([128, NB * 128], BF16, f"{name}_VT{i}") for i in range(3)]
    Y0G = p.sb([128, NB * 128], BF16, f"{name}_y0g")
    Y1G = p.sb([128, 32 * 128], BF16, f"{name}_y1g")
    YO = p.sb([128, 32 * 128], BF16, f"{name}_yo")
    PSC = [p.ps([128, 35], F32) for i in range(4)]
    PSTa = p.ps([128, 128], BF16)
    PSTb = p.ps([128, 128], BF16)
    s_tb = p.sems(f"{name}_stb", 2)
    s_vt = p.sems(f"{name}_svt", 3)
    s_cm = p.sem(f"{name}_cm")
    s_tr = p.sem(f"{name}_tr")
    s_yo = p.sem(f"{name}_yo")
    chv = p.chain["vector"]

    for i in range(4):
        p.wait("sync", s_bt[i])
        p.wait("scalar", s_bt[i])

    gate2_last = 0
    conv_state = {}
    for dt_i in range(8):
        ch0 = dt_i * 128
        if dt_i > 0:
            p.wait("sync", s_cm, conv_state["cm_after_conv1"])
        p.dma("sync", TB[0][:, :],
              bass.AP(d["tmats"].tensor, 0 * TSZ + ch0 * 128, [[1024 * 128, 128], [1, 128 * 128]]),
              s_tb[0])
        p.dma("sync", TB[1][:, :],
              bass.AP(d["tmats"].tensor, 1 * TSZ + ch0 * 128, [[1024 * 128, 128], [1, 128 * 128]]),
              s_tb[1])
        tb_ready = (s_tb[0].n, s_tb[1].n)
        if dt_i > 0:
            p.wait("sync", s_cm, conv_state["cm_after_conv0"])
            p.wait("sync", chv, conv_state["gate1_last"])
            p.wait("sync", chv, conv_state["gate2_last"])
        for vi in range(3):
            for tb_j in range(NB):
                p.dma(
                    "sync",
                    VT[vi][:, tb_j * 128 : (tb_j + 1) * 128],
                    d["proj"][vi * 1024 + ch0 : vi * 1024 + ch0 + 128,
                              tb_j * 128 : (tb_j + 1) * 128],
                    s_vt[vi],
                    transpose=True,
                )
        vt_ready = [s_vt[vi].n for vi in range(3)]
        for vi in range(3):
            p.wait("vector", s_vt[vi], vt_ready[vi])
        for vi in range(3):
            p.cop("vector", lambda v, vi=vi: v.tensor_scalar(
                VT[vi][:, 0:512], VT[vi][:, 0:512], HMC[:, 0:1], None, ALU.mult))
        zdone = p.vchain("vector")
        p.wait("tensor", chv, zdone)

        p.wait("tensor", s_tb[0], tb_ready[0])
        p.wait("tensor", s_tb[1], tb_ready[1])
        p.wait("tensor", s_vt[0], vt_ready[0])
        p.wait("vector", s_vt[1], vt_ready[1])

        if dt_i > 0:
            p.wait("tensor", chv, conv_state["gate2_last"])
        gate1_vals = {}
        for ch in range(128):
            bnk = ch % 4
            if ch >= 4:
                p.wait("tensor", chv, gate1_vals[ch - 4])

            def c0mm(t, ch=ch, bnk=bnk):
                t.matmul(PSC[bnk][:, :], TB[0][:, ch * 128 : (ch + 1) * 128],
                         VT[0][:, 128 + ch : 128 + ch + 34 * 128 + 1 : 128],
                         start=True, stop=False)
                return t.matmul(PSC[bnk][:, :], TB[1][:, ch * 128 : (ch + 1) * 128],
                                VT[0][:, ch : ch + 34 * 128 + 1 : 128],
                                start=False, stop=True)

            mmv = p.top(c0mm, sem=s_cm)
            p.wait("vector", s_cm, mmv)
            gate1_vals[ch] = p.cop(
                "vector",
                lambda v, ch=ch, bnk=bnk: v.tensor_tensor(
                    Y0G[:, 128 + ch : 128 + ch + 34 * 128 + 1 : 128],
                    PSC[bnk][:, :],
                    VT[1][:, 128 + ch : 128 + ch + 34 * 128 + 1 : 128],
                    ALU.mult,
                ),
            )
        conv_state["cm_after_conv0"] = s_cm.n
        conv_state["gate1_last"] = gate1_vals[127]

        p.wait("sync", s_cm, conv_state["cm_after_conv0"])
        p.dma("sync", TB[0][:, :],
              bass.AP(d["tmats"].tensor, 2 * TSZ + ch0 * 128, [[1024 * 128, 128], [1, 128 * 128]]),
              s_tb[0])
        p.dma("sync", TB[1][:, :],
              bass.AP(d["tmats"].tensor, 3 * TSZ + ch0 * 128, [[1024 * 128, 128], [1, 128 * 128]]),
              s_tb[1])
        p.wait("tensor", s_tb[0])
        p.wait("tensor", s_tb[1])
        p.wait("vector", s_vt[2], vt_ready[2])

        gate2_vals = {}
        p.wait("tensor", chv, conv_state["gate1_last"])
        for ch in range(128):
            bnk = ch % 4
            if ch >= 4:
                p.wait("tensor", chv, gate2_vals[ch - 4])

            def c1mm(t, ch=ch, bnk=bnk):
                t.matmul(PSC[bnk][:, :32], TB[0][:, ch * 128 : (ch + 1) * 128],
                         Y0G[:, 4 * 128 + ch : 4 * 128 + ch + 31 * 128 + 1 : 128],
                         start=True, stop=False)
                return t.matmul(PSC[bnk][:, :32], TB[1][:, ch * 128 : (ch + 1) * 128],
                                Y0G[:, 3 * 128 + ch : 3 * 128 + ch + 31 * 128 + 1 : 128],
                                start=False, stop=True)

            mmv = p.top(c1mm, sem=s_cm)
            p.wait("vector", s_cm, mmv)
            gate2_vals[ch] = p.cop(
                "vector",
                lambda v, ch=ch, bnk=bnk: v.tensor_tensor(
                    Y1G[:, ch : ch + 31 * 128 + 1 : 128],
                    PSC[bnk][:, :32],
                    VT[2][:, 4 * 128 + ch : 4 * 128 + ch + 31 * 128 + 1 : 128],
                    ALU.mult,
                ),
            )
        conv_state["cm_after_conv1"] = s_cm.n
        conv_state["gate2_last"] = gate2_vals[127]

        p.wait("tensor", chv, gate2_vals[127])
        cp_vals = {}
        for tb_j in range(32):
            pst = PSTa if tb_j % 2 == 0 else PSTb
            if tb_j >= 2:
                p.wait("tensor", chv, cp_vals[tb_j - 2])
            if dt_i > 0 and tb_j < 2:
                p.wait("tensor", chv, conv_state.get("cp_last", 0))
            trv = p.top(
                lambda t, tb_j=tb_j, pst=pst: t.transpose(
                    pst[:, :], Y1G[:, tb_j * 128 : (tb_j + 1) * 128], IDENT[:, :]
                ),
                sem=s_tr,
            )
            p.wait("vector", s_tr, trv)
            if dt_i > 0 and tb_j == 0:
                p.wait("vector", s_yo, s_yo.base + (dt_i) * 16)
            cp_vals[tb_j] = p.cop(
                "vector",
                lambda v, tb_j=tb_j, pst=pst: v.tensor_copy(
                    YO[:, tb_j * 128 : (tb_j + 1) * 128], pst[:, :]
                ),
            )
        conv_state["cp_last"] = cp_vals[31]
        p.wait("scalar", chv, cp_vals[31])
        p.dma("scalar", d["ycv"][ch0 : ch0 + 128, :], YO[:, :], s_yo)

    p.barrier(waits=[(s_yo, s_yo.n)])
    p.reset(mln)

    RES = [p.sb([128, TC], BF16, f"{name}_res{i}") for i in range(4)]
    OBF = [p.sb([128, TC], F32, f"{name}_obf{i}") for i in range(4)]
    s_res = p.sems(f"{name}_res", 4)
    s_so = p.sems(f"{name}_so", 4)
    ctr = [0]

    res_consumed = {}

    def epi_wout(p, c, o, psum, tc_here, s_mm):
        i = ctr[0]
        ctr[0] += 1
        sl = i % 4
        if i >= 4:
            p.wait("sync", chv, res_consumed[i - 4])
        p.dma(
            "sync", RES[sl][:, :tc_here],
            d["xn"][o * 128 : (o + 1) * 128, HALO + c * TC : HALO + c * TC + tc_here],
            s_res[sl],
        )
        p.wait("vector", s_mm)
        p.wait("vector", s_res[sl])
        if i >= 4:
            p.wait("vector", s_so[sl])
        val = p.cop(
            "vector",
            lambda v, sl=sl, psum=psum, o=o, tc_here=tc_here: v.scalar_tensor_tensor(
                OBF[sl][:, :tc_here], psum, BOUT[:, o : o + 1], RES[sl][:, :tc_here],
                ALU.add, ALU.add,
            ),
        )
        res_consumed[i] = val
        p.wait("scalar", chv, val)
        p.dma(
            "scalar", d["s0"][o * 128 : (o + 1) * 128, c * TC : c * TC + tc_here],
            OBF[sl][:, :tc_here], s_so[sl],
        )
        return (chv, val)

    emit_gemm(
        p, name=f"{name}_wo", src=d["ycv"], src_c0=0, wsrc=d["w_out_p"],
        d_in=D, d_out=D, cols=TL, epi=epi_wout,
    )
    p.barrier(waits=[(sm, sm.n) for sm in s_so])
    p.reset(m0)


def l0attn_host(inp, core_args):
    K0 = np.asarray(inp["L0_attn_k0"])[0]
    K1 = np.asarray(inp["L0_attn_k1"])[0]
    Z = np.zeros((1024, 128), np.float32)
    out = {
        "zlo0": np.concatenate([Z, K0], 1),
        "zhi0": np.concatenate([K0, Z], 1),
        "zlo1": np.concatenate([Z, K1], 1),
        "zhi1": np.concatenate([K1, Z], 1),
        "w_in_p": pack_w(np.asarray(inp["L0_attn_W_in"])),
        "b_in_p": pack_vec(np.asarray(inp["L0_attn_b_in"])),
        "gb_p": np.concatenate(
            [pack_vec(np.asarray(inp["L0_attn_gamma"])), pack_vec(np.asarray(inp["L0_attn_beta"]))], 1
        ),
        "w_out_p": pack_w(np.asarray(inp["L0_attn_W_out"])),
        "b_out_p": pack_vec(np.asarray(inp["L0_attn_b_out"])),
        "ident_p": np.eye(128, dtype=np.float32),
    }
    return out



FFN_HID = 2816
FFN_W1O = 2 * FFN_HID


def epi_res_store(p, *, name, dst, dst_c0, res, res_c0, bias_sb=None, dtype=BF16,
                  post=None):
    RES = [p.sb([128, TC], BF16, f"{name}_res{i}") for i in range(4)]
    OBF = [p.sb([128, TC], dtype, f"{name}_obf{i}") for i in range(4)]
    s_res = p.sems(f"{name}_resp", 4)
    s_so = p.sems(f"{name}_so", 4)
    chv = p.chain["vector"]
    ctr = [0]
    consumed = {}

    def epi(p, c, o, psum, tc_here, s_mm):
        i = ctr[0]
        ctr[0] += 1
        sl = i % 4
        if i >= 4:
            p.wait("sync", chv, consumed[i - 4])
        p.dma(
            "sync", RES[sl][:, :tc_here],
            res[o * 128 : o * 128 + psum.shape[0], res_c0 + c * TC : res_c0 + c * TC + tc_here],
            s_res[sl],
        )
        p.wait("vector", s_mm)
        p.wait("vector", s_res[sl])
        if i >= 4:
            p.wait("vector", s_so[sl])
        if bias_sb is not None:
            p.cop(
                "vector",
                lambda v, sl=sl, psum=psum, o=o, tc_here=tc_here: v.scalar_tensor_tensor(
                    OBF[sl][: psum.shape[0], :tc_here], psum, bias_sb[:, o : o + 1],
                    RES[sl][:, :tc_here], ALU.add, ALU.add,
                ),
            )
        else:
            p.cop(
                "vector",
                lambda v, sl=sl, psum=psum, tc_here=tc_here: v.tensor_tensor(
                    OBF[sl][: psum.shape[0], :tc_here], psum, RES[sl][:, :tc_here], ALU.add
                ),
            )
        if post is not None:
            p.cop("vector", lambda v, sl=sl, c=c, o=o, tc_here=tc_here: post(
                v, OBF[sl][:, :tc_here], c, o, tc_here))
        val = p.vchain("vector")
        consumed[i] = val
        p.wait("scalar", chv, val)
        p.dma(
            "scalar", dst[o * 128 : o * 128 + psum.shape[0], dst_c0 + c * TC : dst_c0 + c * TC + tc_here],
            OBF[sl][: psum.shape[0], :tc_here], s_so[sl],
        )
        return (chv, val)

    return epi, s_so


def emit_ffn_inner(p, d, *, name, src, dst, q0, wk):
    kv0 = q0 - 512
    CK = HALO + kv0
    ncols_kv = TL - kv0
    nch_kv = ncols_kv // TC
    nkb = ncols_kv // 128
    nw = (TL - q0) // 512
    NH = 8
    m0 = p.mark()
    chv = p.chain["vector"]
    chs = p.chain["scalar"]
    chg = p.chain["gpsimd"]

    QKC = p.sb([128, 2], F32, f"{name}_qkc")
    QKR = p.sb([1, 128], F32, f"{name}_qkr")
    ROTQ = p.sb([128, 128], BF16, f"{name}_rot")
    IDT = p.sb([128, 128], BF16, f"{name}_idt")
    MSK = p.sb([128, 256], F32, f"{name}_msk")
    MSKH = p.sb([128, 128], F32, f"{name}_mskh")
    IND2 = p.sb([2, 128], BF16, f"{name}_ind2")
    IND2T = p.sb([128, 2], BF16, f"{name}_ind2t")
    ONESR = p.sb([1, 128], F32, f"{name}_onesr")
    NEGC = p.sb([128, 1], F32, f"{name}_negc")
    TMP1 = p.sb([1, 128], F32, f"{name}_tmp1")
    MX = p.sb([1, 4], F32, f"{name}_mx")
    s_k0 = p.sem(f"{name}_k0")
    s_k1 = p.sem(f"{name}_k1")
    p.dma("sync", QKC[:, :], wk["qk_col"], s_k0)
    p.dma("sync", QKR[:, :], wk["qk_row"], s_k0)
    p.dma("sync", MSK[:, :], wk["masks"], s_k0)
    p.dma("sync", MSKH[:, :], wk["maskh"], s_k0)
    p.dma("sync", ONESR[:, :], wk["ones_p"], s_k0)
    p.dma("gpsimd", ROTQ[:, :], wk["rotq_lhsT"], s_k1)
    p.dma("gpsimd", IDT[:, :], wk["ident_p"], s_k1)
    p.dma("gpsimd", IND2[:, :], wk["ind2"], s_k1)
    p.dma("gpsimd", IND2T[:, :], wk["ind2t"], s_k1)
    p.wait("vector", s_k0)
    p.wait("tensor", s_k0)
    p.wait("tensor", s_k1)
    p.cop("vector", lambda v: v.tensor_scalar(TMP1[0:1, :], QKR[0:1, :], -1.0, None, ALU.mult))
    p.cop("vector", lambda v: v.tensor_tensor(TMP1[0:1, :], TMP1[0:1, :], QKR[0:1, :], ALU.max))
    p.cop("vector", lambda v: v.reduce_max(MX[0:1, 0:1], TMP1[0:1, 0:64], axis=AX.X))
    p.cop("vector", lambda v: v.reduce_max(MX[0:1, 1:2], TMP1[0:1, 64:128], axis=AX.X))
    p.cop("vector", lambda v: v.tensor_mul(MX[0:1, 2:3], MX[0:1, 0:1], MX[0:1, 1:2]))
    cv = p.cop("vector", lambda v: v.tensor_scalar(MX[0:1, 2:3], MX[0:1, 2:3], -8.0, None, ALU.mult))
    p.wait("tensor", chv, cv)
    PSB = p.ps([128, 512], F32)
    s_bc0 = p.sem(f"{name}_bc0")
    p.top(lambda t: t.matmul(PSB[:, 0:1], ONESR[0:1, :], MX[0:1, 2:3], start=True, stop=True),
          sem=s_bc0)
    p.wait("vector", s_bc0)
    ncv = p.cop("vector", lambda v: v.tensor_copy(NEGC[:, :], PSB[:, 0:1]))
    p.wait("scalar", chv, ncv)

    ZC = p.sb([128, 8], BF16, f"{name}_zc")
    s_zc0 = p.sem(f"{name}_zc0")
    zcv = p.cop("vector", lambda v: v.memset(ZC[:, :], 0.0))
    p.wait("sync", chv, zcv)
    if CK >= 1:
        p.dma("sync",
              src.rearrange("(kt q) t -> q kt t", q=128)[:, :, CK - 1 : CK],
              ZC[:, :].rearrange("q (kt t) -> q kt t", t=1), s_zc0,
              allow_slow_non_contiguous=True)
    p.dma("sync",
          d["xmid"].rearrange("(kt q) t -> q kt t", q=128)[:, :, HALO + q0 - 1 : HALO + q0],
          ZC[:, :].rearrange("q (kt t) -> q kt t", t=1), s_zc0,
          allow_slow_non_contiguous=True)
    p.wait("sync", s_zc0)

    mqkv = p.mark()
    epi_q, s_oc_q = epi_store(p, name=f"{name}_qe", dst=d["qkv"], dst_c0=CK)
    emit_gemm(
        p, name=f"{name}_qg", src=src, src_c0=CK, wsrc=wk["wqkv_p"],
        d_in=D, d_out=1536, cols=ncols_kv, epi=epi_q, shift_half=True,
    )
    p.barrier(waits=[(sm, sm.n) for sm in s_oc_q])
    p.reset(mqkv)

    QR = p.sb([128, 4 * ncols_kv], BF16, f"{name}_QR")
    KR = p.sb([128, 4 * ncols_kv], BF16, f"{name}_KR")
    VTT = p.sb([128, 65 * NH * nkb], BF16, f"{name}_VTT")
    QKB = [p.sb([128, 4 * TC], BF16, f"{name}_qkb{i}") for i in range(2)]
    VB = [p.sb([128, 4 * TC], BF16, f"{name}_vb{i}") for i in range(2)]
    TAB = [p.sb([128, 2 * TC], BF16, f"{name}_tab{i}") for i in range(2)]
    SQB = p.sb([128, TC], BF16, f"{name}_sqb")
    SSF = p.sb([2, TC], F32, f"{name}_ssf")
    SSR = p.sb([2, TC], BF16, f"{name}_ssr")
    EPS24 = p.sb([2, 1], F32, f"{name}_eps24")
    T1 = p.sb([128, TC], F32, f"{name}_t1")
    T2 = p.sb([128, TC], F32, f"{name}_t2")
    PS_SUM = p.ps([2, 512], F32)
    PS_BC = p.ps([128, 512], F32)
    PS_ROT = p.ps([128, 512], F32)
    PS_TR = p.ps([128, 128], BF16)
    PS_TR2 = p.ps([128, 128], BF16)
    s_qkl = p.sems(f"{name}_qkl", 2)
    s_vl = p.sems(f"{name}_vl", 2)
    s_tl = p.sems(f"{name}_tl", 2)
    s_sum = p.sem(f"{name}_sum")
    s_bcq = p.sem(f"{name}_bcq")
    s_rot = p.sem(f"{name}_rot")
    s_vtr = p.sem(f"{name}_vtr")

    ov = p.cop("vector", lambda v: v.memset(
        VTT[:, 64 : 64 + (NH * nkb - 1) * 65 + 1 : 65], 1.0))
    e24v = p.cop("vector", lambda v: v.memset(EPS24[:, :], 1e-24))
    p.wait("scalar", chv, e24v)
    p.wait("tensor", chv, ov)

    prev = dict(sq=None, sm=None, sq2=None, bc=None, qn=None, rot=None, t2=None,
                g1=None, o2=None)
    prep_done = {}
    for qk in range(2):
        DSTB = QR if qk == 0 else KR
        for c in range(nch_kv):
            ci = qk * nch_kv + c
            sl = ci % 2
            if ci >= 2:
                p.wait("sync", chv, prep_done[ci - 2])
                p.wait("gpsimd", chv, prep_done[ci - 2])
            p.dma(
                "sync",
                QKB[sl][:, :].rearrange("q (kt t) -> q kt t", t=TC),
                d["qkv"].rearrange("(kt q) t -> q kt t", q=128)[
                    :, 4 * qk : 4 * qk + 4, CK + c * TC : CK + (c + 1) * TC
                ],
                s_qkl[sl],
            )
            p.dma(
                "gpsimd",
                TAB[sl][:, :].rearrange("q (a t) -> q a t", t=TC),
                wk["tab"].rearrange("(a q) t -> q a t", q=128)[
                    :, 2 * qk : 2 * qk + 2, c * TC : (c + 1) * TC
                ],
                s_tl[sl],
            )
            qkl_v = s_qkl[sl].n
            tl_v = s_tl[sl].n
            p.wait("vector", s_tl[sl], tl_v)
            p.wait("vector", s_qkl[sl], qkl_v)
            for kt in range(4):
                p.wait("scalar", s_qkl[sl], qkl_v)
                if prev["sm"] is not None:
                    p.wait("scalar", s_sum, prev["sm"])
                prev["sq"] = p.cop("scalar", lambda sc, sl=sl, kt=kt: sc.activation(
                    SQB[:, :], QKB[sl][:, kt * TC : (kt + 1) * TC], AF.Square))
                p.wait("tensor", chs, prev["sq"])
                prev["sm"] = p.top(lambda t: t.matmul(
                    PS_SUM[:, :], IND2T[:, :], SQB[:, :], start=True, stop=True), sem=s_sum)
                p.wait("scalar", s_sum, prev["sm"])
                if prev["bc"] is not None:
                    p.wait("scalar", s_bcq, prev["bc"])
                p.cop("scalar", lambda sc: sc.activation(
                    SSF[:, :], PS_SUM[:, :], AF.Ln, bias=EPS24[:, 0:1]))
                prev["sq2"] = p.cop("scalar", lambda sc: sc.activation(
                    SSR[:, :], SSF[:, :], AF.Exp, scale=-0.5))
                p.wait("tensor", chs, prev["sq2"])
                if prev["qn"] is not None:
                    p.wait("tensor", chv, prev["qn"])
                prev["bc"] = p.top(lambda t: t.matmul(
                    PS_BC[:, :], IND2[:, :], SSR[:, :], start=True, stop=True), sem=s_bcq)
                p.wait("vector", s_bcq, prev["bc"])
                p.cop("vector", lambda v, sl=sl, kt=kt: v.tensor_tensor(
                    T2[:, :], QKB[sl][:, kt * TC : (kt + 1) * TC], PS_BC[:, :], ALU.mult))
                prev["qn"] = p.cop("vector", lambda v, sl=sl, kt=kt, qk=qk: v.tensor_scalar(
                    QKB[sl][:, kt * TC : (kt + 1) * TC], T2[:, :], QKC[:, qk : qk + 1],
                    None, ALU.mult))
                p.wait("tensor", chv, prev["qn"])
                if prev["t2"] is not None:
                    p.wait("tensor", chv, prev["t2"])
                prev["rot"] = p.top(lambda t, sl=sl, kt=kt: t.matmul(
                    PS_ROT[:, :], ROTQ[:, :], QKB[sl][:, kt * TC : (kt + 1) * TC],
                    start=True, stop=True), sem=s_rot)
                p.wait("gpsimd", s_tl[sl], tl_v)
                p.wait("gpsimd", chv, prev["qn"])
                if prev["o2"] is not None:
                    p.wait("gpsimd", chv, prev["o2"])
                prev["g1"] = p.cop("gpsimd", lambda g, sl=sl, kt=kt: g.tensor_tensor(
                    T1[:, :], QKB[sl][:, kt * TC : (kt + 1) * TC], TAB[sl][:, 0:TC], ALU.mult))
                p.wait("vector", s_rot, prev["rot"])
                prev["t2"] = p.cop("vector", lambda v, sl=sl: v.tensor_tensor(
                    T2[:, :], PS_ROT[:, :], TAB[sl][:, TC:], ALU.mult))
                p.wait("vector", chg, prev["g1"])
                prev["o2"] = p.cop("vector", lambda v, c=c, kt=kt, DSTB=DSTB: v.tensor_tensor(
                    DSTB[:, kt * ncols_kv + c * TC : kt * ncols_kv + (c + 1) * TC],
                    T1[:, :], T2[:, :], ALU.add))
            prep_done[ci] = prev["o2"]

    if "dbg" in d:
        s_dbg = p.sem(f"{name}_dbg")
        p.wait("scalar", chv)
        p.dma("scalar", d["dbg"][0:128, 0 : 4 * ncols_kv], QR[:, :], s_dbg)
        p.dma("scalar", d["dbg"][128:256, 0 : 4 * ncols_kv], KR[:, :], s_dbg)
        p.wait("scalar", s_dbg)

    cp_hist = {}
    for c in range(nch_kv):
        sl = c % 2
        if c >= 2:
            p.wait("sync", chv, cp_hist[("blkdone", c - 2)])
        p.dma(
            "sync",
            VB[sl][:, :].rearrange("q (kt t) -> q kt t", t=TC),
            d["qkv"].rearrange("(kt q) t -> q kt t", q=128)[
                :, 8:12, CK + c * TC : CK + (c + 1) * TC
            ],
            s_vl[sl],
        )
        p.wait("tensor", s_vl[sl])
        for kb in range(4):
            blk = c * 4 + kb
            for h in range(NH):
                gi = blk * NH + h
                tile, base = h // 2, (h % 2) * 64
                pst = PS_TR if gi % 2 == 0 else PS_TR2
                pv = cp_hist.get(gi - 2)
                if pv is not None:
                    p.wait("tensor", chv, pv)
                tv = p.top(lambda t, sl=sl, kb=kb, tile=tile, base=base, pst=pst: t.transpose(
                    pst[0:128, 0:64],
                    QKB_V_SLICE(VB[sl], tile, base, kb),
                    IDT[base : base + 64, base : base + 64],
                ), sem=s_vtr)
                p.wait("vector", s_vtr, tv)
                slot = h * nkb + blk
                cp_hist[gi] = p.cop("vector", lambda v, slot=slot, pst=pst: v.tensor_copy(
                    VTT[:, slot * 65 : slot * 65 + 64], pst[0:128, 0:64]))
        cp_hist[("blkdone", c)] = p.vchain("vector")
    p.wait("tensor", chv, cp_hist[("blkdone", nch_kv - 1)])
    p.wait("tensor", chv, cp_hist[("blkdone", nch_kv - 2)])
    p.barrier()
    p.psum_off = mqkv[1]

    OS = p.sb([128, 512], BF16, f"{name}_os")
    SROW = p.sb([1, 1024], F32, f"{name}_srow")
    SRECIP = p.sb([1, 512], F32, f"{name}_srec")
    TINY30 = p.sb([1, 1], F32, f"{name}_tiny30")
    t30v = p.cop("vector", lambda v: v.memset(TINY30[:, :], 1e-30))
    p.wait("scalar", chv, t30v)
    ONRMW = [p.sb([128, 4 * 512], BF16, f"{name}_onrmw{i}") for i in range(2)]
    EXPT = [p.sb([128, 640], BF16, f"{name}_expt{i}") for i in range(2)]
    PS_SIM = [p.ps([128, 640], F32) for i in range(2)]
    PS_AV = [p.ps([128, 512], F32) for i in range(2)]
    PS_NBC = p.ps([128, 512], F32)
    s_sim = p.sem(f"{name}_sim")
    s_av = p.sem(f"{name}_av")
    s_nbc = p.sem(f"{name}_nbc")
    s_ow = p.sems(f"{name}_ow", 2)
    ahist = {}
    gctr = [0]
    for w in range(nw):
        qa0 = q0 + w * 512
        wsl = w % 2
        if w >= 2:
            p.wait("vector", s_ow[wsl])
        for tile in range(4):
            for half in range(2):
                h = 2 * tile + half
                base = half * 64
                srg = ahist.get(("srg", half))
                if srg is not None:
                    p.wait("vector", chs, srg)
                for a in range(4):
                    gi = gctr[0]
                    gctr[0] += 1
                    par = gi % 2
                    qt0 = qa0 + a * 128
                    kt0 = qt0 - 512
                    qc = qt0 - kv0
                    kc = kt0 - kv0
                    w0m = (0 - q0) // 512
                    pv = ahist.get(("exp", gi - 2))
                    if pv is not None:
                        p.wait("tensor", chs, pv)

                    def simf(t, tile=tile, base=base, qc=qc, kc=kc, par=par):
                        ins = None
                        for j in range(5):
                            ins = t.matmul(
                                PS_SIM[par][0:128, j * 128 : (j + 1) * 128],
                                KR[base : base + 64,
                                   tile * ncols_kv + kc + j * 128 :
                                   tile * ncols_kv + kc + (j + 1) * 128],
                                QR[base : base + 64,
                                   tile * ncols_kv + qc : tile * ncols_kv + qc + 128],
                                start=True, stop=True,
                            )
                        return ins

                    simv = p.top(simf, sem=s_sim)
                    p.wait("vector", s_sim, simv)
                    p.cop("vector", lambda v, par=par: v.tensor_tensor(
                        PS_SIM[par][0:128, 0:128], PS_SIM[par][0:128, 0:128],
                        MSK[:, 0:128], ALU.add))
                    mv = p.cop("vector", lambda v, par=par: v.tensor_tensor(
                        PS_SIM[par][0:128, 512:640], PS_SIM[par][0:128, 512:640],
                        MSK[:, 128:256], ALU.add))
                    if w == w0m and a < 4:
                        for j in range(4 - a):
                            mv = p.cop("vector", lambda v, par=par, j=j: v.tensor_tensor(
                                PS_SIM[par][0:128, j * 128 : (j + 1) * 128],
                                PS_SIM[par][0:128, j * 128 : (j + 1) * 128],
                                MSKH[:, :], ALU.add))
                    p.wait("scalar", chv, mv)
                    pv = ahist.get(("avv", gi - 2))
                    if pv is not None:
                        p.wait("scalar", s_av, pv)
                    ev = p.cop("scalar", lambda sc, par=par: sc.activation(
                        EXPT[par][:, :], PS_SIM[par][0:128, :],
                        AF.Exp, bias=NEGC[:, 0:1], scale=8.0))
                    ahist[("exp", gi)] = ev
                    pv = ahist.get(("evac", gi - 2))
                    if pv is not None:
                        p.wait("tensor", chv, pv)
                    p.wait("tensor", chs, ev)

                    def avf(t, h=h, kc=kc, par=par):
                        ins = None
                        kb0 = kc // 128
                        for j in range(5):
                            slot = h * nkb + kb0 + j
                            ins = t.matmul(
                                PS_AV[par][0:65, 0:128],
                                VTT[:, slot * 65 : (slot + 1) * 65],
                                EXPT[par][:, j * 128 : (j + 1) * 128],
                                start=(j == 0), stop=(j == 4),
                            )
                        return ins

                    avv = p.top(avf, sem=s_av)
                    ahist[("avv", gi)] = avv
                    p.wait("vector", s_av, avv)
                    qcol = qt0 - qa0
                    p.cop("vector", lambda v, base=base, qcol=qcol, par=par: v.tensor_copy(
                        OS[base : base + 64, qcol : qcol + 128], PS_AV[par][0:64, 0:128]))
                    ahist[("evac", gi)] = p.cop(
                        "vector", lambda v, half=half, qcol=qcol, par=par: v.tensor_copy(
                            SROW[0:1, half * 512 + qcol : half * 512 + qcol + 128],
                            PS_AV[par][64:65, 0:128]))
                p.wait("scalar", chv)
                pbv = ahist.get(("nbv", half))
                if pbv is not None:
                    p.wait("scalar", s_nbc, pbv)
                p.cop("scalar", lambda sc, half=half: sc.activation(
                    SRECIP[0:1, :], SROW[0:1, half * 512 : (half + 1) * 512],
                    AF.Ln, bias=TINY30[0:1, 0:1]))
                rv = p.cop("scalar", lambda sc: sc.activation(
                    SRECIP[0:1, :], SRECIP[0:1, :], AF.Exp, scale=-1.0))
                ahist[("srg", half)] = rv
                p.wait("tensor", chs, rv)
                pv = ahist.get(("nm", h - 1, w), ahist.get(("nm", 7, w - 1)))
                if pv is not None:
                    p.wait("tensor", chv, pv)
                bv = p.top(lambda t, base=base: t.matmul(
                    PS_NBC[base : base + 64, 0:512], ONESR[0:1, 0:64], SRECIP[0:1, :],
                    start=True, stop=True), sem=s_nbc)
                ahist[("nbv", half)] = bv
                p.wait("vector", s_nbc, bv)
                ahist[("nm", h, w)] = p.cop(
                    "vector", lambda v, wsl=wsl, tile=tile, base=base: v.tensor_tensor(
                        ONRMW[wsl][base : base + 64, tile * 512 : (tile + 1) * 512],
                        OS[base : base + 64, :],
                        PS_NBC[base : base + 64, 0:512], ALU.mult))
        dn = p.vchain("vector")
        p.wait("scalar", chv, dn)
        p.dma("scalar",
              d["onrm"].rearrange("(kt q) t -> q kt t", q=128)[
                  :, :, HALO + qa0 : HALO + qa0 + 512],
              ONRMW[wsl][:, :].rearrange("q (kt t) -> q kt t", t=512), s_ow[wsl])
    p.barrier(waits=[(s_ow[0], s_ow[0].n), (s_ow[1], s_ow[1].n)])
    p.reset(mqkv)

    epi_wo, s_so_wo = epi_res_store(
        p, name=f"{name}_woe", dst=d["xmid"], dst_c0=HALO + q0, res=src,
        res_c0=HALO + q0)
    emit_gemm(
        p, name=f"{name}_wog", src=d["onrm"], src_c0=HALO + q0, wsrc=wk["wo_p"],
        d_in=512, d_out=D, cols=TL - q0, epi=epi_wo,
    )
    p.barrier(waits=[(sm, sm.n) for sm in s_so_wo])
    p.reset(mqkv)

    fq0 = q0
    fcols = TL - fq0
    fch = fcols // TC
    LNG = p.sb([128, 16], F32, f"{name}_lng")
    s_lng = p.sem(f"{name}_lng")
    p.dma("sync", LNG[:, :], wk["lng_col"], s_lng)
    p.wait("vector", s_lng)
    mln2 = p.mark()
    ln2_done = emit_lngb(
        p, name=f"{name}_ln2", xsrc=d["xmid"], xdst=d["xf"], c0=HALO + fq0,
        cols=fcols, gamma_sb=LNG[:, 0:8], beta_sb=LNG[:, 8:16], src_f32=False,
        shift_half=True,
    )
    p.barrier(waits=ln2_done)
    p.reset(mln2)

    NO1 = FFN_W1O // 128
    W1 = p.sb([128, 8 * FFN_W1O], BF16, f"{name}_w1")
    W2 = p.sb([128, 22 * 1024], BF16, f"{name}_w2")
    B1 = p.sb([128, 44], F32, f"{name}_b1")
    B2 = p.sb([128, 8], F32, f"{name}_b2")
    XF = [p.sb([128, 8 * TC], BF16, f"{name}_xf{i}") for i in range(2)]
    HB = p.sb([128, 22 * TC], BF16, f"{name}_hb")
    ASB = p.sb([128, TC], F32, f"{name}_asb")
    GSB = p.sb([128, TC], F32, f"{name}_gsb")
    PS_W1 = [p.ps([128, 512], F32) for i in range(2)]
    PS_W2 = [p.ps([128, 512], F32) for i in range(2)]
    s_w1l = p.sem(f"{name}_w1l")
    s_bl = p.sem(f"{name}_bl")
    s_xfl = p.sems(f"{name}_xfl", 2)
    s_m1 = p.sem(f"{name}_m1")
    s_m2 = p.sem(f"{name}_m2")
    p.dma("gpsimd", W1[:, :], wk["w1_p"], s_w1l)
    p.dma("gpsimd", W2[:, :], wk["w2_p"], s_w1l)
    p.dma("sync", B1[:, :], wk["b1_col"], s_bl)
    p.dma("sync", B2[:, :], wk["b2_col"], s_bl)
    p.wait("tensor", s_w1l)
    p.wait("vector", s_bl)
    p.wait("scalar", s_bl)
    epi_f, s_so_f = epi_res_store(
        p, name=f"{name}_fe", dst=dst, dst_c0=HALO + fq0, res=d["xmid"],
        res_c0=HALO + fq0, bias_sb=B2)
    fh = {}
    for c in range(fch):
        sl = c % 2
        if c >= 2:
            p.wait("sync", s_m1, fh[("xf_cons", c - 2)])
        p.dma(
            "sync",
            XF[sl][:, :].rearrange("q (kt t) -> q kt t", t=TC),
            d["xf"].rearrange("(kt q) t -> q kt t", q=128)[
                :, :, HALO + fq0 + c * TC : HALO + fq0 + (c + 1) * TC],
            s_xfl[sl],
        )
        p.wait("tensor", s_xfl[sl])
        if c >= 1:
            p.wait("vector", s_m2, s_m2.base + c * 8)
        for o2 in range(22):
            for ag in range(2):
                o = o2 * 2 + ag
                par = o % 2
                pv = fh.get(("h", c, o2 - 1))
                if ag == 0 and pv is not None:
                    p.wait("tensor", chv, pv)
                if ag == 1 and fh.get(("g", c, o2 - 1)) is not None:
                    p.wait("tensor", chs, fh[("g", c, o2 - 1)])

                def w1f(t, sl=sl, o=o, par=par):
                    ins = None
                    for k in range(8):
                        ins = t.matmul(
                            PS_W1[par][:, :],
                            W1[:, k * FFN_W1O + o * 128 : k * FFN_W1O + (o + 1) * 128],
                            XF[sl][:, k * TC : (k + 1) * TC],
                            start=(k == 0), stop=(k == 7))
                    return ins

                mv = p.top(w1f, sem=s_m1)
                if ag == 0:
                    p.wait("vector", s_m1, mv)
                    fh[("a", c, o2)] = p.cop("vector", lambda v, o2=o2, o=o: v.tensor_scalar(
                        ASB[:, :], PS_W1[0][:, :], B1[:, o : o + 1], None, ALU.add))
                else:
                    p.wait("scalar", s_m1, mv)
                    if fh.get(("hh", c, o2 - 1)) is not None:
                        p.wait("scalar", chv, fh[("hh", c, o2 - 1)])
                    fh[("g", c, o2)] = p.cop("scalar", lambda sc, o=o: sc.activation(
                        GSB[:, :], PS_W1[1][:, :], AF.Gelu, bias=B1[:, o : o + 1]))
                    p.wait("vector", chs, fh[("g", c, o2)])
                    p.wait("vector", chv, fh[("a", c, o2)])
                    fh[("hh", c, o2)] = p.cop("vector", lambda v, o2=o2: v.tensor_mul(
                        HB[:, o2 * TC : (o2 + 1) * TC], ASB[:, :], GSB[:, :]))
                    fh[("h", c, o2)] = fh[("hh", c, o2)]
        fh[("xf_cons", c)] = s_m1.n
        hready = fh[("hh", c, 21)]
        p.wait("tensor", chv, hready)
        for o in range(8):
            par = o % 2
            pv = fh.get(("w2e", c, o - 2))
            if pv is None and c > 0:
                pv = fh.get(("w2e", c - 1, o + 6))
            if pv is not None:
                p.wait("tensor", pv[0], pv[1])

            def w2f(t, o=o, par=par):
                ins = None
                for k in range(22):
                    ins = t.matmul(
                        PS_W2[par][:, :],
                        W2[:, k * 1024 + o * 128 : k * 1024 + (o + 1) * 128],
                        HB[:, k * TC : (k + 1) * TC],
                        start=(k == 0), stop=(k == 21))
                return ins

            mv = p.top(w2f, sem=s_m2)
            fh[("w2e", c, o)] = epi_f(p, c, o, PS_W2[par][:, :], TC, s_m2)
    p.barrier(waits=[(sm, sm.n) for sm in s_so_f])
    p.reset(m0)


def QKB_V_SLICE(vb, tile, base, kb):
    return vb.rearrange("q (kt t) -> q kt t", t=TC)[base : base + 64, tile, kb * 128 : (kb + 1) * 128]


def ffn_host_layer(wqkv, qs, ks, wo, lg, lb, w1, b1, w2, b2):
    HID = 2730
    w1a, w1g = w1[:, :HID], w1[:, HID:]
    w1a = pad_cols(w1a, FFN_HID)
    w1g = pad_cols(w1g, FFN_HID)
    w1i = np.zeros((1024, FFN_W1O), np.float32)
    b1i = np.zeros(FFN_W1O, np.float32)
    b1a = np.concatenate([b1[:HID], np.zeros(FFN_HID - HID, np.float32)])
    b1g = np.concatenate([b1[HID:], np.zeros(FFN_HID - HID, np.float32)])
    for j in range(22):
        w1i[:, (2 * j) * 128 : (2 * j + 1) * 128] = w1a[:, j * 128 : (j + 1) * 128]
        w1i[:, (2 * j + 1) * 128 : (2 * j + 2) * 128] = w1g[:, j * 128 : (j + 1) * 128]
        b1i[(2 * j) * 128 : (2 * j + 1) * 128] = b1a[j * 128 : (j + 1) * 128]
        b1i[(2 * j + 1) * 128 : (2 * j + 2) * 128] = b1g[j * 128 : (j + 1) * 128]
    return dict(
        wqkv_p=pack_w(wqkv),
        wo_p=pack_w(wo),
        qk_col=np.stack([np.tile(qs, 2), np.tile(ks, 2)], 1).astype(np.float32),
        qk_row=np.concatenate([qs, ks]).astype(np.float32).reshape(1, 128),
        lng_col=np.concatenate([pack_vec(lg), pack_vec(lb)], 1),
        w1_p=pack_w(w1i),
        b1_col=pack_vec(b1i),
        w2_p=pack_w(pad_rows(w2, FFN_HID)),
        b2_col=pack_vec(b2),
    )


def ffn_host_consts():
    TW = T + 1024
    tg = np.arange(-512, TW - 512, dtype=np.float64)
    half = np.arange(0, 64, 2, dtype=np.float64)
    inv_freq = 1.0 / 10000.0 ** (half / 64.0)
    freqs32 = tg[None, :] * inv_freq[:, None]
    freqs = np.concatenate([freqs32, freqs32], 0)
    base = (half + 0.4 * 64.0) / (1.4 * 64.0)
    power = (tg - 4096.0) / 256.0
    scale32 = base[:, None] ** power[None, :]
    scale = np.concatenate([scale32, scale32], 0)
    cs_q = (np.cos(freqs) * scale).astype(np.float32)
    ss_q = (np.sin(freqs) * scale).astype(np.float32)
    cs_k = (np.cos(freqs) / scale).astype(np.float32)
    ss_k = (np.sin(freqs) / scale).astype(np.float32)
    tab = np.concatenate(
        [np.tile(x, (2, 1)) for x in (cs_q, ss_q, cs_k, ss_k)], 0
    )
    R = np.zeros((64, 64), np.float32)
    for i in range(32):
        R[i, i + 32] = -1.0
        R[i + 32, i] = 1.0
    R128 = np.zeros((128, 128), np.float32)
    R128[:64, :64] = R
    R128[64:, 64:] = R
    rotq_lhsT = np.ascontiguousarray(R128.T)
    j_ = np.arange(128)[:, None]
    i_ = np.arange(128)[None, :]
    M0 = np.where(j_ >= i_, 0.0, -1e30).astype(np.float32)
    M4 = np.where(j_ <= i_, 0.0, -1e30).astype(np.float32)
    masks = np.concatenate([M0, M4], 1)
    ind2 = np.zeros((2, 128), np.float32)
    ind2[0, :64] = 1.0
    ind2[1, 64:] = 1.0
    ind2t = np.ascontiguousarray(ind2.T)
    ones_p = np.ones((1, 128), np.float32)
    ident_p = np.eye(128, dtype=np.float32)
    return dict(tab=tab, rotq_lhsT=rotq_lhsT, masks=masks, ind2=ind2,
                ind2t=ind2t, ones_p=ones_p, ident_p=ident_p)




def emit_l3ffn(p, d, *, name, wk):
    m0 = p.mark()
    chv = p.chain["vector"]
    MU = p.sb([128, 32], F32, f"{name}_mu")
    s_mu = p.sem(f"{name}_mu")
    p.dma("sync", MU[:, 0:8], wk["mur_col"], s_mu)
    p.dma("sync", MU[:, 16:24], wk["muk_col"], s_mu)
    p.wait("vector", s_mu)
    p.cop("vector", lambda v: v.tensor_scalar(MU[:, 8:16], MU[:, 0:8], -1.0, 1.0,
                                              ALU.mult, ALU.add))
    p.cop("vector", lambda v: v.tensor_scalar(MU[:, 24:32], MU[:, 16:24], -1.0, 1.0,
                                              ALU.mult, ALU.add))

    XL = [p.sb([128, 8 * TC], BF16, f"{name}_xl{i}") for i in range(2)]
    XP = [p.sb([128, 8 * TC], BF16, f"{name}_xp{i}") for i in range(2)]
    TT = p.sb([128, TC], F32, f"{name}_tt")
    XRO = p.sb([128, 8 * TC], BF16, f"{name}_xro")
    XKO = p.sb([128, 8 * TC], BF16, f"{name}_xko")
    s_xl = p.sems(f"{name}_xl", 2)
    s_xpl = p.sems(f"{name}_xpl", 2)
    s_ro = p.sems(f"{name}_ro", 2)
    lerp_done = []
    nch = TL // TC
    for c in range(nch):
        sl = c % 2
        if c >= 2:
            p.wait("sync", chv, lerp_done[c - 2][2])
        p.dma("sync", XL[sl][:, :].rearrange("q (kt t) -> q kt t", t=TC),
              d["xn"].rearrange("(kt q) t -> q kt t", q=128)[
                  :, :, HALO + c * TC : HALO + (c + 1) * TC], s_xl[sl])
        p.dma("sync", XP[sl][:, :].rearrange("q (kt t) -> q kt t", t=TC),
              d["xn"].rearrange("(kt q) t -> q kt t", q=128)[
                  :, :, HALO + c * TC - 1 : HALO + (c + 1) * TC - 1], s_xpl[sl])
        p.wait("vector", s_xl[sl])
        p.wait("vector", s_xpl[sl])
        if c >= 1:
            p.wait("vector", s_ro[1 - sl])
        if c >= 2:
            p.wait("vector", s_ro[sl])
        for kt in range(NKT):
            for rk in range(2):
                p.cop("vector", lambda v, sl=sl, kt=kt, rk=rk: v.tensor_scalar(
                    TT[:, :], XL[sl][:, kt * TC : (kt + 1) * TC],
                    MU[:, 16 * rk + kt : 16 * rk + kt + 1], None, ALU.mult))
                p.cop("vector", lambda v, sl=sl, kt=kt, rk=rk: v.scalar_tensor_tensor(
                    (XRO if rk == 0 else XKO)[:, kt * TC : (kt + 1) * TC],
                    XP[sl][:, kt * TC : (kt + 1) * TC],
                    MU[:, 16 * rk + 8 + kt : 16 * rk + 8 + kt + 1],
                    TT[:, :], ALU.mult, ALU.add))
        dn = p.vchain("vector")
        p.wait("scalar", chv, dn)
        p.dma("scalar", d["xr"].rearrange("(kt q) t -> q kt t", q=128)[
            :, :, c * TC : (c + 1) * TC],
            XRO[:, :].rearrange("q (kt t) -> q kt t", t=TC), s_ro[sl])
        p.dma("scalar", d["xk"].rearrange("(kt q) t -> q kt t", q=128)[
            :, :, c * TC : (c + 1) * TC],
            XKO[:, :].rearrange("q (kt t) -> q kt t", t=TC), s_ro[sl])
        lerp_done.append((s_ro[sl], s_ro[sl].n, p.vchain("vector")))
    p.barrier(waits=[(a, b) for a, b, _ in lerp_done])
    p.reset(m0)

    epi_r, s_so_r = epi_store(p, name=f"{name}_re", dst=d["rform"], dst_c0=0,
                              act=AF.Sigmoid)
    emit_gemm(p, name=f"{name}_rg", src=d["xr"], src_c0=0, wsrc=wk["wr_p"],
              d_in=D, d_out=D, cols=TL, epi=epi_r)
    p.barrier(waits=[(sm, sm.n) for sm in s_so_r])
    p.reset(m0)

    OBK = [p.sb([128, TC], BF16, f"{name}_obk{i}") for i in range(4)]
    TK = p.sb([128, TC], F32, f"{name}_tk")
    s_sok = p.sems(f"{name}_sok", 4)
    kctr = [0]

    def epi_k(p, c, o, psum, tc_here, s_mm):
        i = kctr[0]
        kctr[0] += 1
        sl = i % 4
        p.wait("vector", s_mm)
        if i >= 4:
            p.wait("vector", s_sok[sl])
        p.cop("vector", lambda v, psum=psum: v.tensor_scalar(
            TK[:, :], psum, 0.0, None, ALU.max))
        val = p.cop("vector", lambda v, sl=sl: v.tensor_mul(OBK[sl][:, :], TK[:, :], TK[:, :]))
        p.wait("scalar", chv, val)
        p.dma("scalar", d["k2"][o * 128 : (o + 1) * 128, c * TC : (c + 1) * TC],
              OBK[sl][:, :], s_sok[sl])
        return (chv, val)

    emit_gemm(p, name=f"{name}_kg", src=d["xk"], src_c0=0, wsrc=wk["wk_p"],
              d_in=D, d_out=D, cols=TL, epi=epi_k)
    p.barrier(waits=[(sm, sm.n) for sm in s_sok])
    p.reset(m0)

    OBV = [p.sb([128, TC], BF16, f"{name}_obv{i}") for i in range(4)]
    RV = [p.sb([128, TC], BF16, f"{name}_rv{i}") for i in range(4)]
    s_sov = p.sems(f"{name}_sov", 4)
    s_rl = p.sems(f"{name}_rl", 4)
    vctr = [0]
    vcons = {}

    def epi_v(p, c, o, psum, tc_here, s_mm):
        i = vctr[0]
        vctr[0] += 1
        sl = i % 4
        if i >= 4:
            p.wait("sync", chv, vcons[i - 4])
        p.dma("sync", RV[sl][:, :], d["rform"][o * 128 : (o + 1) * 128,
              c * TC : (c + 1) * TC], s_rl[sl])
        p.wait("vector", s_mm)
        p.wait("vector", s_rl[sl])
        if i >= 4:
            p.wait("vector", s_sov[sl])
        val = p.cop("vector", lambda v, sl=sl, psum=psum: v.tensor_tensor(
            OBV[sl][:, :], psum, RV[sl][:, :], ALU.mult))
        vcons[i] = val
        p.wait("scalar", chv, val)
        p.dma("scalar", d["g3"][o * 128 : (o + 1) * 128, c * TC : (c + 1) * TC],
              OBV[sl][:, :], s_sov[sl])
        return (chv, val)

    emit_gemm(p, name=f"{name}_vg", src=d["k2"], src_c0=0, wsrc=wk["wv_p"],
              d_in=D, d_out=D, cols=TL, epi=epi_v)
    p.barrier(waits=[(sm, sm.n) for sm in s_sov])
    p.reset(m0)


def l3ffn_host(inp):
    mur = np.asarray(inp["L3_ffn_mu_r"], np.float32)
    muk = np.asarray(inp["L3_ffn_mu_k"], np.float32)
    return dict(
        wr_p=pack_wT(np.asarray(inp["L3_ffn_W_r"])),
        wk_p=pack_wT(np.asarray(inp["L3_ffn_W_k"])),
        wv_p=pack_wT(np.asarray(inp["L3_ffn_W_v"])),
        mur_col=pack_vec(mur),
        muk_col=pack_vec(muk),
    )




def emit_combine(p, d, *, name, s_key, g_key, xin_key, xout_key, s_c0=0, g_c0=0,
                 s_dt=BF16, g_dt=BF16, out_is_flat=False):
    m0 = p.mark()
    chv = p.chain["vector"]
    chs = p.chain["scalar"]
    SB_ = [p.sb([128, 8 * TC], s_dt, f"{name}_s{i}") for i in range(2)]
    GB_ = [p.sb([128, 8 * TC], g_dt, f"{name}_g{i}") for i in range(2)]
    XB_ = [p.sb([128, 8 * TC], F32, f"{name}_x{i}") for i in range(2)]
    SGT = p.sb([128, TC], F32, f"{name}_sg")
    XO_ = [p.sb([128, 8 * TC], F32, f"{name}_xo{i}") for i in range(2)]
    s_sl = p.sems(f"{name}_sl", 2)
    s_gl = p.sems(f"{name}_gl", 2)
    s_xl = p.sems(f"{name}_xl", 2)
    s_oc = p.sems(f"{name}_oc", 2)
    done = []
    nch = TL // TC
    out_c0 = 0 if out_is_flat else HALO
    for c in range(nch):
        sl = c % 2
        if c >= 2:
            p.wait("sync", chv, done[c - 2][2])
        p.dma("sync", SB_[sl][:, :].rearrange("q (kt t) -> q kt t", t=TC),
              d[s_key].rearrange("(kt q) t -> q kt t", q=128)[
                  :, :, s_c0 + c * TC : s_c0 + (c + 1) * TC], s_sl[sl])
        p.dma("sync", GB_[sl][:, :].rearrange("q (kt t) -> q kt t", t=TC),
              d[g_key].rearrange("(kt q) t -> q kt t", q=128)[
                  :, :, g_c0 + c * TC : g_c0 + (c + 1) * TC], s_gl[sl])
        p.dma("sync", XB_[sl][:, :].rearrange("q (kt t) -> q kt t", t=TC),
              d[xin_key].rearrange("(kt q) t -> q kt t", q=128)[
                  :, :, HALO + c * TC : HALO + (c + 1) * TC], s_xl[sl])
        p.wait("scalar", s_gl[sl])
        p.wait("vector", s_sl[sl])
        p.wait("vector", s_xl[sl])
        if c >= 2:
            p.wait("vector", s_oc[sl])
            p.wait("scalar", s_oc[sl])
        for kt in range(NKT):
            gv = p.cop("scalar", lambda sc, sl=sl, kt=kt: sc.activation(
                GB_[sl][:, kt * TC : (kt + 1) * TC], GB_[sl][:, kt * TC : (kt + 1) * TC],
                AF.Silu))
            p.wait("vector", chs, gv)
            p.cop("vector", lambda v, sl=sl, kt=kt: v.tensor_tensor(
                SGT[:, :], SB_[sl][:, kt * TC : (kt + 1) * TC],
                GB_[sl][:, kt * TC : (kt + 1) * TC], ALU.mult))
            p.cop("vector", lambda v, sl=sl, kt=kt: v.tensor_tensor(
                XO_[sl][:, kt * TC : (kt + 1) * TC], XB_[sl][:, kt * TC : (kt + 1) * TC],
                SGT[:, :], ALU.add))
        dn = p.vchain("vector")
        p.wait("scalar", chv, dn)
        p.dma("scalar", d[xout_key].rearrange("(kt q) t -> q kt t", q=128)[
            :, :, out_c0 + c * TC : out_c0 + (c + 1) * TC],
            XO_[sl][:, :].rearrange("q (kt t) -> q kt t", t=TC), s_oc[sl])
        done.append((s_oc[sl], s_oc[sl].n, p.vchain("vector")))
    p.barrier(waits=[(a_, b_) for a_, b_, _ in done])
    p.reset(m0)
    return done




def emit_l2attn(p, d, *, name, wk):
    m0 = p.mark()
    chv = p.chain["vector"]
    chs = p.chain["scalar"]
    for nm, wkey, dst, act in (("r", "wr_p", "r2", AF.Sigmoid), ("k", "wk_p", "k2b", None),
                               ("v", "wv_p", "v2", None)):
        mm = p.mark()
        epi, s_so = epi_store(p, name=f"{name}_{nm}e", dst=d[dst], dst_c0=0, act=act)
        emit_gemm(p, name=f"{name}_{nm}g", src=d["xn"], src_c0=HALO, wsrc=wk[wkey],
                  d_in=D, d_out=D, cols=TL, epi=epi)
        p.barrier(waits=[(sm, sm.n) for sm in s_so])
        p.reset(mm)

    DC = p.sb([128, 8], F32, f"{name}_dc")
    LNW = p.sb([128, 8], F32, f"{name}_lnw")
    s_dc = p.sem(f"{name}_dc")
    p.dma("sync", DC[:, :], wk["decay_col"], s_dc)
    p.dma("sync", LNW[:, :], wk["lnw_col"], s_dc)
    p.wait("scalar", s_dc)
    p.wait("vector", s_dc)
    sv = p.cop("scalar", lambda sc: sc.activation(DC[:, :], DC[:, :], AF.Sigmoid))
    p.wait("vector", chs, sv)
    cv = p.cop("vector", lambda v: v.tensor_scalar(DC[:, :], DC[:, :], 1e-7, None, ALU.max))
    p.wait("scalar", chv, cv)
    lv = p.cop("scalar", lambda sc: sc.activation(DC[:, :], DC[:, :], AF.Ln))
    p.wait("vector", chs, lv)
    p.wait("tensor", chs, lv)
    NDC = p.sb([128, 8], F32, f"{name}_ndc")
    ndv = p.cop("vector", lambda v: v.tensor_scalar(NDC[:, :], DC[:, :], -1.0, None, ALU.mult))
    p.wait("scalar", chv, ndv)

    KT_ = p.sb([128, TL], BF16, f"{name}_kt")
    VT_ = p.sb([128, TL], BF16, f"{name}_vt")
    RT_ = p.sb([128, TL], BF16, f"{name}_rt")
    IOT = p.sb([128, TL], F32, f"{name}_iot")
    SCL = p.sb([128, TL], F32, f"{name}_scl")
    KV = p.sb([128, TL], F32, f"{name}_kv")
    CUM = p.sb([128, TL], F32, f"{name}_cum")
    YT2 = p.sb([128, TL], F32, f"{name}_yt2")
    SQT = p.sb([128, TC], BF16, f"{name}_sqt")
    ZR = p.sb([128, 8], F32, f"{name}_zr")
    STL = p.sb([128, 16], F32, f"{name}_stl")
    ONESB = p.sb([128, 1], BF16, f"{name}_onesb")
    ONES1 = p.sb([1, 128], F32, f"{name}_ones1f")
    RSD = p.sb([1, 8 * TC], F32, f"{name}_rsd")
    YN = [p.sb([128, 8 * TC], BF16, f"{name}_yn{i}") for i in range(2)]
    PS_SS = p.ps([1, 512], F32)
    PS_RB = p.ps([128, 512], F32)
    s_ld = p.sems(f"{name}_ld", 4)
    s_io = p.sems(f"{name}_io", 2)
    s_ag = p.sem(f"{name}_ag")
    s_st = p.sem(f"{name}_st")
    s_ssum = p.sem(f"{name}_ssum")
    s_rb = p.sem(f"{name}_rb")
    s_yn = p.sems(f"{name}_yn", 2)
    iv = p.cop("vector", lambda v: v.memset(ONESB[:, :], 1.0))
    iv2 = p.cop("vector", lambda v: v.memset(ONES1[:, :], 1.0))
    zv = p.cop("vector", lambda v: v.memset(ZR[:, :], 0.0))
    p.wait("tensor", chv, zv)

    tile_done = {}
    prev_ev = None
    for dt_i in range(8):
        if dt_i >= 1:
            p.wait("sync", chv, tile_done.get(dt_i - 1, 0))
            p.wait("vector", s_st)
            p.wait("gpsimd", chv, prev_ev)
        p.dma("sync", KT_[:, :], d["k2b"][dt_i * 128 : (dt_i + 1) * 128, :], s_ld[0])
        p.dma("sync", VT_[:, :], d["v2"][dt_i * 128 : (dt_i + 1) * 128, :], s_ld[1])
        p.dma("gpsimd", IOT[:, :], wk["iota"][:, 0:TL], s_io[0])
        p.wait("vector", s_ld[0])
        p.wait("vector", s_ld[1])
        p.wait("vector", s_io[0])
        mv0 = p.cop("vector", lambda v, dt_i=dt_i: v.tensor_scalar(
            SCL[:, :], IOT[:, :], NDC[:, dt_i : dt_i + 1], 23.025850929940457,
            ALU.mult, ALU.min))
        prev_ev = mv0
        p.wait("scalar", chv, mv0)
        ev = p.cop("scalar", lambda sc: sc.activation(SCL[:, :], SCL[:, :], AF.Exp))
        p.cop("vector", lambda v: v.tensor_mul(KV[:, :], KT_[:, :], VT_[:, :]))
        p.wait("vector", chs, ev)
        p.cop("vector", lambda v: v.tensor_tensor(KV[:, :], KV[:, :], SCL[:, :], ALU.mult))
        p.cop("vector", lambda v: v.tensor_tensor_scan(
            CUM[:, :], KV[:, :], KV[:, :], 0.0, ALU.add, ALU.bypass))
        sv2 = p.cop("vector", lambda v, dt_i=dt_i: v.tensor_copy(
            STL[:, dt_i : dt_i + 1], CUM[:, TL - 1 : TL]))
        tile_done[dt_i] = sv2
        p.wait("scalar", chv, sv2)
        p.dma("scalar", d["cum2"][dt_i * 128 : (dt_i + 1) * 128, :], CUM[:, :], s_st)
    p.wait("gpsimd", chv, tile_done[7])
    s_cc = p.sem(f"{name}_cc")
    p.dma("gpsimd", d["cc2_in"], STL[:, 0:8], s_cc)
    p.wait("gpsimd", s_cc)
    p.raw("gpsimd", lambda g: g.collective_compute(
        "AllGather", ALU.bypass, replica_groups=[[0, 1], [2, 3], [4, 5], [6, 7]],
        ins=[d["cc2_in"].opt()], outs=[d["cc2_out"].opt()],
    ).then_inc(s_ag.h, 1))
    s_ag.n += 1
    p.wait("gpsimd", s_ag)
    s_si = p.sem(f"{name}_si")
    p.dma("gpsimd", STL[:, 8:16], d["cc2_out"][0:128, :], s_si)
    HM2 = p.sb([128, 1], F32, f"{name}_hm2")
    s_hm2 = p.sem(f"{name}_hm2")
    p.dma("sync", HM2[:, :], d["hm_col"], s_hm2)
    p.wait("vector", s_si)
    p.wait("vector", s_hm2)
    p.cop("vector", lambda v: v.tensor_scalar(STL[:, 8:16], STL[:, 8:16], HM2[:, 0:1],
                                              None, ALU.mult))
    p.wait("vector", s_st)
    p.wait("sync", s_st)

    prev_ev2 = None
    prev_yv = None
    for dt_i in range(8):
        if dt_i >= 1:
            p.wait("vector", s_st)
            p.wait("gpsimd", chs, prev_ev2)
            p.wait("sync", chv, prev_yv)
            p.wait("sync", s_st)
        p.dma("sync", CUM[:, :], d["cum2"][dt_i * 128 : (dt_i + 1) * 128, :], s_ld[2])
        p.dma("sync", RT_[:, :], d["r2"][dt_i * 128 : (dt_i + 1) * 128, :], s_ld[3])
        p.dma("gpsimd", IOT[:, :], wk["iota"][:, 0:TL], s_io[1])
        p.wait("vector", s_ld[2])
        p.wait("vector", s_ld[3])
        p.wait("scalar", s_io[1])
        ev = p.cop("scalar", lambda sc, dt_i=dt_i: sc.activation(
            SCL[:, :], IOT[:, :], AF.Exp, scale=DC[:, dt_i : dt_i + 1]))
        prev_ev2 = ev
        p.wait("vector", chs, ev)
        stcol = STL[:, 8 + dt_i : 9 + dt_i]
        p.cop("vector", lambda v, stcol=stcol: v.tensor_scalar(
            CUM[:, :], CUM[:, :], stcol, None, ALU.add))
        p.cop("vector", lambda v: v.tensor_mul(CUM[:, :], CUM[:, :], SCL[:, :]))
        yv = p.cop("vector", lambda v: v.tensor_mul(YT2[:, :], CUM[:, :], RT_[:, :]))
        prev_yv = yv
        p.wait("scalar", chv, yv)
        p.dma("scalar", d["y2s"][dt_i * 128 : (dt_i + 1) * 128, :], YT2[:, :], s_st)
        for c in range(TL // TC):
            if dt_i > 0 or c > 0:
                p.wait("scalar", s_ssum)
            sq = p.cop("scalar", lambda sc, c=c: sc.activation(
                SQT[:, :], YT2[:, c * TC : (c + 1) * TC], AF.Square))
            last_sq = sq
            p.wait("tensor", chs, sq)
            p.wait("tensor", chv)
            mv = p.top(lambda t: t.matmul(
                PS_SS[0:1, :], ONESB[:, :], SQT[:, :], start=True, stop=True),
                sem=s_ssum)
            p.wait("vector", s_ssum, mv)
            if dt_i == 0:
                p.cop("vector", lambda v, c=c: v.tensor_copy(
                    RSD[0:1, c * TC : (c + 1) * TC], PS_SS[0:1, :]))
            else:
                p.cop("vector", lambda v, c=c: v.tensor_tensor(
                    RSD[0:1, c * TC : (c + 1) * TC],
                    RSD[0:1, c * TC : (c + 1) * TC], PS_SS[0:1, :], ALU.add))
    EPS2 = p.sb([1, 1], F32, f"{name}_eps2")
    p.cop("vector", lambda v: v.memset(EPS2[:, :], 1e-6))
    rv = p.cop("vector", lambda v: v.tensor_scalar(RSD[0:1, :], RSD[0:1, :], 1.0 / D, None, ALU.mult))
    p.wait("scalar", chv, rv)
    p.cop("scalar", lambda sc: sc.activation(RSD[0:1, :], RSD[0:1, :], AF.Ln, bias=EPS2[0:1, 0:1]))
    rv2 = p.cop("scalar", lambda sc: sc.activation(RSD[0:1, :], RSD[0:1, :], AF.Exp, scale=-0.5))
    p.wait("tensor", chs, rv2)
    p.wait("vector", s_st)
    p.wait("sync", chs, last_sq)
    p.wait("sync", s_st)

    ynctr = [0]
    for dt_i in range(8):
        sl = dt_i % 2
        if dt_i >= 1:
            p.wait("sync", chv)
        p.dma("sync", YT2[:, :], d["y2s"][dt_i * 128 : (dt_i + 1) * 128, :], s_ld[0])
        p.wait("vector", s_ld[0])
        if dt_i >= 2:
            p.wait("vector", s_yn[sl])
        for c in range(TL // TC):
            if ynctr[0] > 0:
                p.wait("tensor", chv, ynctr[1] if False else ynctr[0])
            bv = p.top(lambda t, c=c: t.matmul(
                PS_RB[:, :], ONES1[0:1, :], RSD[0:1, c * TC : (c + 1) * TC],
                start=True, stop=True), sem=s_rb)
            p.wait("vector", s_rb, bv)
            p.cop("vector", lambda v, c=c: v.tensor_tensor(
                YT2[:, c * TC : (c + 1) * TC], YT2[:, c * TC : (c + 1) * TC],
                PS_RB[:, :], ALU.mult))
            vv = p.cop("vector", lambda v, sl=sl, c=c, dt_i=dt_i: v.tensor_scalar(
                YN[sl][:, c * TC : (c + 1) * TC], YT2[:, c * TC : (c + 1) * TC],
                LNW[:, dt_i : dt_i + 1], None, ALU.mult))
            ynctr[0] = vv
        dn = p.vchain("vector")
        p.wait("scalar", chv, dn)
        p.dma("scalar", d["yn2"][dt_i * 128 : (dt_i + 1) * 128, :],
              YN[sl][:, :].rearrange("q (c t) -> q (c t)", t=TC), s_yn[sl])
    p.barrier(waits=[(s_yn[0], s_yn[0].n), (s_yn[1], s_yn[1].n)])
    p.reset(m0)

    epi_o, s_so_o = epi_store(p, name=f"{name}_oe", dst=d["s2"], dst_c0=0)
    emit_gemm(p, name=f"{name}_og", src=d["yn2"], src_c0=0, wsrc=wk["wo_p"],
              d_in=D, d_out=D, cols=TL, epi=epi_o)
    p.barrier(waits=[(sm, sm.n) for sm in s_so_o])
    p.reset(m0)


def l2attn_host(inp):
    return dict(
        wr_p=pack_wT(np.asarray(inp["L2_attn_Wr"])),
        wk_p=pack_wT(np.asarray(inp["L2_attn_Wk"])),
        wv_p=pack_wT(np.asarray(inp["L2_attn_Wv"])),
        wo_p=pack_wT(np.asarray(inp["L2_attn_Wo"])),
        decay_col=pack_vec(np.asarray(inp["L2_attn_decay"])),
        lnw_col=pack_vec(np.asarray(inp["L2_attn_ln_w"])),
        iota=np.tile(np.arange(T, dtype=np.float32), (128, 1)),
    )


def emit_l2ffn(p, d, *, name, wk):
    m0 = p.mark()
    chv = p.chain["vector"]
    CV = p.sb([128, 4096], I32, f"{name}_cv")
    CB = p.sb([128, 4096], BF16, f"{name}_cb")
    s_cl = p.sem(f"{name}_cl")
    s_cs = p.sem(f"{name}_cs")
    for wi, wb, ncols in (("wg_i", "wg_b", 8 * 3072), ("wu_i", "wu_b", 8 * 3072),
                          ("wd_i", "wd_b", 24 * 1024)):
        for c0 in range(0, ncols, 4096):
            cw = min(4096, ncols - c0)
            p.wait("sync", chv)
            p.dma("sync", CV[:, :cw], d[wi][:, c0 : c0 + cw], s_cl)
            p.wait("vector", s_cl)
            if c0 > 0 or wi != "wg_i":
                p.wait("vector", s_cs)
            vv = p.cop("vector", lambda v, cw=cw: v.tensor_copy(CB[:, :cw], CV[:, :cw]))
            p.wait("scalar", chv, vv)
            p.dma("scalar", d[wb][:, c0 : c0 + cw], CB[:, :cw], s_cs)
    p.wait("sync", s_cs)
    p.wait("gpsimd", s_cs)

    epi_g, s_so_g = epi_store(p, name=f"{name}_ge", dst=d["gsig"], dst_c0=0,
                              act=AF.Silu, scale=1.0 / 127.0)
    emit_gemm(p, name=f"{name}_gg", src=d["xn"], src_c0=HALO, wsrc=d["wg_b"],
              d_in=D, d_out=3072, cols=TL, epi=epi_g, w_bf16=True)
    p.barrier(waits=[(sm, sm.n) for sm in s_so_g])
    p.reset(m0)

    OBU = [p.sb([128, TC], BF16, f"{name}_obu{i}") for i in range(4)]
    GSB2 = [p.sb([128, TC], BF16, f"{name}_gsb{i}") for i in range(4)]
    s_gl2 = p.sems(f"{name}_gl2", 4)
    s_sou = p.sems(f"{name}_sou", 4)
    uctr = [0]
    ucons = {}

    def epi_u(p, c, o, psum, tc_here, s_mm):
        i = uctr[0]
        uctr[0] += 1
        sl = i % 4
        if i >= 4:
            p.wait("sync", chv, ucons[i - 4])
        p.dma("sync", GSB2[sl][:, :], d["gsig"][o * 128 : (o + 1) * 128,
              c * TC : (c + 1) * TC], s_gl2[sl])
        p.wait("vector", s_mm)
        p.wait("vector", s_gl2[sl])
        if i >= 4:
            p.wait("vector", s_sou[sl])
        val = p.cop("vector", lambda v, sl=sl, psum=psum: v.scalar_tensor_tensor(
            OBU[sl][:, :], psum, 1.0 / 127.0, GSB2[sl][:, :], ALU.mult, ALU.mult))
        ucons[i] = val
        p.wait("scalar", chv, val)
        p.dma("scalar", d["up_h"][o * 128 : (o + 1) * 128, c * TC : (c + 1) * TC],
              OBU[sl][:, :], s_sou[sl])
        return (chv, val)

    emit_gemm(p, name=f"{name}_ug", src=d["xn"], src_c0=HALO, wsrc=d["wu_b"],
              d_in=D, d_out=3072, cols=TL, epi=epi_u, w_bf16=True)
    p.barrier(waits=[(sm, sm.n) for sm in s_sou])
    p.reset(m0)

    epi_d, s_so_d = epi_store(p, name=f"{name}_de", dst=d["g2"], dst_c0=0,
                              scale=1.0 / 127.0)
    emit_gemm(p, name=f"{name}_dg", src=d["up_h"], src_c0=0, wsrc=d["wd_b"],
              d_in=3072, d_out=D, cols=TL, epi=epi_d, w_bf16=True)
    p.barrier(waits=[(sm, sm.n) for sm in s_so_d])
    p.reset(m0)


def l2ffn_host(inp):
    return dict(
        wg_i=pack_w(np.ascontiguousarray(np.asarray(inp["L2_ffn_w_gate"]).T)).astype(np.int32),
        wu_i=pack_w(np.ascontiguousarray(np.asarray(inp["L2_ffn_w_up"]).T)).astype(np.int32),
        wd_i=pack_w(np.ascontiguousarray(np.asarray(inp["L2_ffn_w_down"]).T)).astype(np.int32),
    )




def emit_l1attn(p, d, *, name, wk):
    m0 = p.mark()
    chv = p.chain["vector"]
    chs = p.chain["scalar"]
    NBLK = TL // 128

    ABC = p.sb([128, 56], F32, f"{name}_abc")
    IDT = p.sb([128, 128], BF16, f"{name}_idt")
    s_k0 = p.sem(f"{name}_k0")
    s_k1 = p.sem(f"{name}_k1")
    p.dma("sync", ABC[:, :], wk["abc_col"], s_k0)
    p.dma("gpsimd", IDT[:, :], wk["ident_p"], s_k1)
    p.wait("vector", s_k0)
    p.wait("tensor", s_k1)

    PR = p.sb([128, 8 * 128], F32, f"{name}_pr")
    PI = p.sb([128, 8 * 128], F32, f"{name}_pi")
    KK = p.sb([128, 8 * 128], BF16, f"{name}_kk")
    WR = p.sb([128, 8 * 128], F32, f"{name}_wr")
    WI = p.sb([128, 8 * 128], F32, f"{name}_wi")
    ER = p.sb([128, 8 * 128], F32, f"{name}_er")
    EI = p.sb([128, 8 * 128], F32, f"{name}_ei")
    A128 = p.sb([128, 8 * 4], F32, f"{name}_a128")
    TC1 = p.sb([128, 128], F32, f"{name}_tc1")
    TC2 = p.sb([128, 128], F32, f"{name}_tc2")
    for dt_i in range(8):
        pr = PR[:, dt_i * 128 : (dt_i + 1) * 128]
        pi = PI[:, dt_i * 128 : (dt_i + 1) * 128]
        ar = ABC[:, dt_i * 7 + 0 : dt_i * 7 + 1]
        ai = ABC[:, dt_i * 7 + 1 : dt_i * 7 + 2]
        br = ABC[:, dt_i * 7 + 2 : dt_i * 7 + 3]
        bi = ABC[:, dt_i * 7 + 3 : dt_i * 7 + 4]
        cr = ABC[:, dt_i * 7 + 4 : dt_i * 7 + 5]
        ci = ABC[:, dt_i * 7 + 5 : dt_i * 7 + 6]
        p.cop("vector", lambda v, pr=pr: v.memset(pr[:, 0:1], 1.0))
        p.cop("vector", lambda v, pi=pi: v.memset(pi[:, 0:1], 0.0))
        p.cop("vector", lambda v, pr=pr, ar=ar: v.tensor_copy(pr[:, 1:2], ar))
        p.cop("vector", lambda v, pi=pi, ai=ai: v.tensor_copy(pi[:, 1:2], ai))
        kk = 1
        while kk < 127:
            cnt = min(kk, 127 - kk)
            p.cop("vector", lambda v, pr=pr, kk=kk, cnt=cnt: v.tensor_scalar(
                TC1[:, :cnt], pr[:, 1 : 1 + cnt], pr[:, kk : kk + 1], None, ALU.mult))
            p.cop("vector", lambda v, pi=pi, kk=kk, cnt=cnt: v.tensor_scalar(
                TC2[:, :cnt], pi[:, 1 : 1 + cnt], pi[:, kk : kk + 1], None, ALU.mult))
            p.cop("vector", lambda v, cnt=cnt: v.tensor_sub(
                TC1[:, :cnt], TC1[:, :cnt], TC2[:, :cnt]))
            p.cop("vector", lambda v, pr=pr, pi=pi, kk=kk, cnt=cnt: v.tensor_scalar(
                TC2[:, :cnt], pr[:, 1 : 1 + cnt], pi[:, kk : kk + 1], None, ALU.mult))
            p.cop("vector", lambda v, pr=pr, pi=pi, kk=kk, cnt=cnt: v.scalar_tensor_tensor(
                pi[:, kk + 1 : kk + 1 + cnt], pi[:, 1 : 1 + cnt], pr[:, kk : kk + 1],
                TC2[:, :cnt], ALU.mult, ALU.add))
            p.cop("vector", lambda v, pr=pr, kk=kk, cnt=cnt: v.tensor_copy(
                pr[:, kk + 1 : kk + 1 + cnt], TC1[:, :cnt]))
            kk *= 2
        p.cop("vector", lambda v, pr=pr: v.tensor_mul(TC1[:, 0:1], pr[:, 64:65], pr[:, 64:65]))
        p.cop("vector", lambda v, pi=pi: v.tensor_mul(TC2[:, 0:1], pi[:, 64:65], pi[:, 64:65]))
        p.cop("vector", lambda v, dt_i=dt_i: v.tensor_sub(
            A128[:, 4 * dt_i : 4 * dt_i + 1], TC1[:, 0:1], TC2[:, 0:1]))
        p.cop("vector", lambda v, pr=pr, pi=pi: v.tensor_mul(TC1[:, 0:1], pr[:, 64:65], pi[:, 64:65]))
        p.cop("vector", lambda v, dt_i=dt_i: v.tensor_scalar(
            A128[:, 4 * dt_i + 1 : 4 * dt_i + 2], TC1[:, 0:1], 2.0, None, ALU.mult))
        p.cop("vector", lambda v, cr=cr, br=br: v.tensor_mul(TC2[:, 4:5], cr, br))
        p.cop("vector", lambda v, ci=ci, bi=bi: v.tensor_mul(TC2[:, 6:7], ci, bi))
        p.cop("vector", lambda v: v.tensor_sub(TC2[:, 4:5], TC2[:, 4:5], TC2[:, 6:7]))
        p.cop("vector", lambda v, cr=cr, bi=bi: v.tensor_mul(TC2[:, 5:6], cr, bi))
        p.cop("vector", lambda v, ci=ci, br=br: v.scalar_tensor_tensor(
            TC2[:, 5:6], br, ci[:, 0:1], TC2[:, 5:6], ALU.mult, ALU.add))
        p.cop("vector", lambda v, pr=pr: v.tensor_scalar(
            TC1[:, 0:128], pr, TC2[:, 4:5], None, ALU.mult))
        p.cop("vector", lambda v, pi=pi, dt_i=dt_i: v.tensor_scalar(
            ER[:, dt_i * 128 : (dt_i + 1) * 128], pi, TC2[:, 5:6], None, ALU.mult))
        p.cop("vector", lambda v, dt_i=dt_i: v.tensor_sub(
            KK[:, dt_i * 128 : (dt_i + 1) * 128], TC1[:, 0:128],
            ER[:, dt_i * 128 : (dt_i + 1) * 128]))
        p.cop("vector", lambda v, pr=pr, br=br, dt_i=dt_i: v.tensor_scalar(
            WR[:, dt_i * 128 : (dt_i + 1) * 128], pr[:, 127::-1], br[:, 0:1], None, ALU.mult))
        p.cop("vector", lambda v, pi=pi, bi=bi: v.tensor_scalar(
            TC1[:, 0:128], pi[:, 127::-1], bi[:, 0:1], None, ALU.mult))
        p.cop("vector", lambda v, dt_i=dt_i: v.tensor_sub(
            WR[:, dt_i * 128 : (dt_i + 1) * 128],
            WR[:, dt_i * 128 : (dt_i + 1) * 128], TC1[:, 0:128]))
        p.cop("vector", lambda v, pi=pi, br=br, dt_i=dt_i: v.tensor_scalar(
            WI[:, dt_i * 128 : (dt_i + 1) * 128], pi[:, 127::-1], br[:, 0:1], None, ALU.mult))
        p.cop("vector", lambda v, pr=pr, bi=bi, dt_i=dt_i: v.scalar_tensor_tensor(
            WI[:, dt_i * 128 : (dt_i + 1) * 128], pr[:, 127::-1], bi[:, 0:1],
            WI[:, dt_i * 128 : (dt_i + 1) * 128], ALU.mult, ALU.add))
        p.cop("vector", lambda v, pr=pr, cr=cr, dt_i=dt_i: v.tensor_scalar(
            ER[:, dt_i * 128 : dt_i * 128 + 127], pr[:, 1:128], cr[:, 0:1], None, ALU.mult))
        p.cop("vector", lambda v, pi=pi, ci=ci: v.tensor_scalar(
            TC1[:, 0:127], pi[:, 1:128], ci[:, 0:1], None, ALU.mult))
        p.cop("vector", lambda v, dt_i=dt_i: v.tensor_sub(
            ER[:, dt_i * 128 : dt_i * 128 + 127],
            ER[:, dt_i * 128 : dt_i * 128 + 127], TC1[:, 0:127]))
        p.cop("vector", lambda v, pi=pi, cr=cr, dt_i=dt_i: v.tensor_scalar(
            EI[:, dt_i * 128 : dt_i * 128 + 127], pi[:, 1:128], cr[:, 0:1], None, ALU.mult))
        p.cop("vector", lambda v, pr=pr, ci=ci, dt_i=dt_i: v.scalar_tensor_tensor(
            EI[:, dt_i * 128 : dt_i * 128 + 127], pr[:, 1:128], ci[:, 0:1],
            EI[:, dt_i * 128 : dt_i * 128 + 127], ALU.mult, ALU.add))
        p.cop("vector", lambda v, cr=cr, dt_i=dt_i: v.tensor_mul(
            TC1[:, 2:3], cr, A128[:, 4 * dt_i : 4 * dt_i + 1]))
        p.cop("vector", lambda v, ci=ci, dt_i=dt_i: v.tensor_mul(
            TC2[:, 2:3], ci, A128[:, 4 * dt_i + 1 : 4 * dt_i + 2]))
        p.cop("vector", lambda v, dt_i=dt_i: v.tensor_sub(
            ER[:, dt_i * 128 + 127 : (dt_i + 1) * 128], TC1[:, 2:3], TC2[:, 2:3]))
        p.cop("vector", lambda v, ci=ci, dt_i=dt_i: v.tensor_mul(
            TC1[:, 2:3], ci, A128[:, 4 * dt_i : 4 * dt_i + 1]))
        p.cop("vector", lambda v, cr=cr, dt_i=dt_i: v.scalar_tensor_tensor(
            EI[:, dt_i * 128 + 127 : (dt_i + 1) * 128],
            A128[:, 4 * dt_i + 1 : 4 * dt_i + 2], cr[:, 0:1], TC1[:, 2:3],
            ALU.mult, ALU.add))
        kprep = p.vchain("vector")
    p.wait("scalar", chv, kprep)

    ZB1 = p.sb([128, 1024], BF16, f"{name}_zb1")
    s_z1 = p.sem(f"{name}_z1")
    zv = p.cop("vector", lambda v: v.memset(ZB1[:, :], 0.0))
    p.wait("scalar", chv, zv)
    p.dma("scalar", d["z1"].rearrange("(dt q) c -> q dt c", q=128)[:, :, 0:128],
          ZB1[:, :].rearrange("q (dt c) -> q dt c", c=128), s_z1)
    p.dma("scalar", d["z1"].rearrange("(dt q) c -> q dt c", q=128)[:, :, 128:256],
          KK[:, :].rearrange("q (dt c) -> q dt c", c=128), s_z1)
    p.wait("scalar", s_z1)
    s_bt = emit_toeplitz_build(p, name=f"{name}_tb", z_t=d["z1"].tensor, z_off=0,
                               t_t=d["t1m"].tensor, t_off=0)

    UT = p.sb([128, NBLK * 128], BF16, f"{name}_ut")
    TB1 = p.sb([128, 128 * 128], BF16, f"{name}_tb1")
    UB = p.sb([128, TL], BF16, f"{name}_ub")
    SJR = p.sb([128, 8 * NBLK], F32, f"{name}_sjr")
    SJI = p.sb([128, 8 * NBLK], F32, f"{name}_sji")
    TMW = p.sb([128, TL], F32, f"{name}_tmw")
    YTM = p.sb([128, NBLK * 128], BF16, f"{name}_ytm")
    YO1 = p.sb([128, NBLK * 128], F32, f"{name}_yo1")
    PS_C1 = [p.ps([128, 64], F32) for i in range(2)]
    PS_T1 = p.ps([128, 128], BF16)
    PS_T1b = p.ps([128, 128], BF16)
    s_utl = p.sem(f"{name}_utl")
    s_ubl = p.sem(f"{name}_ubl")
    s_tbl = p.sem(f"{name}_tbl")
    s_cm1 = p.sem(f"{name}_cm1")
    s_tr1 = p.sem(f"{name}_tr1")
    s_y1 = p.sem(f"{name}_y1")
    p.wait("sync", s_bt)
    p.wait("scalar", s_bt)

    hist = {}
    for dt_i in range(8):
        if dt_i >= 1:
            p.wait("sync", chv)
            p.wait("sync", s_cm1)
        p.dma("sync", UB[:, :], d["xn"][dt_i * 128 : (dt_i + 1) * 128, HALO:], s_ubl)
        for blk in range(NBLK):
            p.dma("sync", UT[:, blk * 128 : (blk + 1) * 128],
                  d["xn"][dt_i * 128 : (dt_i + 1) * 128,
                          HALO + blk * 128 : HALO + (blk + 1) * 128],
                  s_utl, transpose=True)
        p.dma("sync", TB1[:, :],
              bass.AP(d["t1m"].tensor, dt_i * 128 * 128, [[1024 * 128, 128], [1, 128 * 128]]),
              s_tbl)
        p.wait("vector", s_ubl)
        wre = WR[:, dt_i * 128 : (dt_i + 1) * 128]
        wim = WI[:, dt_i * 128 : (dt_i + 1) * 128]
        for ri, wv_ in ((0, wre), (1, wim)):
            p.cop("vector", lambda v, wv_=wv_: v.tensor_tensor(
                TMW[:, :].rearrange("q (b s) -> q b s", s=128), 
                UB[:, :].rearrange("q (b s) -> q b s", s=128),
                wv_[:, None, :].broadcast_to([128, NBLK, 128]), ALU.mult))
            p.cop("vector", lambda v, ri=ri, dt_i=dt_i: v.reduce_sum(
                (SJR if ri == 0 else SJI)[:, dt_i * NBLK : (dt_i + 1) * NBLK].rearrange(
                    "q b -> q b 1" if False else "q b -> q b", ) if False else
                (SJR if ri == 0 else SJI)[:, dt_i * NBLK : (dt_i + 1) * NBLK],
                TMW[:, :].rearrange("q (b s) -> q b s", s=128), axis=AX.X))
        p.wait("tensor", s_tbl)
        p.wait("tensor", s_utl)
        p.wait("vector", s_tr1)
        cg = {}
        for ch in range(128):
            bnk = ch % 2
            if ch >= 2:
                p.wait("tensor", chv, cg[ch - 2])
            elif dt_i > 0:
                p.wait("tensor", chv, hist[("cglast", dt_i - 1)])
            mv = p.top(lambda t, ch=ch, bnk=bnk: t.matmul(
                PS_C1[bnk][:, :NBLK], TB1[:, ch * 128 : (ch + 1) * 128],
                UT[:, ch : ch + (NBLK - 1) * 128 + 1 : 128], start=True, stop=True),
                sem=s_cm1)
            p.wait("vector", s_cm1, mv)
            cg[ch] = p.cop("vector", lambda v, ch=ch, bnk=bnk: v.tensor_copy(
                YTM[:, ch : ch + (NBLK - 1) * 128 + 1 : 128], PS_C1[bnk][:, :NBLK]))
        hist[("cglast", dt_i)] = cg[127]
        p.wait("tensor", chv, cg[127])
        p.wait("vector", s_y1)
        cpv = {}
        for blk in range(NBLK):
            pst = PS_T1 if blk % 2 == 0 else PS_T1b
            if blk >= 2:
                p.wait("tensor", chv, cpv[blk - 2])
            tv = p.top(lambda t, blk=blk, pst=pst: t.transpose(
                pst[:, :], YTM[:, blk * 128 : (blk + 1) * 128], IDT[:, :]), sem=s_tr1)
            p.wait("vector", s_tr1, tv)
            cpv[blk] = p.cop("vector", lambda v, blk=blk, pst=pst: v.tensor_copy(
                YO1[:, blk * 128 : (blk + 1) * 128], pst[:, :]))
        p.wait("scalar", chv, cpv[NBLK - 1])
        p.dma("scalar", d["yraw1"][dt_i * 128 : (dt_i + 1) * 128, :], YO1[:, :], s_y1)
    sj_done = p.vchain("vector")

    HR = p.sb([128, 8 * (NBLK + 1)], F32, f"{name}_hr")
    HI = p.sb([128, 8 * (NBLK + 1)], F32, f"{name}_hi")
    T3 = p.sb([128, 8], F32, f"{name}_t3")
    T4 = p.sb([128, 8], F32, f"{name}_t4")
    STI = p.sb([128, 16], F32, f"{name}_sti")
    s_ag1 = p.sem(f"{name}_ag1")
    s_ci = p.sem(f"{name}_ci")
    hr3 = HR[:, :].rearrange("q (dt j) -> q j dt", dt=8)
    hi3 = HI[:, :].rearrange("q (dt j) -> q j dt", dt=8)
    sjr3 = SJR[:, :].rearrange("q (dt j) -> q j dt", dt=8)
    sji3 = SJI[:, :].rearrange("q (dt j) -> q j dt", dt=8)
    a128r = A128[:, 0::4]
    a128i = A128[:, 1::4]
    def emit_hscan(init_from_sti):
        p.cop("vector", lambda v: v.memset(hr3[:, 0, :], 0.0))
        p.cop("vector", lambda v: v.memset(hi3[:, 0, :], 0.0))
        for J in range(NBLK):
            p.cop("vector", lambda v, J=J: v.tensor_tensor(
                T3[:, :], hr3[:, J, :], a128r.broadcast_to([128, 8]) if False else a128r,
                ALU.mult))
            p.cop("vector", lambda v, J=J: v.tensor_tensor(
                T4[:, :], hi3[:, J, :], a128i, ALU.mult))
            p.cop("vector", lambda v, J=J: v.tensor_sub(T3[:, :], T3[:, :], T4[:, :]))
            p.cop("vector", lambda v, J=J: v.tensor_tensor(
                hr3[:, J + 1, :], T3[:, :], sjr3[:, J, :], ALU.add))
            p.cop("vector", lambda v, J=J: v.tensor_tensor(
                T3[:, :], hr3[:, J, :], a128i, ALU.mult))
            p.cop("vector", lambda v, J=J: v.tensor_tensor(
                T4[:, :], hi3[:, J, :], a128r, ALU.mult))
            p.cop("vector", lambda v, J=J: v.tensor_add(T3[:, :], T3[:, :], T4[:, :]))
            p.cop("vector", lambda v, J=J: v.tensor_tensor(
                hi3[:, J + 1, :], T3[:, :], sji3[:, J, :], ALU.add))

    emit_hscan(False)
    hs_done = p.vchain("vector")
    p.wait("gpsimd", chv, hs_done)
    p.dma("gpsimd", d["cc1_in"][:, 0:8], hr3[:, NBLK, :], s_ci,
          allow_slow_non_contiguous=True)
    p.dma("gpsimd", d["cc1_in"][:, 8:16], hi3[:, NBLK, :], s_ci,
          allow_slow_non_contiguous=True)
    p.wait("gpsimd", s_ci)
    p.raw("gpsimd", lambda g: g.collective_compute(
        "AllGather", ALU.bypass, replica_groups=[[0, 1], [2, 3], [4, 5], [6, 7]],
        ins=[d["cc1_in"].opt()], outs=[d["cc1_out"].opt()]).then_inc(s_ag1.h, 1))
    s_ag1.n += 1
    p.wait("gpsimd", s_ag1)
    s_si1 = p.sem(f"{name}_si1")
    p.dma("gpsimd", STI[:, :], d["cc1_out"][0:128, :], s_si1)
    HM1 = p.sb([128, 1], F32, f"{name}_hm1")
    s_hm1 = p.sem(f"{name}_hm1")
    p.dma("sync", HM1[:, :], d["hm_col"], s_hm1)
    p.wait("vector", s_si1)
    p.wait("vector", s_hm1)
    p.cop("vector", lambda v: v.tensor_scalar(STI[:, :], STI[:, :], HM1[:, 0:1], None, ALU.mult))
    APR = p.sb([128, 8 * (NBLK + 1)], F32, f"{name}_apr")
    API = p.sb([128, 8 * (NBLK + 1)], F32, f"{name}_api")
    apr3 = APR[:, :].rearrange("q (dt j) -> q j dt", dt=8)
    api3 = API[:, :].rearrange("q (dt j) -> q j dt", dt=8)
    p.cop("vector", lambda v: v.memset(apr3[:, 0, :], 1.0))
    p.cop("vector", lambda v: v.memset(api3[:, 0, :], 0.0))
    for J in range(NBLK):
        p.cop("vector", lambda v, J=J: v.tensor_tensor(T3[:, :], apr3[:, J, :], a128r, ALU.mult))
        p.cop("vector", lambda v, J=J: v.tensor_tensor(T4[:, :], api3[:, J, :], a128i, ALU.mult))
        p.cop("vector", lambda v, J=J: v.tensor_sub(apr3[:, J + 1, :], T3[:, :], T4[:, :]))
        p.cop("vector", lambda v, J=J: v.tensor_tensor(T3[:, :], apr3[:, J, :], a128i, ALU.mult))
        p.cop("vector", lambda v, J=J: v.tensor_tensor(T4[:, :], api3[:, J, :], a128r, ALU.mult))
        p.cop("vector", lambda v, J=J: v.tensor_add(api3[:, J + 1, :], T3[:, :], T4[:, :]))
    for J in range(NBLK + 1):
        p.cop("vector", lambda v, J=J: v.tensor_tensor(T3[:, :], apr3[:, J, :], STI[:, 0:8], ALU.mult))
        p.cop("vector", lambda v, J=J: v.tensor_tensor(T4[:, :], api3[:, J, :], STI[:, 8:16], ALU.mult))
        p.cop("vector", lambda v, J=J: v.tensor_sub(T3[:, :], T3[:, :], T4[:, :]))
        p.cop("vector", lambda v, J=J: v.tensor_add(hr3[:, J, :], hr3[:, J, :], T3[:, :]))
        p.cop("vector", lambda v, J=J: v.tensor_tensor(T3[:, :], apr3[:, J, :], STI[:, 8:16], ALU.mult))
        p.cop("vector", lambda v, J=J: v.tensor_tensor(T4[:, :], api3[:, J, :], STI[:, 0:8], ALU.mult))
        p.cop("vector", lambda v, J=J: v.tensor_add(T3[:, :], T3[:, :], T4[:, :]))
        p.cop("vector", lambda v, J=J: v.tensor_add(hi3[:, J, :], hi3[:, J, :], T3[:, :]))

    hcdone = p.vchain("vector")
    p.wait("sync", chv, hcdone)
    YF = [p.sb([128, TL], F32, f"{name}_yf{i}") for i in range(1)]
    UB2 = p.sb([128, TL], BF16, f"{name}_ub2")
    SO1 = p.sb([128, TL], BF16, f"{name}_so1")
    s_yl = p.sem(f"{name}_yl")
    s_ul = p.sem(f"{name}_ul")
    s_os = p.sem(f"{name}_os")
    p.wait("vector", s_y1)
    for dt_i in range(8):
        if dt_i >= 1:
            p.wait("sync", chv)
            p.wait("sync", s_os)
        p.dma("sync", YF[0][:, :], d["yraw1"][dt_i * 128 : (dt_i + 1) * 128, :], s_yl)
        p.dma("sync", UB2[:, :], d["xn"][dt_i * 128 : (dt_i + 1) * 128, HALO:], s_ul)
        p.wait("vector", s_yl)
        p.wait("vector", s_ul)
        er = ER[:, dt_i * 128 : (dt_i + 1) * 128]
        ei = EI[:, dt_i * 128 : (dt_i + 1) * 128]
        for J in range(NBLK):
            p.cop("vector", lambda v, J=J, dt_i=dt_i, er=er: v.tensor_scalar(
                TMW[:, J * 128 : (J + 1) * 128], er,
                hr3[:, J, dt_i : dt_i + 1], None, ALU.mult))
            p.cop("vector", lambda v, J=J, dt_i=dt_i, ei=ei: v.tensor_scalar(
                TC1[:, 0:128], ei, hi3[:, J, dt_i : dt_i + 1], None, ALU.mult))
            p.cop("vector", lambda v, J=J: v.tensor_sub(
                TMW[:, J * 128 : (J + 1) * 128], TMW[:, J * 128 : (J + 1) * 128],
                TC1[:, 0:128]))
        p.cop("vector", lambda v: v.tensor_add(YF[0][:, :], YF[0][:, :], TMW[:, :]))
        vv = p.cop("vector", lambda v, dt_i=dt_i: v.scalar_tensor_tensor(
            SO1[:, :], UB2[:, :], ABC[:, dt_i * 7 + 6 : dt_i * 7 + 7], YF[0][:, :],
            ALU.mult, ALU.add))
        p.wait("scalar", chv, vv)
        p.dma("scalar", d["s1"][dt_i * 128 : (dt_i + 1) * 128, :], SO1[:, :], s_os)
    p.barrier(waits=[(s_os, s_os.n)])
    p.reset(m0)


def l1attn_host(inp):
    cols = []
    for nm in ("A_real", "A_imag", "B_real", "B_imag", "C_real", "C_imag", "D"):
        cols.append(pack_vec(np.asarray(inp[f"L1_attn_{nm}"])))
    out = np.zeros((128, 56), np.float32)
    for i in range(7):
        for dt in range(8):
            out[:, dt * 7 + i] = cols[i][:, dt]
    return dict(abc_col=out, ident_p=np.eye(128, dtype=np.float32))



LN1E8 = -18.420680743952367
E20 = 485165195.4097903


def emit_l3attn(p, d, *, name, wk):
    m0 = p.mark()
    chv = p.chain["vector"]
    chs = p.chain["scalar"]
    TLh = TL + 128

    CVW = p.sb([128, 64], F32, f"{name}_cvw")
    CVB = p.sb([128, 16], F32, f"{name}_cvb")
    DTW = p.sb([128, 16], F32, f"{name}_dtw")
    DTB = p.sb([128, 16], F32, f"{name}_dtb")
    ALOG = p.sb([128, 256], F32, f"{name}_alog")
    DPAR = p.sb([128, 16], F32, f"{name}_dpar")
    ONESR3 = p.sb([1, 128], F32, f"{name}_ones")
    ONESR3B = p.sb([1, 128], BF16, f"{name}_onesb")
    HM3 = p.sb([128, 1], F32, f"{name}_hm")
    s_c0 = p.sem(f"{name}_c0")
    for nmv, buf in (("convw_col", CVW), ("convb_col", CVB), ("dtw_col", DTW),
                     ("dtb_col", DTB), ("alog_col", ALOG), ("dpar_col", DPAR),
                     ("ones_p", ONESR3)):
        p.dma("sync", buf[:, :], wk[nmv], s_c0)
    p.dma("sync", HM3[:, :], d["hm_col"], s_c0)
    p.wait("vector", s_c0)
    p.wait("scalar", s_c0)
    p.wait("tensor", s_c0)
    p.cop("vector", lambda v: v.memset(ONESR3B[:, :], 1.0))
    av = p.cop("scalar", lambda sc: sc.activation(ALOG[:, :], ALOG[:, :], AF.Exp))
    p.wait("vector", chs, av)
    av2 = p.cop("vector", lambda v: v.tensor_scalar(ALOG[:, :], ALOG[:, :], -1.0, None, ALU.mult))

    mseg = p.mark()

    def epi_split(p, c, o, psum, tc_here, s_mm):
        pass

    epi_xz, s_so_xz = epi_store(p, name=f"{name}_xze", dst=d["xz"], dst_c0=0)
    emit_gemm(p, name=f"{name}_xzg", src=d["xn"], src_c0=HALO - 512,
              wsrc=wk["inproj_p"], d_in=D, d_out=4096, cols=TL + 512, epi=epi_xz)
    p.barrier(waits=[(sm, sm.n) for sm in s_so_xz])
    p.reset(mseg)

    XI = p.sb([128, TLh], BF16, f"{name}_xi")
    XC = p.sb([128, TL], BF16, f"{name}_xcb")
    TFD = [p.sb([128, TL], F32, f"{name}_tf{i}") for i in range(2)]
    TF2 = p.sb([128, TL], F32, f"{name}_tf2")
    s_xil = p.sem(f"{name}_xil")
    s_xcs = p.sem(f"{name}_xcs")
    prev_silu = [None]
    for dt16 in range(16):
        if dt16 >= 1:
            p.wait("sync", chv)
            p.wait("sync", s_xcs)
        p.dma("sync", XI[:, :], d["xz"][dt16 * 128 : (dt16 + 1) * 128, 384 : 384 + TLh],
              s_xil)
        p.wait("vector", s_xil)
        p.wait("vector", chs)
        TF = TFD[dt16 % 2]
        p.cop("vector", lambda v, dt16=dt16, TF=TF: v.tensor_scalar(
            TF[:, :], XI[:, 125 : 125 + TL], CVW[:, dt16 * 4 : dt16 * 4 + 1], None, ALU.mult))
        for s_ in (1, 2, 3):
            p.cop("vector", lambda v, dt16=dt16, s_=s_, TF=TF: v.scalar_tensor_tensor(
                TF[:, :], XI[:, 125 + s_ : 125 + s_ + TL],
                CVW[:, dt16 * 4 + s_ : dt16 * 4 + s_ + 1],
                TF[:, :], ALU.mult, ALU.add))
        p.wait("scalar", s_xcs)
        p.wait("scalar", chv)
        vv = p.cop("scalar", lambda sc, dt16=dt16, TF=TF: sc.activation(
            XC[:, :], TF[:, :], AF.Silu, bias=CVB[:, dt16 : dt16 + 1]))
        prev_silu[0] = vv
        p.wait("vector", chs, vv)
        p.wait("scalar", chs, vv)
        p.dma("scalar", d["xc"][dt16 * 128 : (dt16 + 1) * 128, :], XC[:, :], s_xcs)
    p.wait("sync", s_xcs)
    p.wait("gpsimd", s_xcs)

    PS33 = p.ps([33, 512], F32)
    BC3S = p.sb([33, TL], F32, f"{name}_bc3s")
    s_mm3 = p.sem(f"{name}_mm3")
    W33 = p.sb([128, 16 * 33], BF16, f"{name}_w33")
    X33 = [p.sb([128, 16 * TC], BF16, f"{name}_x33{i}") for i in range(2)]
    s_w33 = p.sem(f"{name}_w33")
    s_x33 = p.sems(f"{name}_x33", 2)
    p.dma("gpsimd", W33[:, :], wk["xproj_p"], s_w33)
    p.wait("tensor", s_w33)
    ev33 = {}
    for c in range(TL // TC):
        sl = c % 2
        if c >= 2:
            p.wait("sync", s_mm3, s_mm3.base + c - 1)
        p.dma("sync", X33[sl][:, :].rearrange("q (kt t) -> q kt t", t=TC),
              d["xc"].rearrange("(kt q) t -> q kt t", q=128)[:, :, c * TC : (c + 1) * TC],
              s_x33[sl])
        p.wait("tensor", s_x33[sl])
        if c >= 1:
            p.wait("tensor", chv, ev33[c - 1])

        def mm33(t, sl=sl):
            ins = None
            for k in range(16):
                ins = t.matmul(PS33[:, :], W33[:, k * 33 : (k + 1) * 33],
                               X33[sl][:, k * TC : (k + 1) * TC],
                               start=(k == 0), stop=(k == 15))
            return ins

        p.top(mm33, sem=s_mm3)
        p.wait("vector", s_mm3)
        ev33[c] = p.cop("vector", lambda v, c=c: v.tensor_copy(
            BC3S[:, c * TC : (c + 1) * TC], PS33[:, :]))
    s_bc3 = p.sem(f"{name}_bc3")
    p.wait("scalar", chv, ev33[TL // TC - 1])
    p.dma("scalar", d["bc3"], BC3S[0:32, :], s_bc3)

    PS_OU = p.ps([128, 512], F32)
    DTT = p.sb([128, TL], F32, f"{name}_dtt")
    DTRAW = p.sb([1, TL], F32, f"{name}_dtraw")
    ONE1C = p.sb([128, 1], F32, f"{name}_one1c")
    s_ou = p.sem(f"{name}_ou")
    s_dts = p.sem(f"{name}_dts")
    p.cop("vector", lambda v: v.memset(ONE1C[:, :], 1.0))
    dtr = p.cop("vector", lambda v: v.tensor_copy(DTRAW[0:1, :], BC3S[32:33, :]))
    p.wait("tensor", chv, dtr)
    TF = TFD[0]
    p.wait("vector", chs)
    for dt16 in range(16):
        if dt16 >= 1:
            p.wait("sync", s_dts)
            p.wait("vector", s_dts)
        for c in range(TL // TC):
            if dt16 > 0 or c > 0:
                p.wait("tensor", chv)
            mv = p.top(lambda t, c=c: t.matmul(
                PS_OU[:, :], ONESR3[0:1, :], DTRAW[0:1, c * TC : (c + 1) * TC],
                start=True, stop=True), sem=s_ou)
            p.wait("vector", s_ou, mv)
            p.cop("vector", lambda v, dt16=dt16, c=c: v.tensor_scalar(
                TF[:, c * TC : (c + 1) * TC], PS_OU[:, :],
                DTW[:, dt16 : dt16 + 1], DTB[:, dt16 : dt16 + 1], ALU.mult, ALU.add))
        p.cop("vector", lambda v: v.tensor_scalar(TF2[:, :], TF[:, :], -1.0, None, ALU.mult))
        p.cop("vector", lambda v: v.tensor_tensor(TF2[:, :], TF[:, :], TF2[:, :], ALU.min))
        p.wait("scalar", chv, p.vchain("vector"))
        sv = p.cop("scalar", lambda sc: sc.activation(TF2[:, :], TF2[:, :], AF.Exp))
        sv2 = p.cop("scalar", lambda sc: sc.activation(TF2[:, :], TF2[:, :], AF.Ln,
                                                       bias=ONE1C[:, 0:1]))
        p.wait("vector", chs, sv2)
        p.cop("vector", lambda v: v.tensor_scalar(DTT[:, :], TF[:, :], 0.0, None, ALU.max))
        dv = p.cop("vector", lambda v: v.tensor_add(DTT[:, :], DTT[:, :], TF2[:, :]))
        p.wait("scalar", chv, dv)
        p.dma("scalar", d["dt3"][dt16 * 128 : (dt16 + 1) * 128, :], DTT[:, :], s_dts)
    p.wait("sync", s_dts)
    p.barrier(waits=[(s_bc3, s_bc3.n), (s_dts, s_dts.n), (s_xcs, s_xcs.n)])
    p.reset(mseg)

    B2 = p.sb([128, TL], F32, f"{name}_b2")
    B4 = p.sb([128, TL], F32, f"{name}_b4")
    B1S = p.sb([128, TL], BF16, f"{name}_b1s")
    B3S = p.sb([128, TL], BF16, f"{name}_b3s")
    YA = p.sb([128, TL], F32, f"{name}_ya")
    DTT2 = p.sb([128, TL], F32, f"{name}_dtt2")
    DUB = p.sb([128, TL], BF16, f"{name}_dub")
    XCB2 = p.sb([128, TL], BF16, f"{name}_xcb2")
    ZB3 = p.sb([128, TL], BF16, f"{name}_zb3")
    CMSK = p.sb([128, TL], BF16, f"{name}_cmsk")
    BCROW = p.sb([1, TL], BF16, f"{name}_bcrow")
    H0S = p.sb([128, 128], F32, f"{name}_h0s")
    CPS = p.sb([128, 128], F32, f"{name}_cps")
    HPS = p.sb([128, 128], F32, f"{name}_hps")
    T5 = p.sb([128, 128], F32, f"{name}_t5")
    STO = p.sb([128, 16], F32, f"{name}_sto")
    STI3 = p.sb([128, 16], F32, f"{name}_sti3")
    YMB = p.sb([128, TL], BF16, f"{name}_ymb")
    PS_B = [p.ps([128, 512], F32) for i in range(2)]
    s_ld3 = p.sems(f"{name}_ld3", 4)
    s_bcr = p.sem(f"{name}_bcr")
    s_b13 = p.sem(f"{name}_b13")
    s_b13r = p.sems(f"{name}_b13r", 2)
    s_cmk = p.sem(f"{name}_cmk")
    s_bb = p.sem(f"{name}_bb")
    s_ag3 = p.sem(f"{name}_ag3")
    s_si3 = p.sem(f"{name}_si3")
    s_ci3 = p.sem(f"{name}_ci3")
    s_ym = p.sem(f"{name}_ym")
    p.dma("gpsimd", CMSK[:, :], wk["cmask"], s_cmk)
    p.wait("vector", s_cmk)

    NCH8 = TL // TC
    pb_hist = []
    for dt16 in range(16):
        if dt16 >= 1:
            p.wait("sync", chv)
            p.wait("sync", chs)
            p.wait("sync", s_ym)
        p.dma("sync", DTT2[:, :], d["dt3"][dt16 * 128 : (dt16 + 1) * 128, :], s_ld3[0])
        p.dma("sync", XCB2[:, :], d["xc"][dt16 * 128 : (dt16 + 1) * 128, :], s_ld3[1])
        p.wait("vector", s_ld3[0])
        p.wait("vector", s_ld3[1])
        p.cop("vector", lambda v: v.tensor_tensor(DUB[:, :], DTT2[:, :], XCB2[:, :], ALU.mult))
        prev_e1 = None
        for n in range(16):
            acol = ALOG[:, dt16 * 16 + n : dt16 * 16 + n + 1]
            p.wait("gpsimd", s_bb)
            p.dma("gpsimd", BCROW[0:1, :], d["bc3"][n : n + 1, :], s_bcr)
            p.cop("vector", lambda v, acol=acol: v.tensor_scalar(
                B2[:, :], DTT2[:, :], acol, LN1E8, ALU.mult, ALU.max))
            if prev_e1 is not None:
                p.wait("vector", chs, prev_e1)
            p.cop("vector", lambda v: v.tensor_tensor_scan(
                B4[:, :], CMSK[:, :], B2[:, :], 0.0, ALU.mult, ALU.add))
            p.wait("scalar", s_b13)
            p.wait("scalar", chv)
            e1 = p.cop("scalar", lambda sc: sc.activation(B1S[:, :], B4[:, :], AF.Exp))
            e2 = p.cop("scalar", lambda sc: sc.activation(B2[:, :], B4[:, :], AF.Exp,
                                                          scale=-1.0))
            prev_e1 = e2
            p.wait("vector", chs, e2)
            p.wait("vector", s_b13)
            p.cop("vector", lambda v: v.tensor_scalar(B2[:, :], B2[:, :], E20, None, ALU.min))
            p.wait("tensor", s_bcr)
            for c in range(NCH8):
                pb = PS_B[c % 2]
                if len(pb_hist) >= 2:
                    p.wait("tensor", chv, pb_hist[-2])
                bv = p.top(lambda t, c=c, pb=pb: t.matmul(
                    pb[:, :], ONESR3B[0:1, :], BCROW[0:1, c * TC : (c + 1) * TC],
                    start=True, stop=True), sem=s_bb)
                p.wait("vector", s_bb, bv)
                pb_hist.append(p.cop("vector", lambda v, c=c, pb=pb: v.tensor_tensor(
                    B4[:, c * TC : (c + 1) * TC], DUB[:, c * TC : (c + 1) * TC],
                    pb[:, :], ALU.mult)))
            p.cop("vector", lambda v: v.tensor_tensor(B3S[:, :], B4[:, :], B2[:, :], ALU.mult))
            p.cop("vector", lambda v: v.tensor_tensor_scan(
                B4[:, :], CMSK[:, :], B3S[:, :], 0.0, ALU.mult, ALU.add))
            p.cop("vector", lambda v: v.tensor_tensor(
                T5[:, :], B1S[:, 31::32], B4[:, 31::32], ALU.mult))
            p.cop("vector", lambda v: v.tensor_tensor_scan(
                H0S[:, :], B1S[:, 31::32], T5[:, :], 0.0, ALU.mult, ALU.add))
            sv3 = p.cop("vector", lambda v, n=n: v.tensor_copy(
                STO[:, n : n + 1], H0S[:, 127:128]))
            p.wait("scalar", chv, sv3)
            p.dma("scalar", d["b1d"][dt16 * 16 + n], B1S[:, :], s_b13)
            p.dma("scalar", d["b3d"][dt16 * 16 + n], B3S[:, :], s_b13)
            p.dma("scalar", d["h0d"][dt16 * 16 + n], H0S[:, :], s_b13)
        p.wait("gpsimd", chv)
        p.dma("gpsimd", d["cc3_in"], STO[:, :], s_ci3)
        p.wait("gpsimd", s_ci3)
        p.raw("gpsimd", lambda g: g.collective_compute(
            "AllGather", ALU.bypass, replica_groups=[[0, 1], [2, 3], [4, 5], [6, 7]],
            ins=[d["cc3_in"].opt()], outs=[d["cc3_out"].opt()]).then_inc(s_ag3.h, 1))
        s_ag3.n += 1
        p.wait("gpsimd", s_ag3)
        p.dma("gpsimd", STI3[:, :], d["cc3_out"][0:128, :], s_si3)
        p.wait("vector", s_si3)
        hm_v = p.cop("vector", lambda v: v.tensor_scalar(
            STI3[:, :], STI3[:, :], HM3[:, 0:1], None, ALU.mult))
        p.wait("vector", s_b13)
        yz = p.cop("vector", lambda v: v.memset(YA[:, :], 0.0))
        for n in range(16):
            sl = n % 2
            p.wait("sync", chv)
            p.wait("gpsimd", s_bb)
            p.dma("sync", B1S[:, :], d["b1d"][dt16 * 16 + n], s_b13r[0])
            p.dma("sync", B3S[:, :], d["b3d"][dt16 * 16 + n], s_b13r[0])
            p.dma("sync", H0S[:, :], d["h0d"][dt16 * 16 + n], s_b13r[1])
            p.dma("gpsimd", BCROW[0:1, :], d["bc3"][16 + n : 17 + n, :], s_bcr)
            p.wait("vector", s_b13r[0])
            p.wait("vector", s_b13r[1])
            p.cop("vector", lambda v: v.tensor_tensor_scan(
                CPS[:, :], B1S[:, 31::32], B1S[:, 31::32], 1.0, ALU.mult, ALU.bypass))
            hcol = STI3[:, n : n + 1]
            p.cop("vector", lambda v, hcol=hcol: v.tensor_scalar(
                T5[:, :], CPS[:, :], hcol, None, ALU.mult))
            p.cop("vector", lambda v: v.tensor_add(T5[:, :], T5[:, :], H0S[:, :]))
            p.cop("vector", lambda v: v.tensor_copy(HPS[:, 1:128], T5[:, 0:127]))
            p.cop("vector", lambda v, hcol=hcol: v.tensor_scalar(
                HPS[:, 0:1], hcol, 1.0, None, ALU.mult))
            p.cop("vector", lambda v: v.tensor_tensor(
                B3S[:, 0::32], B3S[:, 0::32], HPS[:, :], ALU.add))
            p.cop("vector", lambda v: v.tensor_tensor_scan(
                B4[:, :], CMSK[:, :], B3S[:, :], 0.0, ALU.mult, ALU.add))
            p.cop("vector", lambda v: v.tensor_tensor(B4[:, :], B4[:, :], B1S[:, :], ALU.mult))
            p.wait("tensor", s_bcr)
            for c in range(NCH8):
                pb = PS_B[c % 2]
                if len(pb_hist) >= 2:
                    p.wait("tensor", chv, pb_hist[-2])
                bv = p.top(lambda t, c=c, pb=pb: t.matmul(
                    pb[:, :], ONESR3B[0:1, :], BCROW[0:1, c * TC : (c + 1) * TC],
                    start=True, stop=True), sem=s_bb)
                p.wait("vector", s_bb, bv)
                pb_hist.append(p.cop("vector", lambda v, c=c, pb=pb: v.tensor_tensor(
                    B2[:, c * TC : (c + 1) * TC], B4[:, c * TC : (c + 1) * TC],
                    pb[:, :], ALU.mult)))
            p.cop("vector", lambda v: v.tensor_add(YA[:, :], YA[:, :], B2[:, :]))
        p.dma("sync", ZB3[:, :], d["xz"][2048 + dt16 * 128 : 2048 + (dt16 + 1) * 128,
                                         512 : 512 + TL], s_ld3[2])
        p.wait("vector", s_ld3[2])
        p.wait("scalar", s_ld3[2])
        p.cop("vector", lambda v, dt16=dt16: v.scalar_tensor_tensor(
            YA[:, :], XCB2[:, :], DPAR[:, dt16 : dt16 + 1], YA[:, :], ALU.mult, ALU.add))
        zs = p.cop("scalar", lambda sc: sc.activation(ZB3[:, :], ZB3[:, :], AF.Silu))
        p.wait("vector", chs, zs)
        ymv = p.cop("vector", lambda v: v.tensor_tensor(YMB[:, :], YA[:, :], ZB3[:, :], ALU.mult))
        p.wait("scalar", chv, ymv)
        p.dma("scalar", d["ym"][dt16 * 128 : (dt16 + 1) * 128, :], YMB[:, :], s_ym)
    p.barrier(waits=[(s_ym, s_ym.n)])
    p.reset(mseg)

    epi_o3, s_so_o3 = epi_store(p, name=f"{name}_oe", dst=d["s3"], dst_c0=0)
    emit_gemm(p, name=f"{name}_og", src=d["ym"], src_c0=0, wsrc=wk["outproj_p"],
              d_in=2048, d_out=D, cols=TL, epi=epi_o3)
    p.barrier(waits=[(sm, sm.n) for sm in s_so_o3])
    p.reset(m0)


def l3attn_host(inp):
    cw = np.asarray(inp["L3_attn_conv_w"])[:, 0, :]
    cwp = np.zeros((128, 64), np.float32)
    for dt in range(16):
        for s_ in range(4):
            cwp[:, dt * 4 + s_] = cw[dt * 128 : (dt + 1) * 128, s_]
    al = np.asarray(inp["L3_attn_A_log"])
    alp = np.zeros((128, 256), np.float32)
    for dt in range(16):
        for n in range(16):
            alp[:, dt * 16 + n] = al[dt * 128 : (dt + 1) * 128, n]
    cm = np.ones(TL, np.float32)
    cm[0::32] = 0.0
    return dict(
        inproj_p=pack_w(np.asarray(inp["L3_attn_in_proj_w"])),
        convw_col=cwp,
        convb_col=pack_vec(np.asarray(inp["L3_attn_conv_b"])),
        xproj_p=pack_w(np.asarray(inp["L3_attn_x_proj_w"])),
        dtw_col=pack_vec(np.asarray(inp["L3_attn_dt_w"])[0]),
        dtb_col=pack_vec(np.asarray(inp["L3_attn_dt_b"])),
        alog_col=alp,
        dpar_col=pack_vec(np.asarray(inp["L3_attn_D_param"])),
        outproj_p=pack_w(np.asarray(inp["L3_attn_out_proj_w"])),
        cmask=np.tile(cm, (128, 1)),
        ones_p=np.ones((1, 128), np.float32),
    )




def emit_halo_exchange(p, d, *, name, xkey):
    m0 = p.mark()
    chv = p.chain["vector"]
    s_hi = p.sems(f"{name}_hi", 2)
    s_ag = p.sem(f"{name}_ag")
    s_ho = p.sems(f"{name}_ho", 2)
    HT = [p.sb([128, 8 * TC], F32, f"{name}_ht{i}") for i in range(2)]
    HMX = p.sb([128, 1], F32, f"{name}_hm")
    s_hm = p.sem(f"{name}_hm")
    p.dma("sync", HMX[:, :], d["hm_col"], s_hm)
    s_hg = p.sem(f"{name}_hg")
    p.dma("gpsimd", d["cch_in"], d[xkey][:, HALO + TL - 2048 : HALO + TL], s_hg)
    p.wait("gpsimd", s_hg)
    p.raw("gpsimd", lambda g: g.collective_compute(
        "AllGather", ALU.bypass, replica_groups=[[0, 1], [2, 3], [4, 5], [6, 7]],
        ins=[d["cch_in"].opt()], outs=[d["cch_out"].opt()]).then_inc(s_ag.h, 1))
    s_ag.n += 1
    p.wait("sync", s_ag)
    p.wait("vector", s_hm)
    done = []
    for c in range(4):
        sl = c % 2
        if c >= 2:
            p.wait("sync", chv, done[c - 2])
            p.wait("sync", s_ho[sl])
        p.dma("sync",
              HT[sl][:, :].rearrange("q (kt t) -> q kt t", t=TC),
              d["cch_out"][0:1024].rearrange("(kt q) t -> q kt t", q=128)[
                  :, :, c * TC : (c + 1) * TC],
              s_hi[sl])
        p.wait("vector", s_hi[sl])
        if c >= 2:
            p.wait("vector", s_ho[sl])
        vv = p.cop("vector", lambda v, sl=sl: v.tensor_scalar(
            HT[sl][:, :], HT[sl][:, :], HMX[:, 0:1], None, ALU.mult))
        done.append(vv)
        p.wait("scalar", chv, vv)
        p.dma("scalar",
              d[xkey].rearrange("(kt q) t -> q kt t", q=128)[:, :, c * TC : (c + 1) * TC],
              HT[sl][:, :].rearrange("q (kt t) -> q kt t", t=TC),
              s_ho[sl])
    p.barrier(waits=[(s_ho[0], s_ho[0].n), (s_ho[1], s_ho[1].n)])
    p.reset(m0)




def build_megakernel():
    nc = bass.Bass("TRN2", target_bir_lowering=False, debug=False)
    d = {}

    def din(name, shape, dtype=F32):
        d[name] = nc.dram_tensor(name, list(shape), dtype, kind="ExternalInput").ap()

    def dint(name, shape, dtype):
        d[name] = nc.dram_tensor(name, list(shape), dtype, kind="Internal").ap()

    din("xin0", [D, NCOL])
    d["y"] = nc.dram_tensor("y", [D, TL], F32, kind="ExternalOutput").ap()
    din("hm_col", [128, 1])
    for i in (1, 2, 3):
        dint(f"xh{i}", [D, NCOL], F32)
    dint("xn", [D, NCOL], BF16)
    dint("sbr", [D, TL], F32)
    dint("sbr16", [D, TL], BF16)
    dint("gbr", [D, TL], BF16)
    dint("cch_in", [1024, 2048], F32)
    dint("cch_out", [2048, 2048], F32)

    for nm in ("zlo0", "zhi0", "zlo1", "zhi1"):
        din(nm, [1024, 256])
    din("w_in_p", [128, 8 * 3072]); din("b_in_p", [128, 24]); din("gb_p", [128, 16])
    din("w_out_p", [128, 8 * 1024]); din("b_out_p", [128, 8]); din("ident_p", [128, 128])
    dint("zb", [4, 1024, 256], BF16)
    dint("tmats", [4 * 128 * 1024 * 128], BF16)
    dint("xln", [D, 4608], BF16)
    dint("proj", [3072, 4608], BF16)
    dint("ycv", [D, TL], BF16)

    for li in range(4):
        for nm in ("wqkv_p", "wo_p", "qk_col", "qk_row", "lng_col", "w1_p", "b1_col",
                   "w2_p", "b2_col"):
            shp = {"wqkv_p": [128, 8 * 1536], "wo_p": [128, 4 * 1024],
                   "qk_col": [128, 2], "qk_row": [1, 128], "lng_col": [128, 16],
                   "w1_p": [128, 8 * FFN_W1O], "b1_col": [128, 44],
                   "w2_p": [128, 22 * 1024], "b2_col": [128, 8]}[nm]
            din(f"f{li}_{nm}", shp)
    din("tab_i1", [512, TL + 2048]); din("tab_i2", [512, TL + 1024])
    din("rotq_lhsT", [128, 128]); din("masks", [128, 256]); din("maskh", [128, 128])
    din("ind2", [2, 128]); din("ind2t", [128, 2]); din("ones_p", [1, 128])
    dint("qkv", [1536, NCOL], BF16)
    dint("onrm", [512, NCOL], BF16)
    dint("xmid", [D, NCOL], BF16)
    dint("xf", [D, NCOL], BF16)
    dint("fmid", [D, NCOL], BF16)
    dint("graw", [D, NCOL], BF16)

    din("abc_col", [128, 56])
    dint("z1", [1024, 256], BF16)
    dint("t1m", [128 * 1024 * 128], BF16)
    dint("yraw1", [D, TL], F32)
    dint("cc1_in", [128, 16], F32)
    dint("cc1_out", [256, 16], F32)

    din("l2_wr_p", [128, 8 * 1024]); din("l2_wk_p", [128, 8 * 1024])
    din("l2_wv_p", [128, 8 * 1024]); din("l2_wo_p", [128, 8 * 1024])
    din("l2_decay_col", [128, 8]); din("l2_lnw_col", [128, 8])
    din("l2_iota", [128, TL])
    din("wg_i", [128, 8 * 3072], I32); din("wu_i", [128, 8 * 3072], I32)
    din("wd_i", [128, 24 * 1024], I32)
    for nm, shp in (("r2", [D, TL]), ("k2b", [D, TL]), ("v2", [D, TL]),
                    ("yn2", [D, TL])):
        dint(nm, shp, BF16)
    dint("cum2", [D, TL], F32)
    dint("y2s", [D, TL], F32)
    dint("cc2_in", [128, 8], F32)
    dint("cc2_out", [256, 8], F32)
    dint("wg_b", [128, 8 * 3072], BF16); dint("wu_b", [128, 8 * 3072], BF16)
    dint("wd_b", [128, 24 * 1024], BF16)
    dint("gsig", [3072, TL], BF16)
    dint("up_h", [3072, TL], BF16)

    din("inproj_p", [128, 8 * 4096]); din("convw_col", [128, 64])
    din("convb_col", [128, 16]); din("xproj_p", [128, 16 * 33])
    din("dtw_col", [128, 16]); din("dtb_col", [128, 16]); din("alog_col", [128, 256])
    din("dpar_col", [128, 16]); din("outproj_p", [128, 16 * 1024])
    din("cmask", [128, TL])
    dint("xz", [4096, TL + 512], BF16)
    dint("xc", [2048, TL], BF16)
    dint("dt3", [2048, TL], F32)
    dint("bc3", [32, TL], F32)
    dint("b1d", [256, 128, TL], BF16)
    dint("b3d", [256, 128, TL], BF16)
    dint("h0d", [256, 128, 128], F32)
    dint("ym", [2048, TL], BF16)
    dint("cc3_in", [128, 16], F32)
    dint("cc3_out", [256, 16], F32)
    din("l3f_wr_p", [128, 8 * 1024]); din("l3f_wk_p", [128, 8 * 1024])
    din("l3f_wv_p", [128, 8 * 1024])
    din("l3f_mur_col", [128, 8]); din("l3f_muk_col", [128, 8])
    dint("xr", [D, TL], BF16); dint("xk", [D, TL], BF16)
    dint("rform", [D, TL], BF16); dint("k2", [D, TL], BF16)

    with ExitStack() as st:
        p = Prog(nc, st)

        def wkv(prefix, names):
            return {nm: d[f"{prefix}{nm}"] for nm in names}

        for li in range(4):
            xsrc = d["xin0"] if li == 0 else d[f"xh{li}"]
            m = p.mark()
            lnd = emit_lngb(p, name=f"ln{li}", xsrc=xsrc, xdst=d["xn"], c0=0,
                            cols=NCOL, src_f32=True)
            p.barrier(waits=lnd)
            p.reset(m)

            if li == 0:
                d["s0"] = d["sbr"]
                emit_l0attn(p, d, name="l0a")
                s_key, s_c0 = "sbr", 0
            elif li == 1:
                d["s1"] = d["sbr16"]
                emit_l1attn(p, d, name="l1a", wk=wkv("", ["abc_col", "ident_p"]))
                s_key, s_c0 = "sbr16", 0
            elif li == 2:
                d["s2"] = d["sbr16"]
                emit_l2attn(p, d, name="l2a", wk={
                    "wr_p": d["l2_wr_p"], "wk_p": d["l2_wk_p"], "wv_p": d["l2_wv_p"],
                    "wo_p": d["l2_wo_p"], "decay_col": d["l2_decay_col"],
                    "lnw_col": d["l2_lnw_col"], "iota": d["l2_iota"]})
                s_key, s_c0 = "sbr16", 0
            else:
                d["s3"] = d["sbr16"]
                emit_l3attn(p, d, name="l3a", wk=wkv("", [
                    "inproj_p", "convw_col", "convb_col", "xproj_p", "dtw_col",
                    "dtb_col", "alog_col", "dpar_col", "outproj_p", "cmask", "ones_p"]))
                s_key, s_c0 = "sbr16", 0

            if li in (0, 1):
                base = 2 * li
                for inner in range(2):
                    wk = {nm: d[f"f{base+inner}_{nm}"] for nm in (
                        "wqkv_p", "wo_p", "qk_col", "qk_row", "lng_col", "w1_p",
                        "b1_col", "w2_p", "b2_col")}
                    wk["tab"] = d["tab_i1"] if inner == 0 else d["tab_i2"]
                    for nm in ("rotq_lhsT", "masks", "maskh", "ind2", "ind2t",
                               "ones_p", "ident_p"):
                        wk[nm] = d[nm]
                    src = d["xn"] if inner == 0 else d["fmid"]
                    dst = d["fmid"] if inner == 0 else d["graw"]
                    q0 = -1536 if inner == 0 else -512
                    emit_ffn_inner(p, d, name=f"ff{li}{inner}", src=src, dst=dst,
                                   q0=q0, wk=wk)
                g_key, g_c0, g_dt = "graw", HALO, BF16
            elif li == 2:
                d["g2"] = d["gbr"]
                emit_l2ffn(p, d, name="l2f", wk={})
                g_key, g_c0, g_dt = "gbr", 0, BF16
            else:
                d["g3"] = d["gbr"]
                emit_l3ffn(p, d, name="l3f", wk={
                    "wr_p": d["l3f_wr_p"], "wk_p": d["l3f_wk_p"],
                    "wv_p": d["l3f_wv_p"], "mur_col": d["l3f_mur_col"],
                    "muk_col": d["l3f_muk_col"]})
                g_key, g_c0, g_dt = "gbr", 0, BF16

            xout_key = f"xh{li+1}" if li < 3 else "y"
            emit_combine(p, d, name=f"cmb{li}", s_key=s_key, s_c0=s_c0,
                         s_dt=F32 if li == 0 else BF16,
                         g_key=g_key, g_c0=g_c0, xin_key=("xin0" if li == 0 else f"xh{li}"),
                         xout_key=xout_key, out_is_flat=(li == 3))
            if li < 3:
                emit_halo_exchange(p, d, name=f"hx{li}", xkey=f"xh{li+1}")
        p.build()
    return nc, d


def host_prep(inputs):
    sh = {}
    sh.update(l0attn_host(inputs, None))
    c = ffn_host_consts()
    sh.update({k: v for k, v in c.items() if k != "tab"})
    tabfull = None
    TW = T + 2048
    tg = np.arange(-2048, T, dtype=np.float64)
    half = np.arange(0, 64, 2, dtype=np.float64)
    inv_freq = 1.0 / 10000.0 ** (half / 64.0)
    freqs = np.concatenate([tg[None, :] * inv_freq[:, None]] * 2, 0)
    base = (half + 25.6) / 89.6
    power = (tg - 4096.0) / 256.0
    scale = np.concatenate([base[:, None] ** power[None, :]] * 2, 0)
    tabfull = np.concatenate(
        [np.tile(x, (2, 1)) for x in (
            (np.cos(freqs) * scale), (np.sin(freqs) * scale),
            (np.cos(freqs) / scale), (np.sin(freqs) / scale))], 0
    ).astype(np.float32)
    for li in range(2):
        for inner in range(2):
            wqkv = np.asarray(inputs[f"L{li}_ffn_wqkv"])[inner]
            pk = ffn_host_layer(
                wqkv,
                np.asarray(inputs[f"L{li}_ffn_q_scale"])[inner],
                np.asarray(inputs[f"L{li}_ffn_k_scale"])[inner],
                np.asarray(inputs[f"L{li}_ffn_wo"])[inner],
                np.asarray(inputs[f"L{li}_ffn_ln_g"])[inner],
                np.asarray(inputs[f"L{li}_ffn_ln_b"])[inner],
                np.asarray(inputs[f"L{li}_ffn_w1"])[inner],
                np.asarray(inputs[f"L{li}_ffn_b1"])[inner],
                np.asarray(inputs[f"L{li}_ffn_w2"])[inner],
                np.asarray(inputs[f"L{li}_ffn_b2"])[inner],
            )
            for k_, v_ in pk.items():
                sh[f"f{2*li+inner}_{k_}"] = v_
    sh["abc_col"] = l1attn_host(inputs)["abc_col"]
    l2 = l2attn_host(inputs)
    sh.update({"l2_wr_p": l2["wr_p"], "l2_wk_p": l2["wk_p"], "l2_wv_p": l2["wv_p"],
               "l2_wo_p": l2["wo_p"], "l2_decay_col": l2["decay_col"],
               "l2_lnw_col": l2["lnw_col"]})
    sh.update(l2ffn_host(inputs))
    sh.update(l3attn_host(inputs))
    l3f = l3ffn_host(inputs)
    sh.update({"l3f_wr_p": l3f["wr_p"], "l3f_wk_p": l3f["wk_p"],
               "l3f_wv_p": l3f["wv_p"], "l3f_mur_col": l3f["mur_col"],
               "l3f_muk_col": l3f["muk_col"]})

    x0 = np.asarray(inputs["L0_attn_x"], np.float32)
    iota_full = np.tile(np.arange(T, dtype=np.float32), (128, 1))
    percore = []
    for core in range(8):
        b, h = core // 2, core % 2
        g_t0 = h * TL
        m = dict(sh)
        xin = np.zeros((D, NCOL), np.float32)
        xin[:, HALO:] = x0[b, g_t0 : g_t0 + TL].T
        if h == 1:
            xin[:, :HALO] = x0[b, g_t0 - HALO : g_t0].T
        m["xin0"] = xin
        m["hm_col"] = np.full((128, 1), float(h), np.float32)
        m["maskh"] = np.full((128, 128), -1e30 if h == 0 else 0.0, np.float32)
        m["tab_i1"] = np.ascontiguousarray(
            tabfull[:, g_t0 + (-2048) + 2048 : g_t0 + TL + 2048])
        m["tab_i2"] = np.ascontiguousarray(
            tabfull[:, g_t0 + (-1024) + 2048 : g_t0 + TL + 2048])
        m["l2_iota"] = np.ascontiguousarray(iota_full[:, g_t0 : g_t0 + TL])
        percore.append(m)
    return percore


def kernel(**inputs):
    nc, _ = build_megakernel()
    percore = host_prep(inputs)
    res = run_bass_kernel_spmd(nc, percore, core_ids=list(range(8)))
    out = np.zeros((B, T, D), np.float32)
    for core in range(8):
        b, h = core // 2, core % 2
        out[b, h * TL : (h + 1) * TL] = res.results[core]["y"].T
    return out


# version 06
# speedup vs baseline: 1.2418x; 1.0330x over previous
import numpy as np
import ml_dtypes
from contextlib import ExitStack
import concourse.bass as bass
import concourse.mybir as mybir
from concourse.bass_utils import run_bass_kernel_spmd

F32 = mybir.dt.float32
BF16 = mybir.dt.bfloat16
I32 = mybir.dt.int32
AF = mybir.ActivationFunctionType
ALU = mybir.AluOpType
AX = mybir.AxisListType
BF16NP = ml_dtypes.bfloat16

ENGINES = ("sync", "gpsimd", "tensor", "vector", "scalar")

B, T, D = 4, 8192, 1024
TL = 4096
HALO = 2048
NCOL = HALO + TL
TC = 512
NKT = D // 128



class Sem:
    __slots__ = ("h", "n", "name", "base", "home")

    def __init__(self, h, name):
        self.h, self.n, self.name = h, 0, name
        self.base = 0
        self.home = None


SELF_CHAIN = ("vector", "scalar", "gpsimd")


class Prog:

    ARENA_BYTES = 200 * 1024
    PSUM_BANKS = 8

    def __init__(self, nc, st):
        self.nc = nc
        self.st = st
        self.ops = {e: [] for e in ENGINES}
        self._semi = 0
        self._sbi = 0
        self._sem_pool = []
        self._sem_ptr = 0
        self._swdge_ok = set()
        self._dma_active = {}
        self.chain = {}
        for e in SELF_CHAIN:
            self.chain[e] = self.sem(f"ch_{e}")
        self._sem_pool = []
        self._sem_ptr = 0
        self.arena = st.enter_context(
            nc.sbuf_tensor("arena", [128, self.ARENA_BYTES], mybir.dt.uint8)
        )
        self.arena_off = 0
        self.psum_arena = st.enter_context(nc.psum_tensor("psarena", [128, 4096], F32))
        self.psum_off = 0

    def mark(self):
        return (self.arena_off, self.psum_off, self._sem_ptr)

    def reset(self, m):
        self.arena_off, self.psum_off = m[0], m[1]
        if len(m) > 2:
            self._sem_ptr = m[2]

    def sem(self, name=None):
        if self._sem_ptr < len(self._sem_pool):
            sm = self._sem_pool[self._sem_ptr]
        else:
            self._semi += 1
            nm = f"s{self._semi}"
            h = self.st.enter_context(self.nc.semaphore(nm))
            sm = Sem(h, nm)
            self._sem_pool.append(sm)
        self._sem_ptr += 1
        if id(sm) in self._swdge_ok:
            sm.h, sm.n = sm.home
            sm.home = None
            self._swdge_ok.discard(id(sm))
        sm.base = sm.n
        return sm

    def sems(self, name, k):
        return [self.sem(f"{name}{i}") for i in range(k)]

    def sb(self, shape, dtype, name=None):
        nbytes = int(shape[1]) * {F32: 4, BF16: 2, I32: 4}[dtype]
        nbytes = (nbytes + 31) & ~31
        assert self.arena_off + nbytes <= self.ARENA_BYTES, (
            "SBUF arena overflow", name, self.arena_off, nbytes)
        ap = self.arena[0 : shape[0], self.arena_off : self.arena_off + nbytes].bitcast(dtype)
        self.arena_off += nbytes
        return ap[0 : shape[0], 0 : int(shape[1])]

    def ps(self, shape, dtype=F32, name=None):
        eb = {F32: 4, BF16: 2}[dtype]
        nbanks = (int(shape[1]) * eb + 2047) // 2048
        assert self.psum_off + nbanks <= self.PSUM_BANKS, ("PSUM overflow", name)
        b0 = self.psum_off
        self.psum_off += nbanks
        ap = self.psum_arena[:, b0 * 512 : (b0 + nbanks) * 512]
        if dtype != F32:
            ap = ap.bitcast(dtype)
        return ap[0 : shape[0], 0 : int(shape[1])]

    def raw(self, eng, fn):
        self.ops[eng].append(fn)

    def cop(self, eng, fn):
        ch = self.chain[eng]
        self.ops[eng].append(lambda engine, h=ch.h, v=ch.n: engine.wait_ge(h, v))
        ch.n += 1
        self.ops[eng].append(lambda engine, fn=fn, h=ch.h: fn(engine).then_inc(h, 1))
        return ch.n

    def top(self, fn, sem=None, amt=1):
        if sem is None:
            self.ops["tensor"].append(fn)
            return None
        sem.n += amt
        self.ops["tensor"].append(
            lambda engine, fn=fn, h=sem.h, a=amt: fn(engine).then_inc(h, a)
        )
        return sem.n

    def dma(self, q, out, in_, sem, **kw):
        if q == "gpsimd" and id(sem) not in self._swdge_ok:
            self._semi += 1
            nm = f"sw{self._semi}"
            sem.home = (sem.h, sem.n)
            sem.h = self.st.enter_context(self.nc.semaphore(nm))
            sem.n = 0
            sem.base = 0
            self._swdge_ok.add(id(sem))
        sem.n += 16
        h = sem.h
        self._dma_active[id(sem)] = sem

        def run(engine, out=out, in_=in_, h=h, kw=kw):
            engine.dma_start(out=out, in_=in_, **kw).then_inc(h, 16)

        self.ops[q].append(run)
        return sem.n

    def wait(self, eng, sem, val=None):
        v = sem.n if val is None else val
        self.ops[eng].append(lambda engine, h=sem.h, v=v: engine.wait_ge(h, v))
        return v

    def vchain(self, eng):
        return self.chain[eng].n

    def barrier(self, waits=()):
        bs = self.sem()
        for e in ENGINES:
            for sm, v in waits:
                self.wait(e, sm, v)
        for e in SELF_CHAIN:
            self.wait(e, self.chain[e])
        for sm in self._dma_active.values():
            self.wait("sync", sm)
        self._dma_active = {}
        for e in ENGINES:
            bs.n += 1
            self.ops[e].append(lambda eng, h=bs.h: eng.nop().then_inc(h, 1))
        for e in ENGINES:
            self.wait(e, bs, bs.n)
        return bs

    def build(self):
        block = self.st.enter_context(self.nc.Block())
        for e in ENGINES:
            fns = self.ops[e]
            if not fns:
                continue

            def body(engine, fns=fns):
                for f in fns:
                    f(engine)

            getattr(block, e)(body)




def pack_w(w):
    d_in, d_out = w.shape
    nkt = d_in // 128
    return np.ascontiguousarray(
        w.reshape(nkt, 128, d_out).transpose(1, 0, 2).reshape(128, nkt * d_out)
    ).astype(np.float32)


def pack_wT(w):
    return pack_w(np.ascontiguousarray(w.T))


def pack_vec(v, pad_to=None):
    d = v.shape[0]
    if pad_to is not None and d < pad_to:
        v = np.concatenate([v, np.zeros(pad_to - d, v.dtype)])
        d = pad_to
    return np.ascontiguousarray(v.reshape(d // 128, 128).T).astype(np.float32)


def pad_cols(w, to):
    d_in, d_out = w.shape
    if d_out == to:
        return w
    return np.concatenate([w, np.zeros((d_in, to - d_out), w.dtype)], axis=1)


def pad_rows(w, to):
    d_in, d_out = w.shape
    if d_in == to:
        return w
    return np.concatenate([w, np.zeros((to - d_in, d_out), w.dtype)], axis=0)




def emit_gemm(
    p,
    *,
    name,
    src,
    src_c0,
    wsrc,
    d_in,
    d_out,
    cols,
    epi,
    pre=None,
    src_ready=None,
    tc=TC,
    n_ps=4,
    w_bf16=False,
    shift_half=False,
):
    nkt = d_in // 128
    not_ = (d_out + 127) // 128
    nch = cols // tc
    assert cols % tc == 0, (name, cols, tc)

    W = p.sb([128, nkt * d_out], BF16, f"{name}_W")
    X = [p.sb([128, nkt * tc], BF16, f"{name}_X{i}") for i in range(2)]
    PS = [p.ps([128, tc], F32, f"{name}_PS{i}") for i in range(n_ps)]
    s_wp = p.sem(f"{name}_wp")
    s_xp = p.sems(f"{name}_xp", 2)
    s_mm = p.sem(f"{name}_mm")

    p.dma("sync" if w_bf16 else "gpsimd", W[:, :], wsrc, s_wp)
    p.wait("tensor", s_wp)

    psum_freed = []
    for c in range(nch):
        tc_here = tc
        xs = c % 2
        if c >= 2:
            p.wait("sync", s_mm, s_mm.base + (c - 1) * not_)
        if src_ready is not None:
            sm, v = src_ready(c)
            p.wait("sync", sm, v)
        if not shift_half:
            p.dma(
                "sync",
                X[xs][:, :].rearrange("q (kt t) -> q kt t", t=tc),
                src.rearrange("(kt q) t -> q kt t", q=128)[
                    :, :, src_c0 + c * tc : src_c0 + (c + 1) * tc
                ],
                s_xp[xs],
            )
        else:
            hk = nkt // 2
            p.dma(
                "sync",
                X[xs][:, : hk * tc].rearrange("q (kt t) -> q kt t", t=tc),
                src.rearrange("(kt q) t -> q kt t", q=128)[
                    :, :hk, src_c0 + c * tc : src_c0 + (c + 1) * tc
                ],
                s_xp[xs],
            )
            sc0 = max(src_c0 + c * tc - 1, 0)
            p.dma(
                "sync",
                X[xs][:, hk * tc :].rearrange("q (kt t) -> q kt t", t=tc),
                src.rearrange("(kt q) t -> q kt t", q=128)[:, hk:, sc0 : sc0 + tc],
                s_xp[xs],
            )
        p.wait("tensor", s_xp[xs])
        if pre is not None:
            pre(p, c, tc_here)
        for o in range(not_):
            grp = c * not_ + o
            b = grp % n_ps
            ow = min(128, d_out - o * 128)
            if grp >= n_ps:
                sm, v = psum_freed[grp - n_ps]
                p.wait("tensor", sm, v)

            def mms(t, xs=xs, o=o, b=b, ow=ow, tc_here=tc_here):
                for k in range(nkt):
                    ins = t.matmul(
                        PS[b][:ow, :tc_here],
                        W[:, k * d_out + o * 128 : k * d_out + o * 128 + ow],
                        X[xs][:, k * tc : k * tc + tc_here],
                        start=(k == 0),
                        stop=(k == nkt - 1),
                    )
                return ins

            p.top(mms, sem=s_mm)
            psum_freed.append(epi(p, c, o, PS[b][:ow, :tc_here], tc_here, s_mm))
    return s_mm


def epi_store(
    p, *, name, dst, dst_c0, bias_sb=None, act=None, scale=None, tc=TC,
    eng="vector", nslots=4, extra=None,
):
    OB = [p.sb([128, tc], dst.dtype, f"{name}_OB{i}") for i in range(nslots)]
    s_oc = p.sems(f"{name}_oc", nslots)
    ctr = [0]
    engine_for_act = "scalar"

    def epi(p, c, o, psum, tc_here, s_mm):
        i = ctr[0]
        ctr[0] += 1
        sl = i % nslots
        e = engine_for_act if act is not None else eng
        p.wait(e, s_mm)
        if i >= nslots:
            p.wait(e, s_oc[sl])

        def ev(v, psum=psum, sl=sl, tc_here=tc_here, o=o):
            ob = OB[sl][: psum.shape[0], :tc_here]
            if act is not None:
                return v.activation(
                    ob, psum, act,
                    bias=bias_sb[:, o : o + 1] if bias_sb is not None else 0.0,
                    scale=1.0 if scale is None else float(scale),
                )
            if bias_sb is not None:
                return v.tensor_scalar(ob, psum, bias_sb[:, o : o + 1], None, ALU.add)
            if scale is not None:
                return v.tensor_scalar(ob, psum, float(scale), None, ALU.mult)
            return v.tensor_copy(ob, psum)

        val = p.cop(e, ev)
        if extra is not None:
            extra(p, e, OB[sl], c, o, tc_here)
            val = p.vchain(e)
        p.wait("scalar", p.chain[e], val)
        p.dma(
            "scalar",
            dst[o * 128 : o * 128 + psum.shape[0], dst_c0 + c * tc : dst_c0 + c * tc + tc_here],
            OB[sl][: psum.shape[0], :tc_here],
            s_oc[sl],
        )
        return (p.chain[e], val)

    return epi, s_oc




def emit_lngb(p, *, name, xsrc, xdst, c0, cols, gamma_sb=None, beta_sb=None,
              src_f32=True, eps=1e-5, tc=TC, dst_c0=None, shift_half=False):
    if dst_c0 is None:
        dst_c0 = c0

    def _load(p, buf, c, sem):
        if not shift_half:
            p.dma(
                "sync",
                buf[:, :].rearrange("q (kt t) -> q kt t", t=tc),
                xsrc.rearrange("(kt q) t -> q kt t", q=128)[
                    :, :, c0 + c * tc : c0 + (c + 1) * tc
                ],
                sem,
            )
        else:
            hk = NKT // 2
            p.dma(
                "sync",
                buf[:, : hk * tc].rearrange("q (kt t) -> q kt t", t=tc),
                xsrc.rearrange("(kt q) t -> q kt t", q=128)[
                    :, :hk, c0 + c * tc : c0 + (c + 1) * tc
                ],
                sem,
            )
            p.dma(
                "sync",
                buf[:, hk * tc :].rearrange("q (kt t) -> q kt t", t=tc),
                xsrc.rearrange("(kt q) t -> q kt t", q=128)[
                    :, hk:, c0 + c * tc - 1 : c0 + (c + 1) * tc - 1
                ],
                sem,
            )
    """LN over partitions with optional per-partition affine (gamma/beta columns
    (128, NKT)). xsrc dtype fp32 or bf16; xdst bf16."""
    nch = cols // tc
    sdt = F32 if src_f32 else BF16
    X = [p.sb([128, NKT * tc], sdt, f"{name}_X{i}") for i in range(2)]
    XB = [p.sb([128, NKT * tc], BF16, f"{name}_XB{i}") for i in range(2)] if src_f32 else X
    SQ = [p.sb([128, tc], BF16, f"{name}_SQ{i}") for i in range(2)]
    ONES = p.sb([128, 1], BF16, f"{name}_ones")
    ONES1 = p.sb([1, 128], F32, f"{name}_ones1")
    STATS = p.sb([1, 2 * nch * tc], F32, f"{name}_stats")
    AB = p.sb([1, 2 * nch * tc], F32, f"{name}_ab")
    VAR = p.sb([1, nch * tc], F32, f"{name}_var")
    EPST = p.sb([1, 1], F32, f"{name}_eps")
    PSSTa = p.ps([1, tc], F32, f"{name}_pssta")
    PSSTb = p.ps([1, tc], F32, f"{name}_psstb")
    PSBC = [p.ps([128, tc], F32, f"{name}_psbc{i}") for i in range(2)]
    s_xp = p.sems(f"{name}_xp", 2)
    s_x2p = p.sems(f"{name}_x2p", 2)
    s_mmk = p.sem(f"{name}_mmk")
    s_bc = p.sem(f"{name}_bc")
    s_oc = p.sems(f"{name}_oc", 2)
    chv = p.chain["vector"]
    chs = p.chain["scalar"]

    p.cop("vector", lambda v: v.memset(ONES[:, :], 1.0))
    p.cop("vector", lambda v: v.memset(ONES1[:, :], 1.0))
    init_done = p.cop("vector", lambda v: v.memset(EPST[:, :], float(eps)))
    p.wait("tensor", chv, init_done)

    cast_vals = {}
    for c in range(nch):
        xs = c % 2
        if c >= 2:
            if src_f32:
                p.wait("sync", chv, cast_vals[c - 2])
            else:
                p.wait("sync", s_mmk, s_mmk.base + (c - 1) * NKT)
        _load(p, X[xs], c, s_xp[xs])
        if src_f32:
            p.wait("vector", s_xp[xs])
            if c >= 2:
                p.wait("vector", s_mmk, s_mmk.base + (c - 1) * NKT)
            cast_vals[c] = p.cop(
                "vector", lambda v, xs=xs: v.tensor_copy(XB[xs][:, :], X[xs][:, :])
            )
            p.wait("scalar", chv, cast_vals[c])
            p.wait("tensor", chv, cast_vals[c])
        else:
            p.wait("scalar", s_xp[xs])
            p.wait("tensor", s_xp[xs])
        mmv = {}
        for k in range(NKT):
            sqs = k % 2
            if k >= 2:
                p.wait("scalar", s_mmk, mmv[k - 2])
            elif c > 0:
                p.wait("scalar", s_mmk, s_mmk.base + c * NKT - 2 + k + 1)

            def sqf(sc, xs=xs, k=k, sqs=sqs):
                return sc.activation(SQ[sqs][:, :], XB[xs][:, k * tc : (k + 1) * tc], AF.Square)

            sv = p.cop("scalar", sqf)
            p.wait("tensor", chs, sv)

            def mm_sq(t, xs=xs, k=k, sqs=sqs):
                t.matmul(PSSTa[0:1, 0:tc], ONES[:, :], XB[xs][:, k * tc : (k + 1) * tc],
                         start=(k == 0), stop=(k == NKT - 1))
                return t.matmul(PSSTb[0:1, 0:tc], ONES[:, :], SQ[sqs][:, :],
                                start=(k == 0), stop=(k == NKT - 1))

            mmv[k] = p.top(mm_sq, sem=s_mmk)
        p.wait("vector", s_mmk, s_mmk.base + (c + 1) * NKT)
        p.cop(
            "vector",
            lambda v, c=c: v.tensor_copy(STATS[0:1, c * 2 * tc : c * 2 * tc + tc], PSSTa[0:1, :]),
        )
        ev = p.cop(
            "vector",
            lambda v, c=c: v.tensor_copy(
                STATS[0:1, c * 2 * tc + tc : (c + 1) * 2 * tc], PSSTb[0:1, :]
            ),
        )
        p.wait("tensor", chv, ev)

    stats3 = STATS[0:1, :].rearrange("o (c two t) -> o c two t", two=2, t=tc)
    sums = stats3[:, :, 0, :]
    sqs_ = stats3[:, :, 1, :]
    ab3 = AB[0:1, :].rearrange("o (c two t) -> o c two t", two=2, t=tc)
    a_v = ab3[:, :, 0, :]
    b_v = ab3[:, :, 1, :]
    var2 = VAR[0:1, :].rearrange("o (c t) -> o c t", t=tc)
    p.cop("vector", lambda v: v.tensor_scalar(a_v, sums, 1.0 / D, None, ALU.mult))
    p.cop("vector", lambda v: v.tensor_scalar(var2, sqs_, 1.0 / D, None, ALU.mult))
    p.cop("vector", lambda v: v.tensor_mul(b_v, a_v, a_v))
    vv = p.cop("vector", lambda v: v.tensor_sub(var2, var2, b_v))
    p.wait("scalar", chv, vv)
    p.cop("scalar", lambda s: s.activation(VAR[0:1, :], VAR[0:1, :], AF.Ln, bias=EPST[0:1, 0:1]))
    sv = p.cop("scalar", lambda s: s.activation(VAR[0:1, :], VAR[0:1, :], AF.Exp, scale=-0.5))
    p.wait("vector", chs, sv)
    p.cop("vector", lambda v: v.tensor_mul(b_v, a_v, var2))
    p.cop("vector", lambda v: v.tensor_scalar(b_v, b_v, -1.0, None, ALU.mult))
    ab_ready = p.cop("vector", lambda v: v.tensor_copy(a_v, var2))
    p.wait("tensor", chv, ab_ready)

    X2 = X
    XO = XB
    p2_vals = {}
    done = []
    for c in range(nch):
        xs = c % 2
        if c >= 2:
            p.wait("sync", chv, p2_vals[c - 2])
            if not src_f32:
                p.wait("sync", s_oc[xs])
        else:
            p.wait("sync", chv, ab_ready)
        _load(p, X2[xs], c, s_x2p[xs])
        if c >= 1:
            p.wait("tensor", chv, p2_vals[c - 1])

        def bc(t, c=c):
            t.matmul(PSBC[0][:, :], ONES1[:, :], a_v[:, c, :], start=True, stop=True)
            return t.matmul(PSBC[1][:, :], ONES1[:, :], b_v[:, c, :], start=True, stop=True)

        p.top(bc, sem=s_bc)
        p.wait("vector", s_bc)
        p.wait("vector", s_x2p[xs])
        if c >= 2:
            p.wait("vector", s_oc[xs])
        TMP = X2
        for k in range(NKT):
            p.cop(
                "vector",
                lambda v, xs=xs, k=k: v.tensor_tensor(
                    X2[xs][:, k * tc : (k + 1) * tc],
                    X2[xs][:, k * tc : (k + 1) * tc],
                    PSBC[0][:, :],
                    ALU.mult,
                ),
            )
            if gamma_sb is None:
                v2 = p.cop(
                    "vector",
                    lambda v, xs=xs, k=k: v.tensor_tensor(
                        XO[xs][:, k * tc : (k + 1) * tc],
                        X2[xs][:, k * tc : (k + 1) * tc],
                        PSBC[1][:, :],
                        ALU.add,
                    ),
                )
            else:
                p.cop(
                    "vector",
                    lambda v, xs=xs, k=k: v.tensor_tensor(
                        X2[xs][:, k * tc : (k + 1) * tc],
                        X2[xs][:, k * tc : (k + 1) * tc],
                        PSBC[1][:, :],
                        ALU.add,
                    ),
                )
                v2 = p.cop(
                    "vector",
                    lambda v, xs=xs, k=k: v.tensor_scalar(
                        XO[xs][:, k * tc : (k + 1) * tc],
                        X2[xs][:, k * tc : (k + 1) * tc],
                        gamma_sb[:, k : k + 1],
                        beta_sb[:, k : k + 1],
                        ALU.mult,
                        ALU.add,
                    ),
                )
        p2_vals[c] = v2
        p.wait("scalar", chv, v2)
        p.dma(
            "scalar",
            xdst.rearrange("(kt q) t -> q kt t", q=128)[
                :, :, dst_c0 + c * tc : dst_c0 + (c + 1) * tc
            ],
            XO[xs][:, :].rearrange("q (kt t) -> q kt t", t=tc),
            s_oc[xs],
        )
        done.append((s_oc[xs], s_oc[xs].n))
    return done






def emit_toeplitz_build(p, *, name, z_t, z_off, t_t, t_off, nch=1024, K=128):
    s_bt = p.sem(f"{name}_bt")
    for q in range(128):
        p.dma(
            "scalar",
            bass.AP(t_t, t_off + q * nch * K, [[K, nch], [1, K]]),
            bass.AP(z_t, z_off + K - q, [[2 * K, nch], [1, K]]),
            s_bt,
        )
    return s_bt


def emit_l0attn(p, d, *, name):
    C0P = HALO - 512
    PCOLS = 4608
    NB = 36
    TSZ = 128 * 1024 * 128

    m0 = p.mark()

    s_zc = p.sem(f"{name}_zc")
    zb = d["zb"]
    for i, znm in enumerate(("zlo0", "zhi0", "zlo1", "zhi1")):
        p.dma("gpsimd", zb[i], d[znm], s_zc)
    p.wait("scalar", s_zc)
    s_bt = [
        emit_toeplitz_build(
            p, name=f"{name}_tb{i}", z_t=zb.tensor, z_off=i * 1024 * 256,
            t_t=d["tmats"].tensor, t_off=i * TSZ,
        )
        for i in range(4)
    ]

    GB = p.sb([128, 16], F32, f"{name}_gb")
    BIN = p.sb([128, 24], F32, f"{name}_bin")
    BOUT = p.sb([128, 8], F32, f"{name}_bout")
    IDENT = p.sb([128, 128], BF16, f"{name}_id")
    HMC = p.sb([128, 1], F32, f"{name}_hmc")
    s_vec = p.sem(f"{name}_vec")
    p.dma("sync", HMC[:, :], d["hm_col"], s_vec)
    p.dma("sync", GB[:, :], d["gb_p"], s_vec)
    p.dma("sync", BIN[:, :], d["b_in_p"], s_vec)
    p.dma("sync", BOUT[:, :], d["b_out_p"], s_vec)
    s_idl = p.sem(f"{name}_idl")
    p.dma("gpsimd", IDENT[:, :], d["ident_p"], s_idl)
    p.wait("vector", s_vec)
    p.wait("scalar", s_vec)
    p.wait("tensor", s_vec)
    p.wait("tensor", s_idl)

    mln = p.mark()
    ln_done = emit_lngb(
        p, name=f"{name}_ln", xsrc=d["xn"], xdst=d["xln"], c0=C0P, cols=PCOLS,
        gamma_sb=GB[:, 0:8], beta_sb=GB[:, 8:16], src_f32=False, dst_c0=0,
    )
    p.barrier(waits=ln_done)
    p.reset(mln)

    epi, s_oc_pj = epi_store(p, name=f"{name}_pj", dst=d["proj"], dst_c0=0, bias_sb=BIN)
    emit_gemm(
        p, name=f"{name}_pg", src=d["xln"], src_c0=0, wsrc=d["w_in_p"],
        d_in=D, d_out=3072, cols=PCOLS, epi=epi,
    )
    p.barrier(waits=[(sm, sm.n) for sm in s_oc_pj])
    p.reset(mln)

    TB = [p.sb([128, 128 * 128], BF16, f"{name}_TB{i}") for i in range(2)]
    VT = [p.sb([128, NB * 128], BF16, f"{name}_VT{i}") for i in range(3)]
    Y0G = p.sb([128, NB * 128], BF16, f"{name}_y0g")
    Y1G = p.sb([128, 32 * 128], BF16, f"{name}_y1g")
    YO = p.sb([128, 32 * 128], BF16, f"{name}_yo")
    PSC = [p.ps([128, 35], F32) for i in range(4)]
    PSTa = p.ps([128, 128], BF16)
    PSTb = p.ps([128, 128], BF16)
    s_tb = p.sems(f"{name}_stb", 2)
    s_vt = p.sems(f"{name}_svt", 3)
    s_cm = p.sem(f"{name}_cm")
    s_tr = p.sem(f"{name}_tr")
    s_yo = p.sem(f"{name}_yo")
    chv = p.chain["vector"]

    for i in range(4):
        p.wait("sync", s_bt[i])
        p.wait("scalar", s_bt[i])

    gate2_last = 0
    conv_state = {}
    for dt_i in range(8):
        ch0 = dt_i * 128
        if dt_i > 0:
            p.wait("sync", s_cm, conv_state["cm_after_conv1"])
        p.dma("sync", TB[0][:, :],
              bass.AP(d["tmats"].tensor, 0 * TSZ + ch0 * 128, [[1024 * 128, 128], [1, 128 * 128]]),
              s_tb[0])
        p.dma("sync", TB[1][:, :],
              bass.AP(d["tmats"].tensor, 1 * TSZ + ch0 * 128, [[1024 * 128, 128], [1, 128 * 128]]),
              s_tb[1])
        tb_ready = (s_tb[0].n, s_tb[1].n)
        if dt_i > 0:
            p.wait("sync", s_cm, conv_state["cm_after_conv0"])
            p.wait("sync", chv, conv_state["gate1_last"])
            p.wait("sync", chv, conv_state["gate2_last"])
        for vi in range(3):
            for tb_j in range(NB):
                p.dma(
                    "sync",
                    VT[vi][:, tb_j * 128 : (tb_j + 1) * 128],
                    d["proj"][vi * 1024 + ch0 : vi * 1024 + ch0 + 128,
                              tb_j * 128 : (tb_j + 1) * 128],
                    s_vt[vi],
                    transpose=True,
                )
        vt_ready = [s_vt[vi].n for vi in range(3)]
        for vi in range(3):
            p.wait("vector", s_vt[vi], vt_ready[vi])
        for vi in range(3):
            p.cop("vector", lambda v, vi=vi: v.tensor_scalar(
                VT[vi][:, 0:512], VT[vi][:, 0:512], HMC[:, 0:1], None, ALU.mult))
        zdone = p.vchain("vector")
        p.wait("tensor", chv, zdone)

        p.wait("tensor", s_tb[0], tb_ready[0])
        p.wait("tensor", s_tb[1], tb_ready[1])
        p.wait("tensor", s_vt[0], vt_ready[0])
        p.wait("vector", s_vt[1], vt_ready[1])

        if dt_i > 0:
            p.wait("tensor", chv, conv_state["gate2_last"])
        gate1_vals = {}
        for ch in range(128):
            bnk = ch % 4
            if ch >= 4:
                p.wait("tensor", chv, gate1_vals[ch - 4])

            def c0mm(t, ch=ch, bnk=bnk):
                t.matmul(PSC[bnk][:, :], TB[0][:, ch * 128 : (ch + 1) * 128],
                         VT[0][:, 128 + ch : 128 + ch + 34 * 128 + 1 : 128],
                         start=True, stop=False)
                return t.matmul(PSC[bnk][:, :], TB[1][:, ch * 128 : (ch + 1) * 128],
                                VT[0][:, ch : ch + 34 * 128 + 1 : 128],
                                start=False, stop=True)

            mmv = p.top(c0mm, sem=s_cm)
            p.wait("vector", s_cm, mmv)
            gate1_vals[ch] = p.cop(
                "vector",
                lambda v, ch=ch, bnk=bnk: v.tensor_tensor(
                    Y0G[:, 128 + ch : 128 + ch + 34 * 128 + 1 : 128],
                    PSC[bnk][:, :],
                    VT[1][:, 128 + ch : 128 + ch + 34 * 128 + 1 : 128],
                    ALU.mult,
                ),
            )
        conv_state["cm_after_conv0"] = s_cm.n
        conv_state["gate1_last"] = gate1_vals[127]

        p.wait("sync", s_cm, conv_state["cm_after_conv0"])
        p.dma("sync", TB[0][:, :],
              bass.AP(d["tmats"].tensor, 2 * TSZ + ch0 * 128, [[1024 * 128, 128], [1, 128 * 128]]),
              s_tb[0])
        p.dma("sync", TB[1][:, :],
              bass.AP(d["tmats"].tensor, 3 * TSZ + ch0 * 128, [[1024 * 128, 128], [1, 128 * 128]]),
              s_tb[1])
        p.wait("tensor", s_tb[0])
        p.wait("tensor", s_tb[1])
        p.wait("vector", s_vt[2], vt_ready[2])

        gate2_vals = {}
        p.wait("tensor", chv, conv_state["gate1_last"])
        for ch in range(128):
            bnk = ch % 4
            if ch >= 4:
                p.wait("tensor", chv, gate2_vals[ch - 4])

            def c1mm(t, ch=ch, bnk=bnk):
                t.matmul(PSC[bnk][:, :32], TB[0][:, ch * 128 : (ch + 1) * 128],
                         Y0G[:, 4 * 128 + ch : 4 * 128 + ch + 31 * 128 + 1 : 128],
                         start=True, stop=False)
                return t.matmul(PSC[bnk][:, :32], TB[1][:, ch * 128 : (ch + 1) * 128],
                                Y0G[:, 3 * 128 + ch : 3 * 128 + ch + 31 * 128 + 1 : 128],
                                start=False, stop=True)

            mmv = p.top(c1mm, sem=s_cm)
            p.wait("vector", s_cm, mmv)
            gate2_vals[ch] = p.cop(
                "vector",
                lambda v, ch=ch, bnk=bnk: v.tensor_tensor(
                    Y1G[:, ch : ch + 31 * 128 + 1 : 128],
                    PSC[bnk][:, :32],
                    VT[2][:, 4 * 128 + ch : 4 * 128 + ch + 31 * 128 + 1 : 128],
                    ALU.mult,
                ),
            )
        conv_state["cm_after_conv1"] = s_cm.n
        conv_state["gate2_last"] = gate2_vals[127]

        p.wait("tensor", chv, gate2_vals[127])
        cp_vals = {}
        for tb_j in range(32):
            pst = PSTa if tb_j % 2 == 0 else PSTb
            if tb_j >= 2:
                p.wait("tensor", chv, cp_vals[tb_j - 2])
            if dt_i > 0 and tb_j < 2:
                p.wait("tensor", chv, conv_state.get("cp_last", 0))
            trv = p.top(
                lambda t, tb_j=tb_j, pst=pst: t.transpose(
                    pst[:, :], Y1G[:, tb_j * 128 : (tb_j + 1) * 128], IDENT[:, :]
                ),
                sem=s_tr,
            )
            p.wait("vector", s_tr, trv)
            if dt_i > 0 and tb_j == 0:
                p.wait("vector", s_yo, s_yo.base + (dt_i) * 16)
            cp_vals[tb_j] = p.cop(
                "vector",
                lambda v, tb_j=tb_j, pst=pst: v.tensor_copy(
                    YO[:, tb_j * 128 : (tb_j + 1) * 128], pst[:, :]
                ),
            )
        conv_state["cp_last"] = cp_vals[31]
        p.wait("scalar", chv, cp_vals[31])
        p.dma("scalar", d["ycv"][ch0 : ch0 + 128, :], YO[:, :], s_yo)

    p.barrier(waits=[(s_yo, s_yo.n)])
    p.reset(mln)

    RES = [p.sb([128, TC], BF16, f"{name}_res{i}") for i in range(4)]
    OBF = [p.sb([128, TC], F32, f"{name}_obf{i}") for i in range(4)]
    s_res = p.sems(f"{name}_res", 4)
    s_so = p.sems(f"{name}_so", 4)
    ctr = [0]

    res_consumed = {}

    def epi_wout(p, c, o, psum, tc_here, s_mm):
        i = ctr[0]
        ctr[0] += 1
        sl = i % 4
        if i >= 4:
            p.wait("sync", chv, res_consumed[i - 4])
        p.dma(
            "sync", RES[sl][:, :tc_here],
            d["xn"][o * 128 : (o + 1) * 128, HALO + c * TC : HALO + c * TC + tc_here],
            s_res[sl],
        )
        p.wait("vector", s_mm)
        p.wait("vector", s_res[sl])
        if i >= 4:
            p.wait("vector", s_so[sl])
        val = p.cop(
            "vector",
            lambda v, sl=sl, psum=psum, o=o, tc_here=tc_here: v.scalar_tensor_tensor(
                OBF[sl][:, :tc_here], psum, BOUT[:, o : o + 1], RES[sl][:, :tc_here],
                ALU.add, ALU.add,
            ),
        )
        res_consumed[i] = val
        p.wait("scalar", chv, val)
        p.dma(
            "scalar", d["s0"][o * 128 : (o + 1) * 128, c * TC : c * TC + tc_here],
            OBF[sl][:, :tc_here], s_so[sl],
        )
        return (chv, val)

    emit_gemm(
        p, name=f"{name}_wo", src=d["ycv"], src_c0=0, wsrc=d["w_out_p"],
        d_in=D, d_out=D, cols=TL, epi=epi_wout,
    )
    p.barrier(waits=[(sm, sm.n) for sm in s_so])
    p.reset(m0)


def l0attn_host(inp, core_args):
    K0 = np.asarray(inp["L0_attn_k0"])[0]
    K1 = np.asarray(inp["L0_attn_k1"])[0]
    Z = np.zeros((1024, 128), np.float32)
    out = {
        "zlo0": np.concatenate([Z, K0], 1),
        "zhi0": np.concatenate([K0, Z], 1),
        "zlo1": np.concatenate([Z, K1], 1),
        "zhi1": np.concatenate([K1, Z], 1),
        "w_in_p": pack_w(np.asarray(inp["L0_attn_W_in"])),
        "b_in_p": pack_vec(np.asarray(inp["L0_attn_b_in"])),
        "gb_p": np.concatenate(
            [pack_vec(np.asarray(inp["L0_attn_gamma"])), pack_vec(np.asarray(inp["L0_attn_beta"]))], 1
        ),
        "w_out_p": pack_w(np.asarray(inp["L0_attn_W_out"])),
        "b_out_p": pack_vec(np.asarray(inp["L0_attn_b_out"])),
        "ident_p": np.eye(128, dtype=np.float32),
    }
    return out



FFN_HID = 2816
FFN_W1O = 2 * FFN_HID


def epi_res_store(p, *, name, dst, dst_c0, res, res_c0, bias_sb=None, dtype=BF16,
                  post=None):
    RES = [p.sb([128, TC], BF16, f"{name}_res{i}") for i in range(4)]
    OBF = [p.sb([128, TC], dtype, f"{name}_obf{i}") for i in range(4)]
    s_res = p.sems(f"{name}_resp", 4)
    s_so = p.sems(f"{name}_so", 4)
    chv = p.chain["vector"]
    ctr = [0]
    consumed = {}

    def epi(p, c, o, psum, tc_here, s_mm):
        i = ctr[0]
        ctr[0] += 1
        sl = i % 4
        if i >= 4:
            p.wait("sync", chv, consumed[i - 4])
        p.dma(
            "sync", RES[sl][:, :tc_here],
            res[o * 128 : o * 128 + psum.shape[0], res_c0 + c * TC : res_c0 + c * TC + tc_here],
            s_res[sl],
        )
        p.wait("vector", s_mm)
        p.wait("vector", s_res[sl])
        if i >= 4:
            p.wait("vector", s_so[sl])
        if bias_sb is not None:
            p.cop(
                "vector",
                lambda v, sl=sl, psum=psum, o=o, tc_here=tc_here: v.scalar_tensor_tensor(
                    OBF[sl][: psum.shape[0], :tc_here], psum, bias_sb[:, o : o + 1],
                    RES[sl][:, :tc_here], ALU.add, ALU.add,
                ),
            )
        else:
            p.cop(
                "vector",
                lambda v, sl=sl, psum=psum, tc_here=tc_here: v.tensor_tensor(
                    OBF[sl][: psum.shape[0], :tc_here], psum, RES[sl][:, :tc_here], ALU.add
                ),
            )
        if post is not None:
            p.cop("vector", lambda v, sl=sl, c=c, o=o, tc_here=tc_here: post(
                v, OBF[sl][:, :tc_here], c, o, tc_here))
        val = p.vchain("vector")
        consumed[i] = val
        p.wait("scalar", chv, val)
        p.dma(
            "scalar", dst[o * 128 : o * 128 + psum.shape[0], dst_c0 + c * TC : dst_c0 + c * TC + tc_here],
            OBF[sl][: psum.shape[0], :tc_here], s_so[sl],
        )
        return (chv, val)

    return epi, s_so


def emit_ffn_inner(p, d, *, name, src, dst, q0, wk, mlp_q0=None):
    kv0 = q0 - 512
    CK = HALO + kv0
    ncols_kv = TL - kv0
    nch_kv = ncols_kv // TC
    nkb = ncols_kv // 128
    nw = (TL - q0) // 512
    NH = 8
    m0 = p.mark()
    chv = p.chain["vector"]
    chs = p.chain["scalar"]
    chg = p.chain["gpsimd"]

    QKC = p.sb([128, 2], F32, f"{name}_qkc")
    QKR = p.sb([1, 128], F32, f"{name}_qkr")
    ROTQ = p.sb([128, 128], BF16, f"{name}_rot")
    IDT = p.sb([128, 128], BF16, f"{name}_idt")
    MSK = p.sb([128, 256], F32, f"{name}_msk")
    MSKH = p.sb([128, 128], F32, f"{name}_mskh")
    IND2 = p.sb([2, 128], BF16, f"{name}_ind2")
    IND2T = p.sb([128, 2], BF16, f"{name}_ind2t")
    ONESR = p.sb([1, 128], F32, f"{name}_onesr")
    NEGC = p.sb([128, 1], F32, f"{name}_negc")
    TMP1 = p.sb([1, 128], F32, f"{name}_tmp1")
    MX = p.sb([1, 4], F32, f"{name}_mx")
    s_k0 = p.sem(f"{name}_k0")
    s_k1 = p.sem(f"{name}_k1")
    p.dma("sync", QKC[:, :], wk["qk_col"], s_k0)
    p.dma("sync", QKR[:, :], wk["qk_row"], s_k0)
    p.dma("sync", MSK[:, :], wk["masks"], s_k0)
    p.dma("sync", MSKH[:, :], wk["maskh"], s_k0)
    p.dma("sync", ONESR[:, :], wk["ones_p"], s_k0)
    p.dma("gpsimd", ROTQ[:, :], wk["rotq_lhsT"], s_k1)
    p.dma("gpsimd", IDT[:, :], wk["ident_p"], s_k1)
    p.dma("gpsimd", IND2[:, :], wk["ind2"], s_k1)
    p.dma("gpsimd", IND2T[:, :], wk["ind2t"], s_k1)
    p.wait("vector", s_k0)
    p.wait("tensor", s_k0)
    p.wait("tensor", s_k1)
    p.cop("vector", lambda v: v.tensor_scalar(TMP1[0:1, :], QKR[0:1, :], -1.0, None, ALU.mult))
    p.cop("vector", lambda v: v.tensor_tensor(TMP1[0:1, :], TMP1[0:1, :], QKR[0:1, :], ALU.max))
    p.cop("vector", lambda v: v.reduce_max(MX[0:1, 0:1], TMP1[0:1, 0:64], axis=AX.X))
    p.cop("vector", lambda v: v.reduce_max(MX[0:1, 1:2], TMP1[0:1, 64:128], axis=AX.X))
    p.cop("vector", lambda v: v.tensor_mul(MX[0:1, 2:3], MX[0:1, 0:1], MX[0:1, 1:2]))
    cv = p.cop("vector", lambda v: v.tensor_scalar(MX[0:1, 2:3], MX[0:1, 2:3], -8.0, None, ALU.mult))
    p.wait("tensor", chv, cv)
    PSB = p.ps([128, 512], F32)
    s_bc0 = p.sem(f"{name}_bc0")
    p.top(lambda t: t.matmul(PSB[:, 0:1], ONESR[0:1, :], MX[0:1, 2:3], start=True, stop=True),
          sem=s_bc0)
    p.wait("vector", s_bc0)
    ncv = p.cop("vector", lambda v: v.tensor_copy(NEGC[:, :], PSB[:, 0:1]))
    p.wait("scalar", chv, ncv)

    ZC = p.sb([128, 8], BF16, f"{name}_zc")
    s_zc0 = p.sem(f"{name}_zc0")
    zcv = p.cop("vector", lambda v: v.memset(ZC[:, :], 0.0))
    p.wait("sync", chv, zcv)
    if CK >= 1:
        p.dma("sync",
              src.rearrange("(kt q) t -> q kt t", q=128)[:, :, CK - 1 : CK],
              ZC[:, :].rearrange("q (kt t) -> q kt t", t=1), s_zc0,
              allow_slow_non_contiguous=True)
    p.dma("sync",
          d["xmid"].rearrange("(kt q) t -> q kt t", q=128)[:, :, HALO + q0 - 1 : HALO + q0],
          ZC[:, :].rearrange("q (kt t) -> q kt t", t=1), s_zc0,
          allow_slow_non_contiguous=True)
    p.wait("sync", s_zc0)

    mqkv = p.mark()
    epi_q, s_oc_q = epi_store(p, name=f"{name}_qe", dst=d["qkv"], dst_c0=CK)
    emit_gemm(
        p, name=f"{name}_qg", src=src, src_c0=CK, wsrc=wk["wqkv_p"],
        d_in=D, d_out=1536, cols=ncols_kv, epi=epi_q, shift_half=True,
    )
    p.barrier(waits=[(sm, sm.n) for sm in s_oc_q])
    p.reset(mqkv)

    QR = p.sb([128, 4 * ncols_kv], BF16, f"{name}_QR")
    KR = p.sb([128, 4 * ncols_kv], BF16, f"{name}_KR")
    VTT = p.sb([128, 65 * NH * nkb], BF16, f"{name}_VTT")
    QKB = [p.sb([128, 4 * TC], BF16, f"{name}_qkb{i}") for i in range(2)]
    VB = [p.sb([128, 4 * TC], BF16, f"{name}_vb{i}") for i in range(2)]
    TAB = [p.sb([128, 2 * TC], BF16, f"{name}_tab{i}") for i in range(2)]
    SQB = p.sb([128, TC], BF16, f"{name}_sqb")
    SSF = p.sb([2, TC], F32, f"{name}_ssf")
    SSR = p.sb([2, TC], BF16, f"{name}_ssr")
    EPS24 = p.sb([2, 1], F32, f"{name}_eps24")
    T1 = p.sb([128, TC], F32, f"{name}_t1")
    T2 = p.sb([128, TC], F32, f"{name}_t2")
    PS_SUM = p.ps([2, 512], F32)
    PS_BC = p.ps([128, 512], F32)
    PS_ROT = p.ps([128, 512], F32)
    PS_TR = p.ps([128, 128], BF16)
    PS_TR2 = p.ps([128, 128], BF16)
    s_qkl = p.sems(f"{name}_qkl", 2)
    s_vl = p.sems(f"{name}_vl", 2)
    s_tl = p.sems(f"{name}_tl", 2)
    s_sum = p.sem(f"{name}_sum")
    s_bcq = p.sem(f"{name}_bcq")
    s_rot = p.sem(f"{name}_rot")
    s_vtr = p.sem(f"{name}_vtr")

    ov = p.cop("vector", lambda v: v.memset(
        VTT[:, 64 : 64 + (NH * nkb - 1) * 65 + 1 : 65], 1.0))
    e24v = p.cop("vector", lambda v: v.memset(EPS24[:, :], 1e-24))
    p.wait("scalar", chv, e24v)
    p.wait("tensor", chv, ov)

    prev = dict(sq=None, sm=None, sq2=None, bc=None, qn=None, rot=None, t2=None,
                g1=None, o2=None)
    prep_done = {}
    for qk in range(2):
        DSTB = QR if qk == 0 else KR
        for c in range(nch_kv):
            ci = qk * nch_kv + c
            sl = ci % 2
            if ci >= 2:
                p.wait("sync", chv, prep_done[ci - 2])
                p.wait("gpsimd", chv, prep_done[ci - 2])
            p.dma(
                "sync",
                QKB[sl][:, :].rearrange("q (kt t) -> q kt t", t=TC),
                d["qkv"].rearrange("(kt q) t -> q kt t", q=128)[
                    :, 4 * qk : 4 * qk + 4, CK + c * TC : CK + (c + 1) * TC
                ],
                s_qkl[sl],
            )
            p.dma(
                "gpsimd",
                TAB[sl][:, :].rearrange("q (a t) -> q a t", t=TC),
                wk["tab"].rearrange("(a q) t -> q a t", q=128)[
                    :, 2 * qk : 2 * qk + 2, c * TC : (c + 1) * TC
                ],
                s_tl[sl],
            )
            qkl_v = s_qkl[sl].n
            tl_v = s_tl[sl].n
            p.wait("vector", s_tl[sl], tl_v)
            p.wait("vector", s_qkl[sl], qkl_v)
            for kt in range(4):
                p.wait("scalar", s_qkl[sl], qkl_v)
                if prev["sm"] is not None:
                    p.wait("scalar", s_sum, prev["sm"])
                prev["sq"] = p.cop("scalar", lambda sc, sl=sl, kt=kt: sc.activation(
                    SQB[:, :], QKB[sl][:, kt * TC : (kt + 1) * TC], AF.Square))
                p.wait("tensor", chs, prev["sq"])
                prev["sm"] = p.top(lambda t: t.matmul(
                    PS_SUM[:, :], IND2T[:, :], SQB[:, :], start=True, stop=True), sem=s_sum)
                p.wait("scalar", s_sum, prev["sm"])
                if prev["bc"] is not None:
                    p.wait("scalar", s_bcq, prev["bc"])
                p.cop("scalar", lambda sc: sc.activation(
                    SSF[:, :], PS_SUM[:, :], AF.Ln, bias=EPS24[:, 0:1]))
                prev["sq2"] = p.cop("scalar", lambda sc: sc.activation(
                    SSR[:, :], SSF[:, :], AF.Exp, scale=-0.5))
                p.wait("tensor", chs, prev["sq2"])
                if prev["qn"] is not None:
                    p.wait("tensor", chv, prev["qn"])
                prev["bc"] = p.top(lambda t: t.matmul(
                    PS_BC[:, :], IND2[:, :], SSR[:, :], start=True, stop=True), sem=s_bcq)
                p.wait("vector", s_bcq, prev["bc"])
                p.cop("vector", lambda v, sl=sl, kt=kt: v.tensor_tensor(
                    T2[:, :], QKB[sl][:, kt * TC : (kt + 1) * TC], PS_BC[:, :], ALU.mult))
                prev["qn"] = p.cop("vector", lambda v, sl=sl, kt=kt, qk=qk: v.tensor_scalar(
                    QKB[sl][:, kt * TC : (kt + 1) * TC], T2[:, :], QKC[:, qk : qk + 1],
                    None, ALU.mult))
                p.wait("tensor", chv, prev["qn"])
                if prev["t2"] is not None:
                    p.wait("tensor", chv, prev["t2"])
                prev["rot"] = p.top(lambda t, sl=sl, kt=kt: t.matmul(
                    PS_ROT[:, :], ROTQ[:, :], QKB[sl][:, kt * TC : (kt + 1) * TC],
                    start=True, stop=True), sem=s_rot)
                p.wait("gpsimd", s_tl[sl], tl_v)
                p.wait("gpsimd", chv, prev["qn"])
                if prev["o2"] is not None:
                    p.wait("gpsimd", chv, prev["o2"])
                prev["g1"] = p.cop("gpsimd", lambda g, sl=sl, kt=kt: g.tensor_tensor(
                    T1[:, :], QKB[sl][:, kt * TC : (kt + 1) * TC], TAB[sl][:, 0:TC], ALU.mult))
                p.wait("vector", s_rot, prev["rot"])
                prev["t2"] = p.cop("vector", lambda v, sl=sl: v.tensor_tensor(
                    T2[:, :], PS_ROT[:, :], TAB[sl][:, TC:], ALU.mult))
                p.wait("vector", chg, prev["g1"])
                prev["o2"] = p.cop("vector", lambda v, c=c, kt=kt, DSTB=DSTB: v.tensor_tensor(
                    DSTB[:, kt * ncols_kv + c * TC : kt * ncols_kv + (c + 1) * TC],
                    T1[:, :], T2[:, :], ALU.add))
            prep_done[ci] = prev["o2"]

    if "dbg" in d:
        s_dbg = p.sem(f"{name}_dbg")
        p.wait("scalar", chv)
        p.dma("scalar", d["dbg"][0:128, 0 : 4 * ncols_kv], QR[:, :], s_dbg)
        p.dma("scalar", d["dbg"][128:256, 0 : 4 * ncols_kv], KR[:, :], s_dbg)
        p.wait("scalar", s_dbg)

    cp_hist = {}
    for c in range(nch_kv):
        sl = c % 2
        if c >= 2:
            p.wait("sync", chv, cp_hist[("blkdone", c - 2)])
        p.dma(
            "sync",
            VB[sl][:, :].rearrange("q (kt t) -> q kt t", t=TC),
            d["qkv"].rearrange("(kt q) t -> q kt t", q=128)[
                :, 8:12, CK + c * TC : CK + (c + 1) * TC
            ],
            s_vl[sl],
        )
        p.wait("tensor", s_vl[sl])
        for kb in range(4):
            blk = c * 4 + kb
            for h in range(NH):
                gi = blk * NH + h
                tile, base = h // 2, (h % 2) * 64
                pst = PS_TR if gi % 2 == 0 else PS_TR2
                pv = cp_hist.get(gi - 2)
                if pv is not None:
                    p.wait("tensor", chv, pv)
                tv = p.top(lambda t, sl=sl, kb=kb, tile=tile, base=base, pst=pst: t.transpose(
                    pst[0:128, 0:64],
                    QKB_V_SLICE(VB[sl], tile, base, kb),
                    IDT[base : base + 64, base : base + 64],
                ), sem=s_vtr)
                p.wait("vector", s_vtr, tv)
                slot = h * nkb + blk
                cp_hist[gi] = p.cop("vector", lambda v, slot=slot, pst=pst: v.tensor_copy(
                    VTT[:, slot * 65 : slot * 65 + 64], pst[0:128, 0:64]))
        cp_hist[("blkdone", c)] = p.vchain("vector")
    p.wait("tensor", chv, cp_hist[("blkdone", nch_kv - 1)])
    p.wait("tensor", chv, cp_hist[("blkdone", nch_kv - 2)])
    p.barrier()
    p.psum_off = mqkv[1]

    OS = p.sb([128, 512], BF16, f"{name}_os")
    SROW = p.sb([1, 1024], F32, f"{name}_srow")
    SRECIP = p.sb([1, 512], F32, f"{name}_srec")
    TINY30 = p.sb([1, 1], F32, f"{name}_tiny30")
    t30v = p.cop("vector", lambda v: v.memset(TINY30[:, :], 1e-30))
    p.wait("scalar", chv, t30v)
    ONRMW = [p.sb([128, 4 * 512], BF16, f"{name}_onrmw{i}") for i in range(2)]
    EXPT = [p.sb([128, 640], BF16, f"{name}_expt{i}") for i in range(2)]
    PS_SIM = [p.ps([128, 640], F32) for i in range(2)]
    PS_AV = [p.ps([128, 512], F32) for i in range(2)]
    PS_NBC = p.ps([128, 512], F32)
    s_sim = p.sem(f"{name}_sim")
    s_av = p.sem(f"{name}_av")
    s_nbc = p.sem(f"{name}_nbc")
    s_ow = p.sems(f"{name}_ow", 2)
    ahist = {}
    gctr = [0]
    for w in range(nw):
        qa0 = q0 + w * 512
        wsl = w % 2
        if w >= 2:
            p.wait("vector", s_ow[wsl])
        for tile in range(4):
            for half in range(2):
                h = 2 * tile + half
                base = half * 64
                srg = ahist.get(("srg", half))
                if srg is not None:
                    p.wait("vector", chs, srg)
                for a in range(4):
                    gi = gctr[0]
                    gctr[0] += 1
                    par = gi % 2
                    qt0 = qa0 + a * 128
                    kt0 = qt0 - 512
                    qc = qt0 - kv0
                    kc = kt0 - kv0
                    w0m = (0 - q0) // 512
                    pv = ahist.get(("exp", gi - 2))
                    if pv is not None:
                        p.wait("tensor", chs, pv)

                    def simf(t, tile=tile, base=base, qc=qc, kc=kc, par=par):
                        ins = None
                        for j in range(5):
                            ins = t.matmul(
                                PS_SIM[par][0:128, j * 128 : (j + 1) * 128],
                                KR[base : base + 64,
                                   tile * ncols_kv + kc + j * 128 :
                                   tile * ncols_kv + kc + (j + 1) * 128],
                                QR[base : base + 64,
                                   tile * ncols_kv + qc : tile * ncols_kv + qc + 128],
                                start=True, stop=True,
                            )
                        return ins

                    simv = p.top(simf, sem=s_sim)
                    p.wait("vector", s_sim, simv)
                    p.cop("vector", lambda v, par=par: v.tensor_tensor(
                        PS_SIM[par][0:128, 0:128], PS_SIM[par][0:128, 0:128],
                        MSK[:, 0:128], ALU.add))
                    mv = p.cop("vector", lambda v, par=par: v.tensor_tensor(
                        PS_SIM[par][0:128, 512:640], PS_SIM[par][0:128, 512:640],
                        MSK[:, 128:256], ALU.add))
                    if w == w0m and a < 4:
                        for j in range(4 - a):
                            mv = p.cop("vector", lambda v, par=par, j=j: v.tensor_tensor(
                                PS_SIM[par][0:128, j * 128 : (j + 1) * 128],
                                PS_SIM[par][0:128, j * 128 : (j + 1) * 128],
                                MSKH[:, :], ALU.add))
                    p.wait("scalar", chv, mv)
                    pv = ahist.get(("avv", gi - 2))
                    if pv is not None:
                        p.wait("scalar", s_av, pv)
                    ev = p.cop("scalar", lambda sc, par=par: sc.activation(
                        EXPT[par][:, :], PS_SIM[par][0:128, :],
                        AF.Exp, bias=NEGC[:, 0:1], scale=8.0))
                    ahist[("exp", gi)] = ev
                    pv = ahist.get(("evac", gi - 2))
                    if pv is not None:
                        p.wait("tensor", chv, pv)
                    p.wait("tensor", chs, ev)

                    def avf(t, h=h, kc=kc, par=par):
                        ins = None
                        kb0 = kc // 128
                        for j in range(5):
                            slot = h * nkb + kb0 + j
                            ins = t.matmul(
                                PS_AV[par][0:65, 0:128],
                                VTT[:, slot * 65 : (slot + 1) * 65],
                                EXPT[par][:, j * 128 : (j + 1) * 128],
                                start=(j == 0), stop=(j == 4),
                            )
                        return ins

                    avv = p.top(avf, sem=s_av)
                    ahist[("avv", gi)] = avv
                    p.wait("vector", s_av, avv)
                    qcol = qt0 - qa0
                    p.cop("vector", lambda v, base=base, qcol=qcol, par=par: v.tensor_copy(
                        OS[base : base + 64, qcol : qcol + 128], PS_AV[par][0:64, 0:128]))
                    ahist[("evac", gi)] = p.cop(
                        "vector", lambda v, half=half, qcol=qcol, par=par: v.tensor_copy(
                            SROW[0:1, half * 512 + qcol : half * 512 + qcol + 128],
                            PS_AV[par][64:65, 0:128]))
                p.wait("scalar", chv)
                pbv = ahist.get(("nbv", half))
                if pbv is not None:
                    p.wait("scalar", s_nbc, pbv)
                p.cop("scalar", lambda sc, half=half: sc.activation(
                    SRECIP[0:1, :], SROW[0:1, half * 512 : (half + 1) * 512],
                    AF.Ln, bias=TINY30[0:1, 0:1]))
                rv = p.cop("scalar", lambda sc: sc.activation(
                    SRECIP[0:1, :], SRECIP[0:1, :], AF.Exp, scale=-1.0))
                ahist[("srg", half)] = rv
                p.wait("tensor", chs, rv)
                pv = ahist.get(("nm", h - 1, w), ahist.get(("nm", 7, w - 1)))
                if pv is not None:
                    p.wait("tensor", chv, pv)
                bv = p.top(lambda t, base=base: t.matmul(
                    PS_NBC[base : base + 64, 0:512], ONESR[0:1, 0:64], SRECIP[0:1, :],
                    start=True, stop=True), sem=s_nbc)
                ahist[("nbv", half)] = bv
                p.wait("vector", s_nbc, bv)
                ahist[("nm", h, w)] = p.cop(
                    "vector", lambda v, wsl=wsl, tile=tile, base=base: v.tensor_tensor(
                        ONRMW[wsl][base : base + 64, tile * 512 : (tile + 1) * 512],
                        OS[base : base + 64, :],
                        PS_NBC[base : base + 64, 0:512], ALU.mult))
        dn = p.vchain("vector")
        p.wait("scalar", chv, dn)
        p.dma("scalar",
              d["onrm"].rearrange("(kt q) t -> q kt t", q=128)[
                  :, :, HALO + qa0 : HALO + qa0 + 512],
              ONRMW[wsl][:, :].rearrange("q (kt t) -> q kt t", t=512), s_ow[wsl])
    p.barrier(waits=[(s_ow[0], s_ow[0].n), (s_ow[1], s_ow[1].n)])
    p.reset(mqkv)

    epi_wo, s_so_wo = epi_res_store(
        p, name=f"{name}_woe", dst=d["xmid"], dst_c0=HALO + q0, res=src,
        res_c0=HALO + q0)
    emit_gemm(
        p, name=f"{name}_wog", src=d["onrm"], src_c0=HALO + q0, wsrc=wk["wo_p"],
        d_in=512, d_out=D, cols=TL - q0, epi=epi_wo,
    )
    p.barrier(waits=[(sm, sm.n) for sm in s_so_wo])
    p.reset(mqkv)

    fq0 = q0 if mlp_q0 is None else mlp_q0
    fcols = TL - fq0
    fch = fcols // TC
    LNG = p.sb([128, 16], F32, f"{name}_lng")
    s_lng = p.sem(f"{name}_lng")
    p.dma("sync", LNG[:, :], wk["lng_col"], s_lng)
    p.wait("vector", s_lng)
    mln2 = p.mark()
    ln2_done = emit_lngb(
        p, name=f"{name}_ln2", xsrc=d["xmid"], xdst=d["xf"], c0=HALO + fq0,
        cols=fcols, gamma_sb=LNG[:, 0:8], beta_sb=LNG[:, 8:16], src_f32=False,
        shift_half=True,
    )
    p.barrier(waits=ln2_done)
    p.reset(mln2)

    NO1 = FFN_W1O // 128
    W1 = p.sb([128, 8 * FFN_W1O], BF16, f"{name}_w1")
    W2 = p.sb([128, 22 * 1024], BF16, f"{name}_w2")
    B1 = p.sb([128, 44], F32, f"{name}_b1")
    B2 = p.sb([128, 8], F32, f"{name}_b2")
    XF = [p.sb([128, 8 * TC], BF16, f"{name}_xf{i}") for i in range(2)]
    HB = p.sb([128, 22 * TC], BF16, f"{name}_hb")
    ASB = p.sb([128, TC], F32, f"{name}_asb")
    GSB = p.sb([128, TC], F32, f"{name}_gsb")
    PS_W1 = [p.ps([128, 512], F32) for i in range(2)]
    PS_W2 = [p.ps([128, 512], F32) for i in range(2)]
    s_w1l = p.sem(f"{name}_w1l")
    s_bl = p.sem(f"{name}_bl")
    s_xfl = p.sems(f"{name}_xfl", 2)
    s_m1 = p.sem(f"{name}_m1")
    s_m2 = p.sem(f"{name}_m2")
    p.dma("gpsimd", W1[:, :], wk["w1_p"], s_w1l)
    p.dma("gpsimd", W2[:, :], wk["w2_p"], s_w1l)
    p.dma("sync", B1[:, :], wk["b1_col"], s_bl)
    p.dma("sync", B2[:, :], wk["b2_col"], s_bl)
    p.wait("tensor", s_w1l)
    p.wait("vector", s_bl)
    p.wait("scalar", s_bl)
    epi_f, s_so_f = epi_res_store(
        p, name=f"{name}_fe", dst=dst, dst_c0=HALO + fq0, res=d["xmid"],
        res_c0=HALO + fq0, bias_sb=B2)
    fh = {}
    for c in range(fch):
        sl = c % 2
        if c >= 2:
            p.wait("sync", s_m1, fh[("xf_cons", c - 2)])
        p.dma(
            "sync",
            XF[sl][:, :].rearrange("q (kt t) -> q kt t", t=TC),
            d["xf"].rearrange("(kt q) t -> q kt t", q=128)[
                :, :, HALO + fq0 + c * TC : HALO + fq0 + (c + 1) * TC],
            s_xfl[sl],
        )
        p.wait("tensor", s_xfl[sl])
        if c >= 1:
            p.wait("vector", s_m2, s_m2.base + c * 8)
        for o2 in range(22):
            for ag in range(2):
                o = o2 * 2 + ag
                par = o % 2
                pv = fh.get(("h", c, o2 - 1))
                if ag == 0 and pv is not None:
                    p.wait("tensor", chv, pv)
                if ag == 1 and fh.get(("g", c, o2 - 1)) is not None:
                    p.wait("tensor", chs, fh[("g", c, o2 - 1)])

                def w1f(t, sl=sl, o=o, par=par):
                    ins = None
                    for k in range(8):
                        ins = t.matmul(
                            PS_W1[par][:, :],
                            W1[:, k * FFN_W1O + o * 128 : k * FFN_W1O + (o + 1) * 128],
                            XF[sl][:, k * TC : (k + 1) * TC],
                            start=(k == 0), stop=(k == 7))
                    return ins

                mv = p.top(w1f, sem=s_m1)
                if ag == 0:
                    p.wait("vector", s_m1, mv)
                    fh[("a", c, o2)] = p.cop("vector", lambda v, o2=o2, o=o: v.tensor_scalar(
                        ASB[:, :], PS_W1[0][:, :], B1[:, o : o + 1], None, ALU.add))
                else:
                    p.wait("scalar", s_m1, mv)
                    if fh.get(("hh", c, o2 - 1)) is not None:
                        p.wait("scalar", chv, fh[("hh", c, o2 - 1)])
                    fh[("g", c, o2)] = p.cop("scalar", lambda sc, o=o: sc.activation(
                        GSB[:, :], PS_W1[1][:, :], AF.Gelu, bias=B1[:, o : o + 1]))
                    p.wait("vector", chs, fh[("g", c, o2)])
                    p.wait("vector", chv, fh[("a", c, o2)])
                    fh[("hh", c, o2)] = p.cop("vector", lambda v, o2=o2: v.tensor_mul(
                        HB[:, o2 * TC : (o2 + 1) * TC], ASB[:, :], GSB[:, :]))
                    fh[("h", c, o2)] = fh[("hh", c, o2)]
        fh[("xf_cons", c)] = s_m1.n
        hready = fh[("hh", c, 21)]
        p.wait("tensor", chv, hready)
        for o in range(8):
            par = o % 2
            pv = fh.get(("w2e", c, o - 2))
            if pv is None and c > 0:
                pv = fh.get(("w2e", c - 1, o + 6))
            if pv is not None:
                p.wait("tensor", pv[0], pv[1])

            def w2f(t, o=o, par=par):
                ins = None
                for k in range(22):
                    ins = t.matmul(
                        PS_W2[par][:, :],
                        W2[:, k * 1024 + o * 128 : k * 1024 + (o + 1) * 128],
                        HB[:, k * TC : (k + 1) * TC],
                        start=(k == 0), stop=(k == 21))
                return ins

            mv = p.top(w2f, sem=s_m2)
            fh[("w2e", c, o)] = epi_f(p, c, o, PS_W2[par][:, :], TC, s_m2)
    p.barrier(waits=[(sm, sm.n) for sm in s_so_f])
    p.reset(m0)


def QKB_V_SLICE(vb, tile, base, kb):
    return vb.rearrange("q (kt t) -> q kt t", t=TC)[base : base + 64, tile, kb * 128 : (kb + 1) * 128]


def ffn_host_layer(wqkv, qs, ks, wo, lg, lb, w1, b1, w2, b2):
    HID = 2730
    w1a, w1g = w1[:, :HID], w1[:, HID:]
    w1a = pad_cols(w1a, FFN_HID)
    w1g = pad_cols(w1g, FFN_HID)
    w1i = np.zeros((1024, FFN_W1O), np.float32)
    b1i = np.zeros(FFN_W1O, np.float32)
    b1a = np.concatenate([b1[:HID], np.zeros(FFN_HID - HID, np.float32)])
    b1g = np.concatenate([b1[HID:], np.zeros(FFN_HID - HID, np.float32)])
    for j in range(22):
        w1i[:, (2 * j) * 128 : (2 * j + 1) * 128] = w1a[:, j * 128 : (j + 1) * 128]
        w1i[:, (2 * j + 1) * 128 : (2 * j + 2) * 128] = w1g[:, j * 128 : (j + 1) * 128]
        b1i[(2 * j) * 128 : (2 * j + 1) * 128] = b1a[j * 128 : (j + 1) * 128]
        b1i[(2 * j + 1) * 128 : (2 * j + 2) * 128] = b1g[j * 128 : (j + 1) * 128]
    return dict(
        wqkv_p=pack_w(wqkv),
        wo_p=pack_w(wo),
        qk_col=np.stack([np.tile(qs, 2), np.tile(ks, 2)], 1).astype(np.float32),
        qk_row=np.concatenate([qs, ks]).astype(np.float32).reshape(1, 128),
        lng_col=np.concatenate([pack_vec(lg), pack_vec(lb)], 1),
        w1_p=pack_w(w1i),
        b1_col=pack_vec(b1i),
        w2_p=pack_w(pad_rows(w2, FFN_HID)),
        b2_col=pack_vec(b2),
    )


def ffn_host_consts():
    TW = T + 1024
    tg = np.arange(-512, TW - 512, dtype=np.float64)
    half = np.arange(0, 64, 2, dtype=np.float64)
    inv_freq = 1.0 / 10000.0 ** (half / 64.0)
    freqs32 = tg[None, :] * inv_freq[:, None]
    freqs = np.concatenate([freqs32, freqs32], 0)
    base = (half + 0.4 * 64.0) / (1.4 * 64.0)
    power = (tg - 4096.0) / 256.0
    scale32 = base[:, None] ** power[None, :]
    scale = np.concatenate([scale32, scale32], 0)
    cs_q = (np.cos(freqs) * scale).astype(np.float32)
    ss_q = (np.sin(freqs) * scale).astype(np.float32)
    cs_k = (np.cos(freqs) / scale).astype(np.float32)
    ss_k = (np.sin(freqs) / scale).astype(np.float32)
    tab = np.concatenate(
        [np.tile(x, (2, 1)) for x in (cs_q, ss_q, cs_k, ss_k)], 0
    )
    R = np.zeros((64, 64), np.float32)
    for i in range(32):
        R[i, i + 32] = -1.0
        R[i + 32, i] = 1.0
    R128 = np.zeros((128, 128), np.float32)
    R128[:64, :64] = R
    R128[64:, 64:] = R
    rotq_lhsT = np.ascontiguousarray(R128.T)
    j_ = np.arange(128)[:, None]
    i_ = np.arange(128)[None, :]
    M0 = np.where(j_ >= i_, 0.0, -1e30).astype(np.float32)
    M4 = np.where(j_ <= i_, 0.0, -1e30).astype(np.float32)
    masks = np.concatenate([M0, M4], 1)
    ind2 = np.zeros((2, 128), np.float32)
    ind2[0, :64] = 1.0
    ind2[1, 64:] = 1.0
    ind2t = np.ascontiguousarray(ind2.T)
    ones_p = np.ones((1, 128), np.float32)
    ident_p = np.eye(128, dtype=np.float32)
    return dict(tab=tab, rotq_lhsT=rotq_lhsT, masks=masks, ind2=ind2,
                ind2t=ind2t, ones_p=ones_p, ident_p=ident_p)




def emit_l3ffn(p, d, *, name, wk):
    m0 = p.mark()
    chv = p.chain["vector"]
    MU = p.sb([128, 32], F32, f"{name}_mu")
    s_mu = p.sem(f"{name}_mu")
    p.dma("sync", MU[:, 0:8], wk["mur_col"], s_mu)
    p.dma("sync", MU[:, 16:24], wk["muk_col"], s_mu)
    p.wait("vector", s_mu)
    p.cop("vector", lambda v: v.tensor_scalar(MU[:, 8:16], MU[:, 0:8], -1.0, 1.0,
                                              ALU.mult, ALU.add))
    p.cop("vector", lambda v: v.tensor_scalar(MU[:, 24:32], MU[:, 16:24], -1.0, 1.0,
                                              ALU.mult, ALU.add))

    XL = [p.sb([128, 8 * TC], BF16, f"{name}_xl{i}") for i in range(2)]
    XP = [p.sb([128, 8 * TC], BF16, f"{name}_xp{i}") for i in range(2)]
    TT = p.sb([128, TC], F32, f"{name}_tt")
    XRO = p.sb([128, 8 * TC], BF16, f"{name}_xro")
    XKO = p.sb([128, 8 * TC], BF16, f"{name}_xko")
    s_xl = p.sems(f"{name}_xl", 2)
    s_xpl = p.sems(f"{name}_xpl", 2)
    s_ro = p.sems(f"{name}_ro", 2)
    lerp_done = []
    nch = TL // TC
    for c in range(nch):
        sl = c % 2
        if c >= 2:
            p.wait("sync", chv, lerp_done[c - 2][2])
        p.dma("sync", XL[sl][:, :].rearrange("q (kt t) -> q kt t", t=TC),
              d["xn"].rearrange("(kt q) t -> q kt t", q=128)[
                  :, :, HALO + c * TC : HALO + (c + 1) * TC], s_xl[sl])
        p.dma("sync", XP[sl][:, :].rearrange("q (kt t) -> q kt t", t=TC),
              d["xn"].rearrange("(kt q) t -> q kt t", q=128)[
                  :, :, HALO + c * TC - 1 : HALO + (c + 1) * TC - 1], s_xpl[sl])
        p.wait("vector", s_xl[sl])
        p.wait("vector", s_xpl[sl])
        if c >= 1:
            p.wait("vector", s_ro[1 - sl])
        if c >= 2:
            p.wait("vector", s_ro[sl])
        for kt in range(NKT):
            for rk in range(2):
                p.cop("vector", lambda v, sl=sl, kt=kt, rk=rk: v.tensor_scalar(
                    TT[:, :], XL[sl][:, kt * TC : (kt + 1) * TC],
                    MU[:, 16 * rk + kt : 16 * rk + kt + 1], None, ALU.mult))
                p.cop("vector", lambda v, sl=sl, kt=kt, rk=rk: v.scalar_tensor_tensor(
                    (XRO if rk == 0 else XKO)[:, kt * TC : (kt + 1) * TC],
                    XP[sl][:, kt * TC : (kt + 1) * TC],
                    MU[:, 16 * rk + 8 + kt : 16 * rk + 8 + kt + 1],
                    TT[:, :], ALU.mult, ALU.add))
        dn = p.vchain("vector")
        p.wait("scalar", chv, dn)
        p.dma("scalar", d["xr"].rearrange("(kt q) t -> q kt t", q=128)[
            :, :, c * TC : (c + 1) * TC],
            XRO[:, :].rearrange("q (kt t) -> q kt t", t=TC), s_ro[sl])
        p.dma("scalar", d["xk"].rearrange("(kt q) t -> q kt t", q=128)[
            :, :, c * TC : (c + 1) * TC],
            XKO[:, :].rearrange("q (kt t) -> q kt t", t=TC), s_ro[sl])
        lerp_done.append((s_ro[sl], s_ro[sl].n, p.vchain("vector")))
    p.barrier(waits=[(a, b) for a, b, _ in lerp_done])
    p.reset(m0)

    epi_r, s_so_r = epi_store(p, name=f"{name}_re", dst=d["rform"], dst_c0=0,
                              act=AF.Sigmoid)
    emit_gemm(p, name=f"{name}_rg", src=d["xr"], src_c0=0, wsrc=wk["wr_p"],
              d_in=D, d_out=D, cols=TL, epi=epi_r)
    p.barrier(waits=[(sm, sm.n) for sm in s_so_r])
    p.reset(m0)

    OBK = [p.sb([128, TC], BF16, f"{name}_obk{i}") for i in range(4)]
    TK = p.sb([128, TC], F32, f"{name}_tk")
    s_sok = p.sems(f"{name}_sok", 4)
    kctr = [0]

    def epi_k(p, c, o, psum, tc_here, s_mm):
        i = kctr[0]
        kctr[0] += 1
        sl = i % 4
        p.wait("vector", s_mm)
        if i >= 4:
            p.wait("vector", s_sok[sl])
        p.cop("vector", lambda v, psum=psum: v.tensor_scalar(
            TK[:, :], psum, 0.0, None, ALU.max))
        val = p.cop("vector", lambda v, sl=sl: v.tensor_mul(OBK[sl][:, :], TK[:, :], TK[:, :]))
        p.wait("scalar", chv, val)
        p.dma("scalar", d["k2"][o * 128 : (o + 1) * 128, c * TC : (c + 1) * TC],
              OBK[sl][:, :], s_sok[sl])
        return (chv, val)

    emit_gemm(p, name=f"{name}_kg", src=d["xk"], src_c0=0, wsrc=wk["wk_p"],
              d_in=D, d_out=D, cols=TL, epi=epi_k)
    p.barrier(waits=[(sm, sm.n) for sm in s_sok])
    p.reset(m0)

    OBV = [p.sb([128, TC], BF16, f"{name}_obv{i}") for i in range(4)]
    RV = [p.sb([128, TC], BF16, f"{name}_rv{i}") for i in range(4)]
    s_sov = p.sems(f"{name}_sov", 4)
    s_rl = p.sems(f"{name}_rl", 4)
    vctr = [0]
    vcons = {}

    def epi_v(p, c, o, psum, tc_here, s_mm):
        i = vctr[0]
        vctr[0] += 1
        sl = i % 4
        if i >= 4:
            p.wait("sync", chv, vcons[i - 4])
        p.dma("sync", RV[sl][:, :], d["rform"][o * 128 : (o + 1) * 128,
              c * TC : (c + 1) * TC], s_rl[sl])
        p.wait("vector", s_mm)
        p.wait("vector", s_rl[sl])
        if i >= 4:
            p.wait("vector", s_sov[sl])
        val = p.cop("vector", lambda v, sl=sl, psum=psum: v.tensor_tensor(
            OBV[sl][:, :], psum, RV[sl][:, :], ALU.mult))
        vcons[i] = val
        p.wait("scalar", chv, val)
        p.dma("scalar", d["g3"][o * 128 : (o + 1) * 128, c * TC : (c + 1) * TC],
              OBV[sl][:, :], s_sov[sl])
        return (chv, val)

    emit_gemm(p, name=f"{name}_vg", src=d["k2"], src_c0=0, wsrc=wk["wv_p"],
              d_in=D, d_out=D, cols=TL, epi=epi_v)
    p.barrier(waits=[(sm, sm.n) for sm in s_sov])
    p.reset(m0)


def l3ffn_host(inp):
    mur = np.asarray(inp["L3_ffn_mu_r"], np.float32)
    muk = np.asarray(inp["L3_ffn_mu_k"], np.float32)
    return dict(
        wr_p=pack_wT(np.asarray(inp["L3_ffn_W_r"])),
        wk_p=pack_wT(np.asarray(inp["L3_ffn_W_k"])),
        wv_p=pack_wT(np.asarray(inp["L3_ffn_W_v"])),
        mur_col=pack_vec(mur),
        muk_col=pack_vec(muk),
    )




def emit_combine(p, d, *, name, s_key, g_key, xin_key, xout_key, s_c0=0, g_c0=0,
                 s_dt=BF16, g_dt=BF16, out_is_flat=False):
    m0 = p.mark()
    chv = p.chain["vector"]
    chs = p.chain["scalar"]
    SB_ = [p.sb([128, 8 * TC], s_dt, f"{name}_s{i}") for i in range(2)]
    GB_ = [p.sb([128, 8 * TC], g_dt, f"{name}_g{i}") for i in range(2)]
    XB_ = [p.sb([128, 8 * TC], F32, f"{name}_x{i}") for i in range(2)]
    SGT = p.sb([128, TC], F32, f"{name}_sg")
    XO_ = [p.sb([128, 8 * TC], F32, f"{name}_xo{i}") for i in range(2)]
    s_sl = p.sems(f"{name}_sl", 2)
    s_gl = p.sems(f"{name}_gl", 2)
    s_xl = p.sems(f"{name}_xl", 2)
    s_oc = p.sems(f"{name}_oc", 2)
    done = []
    nch = TL // TC
    out_c0 = 0 if out_is_flat else HALO
    for c in range(nch):
        sl = c % 2
        if c >= 2:
            p.wait("sync", chv, done[c - 2][2])
        p.dma("sync", SB_[sl][:, :].rearrange("q (kt t) -> q kt t", t=TC),
              d[s_key].rearrange("(kt q) t -> q kt t", q=128)[
                  :, :, s_c0 + c * TC : s_c0 + (c + 1) * TC], s_sl[sl])
        p.dma("sync", GB_[sl][:, :].rearrange("q (kt t) -> q kt t", t=TC),
              d[g_key].rearrange("(kt q) t -> q kt t", q=128)[
                  :, :, g_c0 + c * TC : g_c0 + (c + 1) * TC], s_gl[sl])
        p.dma("sync", XB_[sl][:, :].rearrange("q (kt t) -> q kt t", t=TC),
              d[xin_key].rearrange("(kt q) t -> q kt t", q=128)[
                  :, :, HALO + c * TC : HALO + (c + 1) * TC], s_xl[sl])
        p.wait("scalar", s_gl[sl])
        p.wait("vector", s_sl[sl])
        p.wait("vector", s_xl[sl])
        if c >= 2:
            p.wait("vector", s_oc[sl])
            p.wait("scalar", s_oc[sl])
        for kt in range(NKT):
            gv = p.cop("scalar", lambda sc, sl=sl, kt=kt: sc.activation(
                GB_[sl][:, kt * TC : (kt + 1) * TC], GB_[sl][:, kt * TC : (kt + 1) * TC],
                AF.Silu))
            p.wait("vector", chs, gv)
            p.cop("vector", lambda v, sl=sl, kt=kt: v.tensor_tensor(
                SGT[:, :], SB_[sl][:, kt * TC : (kt + 1) * TC],
                GB_[sl][:, kt * TC : (kt + 1) * TC], ALU.mult))
            p.cop("vector", lambda v, sl=sl, kt=kt: v.tensor_tensor(
                XO_[sl][:, kt * TC : (kt + 1) * TC], XB_[sl][:, kt * TC : (kt + 1) * TC],
                SGT[:, :], ALU.add))
        dn = p.vchain("vector")
        p.wait("scalar", chv, dn)
        p.dma("scalar", d[xout_key].rearrange("(kt q) t -> q kt t", q=128)[
            :, :, out_c0 + c * TC : out_c0 + (c + 1) * TC],
            XO_[sl][:, :].rearrange("q (kt t) -> q kt t", t=TC), s_oc[sl])
        done.append((s_oc[sl], s_oc[sl].n, p.vchain("vector")))
    p.barrier(waits=[(a_, b_) for a_, b_, _ in done])
    p.reset(m0)
    return done




def emit_l2attn(p, d, *, name, wk):
    m0 = p.mark()
    chv = p.chain["vector"]
    chs = p.chain["scalar"]
    for nm, wkey, dst, act in (("r", "wr_p", "r2", AF.Sigmoid), ("k", "wk_p", "k2b", None),
                               ("v", "wv_p", "v2", None)):
        mm = p.mark()
        epi, s_so = epi_store(p, name=f"{name}_{nm}e", dst=d[dst], dst_c0=0, act=act)
        emit_gemm(p, name=f"{name}_{nm}g", src=d["xn"], src_c0=HALO, wsrc=wk[wkey],
                  d_in=D, d_out=D, cols=TL, epi=epi)
        p.barrier(waits=[(sm, sm.n) for sm in s_so])
        p.reset(mm)

    DC = p.sb([128, 8], F32, f"{name}_dc")
    LNW = p.sb([128, 8], F32, f"{name}_lnw")
    s_dc = p.sem(f"{name}_dc")
    p.dma("sync", DC[:, :], wk["decay_col"], s_dc)
    p.dma("sync", LNW[:, :], wk["lnw_col"], s_dc)
    p.wait("scalar", s_dc)
    p.wait("vector", s_dc)
    sv = p.cop("scalar", lambda sc: sc.activation(DC[:, :], DC[:, :], AF.Sigmoid))
    p.wait("vector", chs, sv)
    cv = p.cop("vector", lambda v: v.tensor_scalar(DC[:, :], DC[:, :], 1e-7, None, ALU.max))
    p.wait("scalar", chv, cv)
    lv = p.cop("scalar", lambda sc: sc.activation(DC[:, :], DC[:, :], AF.Ln))
    p.wait("vector", chs, lv)
    p.wait("tensor", chs, lv)
    NDC = p.sb([128, 8], F32, f"{name}_ndc")
    ndv = p.cop("vector", lambda v: v.tensor_scalar(NDC[:, :], DC[:, :], -1.0, None, ALU.mult))
    p.wait("scalar", chv, ndv)

    KT_ = p.sb([128, TL], BF16, f"{name}_kt")
    VT_ = p.sb([128, TL], BF16, f"{name}_vt")
    RT_ = p.sb([128, TL], BF16, f"{name}_rt")
    IOT = p.sb([128, TL], F32, f"{name}_iot")
    SCL = p.sb([128, TL], F32, f"{name}_scl")
    KV = p.sb([128, TL], F32, f"{name}_kv")
    CUM = p.sb([128, TL], F32, f"{name}_cum")
    YT2 = p.sb([128, TL], F32, f"{name}_yt2")
    SQT = p.sb([128, TC], BF16, f"{name}_sqt")
    ZR = p.sb([128, 8], F32, f"{name}_zr")
    STL = p.sb([128, 16], F32, f"{name}_stl")
    ONESB = p.sb([128, 1], BF16, f"{name}_onesb")
    ONES1 = p.sb([1, 128], F32, f"{name}_ones1f")
    RSD = p.sb([1, 8 * TC], F32, f"{name}_rsd")
    YN = [p.sb([128, 8 * TC], BF16, f"{name}_yn{i}") for i in range(2)]
    PS_SS = p.ps([1, 512], F32)
    PS_RB = p.ps([128, 512], F32)
    s_ld = p.sems(f"{name}_ld", 4)
    s_io = p.sems(f"{name}_io", 2)
    s_ag = p.sem(f"{name}_ag")
    s_st = p.sem(f"{name}_st")
    s_ssum = p.sem(f"{name}_ssum")
    s_rb = p.sem(f"{name}_rb")
    s_yn = p.sems(f"{name}_yn", 2)
    iv = p.cop("vector", lambda v: v.memset(ONESB[:, :], 1.0))
    iv2 = p.cop("vector", lambda v: v.memset(ONES1[:, :], 1.0))
    zv = p.cop("vector", lambda v: v.memset(ZR[:, :], 0.0))
    p.wait("tensor", chv, zv)

    tile_done = {}
    prev_ev = None
    for dt_i in range(8):
        if dt_i >= 1:
            p.wait("sync", chv, tile_done.get(dt_i - 1, 0))
            p.wait("vector", s_st)
            p.wait("gpsimd", chv, prev_ev)
        p.dma("sync", KT_[:, :], d["k2b"][dt_i * 128 : (dt_i + 1) * 128, :], s_ld[0])
        p.dma("sync", VT_[:, :], d["v2"][dt_i * 128 : (dt_i + 1) * 128, :], s_ld[1])
        p.dma("gpsimd", IOT[:, :], wk["iota"][:, 0:TL], s_io[0])
        p.wait("vector", s_ld[0])
        p.wait("vector", s_ld[1])
        p.wait("vector", s_io[0])
        mv0 = p.cop("vector", lambda v, dt_i=dt_i: v.tensor_scalar(
            SCL[:, :], IOT[:, :], NDC[:, dt_i : dt_i + 1], 23.025850929940457,
            ALU.mult, ALU.min))
        prev_ev = mv0
        p.wait("scalar", chv, mv0)
        ev = p.cop("scalar", lambda sc: sc.activation(SCL[:, :], SCL[:, :], AF.Exp))
        p.cop("vector", lambda v: v.tensor_mul(KV[:, :], KT_[:, :], VT_[:, :]))
        p.wait("vector", chs, ev)
        p.cop("vector", lambda v: v.tensor_tensor(KV[:, :], KV[:, :], SCL[:, :], ALU.mult))
        p.cop("vector", lambda v: v.tensor_tensor_scan(
            CUM[:, :], KV[:, :], KV[:, :], 0.0, ALU.add, ALU.bypass))
        sv2 = p.cop("vector", lambda v, dt_i=dt_i: v.tensor_copy(
            STL[:, dt_i : dt_i + 1], CUM[:, TL - 1 : TL]))
        tile_done[dt_i] = sv2
        p.wait("scalar", chv, sv2)
        p.dma("scalar", d["cum2"][dt_i * 128 : (dt_i + 1) * 128, :], CUM[:, :], s_st)
    p.wait("gpsimd", chv, tile_done[7])
    s_cc = p.sem(f"{name}_cc")
    p.dma("gpsimd", d["cc2_in"], STL[:, 0:8], s_cc)
    p.wait("gpsimd", s_cc)
    p.raw("gpsimd", lambda g: g.collective_compute(
        "AllGather", ALU.bypass, replica_groups=[[0, 1], [2, 3], [4, 5], [6, 7]],
        ins=[d["cc2_in"].opt()], outs=[d["cc2_out"].opt()],
    ).then_inc(s_ag.h, 1))
    s_ag.n += 1
    p.wait("gpsimd", s_ag)
    s_si = p.sem(f"{name}_si")
    p.dma("gpsimd", STL[:, 8:16], d["cc2_out"][0:128, :], s_si)
    HM2 = p.sb([128, 1], F32, f"{name}_hm2")
    s_hm2 = p.sem(f"{name}_hm2")
    p.dma("sync", HM2[:, :], d["hm_col"], s_hm2)
    p.wait("vector", s_si)
    p.wait("vector", s_hm2)
    p.cop("vector", lambda v: v.tensor_scalar(STL[:, 8:16], STL[:, 8:16], HM2[:, 0:1],
                                              None, ALU.mult))
    p.wait("vector", s_st)
    p.wait("sync", s_st)

    prev_ev2 = None
    prev_yv = None
    for dt_i in range(8):
        if dt_i >= 1:
            p.wait("vector", s_st)
            p.wait("gpsimd", chs, prev_ev2)
            p.wait("sync", chv, prev_yv)
            p.wait("sync", s_st)
        p.dma("sync", CUM[:, :], d["cum2"][dt_i * 128 : (dt_i + 1) * 128, :], s_ld[2])
        p.dma("sync", RT_[:, :], d["r2"][dt_i * 128 : (dt_i + 1) * 128, :], s_ld[3])
        p.dma("gpsimd", IOT[:, :], wk["iota"][:, 0:TL], s_io[1])
        p.wait("vector", s_ld[2])
        p.wait("vector", s_ld[3])
        p.wait("scalar", s_io[1])
        ev = p.cop("scalar", lambda sc, dt_i=dt_i: sc.activation(
            SCL[:, :], IOT[:, :], AF.Exp, scale=DC[:, dt_i : dt_i + 1]))
        prev_ev2 = ev
        p.wait("vector", chs, ev)
        stcol = STL[:, 8 + dt_i : 9 + dt_i]
        p.cop("vector", lambda v, stcol=stcol: v.tensor_scalar(
            CUM[:, :], CUM[:, :], stcol, None, ALU.add))
        p.cop("vector", lambda v: v.tensor_mul(CUM[:, :], CUM[:, :], SCL[:, :]))
        yv = p.cop("vector", lambda v: v.tensor_mul(YT2[:, :], CUM[:, :], RT_[:, :]))
        prev_yv = yv
        p.wait("scalar", chv, yv)
        p.dma("scalar", d["y2s"][dt_i * 128 : (dt_i + 1) * 128, :], YT2[:, :], s_st)
        for c in range(TL // TC):
            if dt_i > 0 or c > 0:
                p.wait("scalar", s_ssum)
            sq = p.cop("scalar", lambda sc, c=c: sc.activation(
                SQT[:, :], YT2[:, c * TC : (c + 1) * TC], AF.Square))
            last_sq = sq
            p.wait("tensor", chs, sq)
            p.wait("tensor", chv)
            mv = p.top(lambda t: t.matmul(
                PS_SS[0:1, :], ONESB[:, :], SQT[:, :], start=True, stop=True),
                sem=s_ssum)
            p.wait("vector", s_ssum, mv)
            if dt_i == 0:
                p.cop("vector", lambda v, c=c: v.tensor_copy(
                    RSD[0:1, c * TC : (c + 1) * TC], PS_SS[0:1, :]))
            else:
                p.cop("vector", lambda v, c=c: v.tensor_tensor(
                    RSD[0:1, c * TC : (c + 1) * TC],
                    RSD[0:1, c * TC : (c + 1) * TC], PS_SS[0:1, :], ALU.add))
    EPS2 = p.sb([1, 1], F32, f"{name}_eps2")
    p.cop("vector", lambda v: v.memset(EPS2[:, :], 1e-6))
    rv = p.cop("vector", lambda v: v.tensor_scalar(RSD[0:1, :], RSD[0:1, :], 1.0 / D, None, ALU.mult))
    p.wait("scalar", chv, rv)
    p.cop("scalar", lambda sc: sc.activation(RSD[0:1, :], RSD[0:1, :], AF.Ln, bias=EPS2[0:1, 0:1]))
    rv2 = p.cop("scalar", lambda sc: sc.activation(RSD[0:1, :], RSD[0:1, :], AF.Exp, scale=-0.5))
    p.wait("tensor", chs, rv2)
    p.wait("vector", s_st)
    p.wait("sync", chs, last_sq)
    p.wait("sync", s_st)

    ynctr = [0]
    for dt_i in range(8):
        sl = dt_i % 2
        if dt_i >= 1:
            p.wait("sync", chv)
        p.dma("sync", YT2[:, :], d["y2s"][dt_i * 128 : (dt_i + 1) * 128, :], s_ld[0])
        p.wait("vector", s_ld[0])
        if dt_i >= 2:
            p.wait("vector", s_yn[sl])
        for c in range(TL // TC):
            if ynctr[0] > 0:
                p.wait("tensor", chv, ynctr[1] if False else ynctr[0])
            bv = p.top(lambda t, c=c: t.matmul(
                PS_RB[:, :], ONES1[0:1, :], RSD[0:1, c * TC : (c + 1) * TC],
                start=True, stop=True), sem=s_rb)
            p.wait("vector", s_rb, bv)
            p.cop("vector", lambda v, c=c: v.tensor_tensor(
                YT2[:, c * TC : (c + 1) * TC], YT2[:, c * TC : (c + 1) * TC],
                PS_RB[:, :], ALU.mult))
            vv = p.cop("vector", lambda v, sl=sl, c=c, dt_i=dt_i: v.tensor_scalar(
                YN[sl][:, c * TC : (c + 1) * TC], YT2[:, c * TC : (c + 1) * TC],
                LNW[:, dt_i : dt_i + 1], None, ALU.mult))
            ynctr[0] = vv
        dn = p.vchain("vector")
        p.wait("scalar", chv, dn)
        p.dma("scalar", d["yn2"][dt_i * 128 : (dt_i + 1) * 128, :],
              YN[sl][:, :].rearrange("q (c t) -> q (c t)", t=TC), s_yn[sl])
    p.barrier(waits=[(s_yn[0], s_yn[0].n), (s_yn[1], s_yn[1].n)])
    p.reset(m0)

    epi_o, s_so_o = epi_store(p, name=f"{name}_oe", dst=d["s2"], dst_c0=0)
    emit_gemm(p, name=f"{name}_og", src=d["yn2"], src_c0=0, wsrc=wk["wo_p"],
              d_in=D, d_out=D, cols=TL, epi=epi_o)
    p.barrier(waits=[(sm, sm.n) for sm in s_so_o])
    p.reset(m0)


def l2attn_host(inp):
    return dict(
        wr_p=pack_wT(np.asarray(inp["L2_attn_Wr"])),
        wk_p=pack_wT(np.asarray(inp["L2_attn_Wk"])),
        wv_p=pack_wT(np.asarray(inp["L2_attn_Wv"])),
        wo_p=pack_wT(np.asarray(inp["L2_attn_Wo"])),
        decay_col=pack_vec(np.asarray(inp["L2_attn_decay"])),
        lnw_col=pack_vec(np.asarray(inp["L2_attn_ln_w"])),
        iota=np.tile(np.arange(T, dtype=np.float32), (128, 1)),
    )


def emit_l2ffn(p, d, *, name, wk):
    m0 = p.mark()
    chv = p.chain["vector"]
    CV = p.sb([128, 4096], I32, f"{name}_cv")
    CB = p.sb([128, 4096], BF16, f"{name}_cb")
    s_cl = p.sem(f"{name}_cl")
    s_cs = p.sem(f"{name}_cs")
    for wi, wb, ncols in (("wg_i", "wg_b", 8 * 3072), ("wu_i", "wu_b", 8 * 3072),
                          ("wd_i", "wd_b", 24 * 1024)):
        for c0 in range(0, ncols, 4096):
            cw = min(4096, ncols - c0)
            p.wait("sync", chv)
            p.dma("sync", CV[:, :cw], d[wi][:, c0 : c0 + cw], s_cl)
            p.wait("vector", s_cl)
            if c0 > 0 or wi != "wg_i":
                p.wait("vector", s_cs)
            vv = p.cop("vector", lambda v, cw=cw: v.tensor_copy(CB[:, :cw], CV[:, :cw]))
            p.wait("scalar", chv, vv)
            p.dma("scalar", d[wb][:, c0 : c0 + cw], CB[:, :cw], s_cs)
    p.wait("sync", s_cs)
    p.wait("gpsimd", s_cs)

    epi_g, s_so_g = epi_store(p, name=f"{name}_ge", dst=d["gsig"], dst_c0=0,
                              act=AF.Silu, scale=1.0 / 127.0)
    emit_gemm(p, name=f"{name}_gg", src=d["xn"], src_c0=HALO, wsrc=d["wg_b"],
              d_in=D, d_out=3072, cols=TL, epi=epi_g, w_bf16=True)
    p.barrier(waits=[(sm, sm.n) for sm in s_so_g])
    p.reset(m0)

    OBU = [p.sb([128, TC], BF16, f"{name}_obu{i}") for i in range(4)]
    GSB2 = [p.sb([128, TC], BF16, f"{name}_gsb{i}") for i in range(4)]
    s_gl2 = p.sems(f"{name}_gl2", 4)
    s_sou = p.sems(f"{name}_sou", 4)
    uctr = [0]
    ucons = {}

    def epi_u(p, c, o, psum, tc_here, s_mm):
        i = uctr[0]
        uctr[0] += 1
        sl = i % 4
        if i >= 4:
            p.wait("sync", chv, ucons[i - 4])
        p.dma("sync", GSB2[sl][:, :], d["gsig"][o * 128 : (o + 1) * 128,
              c * TC : (c + 1) * TC], s_gl2[sl])
        p.wait("vector", s_mm)
        p.wait("vector", s_gl2[sl])
        if i >= 4:
            p.wait("vector", s_sou[sl])
        val = p.cop("vector", lambda v, sl=sl, psum=psum: v.scalar_tensor_tensor(
            OBU[sl][:, :], psum, 1.0 / 127.0, GSB2[sl][:, :], ALU.mult, ALU.mult))
        ucons[i] = val
        p.wait("scalar", chv, val)
        p.dma("scalar", d["up_h"][o * 128 : (o + 1) * 128, c * TC : (c + 1) * TC],
              OBU[sl][:, :], s_sou[sl])
        return (chv, val)

    emit_gemm(p, name=f"{name}_ug", src=d["xn"], src_c0=HALO, wsrc=d["wu_b"],
              d_in=D, d_out=3072, cols=TL, epi=epi_u, w_bf16=True)
    p.barrier(waits=[(sm, sm.n) for sm in s_sou])
    p.reset(m0)

    epi_d, s_so_d = epi_store(p, name=f"{name}_de", dst=d["g2"], dst_c0=0,
                              scale=1.0 / 127.0)
    emit_gemm(p, name=f"{name}_dg", src=d["up_h"], src_c0=0, wsrc=d["wd_b"],
              d_in=3072, d_out=D, cols=TL, epi=epi_d, w_bf16=True)
    p.barrier(waits=[(sm, sm.n) for sm in s_so_d])
    p.reset(m0)


def l2ffn_host(inp):
    return dict(
        wg_i=pack_w(np.ascontiguousarray(np.asarray(inp["L2_ffn_w_gate"]).T)).astype(np.int32),
        wu_i=pack_w(np.ascontiguousarray(np.asarray(inp["L2_ffn_w_up"]).T)).astype(np.int32),
        wd_i=pack_w(np.ascontiguousarray(np.asarray(inp["L2_ffn_w_down"]).T)).astype(np.int32),
    )




def emit_l1attn(p, d, *, name, wk):
    m0 = p.mark()
    chv = p.chain["vector"]
    chs = p.chain["scalar"]
    NBLK = TL // 128

    ABC = p.sb([128, 56], F32, f"{name}_abc")
    IDT = p.sb([128, 128], BF16, f"{name}_idt")
    s_k0 = p.sem(f"{name}_k0")
    s_k1 = p.sem(f"{name}_k1")
    p.dma("sync", ABC[:, :], wk["abc_col"], s_k0)
    p.dma("gpsimd", IDT[:, :], wk["ident_p"], s_k1)
    p.wait("vector", s_k0)
    p.wait("tensor", s_k1)

    PR = p.sb([128, 8 * 128], F32, f"{name}_pr")
    PI = p.sb([128, 8 * 128], F32, f"{name}_pi")
    KK = p.sb([128, 8 * 128], BF16, f"{name}_kk")
    WR = p.sb([128, 8 * 128], F32, f"{name}_wr")
    WI = p.sb([128, 8 * 128], F32, f"{name}_wi")
    ER = p.sb([128, 8 * 128], F32, f"{name}_er")
    EI = p.sb([128, 8 * 128], F32, f"{name}_ei")
    A128 = p.sb([128, 8 * 4], F32, f"{name}_a128")
    TC1 = p.sb([128, 128], F32, f"{name}_tc1")
    TC2 = p.sb([128, 128], F32, f"{name}_tc2")
    for dt_i in range(8):
        pr = PR[:, dt_i * 128 : (dt_i + 1) * 128]
        pi = PI[:, dt_i * 128 : (dt_i + 1) * 128]
        ar = ABC[:, dt_i * 7 + 0 : dt_i * 7 + 1]
        ai = ABC[:, dt_i * 7 + 1 : dt_i * 7 + 2]
        br = ABC[:, dt_i * 7 + 2 : dt_i * 7 + 3]
        bi = ABC[:, dt_i * 7 + 3 : dt_i * 7 + 4]
        cr = ABC[:, dt_i * 7 + 4 : dt_i * 7 + 5]
        ci = ABC[:, dt_i * 7 + 5 : dt_i * 7 + 6]
        p.cop("vector", lambda v, pr=pr: v.memset(pr[:, 0:1], 1.0))
        p.cop("vector", lambda v, pi=pi: v.memset(pi[:, 0:1], 0.0))
        p.cop("vector", lambda v, pr=pr, ar=ar: v.tensor_copy(pr[:, 1:2], ar))
        p.cop("vector", lambda v, pi=pi, ai=ai: v.tensor_copy(pi[:, 1:2], ai))
        kk = 1
        while kk < 127:
            cnt = min(kk, 127 - kk)
            p.cop("vector", lambda v, pr=pr, kk=kk, cnt=cnt: v.tensor_scalar(
                TC1[:, :cnt], pr[:, 1 : 1 + cnt], pr[:, kk : kk + 1], None, ALU.mult))
            p.cop("vector", lambda v, pi=pi, kk=kk, cnt=cnt: v.tensor_scalar(
                TC2[:, :cnt], pi[:, 1 : 1 + cnt], pi[:, kk : kk + 1], None, ALU.mult))
            p.cop("vector", lambda v, cnt=cnt: v.tensor_sub(
                TC1[:, :cnt], TC1[:, :cnt], TC2[:, :cnt]))
            p.cop("vector", lambda v, pr=pr, pi=pi, kk=kk, cnt=cnt: v.tensor_scalar(
                TC2[:, :cnt], pr[:, 1 : 1 + cnt], pi[:, kk : kk + 1], None, ALU.mult))
            p.cop("vector", lambda v, pr=pr, pi=pi, kk=kk, cnt=cnt: v.scalar_tensor_tensor(
                pi[:, kk + 1 : kk + 1 + cnt], pi[:, 1 : 1 + cnt], pr[:, kk : kk + 1],
                TC2[:, :cnt], ALU.mult, ALU.add))
            p.cop("vector", lambda v, pr=pr, kk=kk, cnt=cnt: v.tensor_copy(
                pr[:, kk + 1 : kk + 1 + cnt], TC1[:, :cnt]))
            kk *= 2
        p.cop("vector", lambda v, pr=pr: v.tensor_mul(TC1[:, 0:1], pr[:, 64:65], pr[:, 64:65]))
        p.cop("vector", lambda v, pi=pi: v.tensor_mul(TC2[:, 0:1], pi[:, 64:65], pi[:, 64:65]))
        p.cop("vector", lambda v, dt_i=dt_i: v.tensor_sub(
            A128[:, 4 * dt_i : 4 * dt_i + 1], TC1[:, 0:1], TC2[:, 0:1]))
        p.cop("vector", lambda v, pr=pr, pi=pi: v.tensor_mul(TC1[:, 0:1], pr[:, 64:65], pi[:, 64:65]))
        p.cop("vector", lambda v, dt_i=dt_i: v.tensor_scalar(
            A128[:, 4 * dt_i + 1 : 4 * dt_i + 2], TC1[:, 0:1], 2.0, None, ALU.mult))
        p.cop("vector", lambda v, cr=cr, br=br: v.tensor_mul(TC2[:, 4:5], cr, br))
        p.cop("vector", lambda v, ci=ci, bi=bi: v.tensor_mul(TC2[:, 6:7], ci, bi))
        p.cop("vector", lambda v: v.tensor_sub(TC2[:, 4:5], TC2[:, 4:5], TC2[:, 6:7]))
        p.cop("vector", lambda v, cr=cr, bi=bi: v.tensor_mul(TC2[:, 5:6], cr, bi))
        p.cop("vector", lambda v, ci=ci, br=br: v.scalar_tensor_tensor(
            TC2[:, 5:6], br, ci[:, 0:1], TC2[:, 5:6], ALU.mult, ALU.add))
        p.cop("vector", lambda v, pr=pr: v.tensor_scalar(
            TC1[:, 0:128], pr, TC2[:, 4:5], None, ALU.mult))
        p.cop("vector", lambda v, pi=pi, dt_i=dt_i: v.tensor_scalar(
            ER[:, dt_i * 128 : (dt_i + 1) * 128], pi, TC2[:, 5:6], None, ALU.mult))
        p.cop("vector", lambda v, dt_i=dt_i: v.tensor_sub(
            KK[:, dt_i * 128 : (dt_i + 1) * 128], TC1[:, 0:128],
            ER[:, dt_i * 128 : (dt_i + 1) * 128]))
        p.cop("vector", lambda v, pr=pr, br=br, dt_i=dt_i: v.tensor_scalar(
            WR[:, dt_i * 128 : (dt_i + 1) * 128], pr[:, 127::-1], br[:, 0:1], None, ALU.mult))
        p.cop("vector", lambda v, pi=pi, bi=bi: v.tensor_scalar(
            TC1[:, 0:128], pi[:, 127::-1], bi[:, 0:1], None, ALU.mult))
        p.cop("vector", lambda v, dt_i=dt_i: v.tensor_sub(
            WR[:, dt_i * 128 : (dt_i + 1) * 128],
            WR[:, dt_i * 128 : (dt_i + 1) * 128], TC1[:, 0:128]))
        p.cop("vector", lambda v, pi=pi, br=br, dt_i=dt_i: v.tensor_scalar(
            WI[:, dt_i * 128 : (dt_i + 1) * 128], pi[:, 127::-1], br[:, 0:1], None, ALU.mult))
        p.cop("vector", lambda v, pr=pr, bi=bi, dt_i=dt_i: v.scalar_tensor_tensor(
            WI[:, dt_i * 128 : (dt_i + 1) * 128], pr[:, 127::-1], bi[:, 0:1],
            WI[:, dt_i * 128 : (dt_i + 1) * 128], ALU.mult, ALU.add))
        p.cop("vector", lambda v, pr=pr, cr=cr, dt_i=dt_i: v.tensor_scalar(
            ER[:, dt_i * 128 : dt_i * 128 + 127], pr[:, 1:128], cr[:, 0:1], None, ALU.mult))
        p.cop("vector", lambda v, pi=pi, ci=ci: v.tensor_scalar(
            TC1[:, 0:127], pi[:, 1:128], ci[:, 0:1], None, ALU.mult))
        p.cop("vector", lambda v, dt_i=dt_i: v.tensor_sub(
            ER[:, dt_i * 128 : dt_i * 128 + 127],
            ER[:, dt_i * 128 : dt_i * 128 + 127], TC1[:, 0:127]))
        p.cop("vector", lambda v, pi=pi, cr=cr, dt_i=dt_i: v.tensor_scalar(
            EI[:, dt_i * 128 : dt_i * 128 + 127], pi[:, 1:128], cr[:, 0:1], None, ALU.mult))
        p.cop("vector", lambda v, pr=pr, ci=ci, dt_i=dt_i: v.scalar_tensor_tensor(
            EI[:, dt_i * 128 : dt_i * 128 + 127], pr[:, 1:128], ci[:, 0:1],
            EI[:, dt_i * 128 : dt_i * 128 + 127], ALU.mult, ALU.add))
        p.cop("vector", lambda v, cr=cr, dt_i=dt_i: v.tensor_mul(
            TC1[:, 2:3], cr, A128[:, 4 * dt_i : 4 * dt_i + 1]))
        p.cop("vector", lambda v, ci=ci, dt_i=dt_i: v.tensor_mul(
            TC2[:, 2:3], ci, A128[:, 4 * dt_i + 1 : 4 * dt_i + 2]))
        p.cop("vector", lambda v, dt_i=dt_i: v.tensor_sub(
            ER[:, dt_i * 128 + 127 : (dt_i + 1) * 128], TC1[:, 2:3], TC2[:, 2:3]))
        p.cop("vector", lambda v, ci=ci, dt_i=dt_i: v.tensor_mul(
            TC1[:, 2:3], ci, A128[:, 4 * dt_i : 4 * dt_i + 1]))
        p.cop("vector", lambda v, cr=cr, dt_i=dt_i: v.scalar_tensor_tensor(
            EI[:, dt_i * 128 + 127 : (dt_i + 1) * 128],
            A128[:, 4 * dt_i + 1 : 4 * dt_i + 2], cr[:, 0:1], TC1[:, 2:3],
            ALU.mult, ALU.add))
        kprep = p.vchain("vector")
    p.wait("scalar", chv, kprep)

    ZB1 = p.sb([128, 1024], BF16, f"{name}_zb1")
    s_z1 = p.sem(f"{name}_z1")
    zv = p.cop("vector", lambda v: v.memset(ZB1[:, :], 0.0))
    p.wait("scalar", chv, zv)
    p.dma("scalar", d["z1"].rearrange("(dt q) c -> q dt c", q=128)[:, :, 0:128],
          ZB1[:, :].rearrange("q (dt c) -> q dt c", c=128), s_z1)
    p.dma("scalar", d["z1"].rearrange("(dt q) c -> q dt c", q=128)[:, :, 128:256],
          KK[:, :].rearrange("q (dt c) -> q dt c", c=128), s_z1)
    p.wait("scalar", s_z1)
    s_bt = emit_toeplitz_build(p, name=f"{name}_tb", z_t=d["z1"].tensor, z_off=0,
                               t_t=d["t1m"].tensor, t_off=0)

    UT = p.sb([128, NBLK * 128], BF16, f"{name}_ut")
    TB1 = p.sb([128, 128 * 128], BF16, f"{name}_tb1")
    UB = p.sb([128, TL], BF16, f"{name}_ub")
    SJR = p.sb([128, 8 * NBLK], F32, f"{name}_sjr")
    SJI = p.sb([128, 8 * NBLK], F32, f"{name}_sji")
    TMW = p.sb([128, TL], F32, f"{name}_tmw")
    YTM = p.sb([128, NBLK * 128], BF16, f"{name}_ytm")
    YO1 = p.sb([128, NBLK * 128], F32, f"{name}_yo1")
    PS_C1 = [p.ps([128, 64], F32) for i in range(2)]
    PS_T1 = p.ps([128, 128], BF16)
    PS_T1b = p.ps([128, 128], BF16)
    s_utl = p.sem(f"{name}_utl")
    s_ubl = p.sem(f"{name}_ubl")
    s_tbl = p.sem(f"{name}_tbl")
    s_cm1 = p.sem(f"{name}_cm1")
    s_tr1 = p.sem(f"{name}_tr1")
    s_y1 = p.sem(f"{name}_y1")
    p.wait("sync", s_bt)
    p.wait("scalar", s_bt)

    hist = {}
    for dt_i in range(8):
        if dt_i >= 1:
            p.wait("sync", chv)
            p.wait("sync", s_cm1)
        p.dma("sync", UB[:, :], d["xn"][dt_i * 128 : (dt_i + 1) * 128, HALO:], s_ubl)
        for blk in range(NBLK):
            p.dma("sync", UT[:, blk * 128 : (blk + 1) * 128],
                  d["xn"][dt_i * 128 : (dt_i + 1) * 128,
                          HALO + blk * 128 : HALO + (blk + 1) * 128],
                  s_utl, transpose=True)
        p.dma("sync", TB1[:, :],
              bass.AP(d["t1m"].tensor, dt_i * 128 * 128, [[1024 * 128, 128], [1, 128 * 128]]),
              s_tbl)
        p.wait("vector", s_ubl)
        wre = WR[:, dt_i * 128 : (dt_i + 1) * 128]
        wim = WI[:, dt_i * 128 : (dt_i + 1) * 128]
        for ri, wv_ in ((0, wre), (1, wim)):
            p.cop("vector", lambda v, wv_=wv_: v.tensor_tensor(
                TMW[:, :].rearrange("q (b s) -> q b s", s=128), 
                UB[:, :].rearrange("q (b s) -> q b s", s=128),
                wv_[:, None, :].broadcast_to([128, NBLK, 128]), ALU.mult))
            p.cop("vector", lambda v, ri=ri, dt_i=dt_i: v.reduce_sum(
                (SJR if ri == 0 else SJI)[:, dt_i * NBLK : (dt_i + 1) * NBLK].rearrange(
                    "q b -> q b 1" if False else "q b -> q b", ) if False else
                (SJR if ri == 0 else SJI)[:, dt_i * NBLK : (dt_i + 1) * NBLK],
                TMW[:, :].rearrange("q (b s) -> q b s", s=128), axis=AX.X))
        p.wait("tensor", s_tbl)
        p.wait("tensor", s_utl)
        p.wait("vector", s_tr1)
        cg = {}
        for ch in range(128):
            bnk = ch % 2
            if ch >= 2:
                p.wait("tensor", chv, cg[ch - 2])
            elif dt_i > 0:
                p.wait("tensor", chv, hist[("cglast", dt_i - 1)])
            mv = p.top(lambda t, ch=ch, bnk=bnk: t.matmul(
                PS_C1[bnk][:, :NBLK], TB1[:, ch * 128 : (ch + 1) * 128],
                UT[:, ch : ch + (NBLK - 1) * 128 + 1 : 128], start=True, stop=True),
                sem=s_cm1)
            p.wait("vector", s_cm1, mv)
            cg[ch] = p.cop("vector", lambda v, ch=ch, bnk=bnk: v.tensor_copy(
                YTM[:, ch : ch + (NBLK - 1) * 128 + 1 : 128], PS_C1[bnk][:, :NBLK]))
        hist[("cglast", dt_i)] = cg[127]
        p.wait("tensor", chv, cg[127])
        p.wait("vector", s_y1)
        cpv = {}
        for blk in range(NBLK):
            pst = PS_T1 if blk % 2 == 0 else PS_T1b
            if blk >= 2:
                p.wait("tensor", chv, cpv[blk - 2])
            tv = p.top(lambda t, blk=blk, pst=pst: t.transpose(
                pst[:, :], YTM[:, blk * 128 : (blk + 1) * 128], IDT[:, :]), sem=s_tr1)
            p.wait("vector", s_tr1, tv)
            cpv[blk] = p.cop("vector", lambda v, blk=blk, pst=pst: v.tensor_copy(
                YO1[:, blk * 128 : (blk + 1) * 128], pst[:, :]))
        p.wait("scalar", chv, cpv[NBLK - 1])
        p.dma("scalar", d["yraw1"][dt_i * 128 : (dt_i + 1) * 128, :], YO1[:, :], s_y1)
    sj_done = p.vchain("vector")

    HR = p.sb([128, 8 * (NBLK + 1)], F32, f"{name}_hr")
    HI = p.sb([128, 8 * (NBLK + 1)], F32, f"{name}_hi")
    T3 = p.sb([128, 8], F32, f"{name}_t3")
    T4 = p.sb([128, 8], F32, f"{name}_t4")
    STI = p.sb([128, 16], F32, f"{name}_sti")
    s_ag1 = p.sem(f"{name}_ag1")
    s_ci = p.sem(f"{name}_ci")
    hr3 = HR[:, :].rearrange("q (dt j) -> q j dt", dt=8)
    hi3 = HI[:, :].rearrange("q (dt j) -> q j dt", dt=8)
    sjr3 = SJR[:, :].rearrange("q (dt j) -> q j dt", dt=8)
    sji3 = SJI[:, :].rearrange("q (dt j) -> q j dt", dt=8)
    a128r = A128[:, 0::4]
    a128i = A128[:, 1::4]
    def emit_hscan(init_from_sti):
        p.cop("vector", lambda v: v.memset(hr3[:, 0, :], 0.0))
        p.cop("vector", lambda v: v.memset(hi3[:, 0, :], 0.0))
        for J in range(NBLK):
            p.cop("vector", lambda v, J=J: v.tensor_tensor(
                T3[:, :], hr3[:, J, :], a128r.broadcast_to([128, 8]) if False else a128r,
                ALU.mult))
            p.cop("vector", lambda v, J=J: v.tensor_tensor(
                T4[:, :], hi3[:, J, :], a128i, ALU.mult))
            p.cop("vector", lambda v, J=J: v.tensor_sub(T3[:, :], T3[:, :], T4[:, :]))
            p.cop("vector", lambda v, J=J: v.tensor_tensor(
                hr3[:, J + 1, :], T3[:, :], sjr3[:, J, :], ALU.add))
            p.cop("vector", lambda v, J=J: v.tensor_tensor(
                T3[:, :], hr3[:, J, :], a128i, ALU.mult))
            p.cop("vector", lambda v, J=J: v.tensor_tensor(
                T4[:, :], hi3[:, J, :], a128r, ALU.mult))
            p.cop("vector", lambda v, J=J: v.tensor_add(T3[:, :], T3[:, :], T4[:, :]))
            p.cop("vector", lambda v, J=J: v.tensor_tensor(
                hi3[:, J + 1, :], T3[:, :], sji3[:, J, :], ALU.add))

    emit_hscan(False)
    hs_done = p.vchain("vector")
    p.wait("gpsimd", chv, hs_done)
    p.dma("gpsimd", d["cc1_in"][:, 0:8], hr3[:, NBLK, :], s_ci,
          allow_slow_non_contiguous=True)
    p.dma("gpsimd", d["cc1_in"][:, 8:16], hi3[:, NBLK, :], s_ci,
          allow_slow_non_contiguous=True)
    p.wait("gpsimd", s_ci)
    p.raw("gpsimd", lambda g: g.collective_compute(
        "AllGather", ALU.bypass, replica_groups=[[0, 1], [2, 3], [4, 5], [6, 7]],
        ins=[d["cc1_in"].opt()], outs=[d["cc1_out"].opt()]).then_inc(s_ag1.h, 1))
    s_ag1.n += 1
    p.wait("gpsimd", s_ag1)
    s_si1 = p.sem(f"{name}_si1")
    p.dma("gpsimd", STI[:, :], d["cc1_out"][0:128, :], s_si1)
    HM1 = p.sb([128, 1], F32, f"{name}_hm1")
    s_hm1 = p.sem(f"{name}_hm1")
    p.dma("sync", HM1[:, :], d["hm_col"], s_hm1)
    p.wait("vector", s_si1)
    p.wait("vector", s_hm1)
    p.cop("vector", lambda v: v.tensor_scalar(STI[:, :], STI[:, :], HM1[:, 0:1], None, ALU.mult))
    APR = p.sb([128, 8 * (NBLK + 1)], F32, f"{name}_apr")
    API = p.sb([128, 8 * (NBLK + 1)], F32, f"{name}_api")
    apr3 = APR[:, :].rearrange("q (dt j) -> q j dt", dt=8)
    api3 = API[:, :].rearrange("q (dt j) -> q j dt", dt=8)
    p.cop("vector", lambda v: v.memset(apr3[:, 0, :], 1.0))
    p.cop("vector", lambda v: v.memset(api3[:, 0, :], 0.0))
    for J in range(NBLK):
        p.cop("vector", lambda v, J=J: v.tensor_tensor(T3[:, :], apr3[:, J, :], a128r, ALU.mult))
        p.cop("vector", lambda v, J=J: v.tensor_tensor(T4[:, :], api3[:, J, :], a128i, ALU.mult))
        p.cop("vector", lambda v, J=J: v.tensor_sub(apr3[:, J + 1, :], T3[:, :], T4[:, :]))
        p.cop("vector", lambda v, J=J: v.tensor_tensor(T3[:, :], apr3[:, J, :], a128i, ALU.mult))
        p.cop("vector", lambda v, J=J: v.tensor_tensor(T4[:, :], api3[:, J, :], a128r, ALU.mult))
        p.cop("vector", lambda v, J=J: v.tensor_add(api3[:, J + 1, :], T3[:, :], T4[:, :]))
    for J in range(NBLK + 1):
        p.cop("vector", lambda v, J=J: v.tensor_tensor(T3[:, :], apr3[:, J, :], STI[:, 0:8], ALU.mult))
        p.cop("vector", lambda v, J=J: v.tensor_tensor(T4[:, :], api3[:, J, :], STI[:, 8:16], ALU.mult))
        p.cop("vector", lambda v, J=J: v.tensor_sub(T3[:, :], T3[:, :], T4[:, :]))
        p.cop("vector", lambda v, J=J: v.tensor_add(hr3[:, J, :], hr3[:, J, :], T3[:, :]))
        p.cop("vector", lambda v, J=J: v.tensor_tensor(T3[:, :], apr3[:, J, :], STI[:, 8:16], ALU.mult))
        p.cop("vector", lambda v, J=J: v.tensor_tensor(T4[:, :], api3[:, J, :], STI[:, 0:8], ALU.mult))
        p.cop("vector", lambda v, J=J: v.tensor_add(T3[:, :], T3[:, :], T4[:, :]))
        p.cop("vector", lambda v, J=J: v.tensor_add(hi3[:, J, :], hi3[:, J, :], T3[:, :]))

    hcdone = p.vchain("vector")
    p.wait("sync", chv, hcdone)
    YF = [p.sb([128, TL], F32, f"{name}_yf{i}") for i in range(1)]
    UB2 = p.sb([128, TL], BF16, f"{name}_ub2")
    SO1 = p.sb([128, TL], BF16, f"{name}_so1")
    s_yl = p.sem(f"{name}_yl")
    s_ul = p.sem(f"{name}_ul")
    s_os = p.sem(f"{name}_os")
    p.wait("vector", s_y1)
    for dt_i in range(8):
        if dt_i >= 1:
            p.wait("sync", chv)
            p.wait("sync", s_os)
        p.dma("sync", YF[0][:, :], d["yraw1"][dt_i * 128 : (dt_i + 1) * 128, :], s_yl)
        p.dma("sync", UB2[:, :], d["xn"][dt_i * 128 : (dt_i + 1) * 128, HALO:], s_ul)
        p.wait("vector", s_yl)
        p.wait("vector", s_ul)
        er = ER[:, dt_i * 128 : (dt_i + 1) * 128]
        ei = EI[:, dt_i * 128 : (dt_i + 1) * 128]
        for J in range(NBLK):
            p.cop("vector", lambda v, J=J, dt_i=dt_i, er=er: v.tensor_scalar(
                TMW[:, J * 128 : (J + 1) * 128], er,
                hr3[:, J, dt_i : dt_i + 1], None, ALU.mult))
            p.cop("vector", lambda v, J=J, dt_i=dt_i, ei=ei: v.tensor_scalar(
                TC1[:, 0:128], ei, hi3[:, J, dt_i : dt_i + 1], None, ALU.mult))
            p.cop("vector", lambda v, J=J: v.tensor_sub(
                TMW[:, J * 128 : (J + 1) * 128], TMW[:, J * 128 : (J + 1) * 128],
                TC1[:, 0:128]))
        p.cop("vector", lambda v: v.tensor_add(YF[0][:, :], YF[0][:, :], TMW[:, :]))
        vv = p.cop("vector", lambda v, dt_i=dt_i: v.scalar_tensor_tensor(
            SO1[:, :], UB2[:, :], ABC[:, dt_i * 7 + 6 : dt_i * 7 + 7], YF[0][:, :],
            ALU.mult, ALU.add))
        p.wait("scalar", chv, vv)
        p.dma("scalar", d["s1"][dt_i * 128 : (dt_i + 1) * 128, :], SO1[:, :], s_os)
    p.barrier(waits=[(s_os, s_os.n)])
    p.reset(m0)


def l1attn_host(inp):
    cols = []
    for nm in ("A_real", "A_imag", "B_real", "B_imag", "C_real", "C_imag", "D"):
        cols.append(pack_vec(np.asarray(inp[f"L1_attn_{nm}"])))
    out = np.zeros((128, 56), np.float32)
    for i in range(7):
        for dt in range(8):
            out[:, dt * 7 + i] = cols[i][:, dt]
    return dict(abc_col=out, ident_p=np.eye(128, dtype=np.float32))



LN1E8 = -18.420680743952367
E20 = 485165195.4097903


def emit_l3attn(p, d, *, name, wk):
    m0 = p.mark()
    chv = p.chain["vector"]
    chs = p.chain["scalar"]
    TLh = TL + 128

    CVW = p.sb([128, 64], F32, f"{name}_cvw")
    CVB = p.sb([128, 16], F32, f"{name}_cvb")
    DTW = p.sb([128, 16], F32, f"{name}_dtw")
    DTB = p.sb([128, 16], F32, f"{name}_dtb")
    ALOG = p.sb([128, 256], F32, f"{name}_alog")
    DPAR = p.sb([128, 16], F32, f"{name}_dpar")
    ONESR3 = p.sb([1, 128], F32, f"{name}_ones")
    ONESR3B = p.sb([1, 128], BF16, f"{name}_onesb")
    HM3 = p.sb([128, 1], F32, f"{name}_hm")
    s_c0 = p.sem(f"{name}_c0")
    for nmv, buf in (("convw_col", CVW), ("convb_col", CVB), ("dtw_col", DTW),
                     ("dtb_col", DTB), ("alog_col", ALOG), ("dpar_col", DPAR),
                     ("ones_p", ONESR3)):
        p.dma("sync", buf[:, :], wk[nmv], s_c0)
    p.dma("sync", HM3[:, :], d["hm_col"], s_c0)
    p.wait("vector", s_c0)
    p.wait("scalar", s_c0)
    p.wait("tensor", s_c0)
    p.cop("vector", lambda v: v.memset(ONESR3B[:, :], 1.0))
    av = p.cop("scalar", lambda sc: sc.activation(ALOG[:, :], ALOG[:, :], AF.Exp))
    p.wait("vector", chs, av)
    av2 = p.cop("vector", lambda v: v.tensor_scalar(ALOG[:, :], ALOG[:, :], -1.0, None, ALU.mult))

    mseg = p.mark()

    def epi_split(p, c, o, psum, tc_here, s_mm):
        pass

    epi_xz, s_so_xz = epi_store(p, name=f"{name}_xze", dst=d["xz"], dst_c0=0)
    emit_gemm(p, name=f"{name}_xzg", src=d["xn"], src_c0=HALO - 512,
              wsrc=wk["inproj_p"], d_in=D, d_out=4096, cols=TL + 512, epi=epi_xz)
    p.barrier(waits=[(sm, sm.n) for sm in s_so_xz])
    p.reset(mseg)

    XI = p.sb([128, TLh], BF16, f"{name}_xi")
    XC = p.sb([128, TL], BF16, f"{name}_xcb")
    TFD = [p.sb([128, TL], F32, f"{name}_tf{i}") for i in range(2)]
    TF2 = p.sb([128, TL], F32, f"{name}_tf2")
    s_xil = p.sem(f"{name}_xil")
    s_xcs = p.sem(f"{name}_xcs")
    prev_silu = [None]
    for dt16 in range(16):
        if dt16 >= 1:
            p.wait("sync", chv)
            p.wait("sync", s_xcs)
        p.dma("sync", XI[:, :], d["xz"][dt16 * 128 : (dt16 + 1) * 128, 384 : 384 + TLh],
              s_xil)
        p.wait("vector", s_xil)
        p.wait("vector", chs)
        TF = TFD[dt16 % 2]
        p.cop("vector", lambda v, dt16=dt16, TF=TF: v.tensor_scalar(
            TF[:, :], XI[:, 125 : 125 + TL], CVW[:, dt16 * 4 : dt16 * 4 + 1], None, ALU.mult))
        for s_ in (1, 2, 3):
            p.cop("vector", lambda v, dt16=dt16, s_=s_, TF=TF: v.scalar_tensor_tensor(
                TF[:, :], XI[:, 125 + s_ : 125 + s_ + TL],
                CVW[:, dt16 * 4 + s_ : dt16 * 4 + s_ + 1],
                TF[:, :], ALU.mult, ALU.add))
        p.wait("scalar", s_xcs)
        p.wait("scalar", chv)
        vv = p.cop("scalar", lambda sc, dt16=dt16, TF=TF: sc.activation(
            XC[:, :], TF[:, :], AF.Silu, bias=CVB[:, dt16 : dt16 + 1]))
        prev_silu[0] = vv
        p.wait("vector", chs, vv)
        p.wait("scalar", chs, vv)
        p.dma("scalar", d["xc"][dt16 * 128 : (dt16 + 1) * 128, :], XC[:, :], s_xcs)
    p.wait("sync", s_xcs)
    p.wait("gpsimd", s_xcs)

    PS33 = p.ps([33, 512], F32)
    BC3S = p.sb([33, TL], F32, f"{name}_bc3s")
    s_mm3 = p.sem(f"{name}_mm3")
    W33 = p.sb([128, 16 * 33], BF16, f"{name}_w33")
    X33 = [p.sb([128, 16 * TC], BF16, f"{name}_x33{i}") for i in range(2)]
    s_w33 = p.sem(f"{name}_w33")
    s_x33 = p.sems(f"{name}_x33", 2)
    p.dma("gpsimd", W33[:, :], wk["xproj_p"], s_w33)
    p.wait("tensor", s_w33)
    ev33 = {}
    for c in range(TL // TC):
        sl = c % 2
        if c >= 2:
            p.wait("sync", s_mm3, s_mm3.base + c - 1)
        p.dma("sync", X33[sl][:, :].rearrange("q (kt t) -> q kt t", t=TC),
              d["xc"].rearrange("(kt q) t -> q kt t", q=128)[:, :, c * TC : (c + 1) * TC],
              s_x33[sl])
        p.wait("tensor", s_x33[sl])
        if c >= 1:
            p.wait("tensor", chv, ev33[c - 1])

        def mm33(t, sl=sl):
            ins = None
            for k in range(16):
                ins = t.matmul(PS33[:, :], W33[:, k * 33 : (k + 1) * 33],
                               X33[sl][:, k * TC : (k + 1) * TC],
                               start=(k == 0), stop=(k == 15))
            return ins

        p.top(mm33, sem=s_mm3)
        p.wait("vector", s_mm3)
        ev33[c] = p.cop("vector", lambda v, c=c: v.tensor_copy(
            BC3S[:, c * TC : (c + 1) * TC], PS33[:, :]))
    s_bc3 = p.sem(f"{name}_bc3")
    p.wait("scalar", chv, ev33[TL // TC - 1])
    p.dma("scalar", d["bc3"], BC3S[0:32, :], s_bc3)

    PS_OU = p.ps([128, 512], F32)
    DTT = p.sb([128, TL], F32, f"{name}_dtt")
    DTRAW = p.sb([1, TL], F32, f"{name}_dtraw")
    ONE1C = p.sb([128, 1], F32, f"{name}_one1c")
    s_ou = p.sem(f"{name}_ou")
    s_dts = p.sem(f"{name}_dts")
    p.cop("vector", lambda v: v.memset(ONE1C[:, :], 1.0))
    dtr = p.cop("vector", lambda v: v.tensor_copy(DTRAW[0:1, :], BC3S[32:33, :]))
    p.wait("tensor", chv, dtr)
    TF = TFD[0]
    p.wait("vector", chs)
    for dt16 in range(16):
        if dt16 >= 1:
            p.wait("sync", s_dts)
            p.wait("vector", s_dts)
        for c in range(TL // TC):
            if dt16 > 0 or c > 0:
                p.wait("tensor", chv)
            mv = p.top(lambda t, c=c: t.matmul(
                PS_OU[:, :], ONESR3[0:1, :], DTRAW[0:1, c * TC : (c + 1) * TC],
                start=True, stop=True), sem=s_ou)
            p.wait("vector", s_ou, mv)
            p.cop("vector", lambda v, dt16=dt16, c=c: v.tensor_scalar(
                TF[:, c * TC : (c + 1) * TC], PS_OU[:, :],
                DTW[:, dt16 : dt16 + 1], DTB[:, dt16 : dt16 + 1], ALU.mult, ALU.add))
        p.cop("vector", lambda v: v.tensor_scalar(TF2[:, :], TF[:, :], -1.0, None, ALU.mult))
        p.cop("vector", lambda v: v.tensor_tensor(TF2[:, :], TF[:, :], TF2[:, :], ALU.min))
        p.wait("scalar", chv, p.vchain("vector"))
        sv = p.cop("scalar", lambda sc: sc.activation(TF2[:, :], TF2[:, :], AF.Exp))
        sv2 = p.cop("scalar", lambda sc: sc.activation(TF2[:, :], TF2[:, :], AF.Ln,
                                                       bias=ONE1C[:, 0:1]))
        p.wait("vector", chs, sv2)
        p.cop("vector", lambda v: v.tensor_scalar(DTT[:, :], TF[:, :], 0.0, None, ALU.max))
        dv = p.cop("vector", lambda v: v.tensor_add(DTT[:, :], DTT[:, :], TF2[:, :]))
        p.wait("scalar", chv, dv)
        p.dma("scalar", d["dt3"][dt16 * 128 : (dt16 + 1) * 128, :], DTT[:, :], s_dts)
    p.wait("sync", s_dts)
    p.barrier(waits=[(s_bc3, s_bc3.n), (s_dts, s_dts.n), (s_xcs, s_xcs.n)])
    p.reset(mseg)

    B2 = p.sb([128, TL], F32, f"{name}_b2")
    B4 = p.sb([128, TL], F32, f"{name}_b4")
    DB = p.sb([128, TL], F32, f"{name}_db")
    B1S = p.sb([128, TL], BF16, f"{name}_b1s")
    B3S = p.sb([128, TL], BF16, f"{name}_b3s")
    B4S = p.sb([128, TL], BF16, f"{name}_b4s")
    YA = p.sb([128, TL], F32, f"{name}_ya")
    DTT2 = p.sb([128, TL], F32, f"{name}_dtt2")
    DUB = p.sb([128, TL], BF16, f"{name}_dub")
    XCB2 = p.sb([128, TL], BF16, f"{name}_xcb2")
    ZB3 = p.sb([128, TL], BF16, f"{name}_zb3")
    CMSK = p.sb([128, TL], BF16, f"{name}_cmsk")
    BCROW = p.sb([1, TL], BF16, f"{name}_bcrow")
    H0S = p.sb([128, 128], F32, f"{name}_h0s")
    CPS = p.sb([128, 128], F32, f"{name}_cps")
    HPS = p.sb([128, 128], F32, f"{name}_hps")
    T5 = p.sb([128, 128], F32, f"{name}_t5")
    STO = p.sb([128, 256], F32, f"{name}_sto")
    STI3 = p.sb([128, 256], F32, f"{name}_sti3")
    YMB = p.sb([128, TL], BF16, f"{name}_ymb")
    PS_B = [p.ps([128, 512], F32) for i in range(2)]
    s_ld3 = p.sems(f"{name}_ld3", 4)
    s_bcr = p.sem(f"{name}_bcr")
    s_b13 = p.sem(f"{name}_b13")
    s_b4g = p.sem(f"{name}_b4g")
    s_b13r = p.sems(f"{name}_b13r", 2)
    s_cmk = p.sem(f"{name}_cmk")
    s_bb = p.sem(f"{name}_bb")
    s_ag3 = p.sem(f"{name}_ag3")
    s_si3 = p.sem(f"{name}_si3")
    s_ci3 = p.sem(f"{name}_ci3")
    s_ym = p.sem(f"{name}_ym")
    p.dma("gpsimd", CMSK[:, :], wk["cmask"], s_cmk)
    p.wait("vector", s_cmk)

    NCH8 = TL // TC
    pb_hist = []
    prev_e1 = None
    for dt16 in range(16):
        if dt16 >= 1:
            p.wait("sync", chv)
        p.dma("sync", DTT2[:, :], d["dt3"][dt16 * 128 : (dt16 + 1) * 128, :], s_ld3[0])
        p.dma("sync", XCB2[:, :], d["xc"][dt16 * 128 : (dt16 + 1) * 128, :], s_ld3[1])
        p.wait("vector", s_ld3[0])
        p.wait("vector", s_ld3[1])
        p.cop("vector", lambda v: v.tensor_tensor(DUB[:, :], DTT2[:, :], XCB2[:, :], ALU.mult))
        for n in range(16):
            acol = ALOG[:, dt16 * 16 + n : dt16 * 16 + n + 1]
            p.wait("gpsimd", s_bb)
            p.dma("gpsimd", BCROW[0:1, :], d["bc3"][n : n + 1, :], s_bcr)
            p.cop("vector", lambda v, acol=acol: v.tensor_scalar(
                B2[:, :], DTT2[:, :], acol, LN1E8, ALU.mult, ALU.max))
            if prev_e1 is not None:
                p.wait("vector", chs, prev_e1)
            p.wait("vector", s_b4g)
            p.cop("vector", lambda v: v.tensor_tensor_scan(
                B4[:, :], CMSK[:, :], B2[:, :], 0.0, ALU.mult, ALU.add))
            p.wait("scalar", s_b13)
            p.wait("scalar", chv)
            e1 = p.cop("scalar", lambda sc: sc.activation(B1S[:, :], B4[:, :], AF.Exp))
            e2 = p.cop("scalar", lambda sc: sc.activation(B2[:, :], B4[:, :], AF.Exp,
                                                          scale=-1.0))
            prev_e1 = e2
            p.wait("tensor", s_bcr)
            for c in range(NCH8):
                pb = PS_B[c % 2]
                if len(pb_hist) >= 2:
                    p.wait("tensor", chv, pb_hist[-2])
                bv = p.top(lambda t, c=c, pb=pb: t.matmul(
                    pb[:, :], ONESR3B[0:1, :], BCROW[0:1, c * TC : (c + 1) * TC],
                    start=True, stop=True), sem=s_bb)
                p.wait("vector", s_bb, bv)
                pb_hist.append(p.cop("vector", lambda v, c=c, pb=pb: v.tensor_tensor(
                    DB[:, c * TC : (c + 1) * TC], DUB[:, c * TC : (c + 1) * TC],
                    pb[:, :], ALU.mult)))
            p.wait("vector", chs, e2)
            p.wait("vector", s_b13)
            p.cop("vector", lambda v: v.tensor_scalar(B2[:, :], B2[:, :], E20, None, ALU.min))
            p.cop("vector", lambda v: v.tensor_tensor(B3S[:, :], DB[:, :], B2[:, :], ALU.mult))
            p.cop("vector", lambda v: v.tensor_tensor_scan(
                B4[:, :], CMSK[:, :], B3S[:, :], 0.0, ALU.mult, ALU.add))
            p.cop("vector", lambda v: v.tensor_tensor(
                T5[:, :], B1S[:, 31::32], B4[:, 31::32], ALU.mult))
            p.cop("vector", lambda v: v.tensor_tensor_scan(
                H0S[:, :], B1S[:, 31::32], T5[:, :], 0.0, ALU.mult, ALU.add))
            sv3 = p.cop("vector", lambda v, dt16=dt16, n=n: v.tensor_copy(
                STO[:, dt16 * 16 + n : dt16 * 16 + n + 1], H0S[:, 127:128]))
            p.wait("scalar", chv, sv3)
            p.wait("gpsimd", chv, sv3)
            p.dma("scalar", d["b1d"][dt16 * 16 + n], B1S[:, :], s_b13)
            p.dma("gpsimd", d["b3d"][dt16 * 16 + n], B4[:, :], s_b4g)
            p.dma("scalar", d["h0d"][dt16 * 16 + n], H0S[:, :], s_b13)
    p.wait("gpsimd", chv)
    p.dma("gpsimd", d["cc3_in"], STO[:, :], s_ci3)
    p.wait("gpsimd", s_ci3)
    p.raw("gpsimd", lambda g: g.collective_compute(
        "AllGather", ALU.bypass, replica_groups=[[0, 1], [2, 3], [4, 5], [6, 7]],
        ins=[d["cc3_in"].opt()], outs=[d["cc3_out"].opt()]).then_inc(s_ag3.h, 1))
    s_ag3.n += 1
    p.wait("gpsimd", s_ag3)
    p.dma("gpsimd", STI3[:, :], d["cc3_out"][0:128, :], s_si3)
    p.wait("vector", s_si3)
    hm_v = p.cop("vector", lambda v: v.tensor_scalar(
        STI3[:, :], STI3[:, :], HM3[:, 0:1], None, ALU.mult))
    p.wait("vector", s_b13)
    p.wait("vector", s_b4g)
    for dt16 in range(16):
        p.wait("sync", chv)
        p.wait("sync", s_ym)
        p.dma("sync", XCB2[:, :], d["xc"][dt16 * 128 : (dt16 + 1) * 128, :], s_ld3[1])
        p.wait("vector", s_ld3[1])
        yz = p.cop("vector", lambda v: v.memset(YA[:, :], 0.0))
        for n in range(16):
            sl = n % 2
            p.wait("sync", chv)
            p.wait("gpsimd", s_bb)
            p.dma("sync", B1S[:, :], d["b1d"][dt16 * 16 + n], s_b13r[0])
            p.dma("sync", B4S[:, :], d["b3d"][dt16 * 16 + n], s_b13r[0])
            p.dma("sync", H0S[:, :], d["h0d"][dt16 * 16 + n], s_b13r[1])
            p.dma("gpsimd", BCROW[0:1, :], d["bc3"][16 + n : 17 + n, :], s_bcr)
            p.wait("vector", s_b13r[0])
            p.wait("vector", s_b13r[1])
            p.cop("vector", lambda v: v.tensor_tensor_scan(
                CPS[:, :], B1S[:, 31::32], B1S[:, 31::32], 1.0, ALU.mult, ALU.bypass))
            hcol = STI3[:, dt16 * 16 + n : dt16 * 16 + n + 1]
            p.cop("vector", lambda v, hcol=hcol: v.tensor_scalar(
                T5[:, :], CPS[:, :], hcol, None, ALU.mult))
            p.cop("vector", lambda v: v.tensor_add(T5[:, :], T5[:, :], H0S[:, :]))
            p.cop("vector", lambda v: v.tensor_copy(HPS[:, 1:128], T5[:, 0:127]))
            p.cop("vector", lambda v, hcol=hcol: v.tensor_scalar(
                HPS[:, 0:1], hcol, 1.0, None, ALU.mult))
            p.cop("vector", lambda v: v.tensor_tensor(
                B4[:, :].rearrange("p (c k) -> p c k", k=32),
                B4S[:, :].rearrange("p (c k) -> p c k", k=32),
                HPS[:, :].unsqueeze(2).broadcast_to([128, 128, 32]), ALU.add))
            p.cop("vector", lambda v: v.tensor_tensor(B4[:, :], B4[:, :], B1S[:, :], ALU.mult))
            p.wait("tensor", s_bcr)
            for c in range(NCH8):
                pb = PS_B[c % 2]
                if len(pb_hist) >= 2:
                    p.wait("tensor", chv, pb_hist[-2])
                bv = p.top(lambda t, c=c, pb=pb: t.matmul(
                    pb[:, :], ONESR3B[0:1, :], BCROW[0:1, c * TC : (c + 1) * TC],
                    start=True, stop=True), sem=s_bb)
                p.wait("vector", s_bb, bv)
                pb_hist.append(p.cop("vector", lambda v, c=c, pb=pb: v.tensor_tensor(
                    B2[:, c * TC : (c + 1) * TC], B4[:, c * TC : (c + 1) * TC],
                    pb[:, :], ALU.mult)))
            p.cop("vector", lambda v: v.tensor_add(YA[:, :], YA[:, :], B2[:, :]))
        p.dma("sync", ZB3[:, :], d["xz"][2048 + dt16 * 128 : 2048 + (dt16 + 1) * 128,
                                         512 : 512 + TL], s_ld3[2])
        p.wait("vector", s_ld3[2])
        p.wait("scalar", s_ld3[2])
        p.cop("vector", lambda v, dt16=dt16: v.scalar_tensor_tensor(
            YA[:, :], XCB2[:, :], DPAR[:, dt16 : dt16 + 1], YA[:, :], ALU.mult, ALU.add))
        zs = p.cop("scalar", lambda sc: sc.activation(ZB3[:, :], ZB3[:, :], AF.Silu))
        p.wait("vector", chs, zs)
        ymv = p.cop("vector", lambda v: v.tensor_tensor(YMB[:, :], YA[:, :], ZB3[:, :], ALU.mult))
        p.wait("scalar", chv, ymv)
        p.dma("scalar", d["ym"][dt16 * 128 : (dt16 + 1) * 128, :], YMB[:, :], s_ym)
    p.barrier(waits=[(s_ym, s_ym.n)])
    p.reset(mseg)

    epi_o3, s_so_o3 = epi_store(p, name=f"{name}_oe", dst=d["s3"], dst_c0=0)
    emit_gemm(p, name=f"{name}_og", src=d["ym"], src_c0=0, wsrc=wk["outproj_p"],
              d_in=2048, d_out=D, cols=TL, epi=epi_o3)
    p.barrier(waits=[(sm, sm.n) for sm in s_so_o3])
    p.reset(m0)


def l3attn_host(inp):
    cw = np.asarray(inp["L3_attn_conv_w"])[:, 0, :]
    cwp = np.zeros((128, 64), np.float32)
    for dt in range(16):
        for s_ in range(4):
            cwp[:, dt * 4 + s_] = cw[dt * 128 : (dt + 1) * 128, s_]
    al = np.asarray(inp["L3_attn_A_log"])
    alp = np.zeros((128, 256), np.float32)
    for dt in range(16):
        for n in range(16):
            alp[:, dt * 16 + n] = al[dt * 128 : (dt + 1) * 128, n]
    cm = np.ones(TL, np.float32)
    cm[0::32] = 0.0
    return dict(
        inproj_p=pack_w(np.asarray(inp["L3_attn_in_proj_w"])),
        convw_col=cwp,
        convb_col=pack_vec(np.asarray(inp["L3_attn_conv_b"])),
        xproj_p=pack_w(np.asarray(inp["L3_attn_x_proj_w"])),
        dtw_col=pack_vec(np.asarray(inp["L3_attn_dt_w"])[0]),
        dtb_col=pack_vec(np.asarray(inp["L3_attn_dt_b"])),
        alog_col=alp,
        dpar_col=pack_vec(np.asarray(inp["L3_attn_D_param"])),
        outproj_p=pack_w(np.asarray(inp["L3_attn_out_proj_w"])),
        cmask=np.tile(cm, (128, 1)),
        ones_p=np.ones((1, 128), np.float32),
    )




def emit_halo_exchange(p, d, *, name, xkey):
    m0 = p.mark()
    chv = p.chain["vector"]
    s_hi = p.sems(f"{name}_hi", 2)
    s_ag = p.sem(f"{name}_ag")
    s_ho = p.sems(f"{name}_ho", 2)
    HT = [p.sb([128, 8 * TC], F32, f"{name}_ht{i}") for i in range(2)]
    HMX = p.sb([128, 1], F32, f"{name}_hm")
    s_hm = p.sem(f"{name}_hm")
    p.dma("sync", HMX[:, :], d["hm_col"], s_hm)
    s_hg = p.sem(f"{name}_hg")
    p.dma("gpsimd", d["cch_in"], d[xkey][:, HALO + TL - 2048 : HALO + TL], s_hg)
    p.wait("gpsimd", s_hg)
    p.raw("gpsimd", lambda g: g.collective_compute(
        "AllGather", ALU.bypass, replica_groups=[[0, 1], [2, 3], [4, 5], [6, 7]],
        ins=[d["cch_in"].opt()], outs=[d["cch_out"].opt()]).then_inc(s_ag.h, 1))
    s_ag.n += 1
    p.wait("sync", s_ag)
    p.wait("vector", s_hm)
    done = []
    for c in range(4):
        sl = c % 2
        if c >= 2:
            p.wait("sync", chv, done[c - 2])
            p.wait("sync", s_ho[sl])
        p.dma("sync",
              HT[sl][:, :].rearrange("q (kt t) -> q kt t", t=TC),
              d["cch_out"][0:1024].rearrange("(kt q) t -> q kt t", q=128)[
                  :, :, c * TC : (c + 1) * TC],
              s_hi[sl])
        p.wait("vector", s_hi[sl])
        if c >= 2:
            p.wait("vector", s_ho[sl])
        vv = p.cop("vector", lambda v, sl=sl: v.tensor_scalar(
            HT[sl][:, :], HT[sl][:, :], HMX[:, 0:1], None, ALU.mult))
        done.append(vv)
        p.wait("scalar", chv, vv)
        p.dma("scalar",
              d[xkey].rearrange("(kt q) t -> q kt t", q=128)[:, :, c * TC : (c + 1) * TC],
              HT[sl][:, :].rearrange("q (kt t) -> q kt t", t=TC),
              s_ho[sl])
    p.barrier(waits=[(s_ho[0], s_ho[0].n), (s_ho[1], s_ho[1].n)])
    p.reset(m0)




def build_megakernel():
    nc = bass.Bass("TRN2", target_bir_lowering=False, debug=False)
    d = {}

    def din(name, shape, dtype=F32):
        d[name] = nc.dram_tensor(name, list(shape), dtype, kind="ExternalInput").ap()

    def dint(name, shape, dtype):
        d[name] = nc.dram_tensor(name, list(shape), dtype, kind="Internal").ap()

    din("xin0", [D, NCOL])
    d["y"] = nc.dram_tensor("y", [D, TL], F32, kind="ExternalOutput").ap()
    din("hm_col", [128, 1])
    for i in (1, 2, 3):
        dint(f"xh{i}", [D, NCOL], F32)
    dint("xn", [D, NCOL], BF16)
    dint("sbr", [D, TL], F32)
    dint("sbr16", [D, TL], BF16)
    dint("gbr", [D, TL], BF16)
    dint("cch_in", [1024, 2048], F32)
    dint("cch_out", [2048, 2048], F32)

    for nm in ("zlo0", "zhi0", "zlo1", "zhi1"):
        din(nm, [1024, 256])
    din("w_in_p", [128, 8 * 3072]); din("b_in_p", [128, 24]); din("gb_p", [128, 16])
    din("w_out_p", [128, 8 * 1024]); din("b_out_p", [128, 8]); din("ident_p", [128, 128])
    dint("zb", [4, 1024, 256], BF16)
    dint("tmats", [4 * 128 * 1024 * 128], BF16)
    dint("xln", [D, 4608], BF16)
    dint("proj", [3072, 4608], BF16)
    dint("ycv", [D, TL], BF16)

    for li in range(4):
        for nm in ("wqkv_p", "wo_p", "qk_col", "qk_row", "lng_col", "w1_p", "b1_col",
                   "w2_p", "b2_col"):
            shp = {"wqkv_p": [128, 8 * 1536], "wo_p": [128, 4 * 1024],
                   "qk_col": [128, 2], "qk_row": [1, 128], "lng_col": [128, 16],
                   "w1_p": [128, 8 * FFN_W1O], "b1_col": [128, 44],
                   "w2_p": [128, 22 * 1024], "b2_col": [128, 8]}[nm]
            din(f"f{li}_{nm}", shp)
    din("tab_i1", [512, TL + 2048]); din("tab_i2", [512, TL + 1024])
    din("rotq_lhsT", [128, 128]); din("masks", [128, 256]); din("maskh", [128, 128])
    din("ind2", [2, 128]); din("ind2t", [128, 2]); din("ones_p", [1, 128])
    dint("qkv", [1536, NCOL], BF16)
    dint("onrm", [512, NCOL], BF16)
    dint("xmid", [D, NCOL], BF16)
    dint("xf", [D, NCOL], BF16)
    dint("fmid", [D, NCOL], BF16)
    dint("graw", [D, NCOL], BF16)

    din("abc_col", [128, 56])
    dint("z1", [1024, 256], BF16)
    dint("t1m", [128 * 1024 * 128], BF16)
    dint("yraw1", [D, TL], F32)
    dint("cc1_in", [128, 16], F32)
    dint("cc1_out", [256, 16], F32)

    din("l2_wr_p", [128, 8 * 1024]); din("l2_wk_p", [128, 8 * 1024])
    din("l2_wv_p", [128, 8 * 1024]); din("l2_wo_p", [128, 8 * 1024])
    din("l2_decay_col", [128, 8]); din("l2_lnw_col", [128, 8])
    din("l2_iota", [128, TL])
    din("wg_i", [128, 8 * 3072], I32); din("wu_i", [128, 8 * 3072], I32)
    din("wd_i", [128, 24 * 1024], I32)
    for nm, shp in (("r2", [D, TL]), ("k2b", [D, TL]), ("v2", [D, TL]),
                    ("yn2", [D, TL])):
        dint(nm, shp, BF16)
    dint("cum2", [D, TL], F32)
    dint("y2s", [D, TL], F32)
    dint("cc2_in", [128, 8], F32)
    dint("cc2_out", [256, 8], F32)
    dint("wg_b", [128, 8 * 3072], BF16); dint("wu_b", [128, 8 * 3072], BF16)
    dint("wd_b", [128, 24 * 1024], BF16)
    dint("gsig", [3072, TL], BF16)
    dint("up_h", [3072, TL], BF16)

    din("inproj_p", [128, 8 * 4096]); din("convw_col", [128, 64])
    din("convb_col", [128, 16]); din("xproj_p", [128, 16 * 33])
    din("dtw_col", [128, 16]); din("dtb_col", [128, 16]); din("alog_col", [128, 256])
    din("dpar_col", [128, 16]); din("outproj_p", [128, 16 * 1024])
    din("cmask", [128, TL])
    dint("xz", [4096, TL + 512], BF16)
    dint("xc", [2048, TL], BF16)
    dint("dt3", [2048, TL], F32)
    dint("bc3", [32, TL], F32)
    dint("b1d", [256, 128, TL], BF16)
    dint("b3d", [256, 128, TL], BF16)
    dint("h0d", [256, 128, 128], F32)
    dint("ym", [2048, TL], BF16)
    dint("cc3_in", [128, 256], F32)
    dint("cc3_out", [256, 256], F32)
    din("l3f_wr_p", [128, 8 * 1024]); din("l3f_wk_p", [128, 8 * 1024])
    din("l3f_wv_p", [128, 8 * 1024])
    din("l3f_mur_col", [128, 8]); din("l3f_muk_col", [128, 8])
    dint("xr", [D, TL], BF16); dint("xk", [D, TL], BF16)
    dint("rform", [D, TL], BF16); dint("k2", [D, TL], BF16)

    with ExitStack() as st:
        p = Prog(nc, st)

        def wkv(prefix, names):
            return {nm: d[f"{prefix}{nm}"] for nm in names}

        for li in range(4):
            xsrc = d["xin0"] if li == 0 else d[f"xh{li}"]
            m = p.mark()
            lnd = emit_lngb(p, name=f"ln{li}", xsrc=xsrc, xdst=d["xn"], c0=0,
                            cols=NCOL, src_f32=True)
            p.barrier(waits=lnd)
            p.reset(m)

            if li == 0:
                d["s0"] = d["sbr"]
                emit_l0attn(p, d, name="l0a")
                s_key, s_c0 = "sbr", 0
            elif li == 1:
                d["s1"] = d["sbr16"]
                emit_l1attn(p, d, name="l1a", wk=wkv("", ["abc_col", "ident_p"]))
                s_key, s_c0 = "sbr16", 0
            elif li == 2:
                d["s2"] = d["sbr16"]
                emit_l2attn(p, d, name="l2a", wk={
                    "wr_p": d["l2_wr_p"], "wk_p": d["l2_wk_p"], "wv_p": d["l2_wv_p"],
                    "wo_p": d["l2_wo_p"], "decay_col": d["l2_decay_col"],
                    "lnw_col": d["l2_lnw_col"], "iota": d["l2_iota"]})
                s_key, s_c0 = "sbr16", 0
            else:
                d["s3"] = d["sbr16"]
                emit_l3attn(p, d, name="l3a", wk=wkv("", [
                    "inproj_p", "convw_col", "convb_col", "xproj_p", "dtw_col",
                    "dtb_col", "alog_col", "dpar_col", "outproj_p", "cmask", "ones_p"]))
                s_key, s_c0 = "sbr16", 0

            if li in (0, 1):
                base = 2 * li
                for inner in range(2):
                    wk = {nm: d[f"f{base+inner}_{nm}"] for nm in (
                        "wqkv_p", "wo_p", "qk_col", "qk_row", "lng_col", "w1_p",
                        "b1_col", "w2_p", "b2_col")}
                    wk["tab"] = d["tab_i1"] if inner == 0 else d["tab_i2"]
                    for nm in ("rotq_lhsT", "masks", "maskh", "ind2", "ind2t",
                               "ones_p", "ident_p"):
                        wk[nm] = d[nm]
                    src = d["xn"] if inner == 0 else d["fmid"]
                    dst = d["fmid"] if inner == 0 else d["graw"]
                    q0 = -1536 if inner == 0 else -512
                    emit_ffn_inner(p, d, name=f"ff{li}{inner}", src=src, dst=dst,
                                   q0=q0, wk=wk, mlp_q0=(0 if inner == 1 else None))
                g_key, g_c0, g_dt = "graw", HALO, BF16
            elif li == 2:
                d["g2"] = d["gbr"]
                emit_l2ffn(p, d, name="l2f", wk={})
                g_key, g_c0, g_dt = "gbr", 0, BF16
            else:
                d["g3"] = d["gbr"]
                emit_l3ffn(p, d, name="l3f", wk={
                    "wr_p": d["l3f_wr_p"], "wk_p": d["l3f_wk_p"],
                    "wv_p": d["l3f_wv_p"], "mur_col": d["l3f_mur_col"],
                    "muk_col": d["l3f_muk_col"]})
                g_key, g_c0, g_dt = "gbr", 0, BF16

            xout_key = f"xh{li+1}" if li < 3 else "y"
            emit_combine(p, d, name=f"cmb{li}", s_key=s_key, s_c0=s_c0,
                         s_dt=F32 if li == 0 else BF16,
                         g_key=g_key, g_c0=g_c0, xin_key=("xin0" if li == 0 else f"xh{li}"),
                         xout_key=xout_key, out_is_flat=(li == 3))
            if li < 3:
                emit_halo_exchange(p, d, name=f"hx{li}", xkey=f"xh{li+1}")
        p.build()
    return nc, d


def host_prep(inputs):
    sh = {}
    sh.update(l0attn_host(inputs, None))
    c = ffn_host_consts()
    sh.update({k: v for k, v in c.items() if k != "tab"})
    tabfull = None
    TW = T + 2048
    tg = np.arange(-2048, T, dtype=np.float64)
    half = np.arange(0, 64, 2, dtype=np.float64)
    inv_freq = 1.0 / 10000.0 ** (half / 64.0)
    freqs = np.concatenate([tg[None, :] * inv_freq[:, None]] * 2, 0)
    base = (half + 25.6) / 89.6
    power = (tg - 4096.0) / 256.0
    scale = np.concatenate([base[:, None] ** power[None, :]] * 2, 0)
    tabfull = np.concatenate(
        [np.tile(x, (2, 1)) for x in (
            (np.cos(freqs) * scale), (np.sin(freqs) * scale),
            (np.cos(freqs) / scale), (np.sin(freqs) / scale))], 0
    ).astype(np.float32)
    for li in range(2):
        for inner in range(2):
            wqkv = np.asarray(inputs[f"L{li}_ffn_wqkv"])[inner]
            pk = ffn_host_layer(
                wqkv,
                np.asarray(inputs[f"L{li}_ffn_q_scale"])[inner],
                np.asarray(inputs[f"L{li}_ffn_k_scale"])[inner],
                np.asarray(inputs[f"L{li}_ffn_wo"])[inner],
                np.asarray(inputs[f"L{li}_ffn_ln_g"])[inner],
                np.asarray(inputs[f"L{li}_ffn_ln_b"])[inner],
                np.asarray(inputs[f"L{li}_ffn_w1"])[inner],
                np.asarray(inputs[f"L{li}_ffn_b1"])[inner],
                np.asarray(inputs[f"L{li}_ffn_w2"])[inner],
                np.asarray(inputs[f"L{li}_ffn_b2"])[inner],
            )
            for k_, v_ in pk.items():
                sh[f"f{2*li+inner}_{k_}"] = v_
    sh["abc_col"] = l1attn_host(inputs)["abc_col"]
    l2 = l2attn_host(inputs)
    sh.update({"l2_wr_p": l2["wr_p"], "l2_wk_p": l2["wk_p"], "l2_wv_p": l2["wv_p"],
               "l2_wo_p": l2["wo_p"], "l2_decay_col": l2["decay_col"],
               "l2_lnw_col": l2["lnw_col"]})
    sh.update(l2ffn_host(inputs))
    sh.update(l3attn_host(inputs))
    l3f = l3ffn_host(inputs)
    sh.update({"l3f_wr_p": l3f["wr_p"], "l3f_wk_p": l3f["wk_p"],
               "l3f_wv_p": l3f["wv_p"], "l3f_mur_col": l3f["mur_col"],
               "l3f_muk_col": l3f["muk_col"]})

    x0 = np.asarray(inputs["L0_attn_x"], np.float32)
    iota_full = np.tile(np.arange(T, dtype=np.float32), (128, 1))
    percore = []
    for core in range(8):
        b, h = core // 2, core % 2
        g_t0 = h * TL
        m = dict(sh)
        xin = np.zeros((D, NCOL), np.float32)
        xin[:, HALO:] = x0[b, g_t0 : g_t0 + TL].T
        if h == 1:
            xin[:, :HALO] = x0[b, g_t0 - HALO : g_t0].T
        m["xin0"] = xin
        m["hm_col"] = np.full((128, 1), float(h), np.float32)
        m["maskh"] = np.full((128, 128), -1e30 if h == 0 else 0.0, np.float32)
        m["tab_i1"] = np.ascontiguousarray(
            tabfull[:, g_t0 + (-2048) + 2048 : g_t0 + TL + 2048])
        m["tab_i2"] = np.ascontiguousarray(
            tabfull[:, g_t0 + (-1024) + 2048 : g_t0 + TL + 2048])
        m["l2_iota"] = np.ascontiguousarray(iota_full[:, g_t0 : g_t0 + TL])
        percore.append(m)
    return percore


def kernel(**inputs):
    nc, _ = build_megakernel()
    percore = host_prep(inputs)
    res = run_bass_kernel_spmd(nc, percore, core_ids=list(range(8)))
    out = np.zeros((B, T, D), np.float32)
    for core in range(8):
        b, h = core // 2, core % 2
        out[b, h * TL : (h + 1) * TL] = res.results[core]["y"].T
    return out
